# version 05
# baseline (speedup 1.0000x reference)
import numpy as np
from contextlib import ExitStack

import concourse.bass as bass
import concourse.mybir as mybir
from concourse.alu_op_type import AluOpType as OP
from concourse.bass_utils import run_bass_kernel_spmd

F32 = mybir.dt.float32
BF16 = mybir.dt.bfloat16
AF = mybir.ActivationFunctionType
AX = mybir.AxisListType


class CFG:
    n_cores = 8
    L = 16384
    D = 1024
    H = 16
    dh = 64
    C = 128
    GS = 512
    EPS = 1e-6
    LN_EPS = 1e-5
    debug = False
    trunc = None

    @property
    def R(self): return self.L // self.n_cores
    @property
    def KT(self): return self.D // 128
    @property
    def NG(self): return self.R // self.GS
    @property
    def NC(self): return self.R // self.C


DMA_STREAMS = ("sp", "aq", "gq", "cc")
STREAM_SECTION = {
    "tensor": "tensor", "vector": "vector", "scalar": "scalar",
    "gpsimd": "gpsimd", "sync": "sync",
    "sp": "sync", "aq": "scalar", "gq": "gpsimd", "cc": "gpsimd",
}


class Pl:

    def __init__(self):
        self.items = []
        self.barrier_snap = {}
        self.ordn = {}

    def add(self, stream, fn, reads=(), writes=(), inc=None):
        if inc is None:
            inc = 16 if stream in DMA_STREAMS else 1
        if stream in DMA_STREAMS and stream != "cc":
            o = self.ordn.get(stream, 0)
            self.ordn[stream] = o + 1
            semid = f"{stream}{o % self.NROT}"
        else:
            semid = stream
        self.items.append(dict(s=stream, fn=fn, r=tuple(reads), w=tuple(writes),
                               inc=inc, deps=set(), bar=dict(self.barrier_snap),
                               semid=semid))

    def barrier(self):
        snap = {}
        for i, it in enumerate(self.items):
            snap[it["semid"]] = i
        self.barrier_snap = snap

    NROT = 4

    marks = []

    def mark(self, name):
        self.marks.append((name, len(self.items)))

    def finalize(self, truncate=None, skip=()):
        if truncate is not None:
            self.items = self.items[:truncate]
        if skip:
            self.items = [it for i, it in enumerate(self.items) if i not in skip]
        lastuse = {}
        lastw, readers = {}, {}
        last_psum_rd = {}
        for i, it in enumerate(self.items):
            dep = set(it["bar"].values())
            for r in it["r"]:
                if r in lastw:
                    dep.add(lastw[r])
                if r.startswith("PSB"):
                    pr = last_psum_rd.get(r)
                    if pr is not None and pr[1] != it["s"]:
                        dep.add(pr[0])
                    last_psum_rd[r] = (i, it["s"])
            for w in it["w"]:
                if w in lastw:
                    dep.add(lastw[w])
                for rd in readers.get(w, ()):
                    dep.add(rd)
            for r in it["r"]:
                readers.setdefault(r, []).append(i)
            for w in it["w"]:
                lastw[w] = i
                readers[w] = []
                if w.startswith("PSB"):
                    last_psum_rd.pop(w, None)
            if it["s"] in DMA_STREAMS:
                if it["semid"] in lastuse:
                    dep.add(lastuse[it["semid"]])
                lastuse[it["semid"]] = i
            it["deps"] = {d for d in dep
                          if not (self.items[d]["s"] == it["s"] and it["s"] == "tensor")}
        need = [False] * len(self.items)
        for i, it in enumerate(self.items):
            if it["s"] in DMA_STREAMS:
                need[i] = True
            for d in it["deps"]:
                need[d] = True
        cnt = {}
        self.after = [0] * len(self.items)
        self.neednc = need
        for i, it in enumerate(self.items):
            c = cnt.get(it["semid"], 0)
            if need[i]:
                c += it["inc"]
            cnt[it["semid"]] = c
            self.after[i] = c

    def emit(self, nc, sems):
        by_section = {}
        for i, it in enumerate(self.items):
            by_section.setdefault(STREAM_SECTION[it["s"]], []).append(i)
        with nc.Block() as block:
            for section in ("sync", "scalar", "vector", "tensor", "gpsimd"):
                idxs = by_section.get(section, [])
                dec = getattr(block, section)

                def make(idxs=idxs):
                    def body(eng):
                        waited = {}
                        for i in idxs:
                            it = self.items[i]
                            wmax = {}
                            for d in it["deps"]:
                                ds = self.items[d]["semid"]
                                v = self.after[d]
                                if v > wmax.get(ds, 0):
                                    wmax[ds] = v
                            for ds, v in sorted(wmax.items()):
                                if waited.get(ds, 0) < v:
                                    eng.wait_ge(sems[ds], v)
                                    waited[ds] = v
                            inst = it["fn"](eng)
                            if self.neednc[i]:
                                inst.then_inc(sems[it["semid"]], it["inc"])
                    return body
                dec(make())


def dup2(ap64):
    return ap64.rearrange("p (o d) -> p o d", o=1).broadcast_to([ap64.shape[0], 2, 64])


def build_program(cfg: CFG, part: str = "p1"):
    R, D, H, dh, C, GS = cfg.R, cfg.D, cfg.H, cfg.dh, cfg.C, cfg.GS
    KT, NG, NC = cfg.KT, cfg.NG, cfg.NC
    NLEV = 6
    assert C == 2 ** (NLEV + 1)
    n_cores = cfg.n_cores
    NT = R // 128

    nc = bass.Bass("TRN2", target_bir_lowering=False)
    P = Pl()

    dt = lambda name, shape, dtype, kind: nc.dram_tensor(name, shape, dtype, kind=kind).ap()
    x_tok_d = dt("x_tok", [R, D], F32, "ExternalInput")
    x_t_d = dt("x_t", [D, R + 3], F32, "ExternalInput")
    x_halo_d = dt("x_halo", [3, D], F32, "ExternalInput")
    WT_d = {p: dt(f"W{p}T", [D, D], F32, "ExternalInput") for p in "qkvo"}
    WbT_d = dt("WbT", [D, H], F32, "ExternalInput")
    conv_d = {p: dt(f"conv_{p}", [D, 4], F32, "ExternalInput") for p in "qkv"}
    onw_d = dt("onw128", [128, 1], F32, "ExternalInput")
    bias_d = dt("biasrow", [1, D], F32, "ExternalInput")
    E1_d = {ri: dt(f"E1{ri}", [4, KT // 2, 128, 512], F32, "ExternalInput") for ri in "RI"}
    E2_d = {ri: dt(f"E2{ri}", [4, 128, 512], F32, "ExternalInput") for ri in "RI"}
    mask_sl_d = dt("mask_sl", [128, 128], F32, "ExternalInput")
    mask_sun_d = dt("mask_sun", [128, 128], F32, "ExternalInput")
    mask_sui_d = dt("mask_sui", [128, 128], F32, "ExternalInput")
    ident_f_d = dt("ident_f", [128, 128], F32, "ExternalInput")
    ident_b_d = dt("ident_b", [128, 128], BF16, "ExternalInput")
    ident64_d = dt("ident64_2", [128, 64], BF16, "ExternalInput")
    ones1_d = dt("ones1", [1, 128], F32, "ExternalInput")
    ppinit_d = dt("ppinit", [64, H * 64], BF16, "ExternalInput")
    onehot_d = dt("onehot", [1, n_cores], F32, "ExternalInput")
    ind_d = dt("ind", [128, KT * H], F32, "ExternalInput")
    indT_d = dt("indT", [H, KT * 128], F32, "ExternalInput")

    out_d = dt("out", [R, D], F32, "ExternalOutput") if part != "p1" else None

    mid_kind = {"full": "Internal", "p1": "ExternalOutput", "p2": "ExternalInput"}[part]
    if part != "p2":
        krm_d = dt("krm_dram", [R, 2 * D], BF16, "Internal")
        vrm_d = dt("vrm_dram", [R, D], BF16, "Internal")
    o0_d = dt("o0_dram", [R, D], BF16, mid_kind)
    qt_d = dt("qt_dram", [128, NC * KT * 128], BF16, mid_kind)
    pt_d = dt("pt_dram", [64, NC * H * 64], BF16, mid_kind)
    msrow_d = dt("msrow_dram", [2, R + 3], F32, mid_kind)
    if part == "full":
        cc_in_d = dt("cc_in", [64, H * 128], F32, "Internal")
        cc_out_d = dt("cc_out", [n_cores * 64, H * 128], F32, "Internal")
    elif part == "p1":
        cc_out1_d = dt("cc_stage_out", [64, H * 128], F32, "ExternalOutput")
    else:
        cc_out_d = dt("cc_gathered", [n_cores * 64, H * 128], F32, "ExternalInput")
    dbg_d = dt("dbg", [128, 32 * 128], F32, "Internal") if cfg.debug else None
    def dbgtap(slot, ap, readkeys):
        if cfg.debug:
            P.add("gq", lambda e, slot=slot, ap=ap: e.dma_start(
                out=dbg_d[:ap.shape[0], slot * 128: slot * 128 + ap.shape[1]], in_=ap),
                reads=tuple(readkeys), writes=("dbg",))

    st = ExitStack()
    sb = lambda name, shape, dtype: st.enter_context(nc.sbuf_tensor(name, shape, dtype))
    ps_ = lambda name, shape, dtype: st.enter_context(nc.psum_tensor(name, shape, dtype))

    sem_names = ["tensor", "vector", "scalar", "gpsimd", "sync", "cc"]
    for s in ("sp", "aq", "gq"):
        sem_names += [f"{s}{r}" for r in range(Pl.NROT)]
    sems = {s: st.enter_context(nc.semaphore(f"sem_{s}")) for s in sem_names}

    psb = [ps_(f"psb{i}", [128, 512], F32) for i in range(6)]
    psT0 = ps_("psT0", [128, 512], BF16)
    psT1 = ps_("psT1", [128, 512], BF16)

    mask_sl = sb("mask_sl_s", [128, 128], F32)
    mask_sun = sb("mask_sun_s", [128, 128], F32)
    mask_sui = sb("mask_sui_s", [128, 128], F32)
    ident_f = sb("ident_f_s", [128, 128], F32)
    ident_b = sb("ident_b_s", [128, 128], BF16)
    ident64 = sb("ident64_s", [128, 64], BF16)
    ones1 = sb("ones1_s", [1, 128], F32)
    onw = sb("onw_s", [128, 1], F32)
    biasrow = sb("biasrow_s", [1, D], F32)
    onehot = sb("onehot_s", [1, n_cores], F32)
    ind = sb("ind_s", [128, KT * H], F32)
    indT = sb("indT_s", [H, KT * 128], F32)
    S_f = sb("S_f", [128, H * 64], F32)
    S_b = sb("S_b", [128, H * 64], BF16)
    PP_b = sb("PP_b", [64, H * 64], BF16)
    N_f = sb("N_f", [128, H * 64], F32)
    N_b = sb("N_b", [128, H * 64], BF16)
    epsln = sb("epsln", [128, 1], F32)
    epsrms = sb("epsrms", [128, 1], F32)

    def ld(dst, src, key, stream="sp"):
        P.add(stream, lambda e, d=dst, s=src: e.dma_start(out=d, in_=s), writes=(key,))

    ld(mask_sl[:], mask_sl_d[:, :], "mask_sl")
    ld(mask_sun[:], mask_sun_d[:, :], "mask_sun")
    ld(mask_sui[:], mask_sui_d[:, :], "mask_sui")
    ld(ident_f[:], ident_f_d[:, :], "ident_f")
    ld(ident_b[:], ident_b_d[:, :], "ident_b")
    ld(ident64[:], ident64_d[:, :], "ident64")
    ld(ones1[:], ones1_d[:, :], "ones1")
    ld(onw[:], onw_d[:, :], "onw")
    ld(biasrow[:], bias_d[:, :], "biasrow")
    ld(onehot[:], onehot_d[:, :], "onehot")
    ld(ind[:], ind_d[:, :], "ind")
    ld(indT[:], indT_d[:, :], "indT")
    P.add("sp", lambda e: e.dma_start(out=PP_b[:], in_=ppinit_d[:, :]), writes=tuple(f"PP_b:{h}" for h in range(H)))
    P.add("gpsimd", lambda e: e.memset(epsln[:], cfg.LN_EPS), writes=("epsln",))
    P.add("gpsimd", lambda e: e.memset(epsrms[:], cfg.EPS), writes=("epsrms",))
    P.add("gpsimd", lambda e: e.memset(S_f[:], 0.0), writes=tuple(f"S_f:{h}" for h in range(H)))
    P.add("gpsimd", lambda e: e.memset(S_b[:], 0.0), writes=tuple(f"S_b:{h}" for h in range(H)))

    run_p1 = part in ("full", "p1")
    run_p2 = part in ("full", "p2")
    stA0 = ExitStack()
    if run_p1:
        sb0 = lambda name, shape, dtype: stA0.enter_context(nc.sbuf_tensor(name, shape, dtype))
        stat_pack = sb0("stat_pack", [128, 2 * NT], F32)
        stat_sum = sb0("stat_sum", [128, NT], F32)
        stat_ssq = sb0("stat_ssq", [128, NT], F32)
        stat_tmp = sb0("stat_tmp", [128, NT], F32)
        evac32 = sb0("evac32", [2 * NT, 128], F32)
        xa = [sb0(f"xa{i}", [128, D], F32) for i in range(2)]
        xsq = sb0("xsq", [128, D], F32)
        for t in range(NT):
            b = t % 2
            P.add("sp", lambda e, b=b, t=t: e.dma_start(out=xa[b][:], in_=x_tok_d[t * 128:(t + 1) * 128, :]),
                  writes=(f"xa{b}",))
            P.add("vector", lambda e, b=b, t=t: e.reduce_sum(stat_sum[:, t:t + 1], xa[b][:], axis=AX.X),
                  reads=(f"xa{b}",), writes=("stat_sum",))
            P.add("scalar", lambda e, b=b, t=t: e.activation(xsq[:], xa[b][:], AF.Square, accum_out=stat_ssq[:, t:t + 1]),
                  reads=(f"xa{b}",), writes=("stat_ssq", "xsq"))
        xh = sb0("xh", [3, D], F32)
        xhsq = sb0("xhsq", [3, D], F32)
        halo_sum = sb0("halo_sum", [3, 1], F32)
        halo_var = sb0("halo_var", [3, 1], F32)
        halo_m2 = sb0("halo_m2", [3, 1], F32)
        halo_s = sb0("halo_s", [3, 2], F32)
        P.add("sp", lambda e: e.dma_start(out=xh[:], in_=x_halo_d[:, :]), writes=("xh",))
        P.add("vector", lambda e: e.reduce_sum(halo_sum[:], xh[:], axis=AX.X),
              reads=("xh",), writes=("halo_sum",))
        P.add("scalar", lambda e: e.activation(xhsq[:], xh[:], AF.Square, accum_out=halo_var[:]),
              reads=("xh",), writes=("halo_var", "xhsq"))
        P.add("vector", lambda e: e.tensor_scalar(stat_sum[:], stat_sum[:], 1.0 / D, 0.0, op0=OP.mult, op1=OP.add),
              reads=("stat_sum",), writes=("stat_sum",))
        P.add("vector", lambda e: e.scalar_tensor_tensor(stat_tmp[:], stat_sum[:], 1.0, stat_sum[:], op0=OP.mult, op1=OP.mult),
              reads=("stat_sum",), writes=("stat_tmp",))
        P.add("vector", lambda e: e.scalar_tensor_tensor(stat_ssq[:], stat_ssq[:], 1.0 / D, stat_tmp[:], op0=OP.mult, op1=OP.subtract),
              reads=("stat_ssq", "stat_tmp"), writes=("stat_ssq",))
        P.add("scalar", lambda e: e.activation(stat_ssq[:], stat_ssq[:], AF.Sqrt, bias=epsln[:]),
              reads=("stat_ssq", "epsln"), writes=("stat_ssq",))
        P.add("vector", lambda e: e.reciprocal(stat_pack[:, NT:2 * NT], stat_ssq[:]),
              reads=("stat_ssq",), writes=("stat_pack",))
        P.add("vector", lambda e: e.scalar_tensor_tensor(stat_pack[:, 0:NT], stat_sum[:], -1.0, stat_pack[:, NT:2 * NT], op0=OP.mult, op1=OP.mult),
              reads=("stat_sum", "stat_pack"), writes=("stat_pack",))
        P.add("vector", lambda e: e.tensor_scalar(halo_sum[:], halo_sum[:], 1.0 / D, 0.0, op0=OP.mult, op1=OP.add),
              reads=("halo_sum",), writes=("halo_sum",))
        P.add("vector", lambda e: e.scalar_tensor_tensor(halo_m2[:], halo_sum[:], 1.0, halo_sum[:], op0=OP.mult, op1=OP.mult),
              reads=("halo_sum",), writes=("halo_m2",))
        P.add("vector", lambda e: e.scalar_tensor_tensor(halo_var[:], halo_var[:], 1.0 / D, halo_m2[:], op0=OP.mult, op1=OP.subtract),
              reads=("halo_var", "halo_m2"), writes=("halo_var",))
        P.add("scalar", lambda e: e.activation(halo_var[:], halo_var[:], AF.Sqrt, bias=epsln[0:3, :]),
              reads=("halo_var", "epsln"), writes=("halo_var",))
        P.add("vector", lambda e: e.reciprocal(halo_s[:, 1:2], halo_var[:]),
              reads=("halo_var",), writes=("halo_s",))
        P.add("vector", lambda e: e.scalar_tensor_tensor(halo_s[:, 0:1], halo_sum[:], -1.0, halo_s[:, 1:2], op0=OP.mult, op1=OP.mult),
              reads=("halo_sum", "halo_s"), writes=("halo_s",))
        P.add("tensor", lambda e: e.transpose(psb[0][0:2 * NT, 0:128], stat_pack[:], ident_f[:]),
              reads=("stat_pack", "ident_f"), writes=("PSB0",))
        P.add("vector", lambda e: e.tensor_copy(evac32[:], psb[0][0:2 * NT, 0:128]),
              reads=("PSB0",), writes=("evac32",))
        P.add("aq", lambda e: e.dma_start(out=msrow_d[0:1, 3:3 + NT * 128].rearrange("o (t p) -> (o t) p", t=NT),
                                          in_=evac32[0:NT, :]),
              reads=("evac32",), writes=("msrow_d",))
        P.add("aq", lambda e: e.dma_start(out=msrow_d[1:2, 3:3 + NT * 128].rearrange("o (t p) -> (o t) p", t=NT),
                                          in_=evac32[NT:2 * NT, :]),
              reads=("evac32",), writes=("msrow_d",))
        P.add("aq", lambda e: e.dma_start(out=msrow_d[0:1, 0:3].rearrange("o t -> t o"), in_=halo_s[:, 0:1]),
              reads=("halo_s",), writes=("msrow_d",))
        P.add("aq", lambda e: e.dma_start(out=msrow_d[1:2, 0:3].rearrange("o t -> t o"), in_=halo_s[:, 1:2]),
              reads=("halo_s",), writes=("msrow_d",))
        P.barrier()
        stA0.close()

        P.mark("A0_done")
        stABC = ExitStack()
        sbB = lambda name, shape, dtype: stABC.enter_context(nc.sbuf_tensor(name, shape, dtype))
        qtf = sbB("qtf", [128, KT * R], BF16)
        ktf = sbB("ktf", [128, KT * R], BF16)
        vtf = sbB("vtf", [128, KT * R], BF16)
        RES = {"q": qtf, "k": ktf, "v": vtf}

        stA = ExitStack()
        sbA = lambda name, shape, dtype: stA.enter_context(nc.sbuf_tensor(name, shape, dtype))
        wbuf = [sbA(f"wbuf{i}", [128, D], BF16) for i in range(2)]
        pbuf = [sbA(f"pbuf{i}", [128, GS + 3], BF16) for i in range(KT)]
        tails = sbA("tails", [128, 3 * 3 * KT], BF16)
        xnf = sbA("xnf", [128, KT * GS], BF16)
        xnf_h = sbA("xnf_h", [128, KT * 3], BF16)
        xtg = sbA("xtg", [128, KT * GS], BF16)
        xtg_h = sbA("xtg_h", [128, KT * 3], BF16)
        msbc = sbA("msbc", [128, GS], BF16)
        sbc = sbA("sbc", [128, GS], BF16)
        msbc_h = sbA("msbc_h", [128, 3], BF16)
        sbc_h = sbA("sbc_h", [128, 3], BF16)
        conv_sb = sbA("conv_sb", [128, KT * 4 * 3], F32)
        ybuf = sbA("ybuf", [128, GS], BF16)
        WbT_sb = sbA("WbT_sb", [128, KT * H], BF16)
        msg = sbA("msg", [1, GS], F32)
        ssg = sbA("ssg", [1, GS], F32)
        msg_h = sbA("msg_h", [1, 3], F32)
        ssg_h = sbA("ssg_h", [1, 3], F32)
        beta_hm = sbA("beta_hm", [H, R], F32)
        scale_q = sbA("scale_q", [H, R], F32)
        scale_k = sbA("scale_k", [H, R], F32)
        scale_v = sbA("scale_v", [H, R], F32)

        for pi, p in enumerate("qkv"):
            P.add("gq", lambda e, pi=pi, p=p: e.dma_start(
                out=conv_sb[:, pi * KT * 4:(pi + 1) * KT * 4].rearrange("p (a k) -> p a k", k=4),
                in_=conv_d[p].rearrange("(a p) k -> p a k", p=128)),
                writes=("conv_sb",))
        P.add("gq", lambda e: e.dma_start(out=WbT_sb[:].rearrange("p (a h) -> p a h", a=KT),
                                          in_=WbT_d.rearrange("(a p) h -> p a h", p=128)),
              writes=("WbT_sb",))

        def emit_xnf(grp, dst, dkey, xsrc, xkey, mb, sbuf_b, with_halo, dst_h=None, hx=None, mbh=None, sbh=None,
                     mrow=None, srw=None, mrow_h=None, srw_h=None):
            c0 = 3 + grp * GS
            for t in range(KT):
                P.add("gq", lambda e, t=t, c0=c0, xsrc=xsrc: e.dma_start(
                    out=xsrc[:, t * GS:(t + 1) * GS], in_=x_t_d[t * 128:(t + 1) * 128, c0:c0 + GS]),
                    writes=(f"{xkey}:{t}",))
            P.add("sp", lambda e, c0=c0, mrow=mrow: e.dma_start(out=mrow[:], in_=msrow_d[0:1, c0:c0 + GS]),
                  reads=("msrow_d",), writes=(f"{dkey}_mrow",))
            P.add("sp", lambda e, c0=c0, srw=srw: e.dma_start(out=srw[:], in_=msrow_d[1:2, c0:c0 + GS]),
                  reads=("msrow_d",), writes=(f"{dkey}_srow",))
            P.add("tensor", lambda e, mrow=mrow: e.matmul(psb[3][:, 0:GS], ones1[:], mrow[:], start=True, stop=True),
                  reads=("ones1", f"{dkey}_mrow"), writes=("PSB3",))
            P.add("tensor", lambda e, srw=srw: e.matmul(psb[4][:, 0:GS], ones1[:], srw[:], start=True, stop=True),
                  reads=("ones1", f"{dkey}_srow"), writes=("PSB4",))
            P.add("scalar", lambda e, mb=mb: e.copy(mb[:], psb[3][:, 0:GS]), reads=("PSB3",), writes=(f"{dkey}_mb",))
            P.add("scalar", lambda e, sbuf_b=sbuf_b: e.copy(sbuf_b[:], psb[4][:, 0:GS]), reads=("PSB4",), writes=(f"{dkey}_sb",))
            if with_halo:
                for t in range(KT):
                    P.add("gq", lambda e, t=t, hx=hx: e.dma_start(
                        out=hx[:, t * 3:(t + 1) * 3], in_=x_t_d[t * 128:(t + 1) * 128, 0:3]),
                        writes=(f"xtg_h:{t}",))
                P.add("sp", lambda e, mrow_h=mrow_h: e.dma_start(out=mrow_h[:], in_=msrow_d[0:1, 0:3]),
                      reads=("msrow_d",), writes=("mrow_h",))
                P.add("sp", lambda e, srw_h=srw_h: e.dma_start(out=srw_h[:], in_=msrow_d[1:2, 0:3]),
                      reads=("msrow_d",), writes=("srow_h",))
                P.add("tensor", lambda e, mrow_h=mrow_h: e.matmul(psb[3][:, 0:3], ones1[:], mrow_h[:], start=True, stop=True),
                      reads=("ones1", "mrow_h"), writes=("PSB3",))
                P.add("tensor", lambda e, srw_h=srw_h: e.matmul(psb[4][:, 0:3], ones1[:], srw_h[:], start=True, stop=True),
                      reads=("ones1", "srow_h"), writes=("PSB4",))
                P.add("scalar", lambda e, mbh=mbh: e.copy(mbh[:], psb[3][:, 0:3]), reads=("PSB3",), writes=("msbc_h",))
                P.add("scalar", lambda e, sbh=sbh: e.copy(sbh[:], psb[4][:, 0:3]), reads=("PSB4",), writes=("sbc_h",))
            for t in range(KT):
                P.add("vector", lambda e, t=t, dst=dst, xsrc=xsrc, mb=mb: e.tensor_tensor(
                    dst[:, t * GS:(t + 1) * GS], xsrc[:, t * GS:(t + 1) * GS], mb[:], op=OP.add),
                    reads=(f"{xkey}:{t}", f"{dkey}_mb"), writes=(f"{dkey}:{t}",))
                P.add("vector", lambda e, t=t, dst=dst, sbuf_b=sbuf_b: e.tensor_tensor(
                    dst[:, t * GS:(t + 1) * GS], dst[:, t * GS:(t + 1) * GS], sbuf_b[:], op=OP.mult),
                    reads=(f"{dkey}:{t}", f"{dkey}_sb"), writes=(f"{dkey}:{t}",))
                if with_halo:
                    P.add("vector", lambda e, t=t, dst_h=dst_h, hx=hx, mbh=mbh: e.tensor_tensor(
                        dst_h[:, t * 3:(t + 1) * 3], hx[:, t * 3:(t + 1) * 3], mbh[:], op=OP.add),
                        reads=(f"xtg_h:{t}", "msbc_h"), writes=(f"xnf_h:{t}",))
                    P.add("vector", lambda e, t=t, dst_h=dst_h, sbh=sbh: e.tensor_tensor(
                        dst_h[:, t * 3:(t + 1) * 3], dst_h[:, t * 3:(t + 1) * 3], sbh[:], op=OP.mult),
                        reads=(f"xnf_h:{t}", "sbc_h"), writes=(f"xnf_h:{t}",))

        for g in range(NG):
            emit_xnf(g, xnf, "xnf", xtg, "xtg", msbc, sbc, g == 0, xnf_h, xtg_h, msbc_h, sbc_h,
                     mrow=msg, srw=ssg, mrow_h=msg_h, srw_h=ssg_h)
            for t in range(KT):
                P.add("tensor", lambda e, t=t: e.matmul(
                    psb[2][0:H, 0:GS], WbT_sb[:, t * H:(t + 1) * H], xnf[:, t * GS:(t + 1) * GS],
                    start=(t == 0), stop=(t == KT - 1)),
                    reads=("WbT_sb", f"xnf:{t}"), writes=("PSB2",))
            P.add("scalar", lambda e, g=g: e.activation(beta_hm[:, g * GS:(g + 1) * GS], psb[2][0:H, 0:GS], AF.Tanh, scale=0.5),
                  reads=("PSB2",), writes=("beta_hm",))
            P.add("vector", lambda e, g=g: e.tensor_scalar(
                beta_hm[:, g * GS:(g + 1) * GS], beta_hm[:, g * GS:(g + 1) * GS], 0.5, 0.5, op0=OP.mult, op1=OP.add),
                reads=("beta_hm",), writes=("beta_hm",))
            for pi, p in enumerate("qkv"):
                for o in range(KT):
                    bk = o % 2
                    wb = o % 2
                    P.add("gq", lambda e, p=p, o=o, wb=wb: e.dma_start(
                        out=wbuf[wb][:].rearrange("p (a n) -> p a n", a=KT),
                        in_=WT_d[p][:, o * 128:(o + 1) * 128].rearrange("(a p) n -> p a n", p=128)),
                        writes=(f"wbuf{wb}",))
                    for k in range(KT):
                        P.add("tensor", lambda e, k=k, wb=wb, bk=bk: e.matmul(
                            psb[bk][:, 0:GS], wbuf[wb][:, k * 128:(k + 1) * 128], xnf[:, k * GS:(k + 1) * GS],
                            start=(k == 0), stop=(k == KT - 1)),
                            reads=(f"wbuf{wb}", f"xnf:{k}"), writes=(f"PSB{bk}",))
                    if g == 0:
                        for k in range(KT):
                            P.add("tensor", lambda e, k=k, wb=wb, bk=bk: e.matmul(
                                psb[5][:, bk * 4:bk * 4 + 3], wbuf[wb][:, k * 128:(k + 1) * 128], xnf_h[:, k * 3:(k + 1) * 3],
                                start=(k == 0), stop=(k == KT - 1)),
                                reads=(f"wbuf{wb}", f"xnf_h:{k}"), writes=("PSB5",))
                        P.add("scalar", lambda e, o=o, bk=bk: e.copy(pbuf[o][:, 0:3], psb[5][:, bk * 4:bk * 4 + 3]),
                              reads=("PSB5",), writes=(f"pbuf:{o}",))
                    else:
                        P.add("gpsimd", lambda e, pi=pi, o=o: e.tensor_copy(
                            pbuf[o][:, 0:3], tails[:, (pi * KT + o) * 3:(pi * KT + o) * 3 + 3]),
                            reads=(f"tails:{pi}:{o}",), writes=(f"pbuf:{o}",))
                    P.add("scalar", lambda e, o=o, bk=bk: e.copy(pbuf[o][:, 3:3 + GS], psb[bk][:, 0:GS]),
                          reads=(f"PSB{bk}",), writes=(f"pbuf:{o}",))
                    P.add("gpsimd", lambda e, pi=pi, o=o: e.tensor_copy(
                        tails[:, (pi * KT + o) * 3:(pi * KT + o) * 3 + 3], pbuf[o][:, GS:GS + 3]),
                        reads=(f"pbuf:{o}",), writes=(f"tails:{pi}:{o}",))
                    cwbase = pi * KT * 4 + o * 4
                    P.add("vector", lambda e, o=o, cwbase=cwbase: e.tensor_scalar(
                        ybuf[:], pbuf[o][:, 0:GS], conv_sb[:, cwbase:cwbase + 1], 0.0, op0=OP.mult, op1=OP.add),
                        reads=(f"pbuf:{o}", "conv_sb"), writes=("ybuf",))
                    for j in (1, 2, 3):
                        P.add("vector", lambda e, o=o, j=j, cwbase=cwbase: e.scalar_tensor_tensor(
                            ybuf[:], pbuf[o][:, j:j + GS], conv_sb[:, cwbase + j:cwbase + j + 1], ybuf[:],
                            op0=OP.mult, op1=OP.add),
                            reads=(f"pbuf:{o}", "conv_sb", "ybuf"), writes=("ybuf",))
                    P.add("scalar", lambda e, p=p, o=o, g=g: e.activation(
                        RES[p][:, o * R + g * GS: o * R + g * GS + GS], ybuf[:], AF.Silu),
                        reads=("ybuf",), writes=(f"res:{p}:{o}",))
        P.barrier()

        P.mark("A_done")
        sqs = [sbA(f"sqs{i}", [128, GS], F32) for i in range(2)]
        sqb = sbA("sqb", [H, R], F32)
        scbc = sbA("scbc", [128, GS], BF16)
        for g in range(NG):
            for p in ("q", "k"):
                scale = scale_q if p == "q" else scale_k
                res = RES[p]
                for t in range(KT):
                    sq = sqs[t % 2]
                    P.add("scalar", lambda e, res=res, t=t, g=g, sq=sq, p=p: e.activation(
                        sq[:], res[:, t * R + g * GS: t * R + g * GS + GS], AF.Square),
                        reads=(f"res:{p}:{t}",), writes=(f"sqs{t % 2}",))
                    P.add("tensor", lambda e, t=t, sq=sq: e.matmul(
                        psb[2][0:H, 0:GS], ind[:, t * H:(t + 1) * H], sq[:],
                        start=(t == 0), stop=(t == KT - 1)),
                        reads=("ind", f"sqs{t % 2}"), writes=("PSB2",))
                P.add("vector", lambda e, scale=scale, g=g, p=p: e.tensor_copy(
                    scale[:, g * GS:(g + 1) * GS], psb[2][0:H, 0:GS]),
                    reads=("PSB2",), writes=(f"scale_{p}",))
        P.add("scalar", lambda e: e.activation(sqb[:], beta_hm[:], AF.Sqrt), reads=("beta_hm",), writes=("sqb",))
        for p in ("q", "k"):
            scale = scale_q if p == "q" else scale_k
            P.add("scalar", lambda e, scale=scale: e.activation(scale[:], scale[:], AF.Sqrt),
                  reads=(f"scale_{p}",), writes=(f"scale_{p}",))
            P.add("vector", lambda e, scale=scale: e.tensor_scalar(scale[:], scale[:], cfg.EPS, 0.0, op0=OP.max, op1=OP.add),
                  reads=(f"scale_{p}",), writes=(f"scale_{p}",))
            P.add("vector", lambda e, scale=scale: e.reciprocal(scale[:], scale[:]),
                  reads=(f"scale_{p}",), writes=(f"scale_{p}",))
        P.add("vector", lambda e: e.tensor_scalar(scale_q[:], scale_q[:], 0.125, 0.0, op0=OP.mult, op1=OP.add),
              reads=("scale_q",), writes=("scale_q",))
        P.add("vector", lambda e: e.tensor_tensor(scale_k[:], scale_k[:], sqb[:], op=OP.mult),
              reads=("scale_k", "sqb"), writes=("scale_k",))
        P.add("vector", lambda e: e.tensor_copy(scale_v[:], sqb[:]), reads=("sqb",), writes=("scale_v",))
        for g in range(NG):
            for p in ("q", "k", "v"):
                scale = {"q": scale_q, "k": scale_k, "v": scale_v}[p]
                res = RES[p]
                for t in range(KT):
                    bb = 3 + (t % 2)
                    P.add("tensor", lambda e, scale=scale, t=t, g=g, bb=bb, p=p: e.matmul(
                        psb[bb][:, 0:GS], indT[:, t * 128:(t + 1) * 128], scale[:, g * GS:(g + 1) * GS],
                        start=True, stop=True),
                        reads=("indT", f"scale_{p}"), writes=(f"PSB{bb}",))
                    P.add("scalar", lambda e, bb=bb: e.copy(scbc[:], psb[bb][:, 0:GS]),
                          reads=(f"PSB{bb}",), writes=("scbc",))
                    P.add("vector", lambda e, res=res, t=t, g=g, p=p: e.tensor_tensor(
                        res[:, t * R + g * GS: t * R + g * GS + GS],
                        res[:, t * R + g * GS: t * R + g * GS + GS], scbc[:], op=OP.mult),
                        reads=(f"res:{p}:{t}", "scbc"), writes=(f"res:{p}:{t}",))
        P.barrier()

        P.mark("B_done")
        stg = [sbA(f"stg{i}", [128, D], BF16) for i in range(2)]
        for nm in ("k", "v"):
            src = RES[nm]
            dram = krm_d if nm == "k" else vrm_d
            for c in range(NC):
                par = c % 2
                for t in range(KT):
                    pt = t % 2
                    pst = psT0 if pt == 0 else psT1
                    pstk = "PSB6" if pt == 0 else "PSB7"
                    P.add("tensor", lambda e, src=src, c=c, t=t, pst=pst: e.transpose(
                        pst[:, 0:128], src[:, t * R + c * 128: t * R + c * 128 + 128], ident_b[:]),
                        reads=(f"res:{nm}:{t}", "ident_b"), writes=(pstk,))
                    P.add("scalar", lambda e, t=t, par=par, pst=pst: e.copy(
                        stg[par][:, t * 128:(t + 1) * 128], pst[:, 0:128]),
                        reads=(pstk,), writes=(f"stg{par}",))
                if nm == "v":
                    P.add("aq", lambda e, dram=dram, c=c, par=par: e.dma_start(
                        out=dram[c * 128:(c + 1) * 128, :], in_=stg[par][:]),
                        reads=(f"stg{par}",), writes=(f"{nm}rmd:{c}",))
                else:
                    for dup in range(2):
                        P.add("aq", lambda e, dram=dram, c=c, par=par, dup=dup: e.dma_start(
                            out=dram[c * 128:(c + 1) * 128, :].rearrange("p (h u d) -> p h u d", h=H, u=2)[:, :, dup],
                            in_=stg[par][:].rearrange("p (h d) -> p h d", h=H)),
                            reads=(f"stg{par}",), writes=(f"{nm}rmd:{c}:{dup}",))
        P.barrier()
        stA.close()

        P.mark("B2_done")
        stC = ExitStack()
        sbC = lambda name, shape, dtype: stC.enter_context(nc.sbuf_tensor(name, shape, dtype))
        krm = [sbC(f"krm{i}", [128, 2 * D], BF16) for i in range(2)]
        vrm = [sbC(f"vrm{i}", [128, D], BF16) for i in range(2)]
        o0stg = [sbC(f"o0stg{i}", [128, H * 64], BF16) for i in range(2)]
        qtstg = [sbC(f"qtstg{i}", [128, KT * 128], BF16) for i in range(2)]
        ptstg = [sbC(f"ptstg{i}", [64, H * 64], BF16) for i in range(2)]

        GsL = [sbC(f"GsL{i}", [128, 256], F32) for i in range(3)]
        A_sb = [sbC(f"A_sb{i}", [128, 128], BF16) for i in range(3)]
        AT_sb = [sbC(f"AT_sb{i}", [128, 128], BF16) for i in range(3)]
        LT_sb = [sbC(f"LT_sb{i}", [128, 128], BF16) for i in range(3)]
        X_sb = [sbC(f"X_sb{i}", [128, 128], BF16) for i in range(6)]
        RRT = [[sbC(f"RRT{s}_{l}", [128, 256], BF16) for l in range(NLEV)] for s in range(3)]
        for c in range(NC):
            cp = c % 2
            P.add("sp", lambda e, c=c, cp=cp: e.dma_start(out=krm[cp][:], in_=krm_d[c * 128:(c + 1) * 128, :]),
                  reads=(f"krmd:{c}:0", f"krmd:{c}:1"), writes=(f"krm{cp}",))
            P.add("sp", lambda e, c=c, cp=cp: e.dma_start(out=vrm[cp][:], in_=vrm_d[c * 128:(c + 1) * 128, :]),
                  reads=(f"vrmd:{c}",), writes=(f"vrm{cp}",))
            for h in range(H):
                t, hw = h // 2, h % 2
                wp = h % 3
                bA, bB = f"PSB{wp}", f"PSB{3 + wp}"
                pA, pB = psb[wp], psb[3 + wp]
                kf = ktf[64 * hw:64 * hw + 64, t * R + c * 128: t * R + c * 128 + 128]
                qf = qtf[64 * hw:64 * hw + 64, t * R + c * 128: t * R + c * 128 + 128]
                rk, rq = f"res:k:{t}", f"res:q:{t}"
                Xc, Xn = X_sb[2 * wp], X_sb[2 * wp + 1]
                XcK, XnK = f"X{2 * wp}", f"X{2 * wp + 1}"
                Sb_sl = S_b[64 * hw:64 * hw + 64, h * 64:(h + 1) * 64]
                P.add("tensor", lambda e, kf=kf, pA=pA, hw=hw: e.matmul(pA[:, 0:128], kf, kf, start=True, stop=True,
                      tile_position=(64 * hw, 0)),
                      reads=(rk,), writes=(bA,))
                P.add("tensor", lambda e, kf=kf, qf=qf, pA=pA, hw=hw: e.matmul(pA[:, 128:256], kf, qf, start=True, stop=True,
                      tile_position=(64 * hw, 0)),
                      reads=(rk, rq), writes=(bA,))
                P.add("tensor", lambda e, kf=kf, pA=pA, Sb_sl=Sb_sl, hw=hw: e.matmul(
                    pA[:, 256:320], kf, Sb_sl, start=True, stop=True, tile_position=(64 * hw, 0)),
                    reads=(rk, f"S_b:{h}"), writes=(bA,))
                P.add("scalar", lambda e, pA=pA, wp=wp: e.copy(GsL[wp][:], pA[:, 0:256]),
                      reads=(bA,), writes=(f"GsL{wp}",))
                P.add("gpsimd", lambda e, wp=wp: e.tensor_tensor(A_sb[wp][:], GsL[wp][:, 0:128], mask_sl[:], op=OP.mult),
                      reads=(f"GsL{wp}", "mask_sl"), writes=(f"A{wp}",))
                P.add("gpsimd", lambda e, wp=wp: e.tensor_tensor(AT_sb[wp][:], GsL[wp][:, 0:128], mask_sun[:], op=OP.mult),
                      reads=(f"GsL{wp}", "mask_sun"), writes=(f"AT{wp}",))
                P.add("vector", lambda e, wp=wp: e.tensor_tensor(LT_sb[wp][:], GsL[wp][:, 128:256], mask_sui[:], op=OP.mult),
                      reads=(f"GsL{wp}", "mask_sui"), writes=(f"LT{wp}",))
                P.add("gpsimd", lambda e, Xc=Xc, cp=cp, h=h: e.tensor_scalar_mul(
                    Xc[:, 0:64], krm[cp][:, h * 128:h * 128 + 64], -1.0),
                    reads=(f"krm{cp}",), writes=(XcK,))
                P.add("vector", lambda e, Xc=Xc, cp=cp, h=h, pA=pA: e.tensor_sub(
                    Xc[:, 64:128], vrm[cp][:, h * 64:(h + 1) * 64], pA[:, 256:320]),
                    reads=(f"vrm{cp}", bA), writes=(XcK,))
                prevR, prevRK = A_sb[wp][:], f"A{wp}"
                prevRT, prevRTK = AT_sb[wp][:], f"AT{wp}"
                for lev in range(1, NLEV + 1):
                    buf = RRT[wp][lev - 1]
                    bK = f"RRT{wp}:{lev}"
                    if lev < NLEV:
                        P.add("tensor", lambda e, pA=pA, prevR=prevR, prevRT=prevRT: e.matmul(
                            pA[:, 0:128], prevRT, prevR, start=True, stop=True),
                            reads=(prevRK, prevRTK), writes=(bA,))
                    P.add("tensor", lambda e, pA=pA, prevR=prevR, prevRT=prevRT: e.matmul(
                        pA[:, 128:256], prevR, prevRT, start=True, stop=True),
                        reads=(prevRK, prevRTK), writes=(bA,))
                    if lev < NLEV:
                        P.add("scalar", lambda e, pA=pA, buf=buf: e.copy(buf[:], pA[:, 0:256]),
                              reads=(bA,), writes=(bK,))
                        prevR, prevRK = buf[:, 0:128], bK
                        prevRT, prevRTK = buf[:, 128:256], bK
                    else:
                        P.add("scalar", lambda e, pA=pA, buf=buf: e.copy(buf[:, 128:256], pA[:, 128:256]),
                              reads=(bA,), writes=(bK,))
                for j in list(range(NLEV, 0, -1)) + [0]:
                    if j > 0:
                        rt, rtK = RRT[wp][j - 1][:, 128:256], f"RRT{wp}:{j}"
                    else:
                        rt, rtK = AT_sb[wp][:], f"AT{wp}"
                    P.add("tensor", lambda e, pB=pB, rt=rt, Xc=Xc: e.matmul(
                        pB[:, 0:128], rt, Xc[:], start=True, stop=True),
                        reads=(rtK, XcK), writes=(bB,))
                    P.add("vector", lambda e, pB=pB, Xn=Xn, Xc=Xc: e.tensor_add(Xn[:], Xc[:], pB[:, 0:128]),
                          reads=(bB, XcK), writes=(XnK,))
                    Xc, Xn = Xn, Xc
                    XcK, XnK = XnK, XcK
                P.add("tensor", lambda e, qf=qf, pB=pB, Sb_sl=Sb_sl, hw=hw: e.matmul(
                    pB[:, 128:192], qf, Sb_sl, start=True, stop=False, tile_position=(64 * hw, 0)),
                    reads=(rq, f"S_b:{h}"), writes=(bB,))
                P.add("tensor", lambda e, pB=pB, wp=wp, Xc=Xc: e.matmul(
                    pB[:, 128:192], LT_sb[wp][:], Xc[:, 64:128], start=False, stop=True),
                    reads=(f"LT{wp}", XcK), writes=(bB,))
                P.add("tensor", lambda e, pB=pB, cp=cp, h=h, Xc=Xc: e.matmul(
                    pB[:, 192:256], krm[cp][:, h * 128:(h + 1) * 128], Xc[:, 64:128], start=True, stop=True),
                    reads=(f"krm{cp}", XcK), writes=(bB,))
                P.add("tensor", lambda e, pB=pB: e.matmul(
                    pB[0:64, 256:320], ident64[0:64, :], ident64[0:64, :], start=True, stop=False),
                    reads=("ident64",), writes=(bB,))
                P.add("tensor", lambda e, pB=pB, cp=cp, h=h, Xc=Xc: e.matmul(
                    pB[0:64, 256:320], Xc[:, 0:64], krm[cp][:, h * 128:h * 128 + 64], start=False, stop=True),
                    reads=(f"krm{cp}", XcK), writes=(bB,))
                P.add("tensor", lambda e, pB=pB, qf=qf, hw=hw: e.matmul(
                    pB[64 * hw:64 * hw + 64, 320:448], ident64[64 * hw:64 * hw + 64, :], qf, start=True, stop=False,
                    tile_position=(64 * hw, 64 * hw)),
                    reads=("ident64", rq), writes=(bB,))
                P.add("tensor", lambda e, pB=pB, wp=wp, Xc=Xc, hw=hw: e.matmul(
                    pB[64 * hw:64 * hw + 64, 320:448], Xc[:, 0:64], LT_sb[wp][:], start=False, stop=True,
                    tile_position=(0, 64 * hw)),
                    reads=(f"LT{wp}", XcK), writes=(bB,))
                P.add("vector", lambda e, pB=pB, h=h: e.tensor_add(
                    S_f[:, h * 64:(h + 1) * 64], S_f[:, h * 64:(h + 1) * 64], pB[:, 192:256]),
                    reads=(bB, f"S_f:{h}"), writes=(f"S_f:{h}",))
                P.add("scalar", lambda e, h=h: e.copy(S_b[:, h * 64:(h + 1) * 64], S_f[:, h * 64:(h + 1) * 64]),
                      reads=(f"S_f:{h}",), writes=(f"S_b:{h}",))
                P.add("scalar", lambda e, pB=pB, cp=cp, h=h: e.copy(
                    ptstg[cp][:, h * 64:(h + 1) * 64], pB[0:64, 256:320]),
                    reads=(bB,), writes=(f"PT:{c}:{h}",))
                P.add("scalar", lambda e, pB=pB, cp=cp, h=h: e.copy(
                    o0stg[cp][:, h * 64:(h + 1) * 64], pB[:, 128:192]),
                    reads=(bB,), writes=(f"o0stg{cp}",))
                P.add("scalar", lambda e, pB=pB, cp=cp, t=t, hw=hw: e.copy(
                    qtstg[cp][64 * hw:64 * hw + 64, t * 128:(t + 1) * 128], pB[64 * hw:64 * hw + 64, 320:448]),
                    reads=(bB,), writes=(f"qtstg{cp}",))
                P.add("tensor", lambda e, pB=pB, cp=cp, h=h: e.matmul(
                    pB[0:64, 448:512], ptstg[cp][:, h * 64:(h + 1) * 64], PP_b[0:64, h * 64:(h + 1) * 64],
                    start=True, stop=True),
                    reads=(f"PT:{c}:{h}", f"PP_b:{h}"), writes=(bB,))
                P.add("vector", lambda e, pB=pB, h=h: e.tensor_copy(PP_b[0:64, h * 64:(h + 1) * 64], pB[0:64, 448:512]),
                      reads=(bB,), writes=(f"PP_b:{h}",))
            P.add("aq", lambda e, c=c, cp=cp: e.dma_start(out=o0_d[c * 128:(c + 1) * 128, :], in_=o0stg[cp][:]),
                  reads=(f"o0stg{cp}",), writes=(f"o0d:{c}",))
            P.add("aq", lambda e, c=c, cp=cp: e.dma_start(
                out=qt_d[:, c * KT * 128:(c + 1) * KT * 128], in_=qtstg[cp][:]),
                reads=(f"qtstg{cp}",), writes=(f"qtd:{c}",))
            P.add("aq", lambda e, c=c, cp=cp: e.dma_start(
                out=pt_d[:, c * H * 64:(c + 1) * H * 64], in_=ptstg[cp][:]),
                reads=tuple(f"PT:{c}:{hh}" for hh in range(H)), writes=(f"ptd:{c}",))
        P.mark("C_done")
        ccstg = sbC("ccstg", [64, H * 128], F32)
        for h in range(H):
            pt = h % 2
            pst = psT0 if pt == 0 else psT1
            pstk = "PSB6" if pt == 0 else "PSB7"
            P.add("tensor", lambda e, h=h, pst=pst: e.transpose(
                pst[0:64, 0:64], PP_b[0:64, h * 64:(h + 1) * 64], ident_b[0:64, 0:64]),
                reads=(f"PP_b:{h}", "ident_b"), writes=(pstk,))
            P.add("scalar", lambda e, h=h, pst=pst: e.copy(ccstg[:, h * 128:h * 128 + 64], pst[0:64, 0:64]),
                  reads=(pstk,), writes=("ccstg",))
            P.add("vector", lambda e, h=h: e.tensor_copy(ccstg[:, h * 128 + 64:h * 128 + 128], S_f[0:64, h * 64:(h + 1) * 64]),
                  reads=(f"S_f:{h}",), writes=("ccstg",))
        if part == "full":
            P.add("gq", lambda e: e.dma_start(out=cc_in_d[:, :], in_=ccstg[:]), reads=("ccstg",), writes=("cc_in",))
            P.add("cc", lambda e: e.collective_compute(
                "AllGather", OP.bypass, replica_groups=[list(range(n_cores))],
                ins=[cc_in_d[:, :]], outs=[cc_out_d[:, :]]),
                reads=("cc_in",), writes=("cc_out",))
        else:
            P.add("aq", lambda e: e.dma_start(out=cc_out1_d[:, :], in_=ccstg[:]), reads=("ccstg",), writes=("cc_out",))
            P.add("gpsimd", lambda e: e.engine_nop(),
                  reads=("cc_out", "msrow_d") + tuple(f"o0d:{c}" for c in range(NC))
                        + tuple(f"qtd:{c}" for c in range(NC)) + tuple(f"ptd:{c}" for c in range(NC)))
        P.barrier()
        stC.close()
        stABC.close()
        P.barrier()

    if run_p2:
        stD = ExitStack()
        sbD = lambda name, shape, dtype: stD.enter_context(nc.sbuf_tensor(name, shape, dtype))
        ccg = [sbD(f"ccgm{i}", [128, H * 192], F32) for i in range(2)]
        Sm = sbD("Sm", [128, H * 64], F32)
        Sacc = sbD("Sacc", [128, H * 64], F32)
        sel_tmp = sbD("sel_tmp", [128, H * 64], F32)
        ohbc = sbD("ohbc", [128, n_cores], F32)
        P.add("gpsimd", lambda e: e.memset(Sm[:], 0.0), writes=("Sm",))
        P.add("gpsimd", lambda e: e.memset(Sacc[:], 0.0), writes=("Sacc",))
        P.add("tensor", lambda e: e.matmul(psb[5][:, 0:n_cores], ones1[:], onehot[:], start=True, stop=True),
              reads=("ones1", "onehot"), writes=("PSB5",))
        P.add("vector", lambda e: e.tensor_copy(ohbc[:], psb[5][:, 0:n_cores]), reads=("PSB5",), writes=("ohbc",))
        for m in range(n_cores):
            if m < n_cores - 1:
                mp = m % 2
                for half in range(2):
                    for dup in range(2):
                        P.add("sp", lambda e, m=m, mp=mp, half=half, dup=dup: e.dma_start(
                            out=ccg[mp][64 * half:64 * half + 64, :].rearrange("p (h x d) -> p h x d", h=H, x=3)[:, :, dup],
                            in_=cc_out_d[m * 64:(m + 1) * 64, :].rearrange("p (h x d) -> p h x d", h=H, x=2)[:, :, 0]),
                            reads=("cc_out",), writes=(f"ccgm{mp}",))
                    P.add("sp", lambda e, m=m, mp=mp, half=half: e.dma_start(
                        out=ccg[mp][64 * half:64 * half + 64, :].rearrange("p (h x d) -> p h x d", h=H, x=3)[:, :, 2],
                        in_=cc_out_d[m * 64:(m + 1) * 64, :].rearrange("p (h x d) -> p h x d", h=H, x=2)[:, :, 1]),
                        reads=("cc_out",), writes=(f"ccgm{mp}",))
            P.add("vector", lambda e, m=m: e.scalar_tensor_tensor(
                sel_tmp[:], Sm[:], ohbc[:, m:m + 1], Sacc[:], op0=OP.mult, op1=OP.add),
                reads=("Sm", "ohbc", "Sacc"), writes=("sel_tmp",))
            P.add("vector", lambda e: e.tensor_copy(Sacc[:], sel_tmp[:]), reads=("sel_tmp",), writes=("Sacc",))
            if m < n_cores - 1:
                mp = m % 2
                for h in range(H):
                    P.add("tensor", lambda e, mp=mp, h=h: e.matmul(
                        psb[5][:, 64:128], ccg[mp][0:64, h * 192:h * 192 + 128],
                        Sm[0:64, h * 64:(h + 1) * 64], start=True, stop=True),
                        reads=(f"ccgm{mp}", "Sm"), writes=("PSB5",))
                    P.add("vector", lambda e, mp=mp, h=h: e.tensor_add(
                        Sm[:, h * 64:(h + 1) * 64], psb[5][:, 64:128],
                        ccg[mp][:, h * 192 + 128:h * 192 + 192]),
                        reads=("PSB5", f"ccgm{mp}"), writes=("Sm",))
        P.add("vector", lambda e: e.tensor_copy(N_f[:], Sacc[:]), reads=("Sacc",),
              writes=tuple(f"N_f:{h}" for h in range(H)))
        P.add("scalar", lambda e: e.copy(N_b[:], N_f[:]), reads=tuple(f"N_f:{h}" for h in range(H)),
              writes=tuple(f"N_b:{h}" for h in range(H)))
        P.barrier()

        Z_sb = sbD("Z_sb", [128, KT * D], BF16)
        e1b = sbD("e1b", [128, 512], BF16)
        e2b = [sbD(f"e2b{i}", [128, 512], BF16) for i in range(4)]
        zacc = sbD("zacc", [128, 512], F32)
        ztmp = sbD("ztmp", [128, 512], F32)
        for ri in "RI":
            for c4 in range(4):
                P.add("gq", lambda e, ri=ri, c4=c4: e.dma_start(out=e2b[c4][:], in_=E2_d[ri][c4]),
                      writes=(f"e2b{c4}",))
            for it in range(4):
                for c4 in range(4):
                    P.add("gq", lambda e, ri=ri, it=it, c4=c4: e.dma_start(out=e1b[:], in_=E1_d[ri][c4, it]),
                          writes=("e1b",))
                    if c4 == 0:
                        P.add("vector", lambda e, c4=c4: e.tensor_tensor(zacc[:], e1b[:], e2b[c4][:], op=OP.mult),
                              reads=("e1b", f"e2b{c4}"), writes=("zacc",))
                    else:
                        P.add("vector", lambda e, c4=c4: e.tensor_tensor(ztmp[:], e1b[:], e2b[c4][:], op=OP.mult),
                              reads=("e1b", f"e2b{c4}"), writes=("ztmp",))
                        P.add("vector", lambda e: e.tensor_add(zacc[:], zacc[:], ztmp[:]),
                              reads=("zacc", "ztmp"), writes=("zacc",))
                if ri == "R":
                    P.add("scalar", lambda e, it=it: e.copy(Z_sb[:, it * D + 0:it * D + 512], zacc[:]),
                          reads=("zacc",), writes=("Z_sb",))
                    P.add("scalar", lambda e, it=it: e.copy(Z_sb[:, (4 + it) * D + 512:(4 + it) * D + 1024], zacc[:]),
                          reads=("zacc",), writes=("Z_sb",))
                else:
                    P.add("scalar", lambda e, it=it: e.copy(Z_sb[:, it * D + 512:it * D + 1024], zacc[:]),
                          reads=("zacc",), writes=("Z_sb",))
                    P.add("scalar", lambda e, it=it: e.mul(Z_sb[:, (4 + it) * D + 0:(4 + it) * D + 512], zacc[:], -1.0),
                          reads=("zacc",), writes=("Z_sb",))
        P.barrier()

        WoT_sb = sbD("WoT_sb", [128, KT * D], BF16)
        P.add("gq", lambda e: e.dma_start(out=WoT_sb[:].rearrange("p (a n) -> p a n", a=KT),
                                          in_=WT_d["o"].rearrange("(a p) n -> p a n", p=128)),
              writes=("WoT_sb",))
        msgD = sbD("msgD", [1, GS], F32)
        ssgD = sbD("ssgD", [1, GS], F32)
        xnfD = sbD("xnfD", [128, KT * GS], BF16)
        xtgD = sbD("xtgD", [128, KT * GS], BF16)
        msbcD = sbD("msbcD", [128, GS], BF16)
        sbcD = sbD("sbcD", [128, GS], BF16)
        o0b = [sbD(f"o0b{i}", [128, D], BF16) for i in range(2)]
        qtb = [sbD(f"qtb{i}", [128, KT * 128], BF16) for i in range(2)]
        ptb = [sbD(f"ptb{i}", [64, H * 128], BF16) for i in range(2)]
        Ofp = [sbD(f"Ofp{i}", [128, D], F32) for i in range(2)]
        ObfR = [sbD(f"ObfR{i}", [128, D], BF16) for i in range(2)]
        Ofeat = sbD("Ofeat", [128, KT * 128], BF16)
        osq = [sbD(f"osq{i}", [128, 128], F32) for i in range(2)]
        srow16 = sbD("srow16", [H, 128], F32)
        obc = sbD("obc", [128, 128], BF16)
        xtok = [sbD(f"xtok{i}", [128, D], F32) for i in range(2)]
        outsb = [sbD(f"outsb{i}", [128, D], F32) for i in range(2)]

        def emit_xnf_D(grp):
            c0 = 3 + grp * GS
            for t in range(KT):
                P.add("gq", lambda e, t=t, c0=c0: e.dma_start(
                    out=xtgD[:, t * GS:(t + 1) * GS], in_=x_t_d[t * 128:(t + 1) * 128, c0:c0 + GS]),
                    writes=(f"xtgD:{t}",))
            P.add("sp", lambda e, c0=c0: e.dma_start(out=msgD[:], in_=msrow_d[0:1, c0:c0 + GS]),
                  reads=("msrow_d",), writes=("msgD",))
            P.add("sp", lambda e, c0=c0: e.dma_start(out=ssgD[:], in_=msrow_d[1:2, c0:c0 + GS]),
                  reads=("msrow_d",), writes=("ssgD",))
            P.add("tensor", lambda e: e.matmul(psb[4][:, 0:GS], ones1[:], msgD[:], start=True, stop=True),
                  reads=("ones1", "msgD"), writes=("PSB4",))
            P.add("tensor", lambda e: e.matmul(psb[5][:, 0:GS], ones1[:], ssgD[:], start=True, stop=True),
                  reads=("ones1", "ssgD"), writes=("PSB5",))
            P.add("scalar", lambda e: e.copy(msbcD[:], psb[4][:, 0:GS]), reads=("PSB4",), writes=("msbcD",))
            P.add("scalar", lambda e: e.copy(sbcD[:], psb[5][:, 0:GS]), reads=("PSB5",), writes=("sbcD",))
            for t in range(KT):
                P.add("vector", lambda e, t=t: e.tensor_tensor(
                    xnfD[:, t * GS:(t + 1) * GS], xtgD[:, t * GS:(t + 1) * GS], msbcD[:], op=OP.add),
                    reads=(f"xtgD:{t}", "msbcD"), writes=(f"xnfD:{t}",))
                P.add("vector", lambda e, t=t: e.tensor_tensor(
                    xnfD[:, t * GS:(t + 1) * GS], xnfD[:, t * GS:(t + 1) * GS], sbcD[:], op=OP.mult),
                    reads=(f"xnfD:{t}", "sbcD"), writes=(f"xnfD:{t}",))

        CPG = GS // C
        for g in range(NG):
            emit_xnf_D(g)
            for ci in range(CPG):
                c = g * CPG + ci
                cp = c % 2
                P.add("sp", lambda e, c=c, cp=cp: e.dma_start(out=o0b[cp][:], in_=o0_d[c * 128:(c + 1) * 128, :]),
                      reads=(f"o0d:{c}",), writes=(f"o0b{cp}",))
                P.add("sp", lambda e, c=c, cp=cp: e.dma_start(
                    out=qtb[cp][:], in_=qt_d[:, c * KT * 128:(c + 1) * KT * 128]),
                    reads=(f"qtd:{c}",), writes=(f"qtb{cp}",))
                P.add("sp", lambda e, c=c, cp=cp: e.dma_start(out=xtok[cp][:], in_=x_tok_d[c * 128:(c + 1) * 128, :]),
                      writes=(f"xtok{cp}",))
                for dup in range(2):
                    P.add("sp", lambda e, c=c, cp=cp, dup=dup: e.dma_start(
                        out=ptb[cp][:].rearrange("p (h u d) -> p h u d", h=H, u=2)[:, :, dup],
                        in_=pt_d[:, c * H * 64:(c + 1) * H * 64].rearrange("p (h d) -> p h d", h=H)),
                        reads=(f"ptd:{c}",), writes=(f"ptb{cp}",))
                for h in range(H):
                    t, hw = h // 2, h % 2
                    P.add("tensor", lambda e, cp=cp, t=t, hw=hw, h=h: e.matmul(
                        psb[2 + hw][:, (h // 2) * 64:(h // 2) * 64 + 64],
                        qtb[cp][64 * hw:64 * hw + 64, t * 128:(t + 1) * 128],
                        N_b[64 * hw:64 * hw + 64, h * 64:(h + 1) * 64], start=True, stop=True,
                        tile_position=(64 * hw, 0)),
                        reads=(f"qtb{cp}", f"N_b:{h}"), writes=(f"PSB{2 + hw}",))
                for h in range(H):
                    hw = h % 2
                    P.add("tensor", lambda e, h=h, cp=cp, hw=hw: e.matmul(
                        psb[4 + hw][:, (h // 2) * 64:(h // 2) * 64 + 64],
                        ptb[cp][:, h * 128:(h + 1) * 128],
                        N_b[0:64, h * 64:(h + 1) * 64], start=True, stop=True),
                        reads=(f"ptb{cp}", f"N_b:{h}"), writes=(f"PSB{4 + hw}",))
                allN = tuple(f"N_b:{hh}" for hh in range(H))
                allNf = tuple(f"N_f:{hh}" for hh in range(H))
                for hw in range(2):
                    P.add("vector", lambda e, cp=cp, hw=hw: e.tensor_add(
                        Ofp[cp][:].rearrange("p (h x d) -> p h x d", h=8, x=2)[:, :, hw],
                        o0b[cp][:].rearrange("p (h x d) -> p h x d", h=8, x=2)[:, :, hw],
                        psb[2 + hw][:, 0:512].rearrange("p (h d) -> p h d", h=8)),
                        reads=(f"o0b{cp}", f"PSB{2 + hw}"), writes=(f"Ofp{cp}",))
                    P.add("vector", lambda e, hw=hw: e.tensor_copy(
                        N_f[:].rearrange("p (h x d) -> p h x d", h=8, x=2)[:, :, hw],
                        psb[4 + hw][:, 0:512].rearrange("p (h d) -> p h d", h=8)),
                        reads=(f"PSB{4 + hw}",), writes=allNf)
                P.add("scalar", lambda e: e.copy(N_b[:], N_f[:]), reads=allNf, writes=allN)
                for half in range(2):
                    P.add("scalar", lambda e, cp=cp, half=half: e.copy(
                        ObfR[cp][:, half * 512:half * 512 + 512], Ofp[cp][:, half * 512:half * 512 + 512]),
                        reads=(f"Ofp{cp}",), writes=(f"ObfR{cp}",))
                for t in range(KT):
                    pt = t % 2
                    pst = psT0 if pt == 0 else psT1
                    pstk = "PSB6" if pt == 0 else "PSB7"
                    P.add("tensor", lambda e, cp=cp, t=t, pst=pst: e.transpose(
                        pst[:, 0:128], ObfR[cp][:, t * 128:(t + 1) * 128], ident_b[:]),
                        reads=(f"ObfR{cp}", "ident_b"), writes=(pstk,))
                    P.add("scalar", lambda e, t=t, pst=pst: e.copy(Ofeat[:, t * 128:(t + 1) * 128], pst[:, 0:128]),
                          reads=(pstk,), writes=(f"Ofeat:{t}",))
                    P.add("scalar", lambda e, t=t: e.activation(osq[t % 2][:], Ofeat[:, t * 128:(t + 1) * 128], AF.Square),
                          reads=(f"Ofeat:{t}",), writes=(f"osq{t % 2}",))
                    P.add("tensor", lambda e, t=t: e.matmul(
                        psb[4][0:H, 0:128], ind[:, t * H:(t + 1) * H], osq[t % 2][:],
                        start=(t == 0), stop=(t == KT - 1)),
                        reads=("ind", f"osq{t % 2}"), writes=("PSB4",))
                P.add("scalar", lambda e: e.activation(srow16[:], psb[4][0:H, 0:128], AF.Sqrt, scale=1.0 / dh, bias=epsrms[0:H, :]),
                      reads=("PSB4", "epsrms"), writes=("srow16",))
                P.add("vector", lambda e: e.reciprocal(srow16[:], srow16[:]), reads=("srow16",), writes=("srow16",))
                for t in range(KT):
                    P.add("tensor", lambda e, t=t: e.matmul(
                        psb[5][:, 0:128], indT[:, t * 128:(t + 1) * 128], srow16[:], start=True, stop=True),
                        reads=("indT", "srow16"), writes=("PSB5",))
                    P.add("scalar", lambda e: e.activation(obc[:], psb[5][:, 0:128], AF.Identity, scale=onw[:]),
                          reads=("PSB5", "onw"), writes=("obc",))
                    P.add("vector", lambda e, t=t: e.tensor_tensor(
                        Ofeat[:, t * 128:(t + 1) * 128], Ofeat[:, t * 128:(t + 1) * 128], obc[:], op=OP.mult),
                        reads=(f"Ofeat:{t}", "obc"), writes=(f"Ofeat:{t}",))
                for half in range(2):
                    bo = half
                    for t in range(KT):
                        P.add("tensor", lambda e, t=t, half=half, bo=bo: e.matmul(
                            psb[bo][:, 0:512], Ofeat[:, t * 128:(t + 1) * 128],
                            WoT_sb[:, t * D + half * 512: t * D + half * 512 + 512],
                            start=(t == 0), stop=False),
                            reads=(f"Ofeat:{t}", "WoT_sb"), writes=(f"PSB{bo}",))
                    for t in range(KT):
                        P.add("tensor", lambda e, t=t, ci=ci, half=half, bo=bo: e.matmul(
                            psb[bo][:, 0:512], xnfD[:, t * GS + ci * C: t * GS + ci * C + 128],
                            Z_sb[:, t * D + half * 512: t * D + half * 512 + 512],
                            start=False, stop=False),
                            reads=(f"xnfD:{t}", "Z_sb"), writes=(f"PSB{bo}",))
                    P.add("tensor", lambda e, half=half, bo=bo: e.matmul(
                        psb[bo][:, 0:512], ones1[:], biasrow[0:1, half * 512:half * 512 + 512],
                        start=False, stop=True),
                        reads=("ones1", "biasrow"), writes=(f"PSB{bo}",))
                    P.add("vector", lambda e, cp=cp, half=half, bo=bo: e.tensor_add(
                        outsb[cp][:, half * 512:half * 512 + 512], psb[bo][:, 0:512],
                        xtok[cp][:, half * 512:half * 512 + 512]),
                        reads=(f"PSB{bo}", f"xtok{cp}"), writes=(f"outsb{cp}",))
                P.add("sp", lambda e, c=c, cp=cp: e.dma_start(out=out_d[c * 128:(c + 1) * 128, :], in_=outsb[cp][:]),
                      reads=(f"outsb{cp}",), writes=("out_final",))

        P.add("gpsimd", lambda e: e.engine_nop(), reads=("out_final",))

    P.mark("end")
    import os
    if os.environ.get("KPRINT_MARKS"):
        print("MARKS:", P.marks)
    P.finalize(truncate=cfg.trunc, skip=getattr(cfg, 'skip_items', ()))
    P.emit(nc, sems)
    return nc


def host_in_maps(cfg: CFG, inputs):
    import ml_dtypes
    n = cfg.n_cores
    R, D, H, KT = cfg.R, cfg.D, cfg.H, cfg.KT
    x = np.asarray(inputs["L0_attn_x"], np.float32).reshape(cfg.L, D)
    WT = {p: np.ascontiguousarray(np.asarray(inputs[f"L0_attn_W{p}"], np.float32).T) for p in "qkvo"}
    WbT = np.ascontiguousarray(np.asarray(inputs["L0_attn_Wb"], np.float32).T)
    onw = np.ascontiguousarray(np.tile(np.asarray(inputs["L0_attn_o_norm_w"], np.float32), 2)[:, None])
    biasrow = np.ascontiguousarray(np.concatenate([
        np.asarray(inputs["L0_ffn_bias_real"], np.float32),
        np.asarray(inputs["L0_ffn_bias_imag"], np.float32)])[None, :])

    def expand(G1, G2):
        G1 = np.asarray(G1, np.float32)[0]
        G2 = np.asarray(G2, np.float32)
        d_of_p = np.arange(128) // 16
        g_of_p = np.arange(128) % 16
        E1 = np.empty((4, 4, 128, 512), np.float32)
        E2 = np.empty((4, 128, 512), np.float32)
        for c4 in range(4):
            colB = G2[c4, :, 0, :]
            E2[c4] = colB.T[g_of_p][:, None, :].repeat(32, axis=1).reshape(128, 512)
            for it in range(4):
                vals = G1[:, c4, it * 8 + d_of_p]
                E1[c4, it] = vals.T[:, :, None].repeat(16, axis=2).reshape(128, 512)
        return E1, E2
    E1R, E2R = expand(inputs["L0_ffn_G1_R"], inputs["L0_ffn_G2_R"])
    E1I, E2I = expand(inputs["L0_ffn_G1_I"], inputs["L0_ffn_G2_I"])

    i_ = np.arange(128)
    mask_sl = -((i_[None, :] < i_[:, None]).astype(np.float32))
    mask_sun = -((i_[None, :] > i_[:, None]).astype(np.float32))
    mask_sui = (i_[None, :] >= i_[:, None]).astype(np.float32)
    ident_f = np.eye(128, dtype=np.float32)
    ident_b = np.eye(128).astype(ml_dtypes.bfloat16)
    ident64_2 = np.ascontiguousarray(np.tile(np.eye(64), (2, 1)).astype(ml_dtypes.bfloat16))
    ones1 = np.ones((1, 128), np.float32)
    ppinit = np.ascontiguousarray(np.tile(np.eye(64), (1, H)).astype(ml_dtypes.bfloat16))
    ind = np.zeros((128, KT * H), np.float32)
    indT = np.zeros((H, KT * 128), np.float32)
    for t in range(KT):
        for p in range(128):
            h = 2 * t + (1 if p >= 64 else 0)
            ind[p, t * H + h] = 1.0
            indT[h, t * 128 + p] = 1.0

    in_maps = []
    for i in range(n):
        r0 = i * R
        xt_sl = np.zeros((D, R + 3), np.float32)
        lo = max(r0 - 3, 0)
        xt_sl[:, 3 - (r0 - lo):] = x[lo:r0 + R].T
        halo = np.zeros((3, D), np.float32)
        if r0 >= 3:
            halo[:] = x[r0 - 3:r0]
        oh = np.zeros((1, n), np.float32)
        oh[0, i] = 1.0
        m = {
            "x_tok": np.ascontiguousarray(x[r0:r0 + R]),
            "x_t": np.ascontiguousarray(xt_sl),
            "x_halo": halo,
            "WbT": WbT,
            "conv_q": np.asarray(inputs["L0_attn_conv_q"], np.float32),
            "conv_k": np.asarray(inputs["L0_attn_conv_k"], np.float32),
            "conv_v": np.asarray(inputs["L0_attn_conv_v"], np.float32),
            "onw128": onw, "biasrow": biasrow,
            "E1R": E1R, "E2R": E2R, "E1I": E1I, "E2I": E2I,
            "mask_sl": mask_sl, "mask_sun": mask_sun, "mask_sui": mask_sui,
            "ident_f": ident_f, "ident_b": ident_b, "ident64_2": ident64_2,
            "ones1": ones1, "ppinit": ppinit, "onehot": oh,
            "ind": ind, "indT": indT,
        }
        for p in "qkvo":
            m[f"W{p}T"] = WT[p]
        in_maps.append(m)
    return in_maps


MID_TENSORS = ("o0_dram", "qt_dram", "pt_dram", "msrow_dram")


def run_pipeline(cfg, inputs, launcher):
    base_maps = host_in_maps(cfg, inputs)
    n = cfg.n_cores
    nc1 = build_program(cfg, "p1")
    res1 = launcher(nc1, base_maps)
    ccg = np.ascontiguousarray(np.concatenate(
        [np.asarray(r["cc_stage_out"], np.float32) for r in res1], axis=0))
    maps2 = []
    for i, m in enumerate(base_maps):
        m2 = dict(m)
        for t in MID_TENSORS:
            m2[t] = np.asarray(res1[i][t])
        m2["cc_gathered"] = ccg
        maps2.append(m2)
    nc2 = build_program(cfg, "p2")
    res2 = launcher(nc2, maps2)
    out = np.concatenate([np.asarray(r["out"], np.float32) for r in res2], axis=0)
    return out


def kernel(**inputs):
    cfg = CFG()

    def launcher(nc, in_maps):
        res = run_bass_kernel_spmd(nc, in_maps, core_ids=list(range(cfg.n_cores)))
        return res.results
    out = run_pipeline(cfg, inputs, launcher)
    return out.reshape(1, cfg.L, cfg.D)


# version 06
# speedup vs baseline: 1.3891x; 1.3891x over previous
import numpy as np
from contextlib import ExitStack

import concourse.bass as bass
import concourse.mybir as mybir
from concourse.alu_op_type import AluOpType as OP
from concourse.bass_utils import run_bass_kernel_spmd

F32 = mybir.dt.float32
BF16 = mybir.dt.bfloat16
AF = mybir.ActivationFunctionType
AX = mybir.AxisListType


class CFG:
    n_cores = 8
    L = 16384
    D = 1024
    H = 16
    dh = 64
    C = 128
    GS = 512
    EPS = 1e-6
    LN_EPS = 1e-5
    debug = False
    trunc = None

    @property
    def R(self): return self.L // self.n_cores
    @property
    def KT(self): return self.D // 128
    @property
    def NG(self): return self.R // self.GS
    @property
    def NC(self): return self.R // self.C


DMA_STREAMS = ("sp", "aq", "gq", "cc")
STREAM_SECTION = {
    "tensor": "tensor", "vector": "vector", "scalar": "scalar",
    "gpsimd": "gpsimd", "sync": "sync",
    "sp": "sync", "aq": "scalar", "gq": "gpsimd", "cc": "gpsimd",
}


class Pl:

    def __init__(self):
        self.items = []
        self.barrier_snap = {}
        self.ordn = {}

    def add(self, stream, fn, reads=(), writes=(), inc=None):
        if inc is None:
            inc = 16 if stream in DMA_STREAMS else 1
        if stream in DMA_STREAMS and stream != "cc":
            o = self.ordn.get(stream, 0)
            self.ordn[stream] = o + 1
            semid = f"{stream}{o % self.NROT}"
        else:
            semid = stream
        self.items.append(dict(s=stream, fn=fn, r=tuple(reads), w=tuple(writes),
                               inc=inc, deps=set(), bar=dict(self.barrier_snap),
                               semid=semid))

    def barrier(self):
        snap = {}
        for i, it in enumerate(self.items):
            snap[it["semid"]] = i
        self.barrier_snap = snap

    NROT = 4

    marks = []

    def mark(self, name):
        self.marks.append((name, len(self.items)))

    def finalize(self, truncate=None, skip=()):
        if truncate is not None:
            self.items = self.items[:truncate]
        if skip:
            self.items = [it for i, it in enumerate(self.items) if i not in skip]
        lastuse = {}
        lastw, readers = {}, {}
        last_psum_rd = {}
        for i, it in enumerate(self.items):
            dep = set(it["bar"].values())
            for r in it["r"]:
                if r in lastw:
                    dep.add(lastw[r])
                if r.startswith("PSB"):
                    pr = last_psum_rd.get(r)
                    if pr is not None and pr[1] != it["s"]:
                        dep.add(pr[0])
                    last_psum_rd[r] = (i, it["s"])
            for w in it["w"]:
                if w in lastw:
                    dep.add(lastw[w])
                for rd in readers.get(w, ()):
                    dep.add(rd)
            for r in it["r"]:
                readers.setdefault(r, []).append(i)
            for w in it["w"]:
                lastw[w] = i
                readers[w] = []
                if w.startswith("PSB"):
                    last_psum_rd.pop(w, None)
            if it["s"] in DMA_STREAMS:
                if it["semid"] in lastuse:
                    dep.add(lastuse[it["semid"]])
                lastuse[it["semid"]] = i
            it["deps"] = {d for d in dep
                          if not (self.items[d]["s"] == it["s"] and it["s"] == "tensor")}
        need = [False] * len(self.items)
        for i, it in enumerate(self.items):
            if it["s"] in DMA_STREAMS:
                need[i] = True
            for d in it["deps"]:
                need[d] = True
        cnt = {}
        self.after = [0] * len(self.items)
        self.neednc = need
        for i, it in enumerate(self.items):
            c = cnt.get(it["semid"], 0)
            if need[i]:
                c += it["inc"]
            cnt[it["semid"]] = c
            self.after[i] = c

    def emit(self, nc, sems):
        by_section = {}
        for i, it in enumerate(self.items):
            by_section.setdefault(STREAM_SECTION[it["s"]], []).append(i)
        with nc.Block() as block:
            for section in ("sync", "scalar", "vector", "tensor", "gpsimd"):
                idxs = by_section.get(section, [])
                dec = getattr(block, section)

                def make(idxs=idxs):
                    def body(eng):
                        waited = {}
                        for i in idxs:
                            it = self.items[i]
                            wmax = {}
                            for d in it["deps"]:
                                ds = self.items[d]["semid"]
                                v = self.after[d]
                                if v > wmax.get(ds, 0):
                                    wmax[ds] = v
                            for ds, v in sorted(wmax.items()):
                                if waited.get(ds, 0) < v:
                                    eng.wait_ge(sems[ds], v)
                                    waited[ds] = v
                            inst = it["fn"](eng)
                            if self.neednc[i]:
                                inst.then_inc(sems[it["semid"]], it["inc"])
                    return body
                dec(make())


class _Rec:

    def __init__(self):
        self.calls = []

    def add(self, *a, **k):
        self.calls.append((a, k))


def _merge_rr(P, recs):
    pos = [0] * len(recs)
    while True:
        any_left = False
        for i, r in enumerate(recs):
            if pos[i] < len(r.calls):
                a, k = r.calls[pos[i]]
                P.add(*a, **k)
                pos[i] += 1
                any_left = True
        if not any_left:
            break


def dup2(ap64):
    return ap64.rearrange("p (o d) -> p o d", o=1).broadcast_to([ap64.shape[0], 2, 64])


def build_program(cfg: CFG, part: str = "p1"):
    R, D, H, dh, C, GS = cfg.R, cfg.D, cfg.H, cfg.dh, cfg.C, cfg.GS
    KT, NG, NC = cfg.KT, cfg.NG, cfg.NC
    NLEV = 6
    assert C == 2 ** (NLEV + 1)
    n_cores = cfg.n_cores
    NT = R // 128

    nc = bass.Bass("TRN2", target_bir_lowering=False)
    P = Pl()

    dt = lambda name, shape, dtype, kind: nc.dram_tensor(name, shape, dtype, kind=kind).ap()
    x_tok_d = dt("x_tok", [R, D], F32, "ExternalInput")
    x_t_d = dt("x_t", [D, R + 3], F32, "ExternalInput")
    x_halo_d = dt("x_halo", [3, D], F32, "ExternalInput")
    WT_d = {p: dt(f"W{p}T", [D, D], F32, "ExternalInput") for p in "qkvo"}
    WbT_d = dt("WbT", [D, H], F32, "ExternalInput")
    conv_d = {p: dt(f"conv_{p}", [D, 4], F32, "ExternalInput") for p in "qkv"}
    onw_d = dt("onw128", [128, 1], F32, "ExternalInput")
    bias_d = dt("biasrow", [1, D], F32, "ExternalInput")
    E1_d = {ri: dt(f"E1{ri}", [4, KT // 2, 128, 512], F32, "ExternalInput") for ri in "RI"}
    E2_d = {ri: dt(f"E2{ri}", [4, 128, 512], F32, "ExternalInput") for ri in "RI"}
    mask_sl_d = dt("mask_sl", [128, 128], F32, "ExternalInput")
    mask_sun_d = dt("mask_sun", [128, 128], F32, "ExternalInput")
    mask_sui_d = dt("mask_sui", [128, 128], F32, "ExternalInput")
    ident_f_d = dt("ident_f", [128, 128], F32, "ExternalInput")
    ident_b_d = dt("ident_b", [128, 128], BF16, "ExternalInput")
    ident64_d = dt("ident64_2", [128, 64], BF16, "ExternalInput")
    ones1_d = dt("ones1", [1, 128], F32, "ExternalInput")
    ppinit_d = dt("ppinit", [64, H * 64], BF16, "ExternalInput")
    onehot_d = dt("onehot", [1, n_cores], F32, "ExternalInput")
    ind_d = dt("ind", [128, KT * H], F32, "ExternalInput")
    indT_d = dt("indT", [H, KT * 128], F32, "ExternalInput")

    out_d = dt("out", [R, D], F32, "ExternalOutput") if part != "p1" else None

    mid_kind = {"full": "Internal", "p1": "ExternalOutput", "p2": "ExternalInput"}[part]
    if part != "p2":
        krm_d = dt("krm_dram", [R, 2 * D], BF16, "Internal")
        vrm_d = dt("vrm_dram", [R, D], BF16, "Internal")
    o0_d = dt("o0_dram", [R, D], BF16, mid_kind)
    qt_d = dt("qt_dram", [128, NC * KT * 128], BF16, mid_kind)
    pt_d = dt("pt_dram", [64, NC * H * 64], BF16, mid_kind)
    msrow_d = dt("msrow_dram", [2, R + 3], F32, mid_kind)
    if part == "full":
        cc_in_d = dt("cc_in", [64, H * 128], F32, "Internal")
        cc_out_d = dt("cc_out", [n_cores * 64, H * 128], F32, "Internal")
    elif part == "p1":
        cc_out1_d = dt("cc_stage_out", [64, H * 128], F32, "ExternalOutput")
    else:
        cc_out_d = dt("cc_gathered", [n_cores * 64, H * 128], F32, "ExternalInput")
    dbg_d = dt("dbg", [128, 32 * 128], F32, "Internal") if cfg.debug else None
    def dbgtap(slot, ap, readkeys):
        if cfg.debug:
            P.add("gq", lambda e, slot=slot, ap=ap: e.dma_start(
                out=dbg_d[:ap.shape[0], slot * 128: slot * 128 + ap.shape[1]], in_=ap),
                reads=tuple(readkeys), writes=("dbg",))

    st = ExitStack()
    sb = lambda name, shape, dtype: st.enter_context(nc.sbuf_tensor(name, shape, dtype))
    ps_ = lambda name, shape, dtype: st.enter_context(nc.psum_tensor(name, shape, dtype))

    sem_names = ["tensor", "vector", "scalar", "gpsimd", "sync", "cc"]
    for s in ("sp", "aq", "gq"):
        sem_names += [f"{s}{r}" for r in range(Pl.NROT)]
    sems = {s: st.enter_context(nc.semaphore(f"sem_{s}")) for s in sem_names}

    psb = [ps_(f"psb{i}", [128, 512], F32) for i in range(6)]
    psT0 = ps_("psT0", [128, 512], BF16)
    psT1 = ps_("psT1", [128, 512], BF16)

    mask_sl = sb("mask_sl_s", [128, 128], F32)
    mask_sun = sb("mask_sun_s", [128, 128], F32)
    mask_sui = sb("mask_sui_s", [128, 128], F32)
    ident_f = sb("ident_f_s", [128, 128], F32)
    ident_b = sb("ident_b_s", [128, 128], BF16)
    ident64 = sb("ident64_s", [128, 64], BF16)
    ones1 = sb("ones1_s", [1, 128], F32)
    onw = sb("onw_s", [128, 1], F32)
    biasrow = sb("biasrow_s", [1, D], F32)
    onehot = sb("onehot_s", [1, n_cores], F32)
    ind = sb("ind_s", [128, KT * H], F32)
    indT = sb("indT_s", [H, KT * 128], F32)
    S_f = sb("S_f", [128, H * 64], F32)
    S_b = sb("S_b", [128, H * 64], BF16)
    PP_b = sb("PP_b", [64, H * 64], BF16)
    N_f = sb("N_f", [128, H * 64], F32)
    N_b = sb("N_b", [128, H * 64], BF16)
    epsln = sb("epsln", [128, 1], F32)
    epsrms = sb("epsrms", [128, 1], F32)

    def ld(dst, src, key, stream="sp"):
        P.add(stream, lambda e, d=dst, s=src: e.dma_start(out=d, in_=s), writes=(key,))

    ld(mask_sl[:], mask_sl_d[:, :], "mask_sl")
    ld(mask_sun[:], mask_sun_d[:, :], "mask_sun")
    ld(mask_sui[:], mask_sui_d[:, :], "mask_sui")
    ld(ident_f[:], ident_f_d[:, :], "ident_f")
    ld(ident_b[:], ident_b_d[:, :], "ident_b")
    ld(ident64[:], ident64_d[:, :], "ident64")
    ld(ones1[:], ones1_d[:, :], "ones1")
    ld(onw[:], onw_d[:, :], "onw")
    ld(biasrow[:], bias_d[:, :], "biasrow")
    ld(onehot[:], onehot_d[:, :], "onehot")
    ld(ind[:], ind_d[:, :], "ind")
    ld(indT[:], indT_d[:, :], "indT")
    P.add("sp", lambda e: e.dma_start(out=PP_b[:], in_=ppinit_d[:, :]), writes=tuple(f"PP_b:{h}" for h in range(H)))
    P.add("gpsimd", lambda e: e.memset(epsln[:], cfg.LN_EPS), writes=("epsln",))
    P.add("gpsimd", lambda e: e.memset(epsrms[:], cfg.EPS), writes=("epsrms",))
    P.add("gpsimd", lambda e: e.memset(S_f[:], 0.0), writes=tuple(f"S_f:{h}" for h in range(H)))
    P.add("gpsimd", lambda e: e.memset(S_b[:], 0.0), writes=tuple(f"S_b:{h}" for h in range(H)))

    run_p1 = part in ("full", "p1")
    run_p2 = part in ("full", "p2")
    stA0 = ExitStack()
    if run_p1:
        sb0 = lambda name, shape, dtype: stA0.enter_context(nc.sbuf_tensor(name, shape, dtype))
        stat_pack = sb0("stat_pack", [128, 2 * NT], F32)
        stat_sum = sb0("stat_sum", [128, NT], F32)
        stat_ssq = sb0("stat_ssq", [128, NT], F32)
        stat_tmp = sb0("stat_tmp", [128, NT], F32)
        evac32 = sb0("evac32", [2 * NT, 128], F32)
        xa = [sb0(f"xa{i}", [128, D], F32) for i in range(2)]
        xsq = sb0("xsq", [128, D], F32)
        for t in range(NT):
            b = t % 2
            P.add("sp", lambda e, b=b, t=t: e.dma_start(out=xa[b][:], in_=x_tok_d[t * 128:(t + 1) * 128, :]),
                  writes=(f"xa{b}",))
            P.add("vector", lambda e, b=b, t=t: e.reduce_sum(stat_sum[:, t:t + 1], xa[b][:], axis=AX.X),
                  reads=(f"xa{b}",), writes=("stat_sum",))
            P.add("scalar", lambda e, b=b, t=t: e.activation(xsq[:], xa[b][:], AF.Square, accum_out=stat_ssq[:, t:t + 1]),
                  reads=(f"xa{b}",), writes=("stat_ssq", "xsq"))
        xh = sb0("xh", [3, D], F32)
        xhsq = sb0("xhsq", [3, D], F32)
        halo_sum = sb0("halo_sum", [3, 1], F32)
        halo_var = sb0("halo_var", [3, 1], F32)
        halo_m2 = sb0("halo_m2", [3, 1], F32)
        halo_s = sb0("halo_s", [3, 2], F32)
        P.add("sp", lambda e: e.dma_start(out=xh[:], in_=x_halo_d[:, :]), writes=("xh",))
        P.add("vector", lambda e: e.reduce_sum(halo_sum[:], xh[:], axis=AX.X),
              reads=("xh",), writes=("halo_sum",))
        P.add("scalar", lambda e: e.activation(xhsq[:], xh[:], AF.Square, accum_out=halo_var[:]),
              reads=("xh",), writes=("halo_var", "xhsq"))
        P.add("vector", lambda e: e.tensor_scalar(stat_sum[:], stat_sum[:], 1.0 / D, 0.0, op0=OP.mult, op1=OP.add),
              reads=("stat_sum",), writes=("stat_sum",))
        P.add("vector", lambda e: e.scalar_tensor_tensor(stat_tmp[:], stat_sum[:], 1.0, stat_sum[:], op0=OP.mult, op1=OP.mult),
              reads=("stat_sum",), writes=("stat_tmp",))
        P.add("vector", lambda e: e.scalar_tensor_tensor(stat_ssq[:], stat_ssq[:], 1.0 / D, stat_tmp[:], op0=OP.mult, op1=OP.subtract),
              reads=("stat_ssq", "stat_tmp"), writes=("stat_ssq",))
        P.add("scalar", lambda e: e.activation(stat_ssq[:], stat_ssq[:], AF.Sqrt, bias=epsln[:]),
              reads=("stat_ssq", "epsln"), writes=("stat_ssq",))
        P.add("vector", lambda e: e.reciprocal(stat_pack[:, NT:2 * NT], stat_ssq[:]),
              reads=("stat_ssq",), writes=("stat_pack",))
        P.add("vector", lambda e: e.scalar_tensor_tensor(stat_pack[:, 0:NT], stat_sum[:], -1.0, stat_pack[:, NT:2 * NT], op0=OP.mult, op1=OP.mult),
              reads=("stat_sum", "stat_pack"), writes=("stat_pack",))
        P.add("vector", lambda e: e.tensor_scalar(halo_sum[:], halo_sum[:], 1.0 / D, 0.0, op0=OP.mult, op1=OP.add),
              reads=("halo_sum",), writes=("halo_sum",))
        P.add("vector", lambda e: e.scalar_tensor_tensor(halo_m2[:], halo_sum[:], 1.0, halo_sum[:], op0=OP.mult, op1=OP.mult),
              reads=("halo_sum",), writes=("halo_m2",))
        P.add("vector", lambda e: e.scalar_tensor_tensor(halo_var[:], halo_var[:], 1.0 / D, halo_m2[:], op0=OP.mult, op1=OP.subtract),
              reads=("halo_var", "halo_m2"), writes=("halo_var",))
        P.add("scalar", lambda e: e.activation(halo_var[:], halo_var[:], AF.Sqrt, bias=epsln[0:3, :]),
              reads=("halo_var", "epsln"), writes=("halo_var",))
        P.add("vector", lambda e: e.reciprocal(halo_s[:, 1:2], halo_var[:]),
              reads=("halo_var",), writes=("halo_s",))
        P.add("vector", lambda e: e.scalar_tensor_tensor(halo_s[:, 0:1], halo_sum[:], -1.0, halo_s[:, 1:2], op0=OP.mult, op1=OP.mult),
              reads=("halo_sum", "halo_s"), writes=("halo_s",))
        P.add("tensor", lambda e: e.transpose(psb[0][0:2 * NT, 0:128], stat_pack[:], ident_f[:]),
              reads=("stat_pack", "ident_f"), writes=("PSB0",))
        P.add("vector", lambda e: e.tensor_copy(evac32[:], psb[0][0:2 * NT, 0:128]),
              reads=("PSB0",), writes=("evac32",))
        P.add("aq", lambda e: e.dma_start(out=msrow_d[0:1, 3:3 + NT * 128].rearrange("o (t p) -> (o t) p", t=NT),
                                          in_=evac32[0:NT, :]),
              reads=("evac32",), writes=("msrow_d",))
        P.add("aq", lambda e: e.dma_start(out=msrow_d[1:2, 3:3 + NT * 128].rearrange("o (t p) -> (o t) p", t=NT),
                                          in_=evac32[NT:2 * NT, :]),
              reads=("evac32",), writes=("msrow_d",))
        P.add("aq", lambda e: e.dma_start(out=msrow_d[0:1, 0:3].rearrange("o t -> t o"), in_=halo_s[:, 0:1]),
              reads=("halo_s",), writes=("msrow_d",))
        P.add("aq", lambda e: e.dma_start(out=msrow_d[1:2, 0:3].rearrange("o t -> t o"), in_=halo_s[:, 1:2]),
              reads=("halo_s",), writes=("msrow_d",))
        P.barrier()
        stA0.close()

        P.mark("A0_done")
        stABC = ExitStack()
        sbB = lambda name, shape, dtype: stABC.enter_context(nc.sbuf_tensor(name, shape, dtype))
        qtf = sbB("qtf", [128, KT * R], BF16)
        ktf = sbB("ktf", [128, KT * R], BF16)
        vtf = sbB("vtf", [128, KT * R], BF16)
        RES = {"q": qtf, "k": ktf, "v": vtf}

        stA = ExitStack()
        sbA = lambda name, shape, dtype: stA.enter_context(nc.sbuf_tensor(name, shape, dtype))
        wbuf = [sbA(f"wbuf{i}", [128, D], BF16) for i in range(2)]
        pbuf = [sbA(f"pbuf{i}", [128, GS + 3], BF16) for i in range(KT)]
        tails = sbA("tails", [128, 3 * 3 * KT], BF16)
        xnf = sbA("xnf", [128, KT * GS], BF16)
        xnf_h = sbA("xnf_h", [128, KT * 3], BF16)
        xtg = sbA("xtg", [128, KT * GS], BF16)
        xtg_h = sbA("xtg_h", [128, KT * 3], BF16)
        msbc = sbA("msbc", [128, GS], BF16)
        sbc = sbA("sbc", [128, GS], BF16)
        msbc_h = sbA("msbc_h", [128, 3], BF16)
        sbc_h = sbA("sbc_h", [128, 3], BF16)
        conv_sb = sbA("conv_sb", [128, KT * 4 * 3], F32)
        ybuf = sbA("ybuf", [128, GS], BF16)
        WbT_sb = sbA("WbT_sb", [128, KT * H], BF16)
        msg = sbA("msg", [1, GS], F32)
        ssg = sbA("ssg", [1, GS], F32)
        msg_h = sbA("msg_h", [1, 3], F32)
        ssg_h = sbA("ssg_h", [1, 3], F32)
        beta_hm = sbA("beta_hm", [H, R], F32)
        scale_q = sbA("scale_q", [H, R], F32)
        scale_k = sbA("scale_k", [H, R], F32)
        scale_v = sbA("scale_v", [H, R], F32)

        for pi, p in enumerate("qkv"):
            P.add("gq", lambda e, pi=pi, p=p: e.dma_start(
                out=conv_sb[:, pi * KT * 4:(pi + 1) * KT * 4].rearrange("p (a k) -> p a k", k=4),
                in_=conv_d[p].rearrange("(a p) k -> p a k", p=128)),
                writes=("conv_sb",))
        P.add("gq", lambda e: e.dma_start(out=WbT_sb[:].rearrange("p (a h) -> p a h", a=KT),
                                          in_=WbT_d.rearrange("(a p) h -> p a h", p=128)),
              writes=("WbT_sb",))

        def emit_xnf(grp, dst, dkey, xsrc, xkey, mb, sbuf_b, with_halo, dst_h=None, hx=None, mbh=None, sbh=None,
                     mrow=None, srw=None, mrow_h=None, srw_h=None):
            c0 = 3 + grp * GS
            for t in range(KT):
                P.add("gq", lambda e, t=t, c0=c0, xsrc=xsrc: e.dma_start(
                    out=xsrc[:, t * GS:(t + 1) * GS], in_=x_t_d[t * 128:(t + 1) * 128, c0:c0 + GS]),
                    writes=(f"{xkey}:{t}",))
            P.add("sp", lambda e, c0=c0, mrow=mrow: e.dma_start(out=mrow[:], in_=msrow_d[0:1, c0:c0 + GS]),
                  reads=("msrow_d",), writes=(f"{dkey}_mrow",))
            P.add("sp", lambda e, c0=c0, srw=srw: e.dma_start(out=srw[:], in_=msrow_d[1:2, c0:c0 + GS]),
                  reads=("msrow_d",), writes=(f"{dkey}_srow",))
            P.add("tensor", lambda e, mrow=mrow: e.matmul(psb[3][:, 0:GS], ones1[:], mrow[:], start=True, stop=True),
                  reads=("ones1", f"{dkey}_mrow"), writes=("PSB3",))
            P.add("tensor", lambda e, srw=srw: e.matmul(psb[4][:, 0:GS], ones1[:], srw[:], start=True, stop=True),
                  reads=("ones1", f"{dkey}_srow"), writes=("PSB4",))
            P.add("scalar", lambda e, mb=mb: e.copy(mb[:], psb[3][:, 0:GS]), reads=("PSB3",), writes=(f"{dkey}_mb",))
            P.add("scalar", lambda e, sbuf_b=sbuf_b: e.copy(sbuf_b[:], psb[4][:, 0:GS]), reads=("PSB4",), writes=(f"{dkey}_sb",))
            if with_halo:
                for t in range(KT):
                    P.add("gq", lambda e, t=t, hx=hx: e.dma_start(
                        out=hx[:, t * 3:(t + 1) * 3], in_=x_t_d[t * 128:(t + 1) * 128, 0:3]),
                        writes=(f"xtg_h:{t}",))
                P.add("sp", lambda e, mrow_h=mrow_h: e.dma_start(out=mrow_h[:], in_=msrow_d[0:1, 0:3]),
                      reads=("msrow_d",), writes=("mrow_h",))
                P.add("sp", lambda e, srw_h=srw_h: e.dma_start(out=srw_h[:], in_=msrow_d[1:2, 0:3]),
                      reads=("msrow_d",), writes=("srow_h",))
                P.add("tensor", lambda e, mrow_h=mrow_h: e.matmul(psb[3][:, 0:3], ones1[:], mrow_h[:], start=True, stop=True),
                      reads=("ones1", "mrow_h"), writes=("PSB3",))
                P.add("tensor", lambda e, srw_h=srw_h: e.matmul(psb[4][:, 0:3], ones1[:], srw_h[:], start=True, stop=True),
                      reads=("ones1", "srow_h"), writes=("PSB4",))
                P.add("scalar", lambda e, mbh=mbh: e.copy(mbh[:], psb[3][:, 0:3]), reads=("PSB3",), writes=("msbc_h",))
                P.add("scalar", lambda e, sbh=sbh: e.copy(sbh[:], psb[4][:, 0:3]), reads=("PSB4",), writes=("sbc_h",))
            for t in range(KT):
                P.add("vector", lambda e, t=t, dst=dst, xsrc=xsrc, mb=mb: e.tensor_tensor(
                    dst[:, t * GS:(t + 1) * GS], xsrc[:, t * GS:(t + 1) * GS], mb[:], op=OP.add),
                    reads=(f"{xkey}:{t}", f"{dkey}_mb"), writes=(f"{dkey}:{t}",))
                P.add("vector", lambda e, t=t, dst=dst, sbuf_b=sbuf_b: e.tensor_tensor(
                    dst[:, t * GS:(t + 1) * GS], dst[:, t * GS:(t + 1) * GS], sbuf_b[:], op=OP.mult),
                    reads=(f"{dkey}:{t}", f"{dkey}_sb"), writes=(f"{dkey}:{t}",))
                if with_halo:
                    P.add("vector", lambda e, t=t, dst_h=dst_h, hx=hx, mbh=mbh: e.tensor_tensor(
                        dst_h[:, t * 3:(t + 1) * 3], hx[:, t * 3:(t + 1) * 3], mbh[:], op=OP.add),
                        reads=(f"xtg_h:{t}", "msbc_h"), writes=(f"xnf_h:{t}",))
                    P.add("vector", lambda e, t=t, dst_h=dst_h, sbh=sbh: e.tensor_tensor(
                        dst_h[:, t * 3:(t + 1) * 3], dst_h[:, t * 3:(t + 1) * 3], sbh[:], op=OP.mult),
                        reads=(f"xnf_h:{t}", "sbc_h"), writes=(f"xnf_h:{t}",))

        for g in range(NG):
            emit_xnf(g, xnf, "xnf", xtg, "xtg", msbc, sbc, g == 0, xnf_h, xtg_h, msbc_h, sbc_h,
                     mrow=msg, srw=ssg, mrow_h=msg_h, srw_h=ssg_h)
            for t in range(KT):
                P.add("tensor", lambda e, t=t: e.matmul(
                    psb[2][0:H, 0:GS], WbT_sb[:, t * H:(t + 1) * H], xnf[:, t * GS:(t + 1) * GS],
                    start=(t == 0), stop=(t == KT - 1)),
                    reads=("WbT_sb", f"xnf:{t}"), writes=("PSB2",))
            P.add("scalar", lambda e, g=g: e.activation(beta_hm[:, g * GS:(g + 1) * GS], psb[2][0:H, 0:GS], AF.Tanh, scale=0.5),
                  reads=("PSB2",), writes=("beta_hm",))
            P.add("vector", lambda e, g=g: e.tensor_scalar(
                beta_hm[:, g * GS:(g + 1) * GS], beta_hm[:, g * GS:(g + 1) * GS], 0.5, 0.5, op0=OP.mult, op1=OP.add),
                reads=("beta_hm",), writes=("beta_hm",))
            for pi, p in enumerate("qkv"):
                for o in range(KT):
                    bk = o % 2
                    wb = o % 2
                    P.add("gq", lambda e, p=p, o=o, wb=wb: e.dma_start(
                        out=wbuf[wb][:].rearrange("p (a n) -> p a n", a=KT),
                        in_=WT_d[p][:, o * 128:(o + 1) * 128].rearrange("(a p) n -> p a n", p=128)),
                        writes=(f"wbuf{wb}",))
                    for k in range(KT):
                        P.add("tensor", lambda e, k=k, wb=wb, bk=bk: e.matmul(
                            psb[bk][:, 0:GS], wbuf[wb][:, k * 128:(k + 1) * 128], xnf[:, k * GS:(k + 1) * GS],
                            start=(k == 0), stop=(k == KT - 1)),
                            reads=(f"wbuf{wb}", f"xnf:{k}"), writes=(f"PSB{bk}",))
                    if g == 0:
                        for k in range(KT):
                            P.add("tensor", lambda e, k=k, wb=wb, bk=bk: e.matmul(
                                psb[5][:, bk * 4:bk * 4 + 3], wbuf[wb][:, k * 128:(k + 1) * 128], xnf_h[:, k * 3:(k + 1) * 3],
                                start=(k == 0), stop=(k == KT - 1)),
                                reads=(f"wbuf{wb}", f"xnf_h:{k}"), writes=("PSB5",))
                        P.add("scalar", lambda e, o=o, bk=bk: e.copy(pbuf[o][:, 0:3], psb[5][:, bk * 4:bk * 4 + 3]),
                              reads=("PSB5",), writes=(f"pbuf:{o}",))
                    else:
                        P.add("gpsimd", lambda e, pi=pi, o=o: e.tensor_copy(
                            pbuf[o][:, 0:3], tails[:, (pi * KT + o) * 3:(pi * KT + o) * 3 + 3]),
                            reads=(f"tails:{pi}:{o}",), writes=(f"pbuf:{o}",))
                    P.add("scalar", lambda e, o=o, bk=bk: e.copy(pbuf[o][:, 3:3 + GS], psb[bk][:, 0:GS]),
                          reads=(f"PSB{bk}",), writes=(f"pbuf:{o}",))
                    P.add("gpsimd", lambda e, pi=pi, o=o: e.tensor_copy(
                        tails[:, (pi * KT + o) * 3:(pi * KT + o) * 3 + 3], pbuf[o][:, GS:GS + 3]),
                        reads=(f"pbuf:{o}",), writes=(f"tails:{pi}:{o}",))
                    cwbase = pi * KT * 4 + o * 4
                    P.add("vector", lambda e, o=o, cwbase=cwbase: e.tensor_scalar(
                        ybuf[:], pbuf[o][:, 0:GS], conv_sb[:, cwbase:cwbase + 1], 0.0, op0=OP.mult, op1=OP.add),
                        reads=(f"pbuf:{o}", "conv_sb"), writes=("ybuf",))
                    for j in (1, 2, 3):
                        P.add("vector", lambda e, o=o, j=j, cwbase=cwbase: e.scalar_tensor_tensor(
                            ybuf[:], pbuf[o][:, j:j + GS], conv_sb[:, cwbase + j:cwbase + j + 1], ybuf[:],
                            op0=OP.mult, op1=OP.add),
                            reads=(f"pbuf:{o}", "conv_sb", "ybuf"), writes=("ybuf",))
                    P.add("scalar", lambda e, p=p, o=o, g=g: e.activation(
                        RES[p][:, o * R + g * GS: o * R + g * GS + GS], ybuf[:], AF.Silu),
                        reads=("ybuf",), writes=(f"res:{p}:{o}",))
        P.barrier()

        P.mark("A_done")
        sqs = [sbA(f"sqs{i}", [128, GS], F32) for i in range(2)]
        sqb = sbA("sqb", [H, R], F32)
        scbc = sbA("scbc", [128, GS], BF16)
        for g in range(NG):
            for p in ("q", "k"):
                scale = scale_q if p == "q" else scale_k
                res = RES[p]
                for t in range(KT):
                    sq = sqs[t % 2]
                    P.add("scalar", lambda e, res=res, t=t, g=g, sq=sq, p=p: e.activation(
                        sq[:], res[:, t * R + g * GS: t * R + g * GS + GS], AF.Square),
                        reads=(f"res:{p}:{t}",), writes=(f"sqs{t % 2}",))
                    P.add("tensor", lambda e, t=t, sq=sq: e.matmul(
                        psb[2][0:H, 0:GS], ind[:, t * H:(t + 1) * H], sq[:],
                        start=(t == 0), stop=(t == KT - 1)),
                        reads=("ind", f"sqs{t % 2}"), writes=("PSB2",))
                P.add("vector", lambda e, scale=scale, g=g, p=p: e.tensor_copy(
                    scale[:, g * GS:(g + 1) * GS], psb[2][0:H, 0:GS]),
                    reads=("PSB2",), writes=(f"scale_{p}",))
        P.add("scalar", lambda e: e.activation(sqb[:], beta_hm[:], AF.Sqrt), reads=("beta_hm",), writes=("sqb",))
        for p in ("q", "k"):
            scale = scale_q if p == "q" else scale_k
            P.add("scalar", lambda e, scale=scale: e.activation(scale[:], scale[:], AF.Sqrt),
                  reads=(f"scale_{p}",), writes=(f"scale_{p}",))
            P.add("vector", lambda e, scale=scale: e.tensor_scalar(scale[:], scale[:], cfg.EPS, 0.0, op0=OP.max, op1=OP.add),
                  reads=(f"scale_{p}",), writes=(f"scale_{p}",))
            P.add("vector", lambda e, scale=scale: e.reciprocal(scale[:], scale[:]),
                  reads=(f"scale_{p}",), writes=(f"scale_{p}",))
        P.add("vector", lambda e: e.tensor_scalar(scale_q[:], scale_q[:], 0.125, 0.0, op0=OP.mult, op1=OP.add),
              reads=("scale_q",), writes=("scale_q",))
        P.add("vector", lambda e: e.tensor_tensor(scale_k[:], scale_k[:], sqb[:], op=OP.mult),
              reads=("scale_k", "sqb"), writes=("scale_k",))
        P.add("vector", lambda e: e.tensor_copy(scale_v[:], sqb[:]), reads=("sqb",), writes=("scale_v",))
        for g in range(NG):
            for p in ("q", "k", "v"):
                scale = {"q": scale_q, "k": scale_k, "v": scale_v}[p]
                res = RES[p]
                for t in range(KT):
                    bb = 3 + (t % 2)
                    P.add("tensor", lambda e, scale=scale, t=t, g=g, bb=bb, p=p: e.matmul(
                        psb[bb][:, 0:GS], indT[:, t * 128:(t + 1) * 128], scale[:, g * GS:(g + 1) * GS],
                        start=True, stop=True),
                        reads=("indT", f"scale_{p}"), writes=(f"PSB{bb}",))
                    P.add("scalar", lambda e, bb=bb: e.copy(scbc[:], psb[bb][:, 0:GS]),
                          reads=(f"PSB{bb}",), writes=("scbc",))
                    P.add("vector", lambda e, res=res, t=t, g=g, p=p: e.tensor_tensor(
                        res[:, t * R + g * GS: t * R + g * GS + GS],
                        res[:, t * R + g * GS: t * R + g * GS + GS], scbc[:], op=OP.mult),
                        reads=(f"res:{p}:{t}", "scbc"), writes=(f"res:{p}:{t}",))
        P.barrier()

        P.mark("B_done")
        stg = [sbA(f"stg{i}", [128, D], BF16) for i in range(2)]
        for nm in ("k", "v"):
            src = RES[nm]
            dram = krm_d if nm == "k" else vrm_d
            for c in range(NC):
                par = c % 2
                for t in range(KT):
                    pt = t % 2
                    pst = psT0 if pt == 0 else psT1
                    pstk = "PSB6" if pt == 0 else "PSB7"
                    P.add("tensor", lambda e, src=src, c=c, t=t, pst=pst: e.transpose(
                        pst[:, 0:128], src[:, t * R + c * 128: t * R + c * 128 + 128], ident_b[:]),
                        reads=(f"res:{nm}:{t}", "ident_b"), writes=(pstk,))
                    P.add("scalar", lambda e, t=t, par=par, pst=pst: e.copy(
                        stg[par][:, t * 128:(t + 1) * 128], pst[:, 0:128]),
                        reads=(pstk,), writes=(f"stg{par}",))
                if nm == "v":
                    P.add("aq", lambda e, dram=dram, c=c, par=par: e.dma_start(
                        out=dram[c * 128:(c + 1) * 128, :], in_=stg[par][:]),
                        reads=(f"stg{par}",), writes=(f"{nm}rmd:{c}",))
                else:
                    for dup in range(2):
                        P.add("aq", lambda e, dram=dram, c=c, par=par, dup=dup: e.dma_start(
                            out=dram[c * 128:(c + 1) * 128, :].rearrange("p (h u d) -> p h u d", h=H, u=2)[:, :, dup],
                            in_=stg[par][:].rearrange("p (h d) -> p h d", h=H)),
                            reads=(f"stg{par}",), writes=(f"{nm}rmd:{c}:{dup}",))
        P.barrier()
        stA.close()

        P.mark("B2_done")
        stC = ExitStack()
        sbC = lambda name, shape, dtype: stC.enter_context(nc.sbuf_tensor(name, shape, dtype))
        krm = [sbC(f"krm{i}", [128, 2 * D], BF16) for i in range(2)]
        vrm = [sbC(f"vrm{i}", [128, D], BF16) for i in range(2)]
        o0stg = [sbC(f"o0stg{i}", [128, H * 64], BF16) for i in range(2)]
        qtstg = [sbC(f"qtstg{i}", [128, KT * 128], BF16) for i in range(2)]
        ptstg = [sbC(f"ptstg{i}", [64, H * 64], BF16) for i in range(2)]

        GsL = [sbC(f"GsL{i}", [128, 256], F32) for i in range(3)]
        A_sb = [sbC(f"A_sb{i}", [128, 128], BF16) for i in range(3)]
        AT_sb = [sbC(f"AT_sb{i}", [128, 128], BF16) for i in range(3)]
        LT_sb = [sbC(f"LT_sb{i}", [128, 128], BF16) for i in range(3)]
        X_sb = [sbC(f"X_sb{i}", [128, 128], BF16) for i in range(6)]
        RRT = [[sbC(f"RRT{s}_{l}", [128, 256], BF16) for l in range(NLEV)] for s in range(3)]
        for c in range(NC):
            cp = c % 2
            P.add("sp", lambda e, c=c, cp=cp: e.dma_start(out=krm[cp][:], in_=krm_d[c * 128:(c + 1) * 128, :]),
                  reads=(f"krmd:{c}:0", f"krmd:{c}:1"), writes=(f"krm{cp}",))
            P.add("sp", lambda e, c=c, cp=cp: e.dma_start(out=vrm[cp][:], in_=vrm_d[c * 128:(c + 1) * 128, :]),
                  reads=(f"vrmd:{c}",), writes=(f"vrm{cp}",))
            for g0 in range(0, H, 3):
              grecs = []
              for h in range(g0, min(g0 + 3, H)):
                PX = _Rec()
                grecs.append(PX)
                t, hw = h // 2, h % 2
                wp = h % 3
                bA, bB = f"PSB{wp}", f"PSB{3 + wp}"
                pA, pB = psb[wp], psb[3 + wp]
                kf = ktf[64 * hw:64 * hw + 64, t * R + c * 128: t * R + c * 128 + 128]
                qf = qtf[64 * hw:64 * hw + 64, t * R + c * 128: t * R + c * 128 + 128]
                rk, rq = f"res:k:{t}", f"res:q:{t}"
                Xc, Xn = X_sb[2 * wp], X_sb[2 * wp + 1]
                XcK, XnK = f"X{2 * wp}", f"X{2 * wp + 1}"
                Sb_sl = S_b[64 * hw:64 * hw + 64, h * 64:(h + 1) * 64]
                PX.add("tensor", lambda e, kf=kf, pA=pA, hw=hw: e.matmul(pA[:, 0:128], kf, kf, start=True, stop=True,
                      tile_position=(64 * hw, 0)),
                      reads=(rk,), writes=(bA,))
                PX.add("tensor", lambda e, kf=kf, qf=qf, pA=pA, hw=hw: e.matmul(pA[:, 128:256], kf, qf, start=True, stop=True,
                      tile_position=(64 * hw, 0)),
                      reads=(rk, rq), writes=(bA,))
                PX.add("tensor", lambda e, kf=kf, pA=pA, Sb_sl=Sb_sl, hw=hw: e.matmul(
                    pA[:, 256:320], kf, Sb_sl, start=True, stop=True, tile_position=(64 * hw, 0)),
                    reads=(rk, f"S_b:{h}"), writes=(bA,))
                PX.add("scalar", lambda e, pA=pA, wp=wp: e.copy(GsL[wp][:], pA[:, 0:256]),
                      reads=(bA,), writes=(f"GsL{wp}",))
                PX.add("gpsimd", lambda e, wp=wp: e.tensor_tensor(A_sb[wp][:], GsL[wp][:, 0:128], mask_sl[:], op=OP.mult),
                      reads=(f"GsL{wp}", "mask_sl"), writes=(f"A{wp}",))
                PX.add("gpsimd", lambda e, wp=wp: e.tensor_tensor(AT_sb[wp][:], GsL[wp][:, 0:128], mask_sun[:], op=OP.mult),
                      reads=(f"GsL{wp}", "mask_sun"), writes=(f"AT{wp}",))
                PX.add("vector", lambda e, wp=wp: e.tensor_tensor(LT_sb[wp][:], GsL[wp][:, 128:256], mask_sui[:], op=OP.mult),
                      reads=(f"GsL{wp}", "mask_sui"), writes=(f"LT{wp}",))
                PX.add("gpsimd", lambda e, Xc=Xc, cp=cp, h=h: e.tensor_scalar_mul(
                    Xc[:, 0:64], krm[cp][:, h * 128:h * 128 + 64], -1.0),
                    reads=(f"krm{cp}",), writes=(XcK,))
                PX.add("vector", lambda e, Xc=Xc, cp=cp, h=h, pA=pA: e.tensor_sub(
                    Xc[:, 64:128], vrm[cp][:, h * 64:(h + 1) * 64], pA[:, 256:320]),
                    reads=(f"vrm{cp}", bA), writes=(XcK,))
                prevR, prevRK = A_sb[wp][:], f"A{wp}"
                prevRT, prevRTK = AT_sb[wp][:], f"AT{wp}"
                for lev in range(1, NLEV + 1):
                    buf = RRT[wp][lev - 1]
                    bK = f"RRT{wp}:{lev}"
                    if lev < NLEV:
                        PX.add("tensor", lambda e, pA=pA, prevR=prevR, prevRT=prevRT: e.matmul(
                            pA[:, 0:128], prevRT, prevR, start=True, stop=True),
                            reads=(prevRK, prevRTK), writes=(bA,))
                    PX.add("tensor", lambda e, pA=pA, prevR=prevR, prevRT=prevRT: e.matmul(
                        pA[:, 128:256], prevR, prevRT, start=True, stop=True),
                        reads=(prevRK, prevRTK), writes=(bA,))
                    if lev < NLEV:
                        PX.add("scalar", lambda e, pA=pA, buf=buf: e.copy(buf[:], pA[:, 0:256]),
                              reads=(bA,), writes=(bK,))
                        prevR, prevRK = buf[:, 0:128], bK
                        prevRT, prevRTK = buf[:, 128:256], bK
                    else:
                        PX.add("scalar", lambda e, pA=pA, buf=buf: e.copy(buf[:, 128:256], pA[:, 128:256]),
                              reads=(bA,), writes=(bK,))
                for j in list(range(NLEV, 0, -1)) + [0]:
                    if j > 0:
                        rt, rtK = RRT[wp][j - 1][:, 128:256], f"RRT{wp}:{j}"
                    else:
                        rt, rtK = AT_sb[wp][:], f"AT{wp}"
                    PX.add("tensor", lambda e, pB=pB, rt=rt, Xc=Xc: e.matmul(
                        pB[:, 0:128], rt, Xc[:], start=True, stop=True),
                        reads=(rtK, XcK), writes=(bB,))
                    PX.add("vector", lambda e, pB=pB, Xn=Xn, Xc=Xc: e.tensor_add(Xn[:], Xc[:], pB[:, 0:128]),
                          reads=(bB, XcK), writes=(XnK,))
                    Xc, Xn = Xn, Xc
                    XcK, XnK = XnK, XcK
                PX.add("tensor", lambda e, qf=qf, pB=pB, Sb_sl=Sb_sl, hw=hw: e.matmul(
                    pB[:, 128:192], qf, Sb_sl, start=True, stop=False, tile_position=(64 * hw, 0)),
                    reads=(rq, f"S_b:{h}"), writes=(bB,))
                PX.add("tensor", lambda e, pB=pB, wp=wp, Xc=Xc: e.matmul(
                    pB[:, 128:192], LT_sb[wp][:], Xc[:, 64:128], start=False, stop=True),
                    reads=(f"LT{wp}", XcK), writes=(bB,))
                PX.add("tensor", lambda e, pB=pB, cp=cp, h=h, Xc=Xc: e.matmul(
                    pB[:, 192:256], krm[cp][:, h * 128:(h + 1) * 128], Xc[:, 64:128], start=True, stop=True),
                    reads=(f"krm{cp}", XcK), writes=(bB,))
                PX.add("tensor", lambda e, pB=pB: e.matmul(
                    pB[0:64, 256:320], ident64[0:64, :], ident64[0:64, :], start=True, stop=False),
                    reads=("ident64",), writes=(bB,))
                PX.add("tensor", lambda e, pB=pB, cp=cp, h=h, Xc=Xc: e.matmul(
                    pB[0:64, 256:320], Xc[:, 0:64], krm[cp][:, h * 128:h * 128 + 64], start=False, stop=True),
                    reads=(f"krm{cp}", XcK), writes=(bB,))
                PX.add("tensor", lambda e, pB=pB, qf=qf, hw=hw: e.matmul(
                    pB[64 * hw:64 * hw + 64, 320:448], ident64[64 * hw:64 * hw + 64, :], qf, start=True, stop=False,
                    tile_position=(64 * hw, 64 * hw)),
                    reads=("ident64", rq), writes=(bB,))
                PX.add("tensor", lambda e, pB=pB, wp=wp, Xc=Xc, hw=hw: e.matmul(
                    pB[64 * hw:64 * hw + 64, 320:448], Xc[:, 0:64], LT_sb[wp][:], start=False, stop=True,
                    tile_position=(0, 64 * hw)),
                    reads=(f"LT{wp}", XcK), writes=(bB,))
                PX.add("vector", lambda e, pB=pB, h=h: e.tensor_add(
                    S_f[:, h * 64:(h + 1) * 64], S_f[:, h * 64:(h + 1) * 64], pB[:, 192:256]),
                    reads=(bB, f"S_f:{h}"), writes=(f"S_f:{h}",))
                PX.add("scalar", lambda e, h=h: e.copy(S_b[:, h * 64:(h + 1) * 64], S_f[:, h * 64:(h + 1) * 64]),
                      reads=(f"S_f:{h}",), writes=(f"S_b:{h}",))
                PX.add("scalar", lambda e, pB=pB, cp=cp, h=h: e.copy(
                    ptstg[cp][:, h * 64:(h + 1) * 64], pB[0:64, 256:320]),
                    reads=(bB,), writes=(f"PT:{c}:{h}",))
                PX.add("scalar", lambda e, pB=pB, cp=cp, h=h: e.copy(
                    o0stg[cp][:, h * 64:(h + 1) * 64], pB[:, 128:192]),
                    reads=(bB,), writes=(f"o0stg{cp}",))
                PX.add("scalar", lambda e, pB=pB, cp=cp, t=t, hw=hw: e.copy(
                    qtstg[cp][64 * hw:64 * hw + 64, t * 128:(t + 1) * 128], pB[64 * hw:64 * hw + 64, 320:448]),
                    reads=(bB,), writes=(f"qtstg{cp}",))
                PX.add("tensor", lambda e, pB=pB, cp=cp, h=h: e.matmul(
                    pB[0:64, 448:512], ptstg[cp][:, h * 64:(h + 1) * 64], PP_b[0:64, h * 64:(h + 1) * 64],
                    start=True, stop=True),
                    reads=(f"PT:{c}:{h}", f"PP_b:{h}"), writes=(bB,))
                PX.add("vector", lambda e, pB=pB, h=h: e.tensor_copy(PP_b[0:64, h * 64:(h + 1) * 64], pB[0:64, 448:512]),
                      reads=(bB,), writes=(f"PP_b:{h}",))
              _merge_rr(P, grecs)
            P.add("aq", lambda e, c=c, cp=cp: e.dma_start(out=o0_d[c * 128:(c + 1) * 128, :], in_=o0stg[cp][:]),
                  reads=(f"o0stg{cp}",), writes=(f"o0d:{c}",))
            P.add("aq", lambda e, c=c, cp=cp: e.dma_start(
                out=qt_d[:, c * KT * 128:(c + 1) * KT * 128], in_=qtstg[cp][:]),
                reads=(f"qtstg{cp}",), writes=(f"qtd:{c}",))
            P.add("aq", lambda e, c=c, cp=cp: e.dma_start(
                out=pt_d[:, c * H * 64:(c + 1) * H * 64], in_=ptstg[cp][:]),
                reads=tuple(f"PT:{c}:{hh}" for hh in range(H)), writes=(f"ptd:{c}",))
        P.mark("C_done")
        ccstg = sbC("ccstg", [64, H * 128], F32)
        for h in range(H):
            pt = h % 2
            pst = psT0 if pt == 0 else psT1
            pstk = "PSB6" if pt == 0 else "PSB7"
            P.add("tensor", lambda e, h=h, pst=pst: e.transpose(
                pst[0:64, 0:64], PP_b[0:64, h * 64:(h + 1) * 64], ident_b[0:64, 0:64]),
                reads=(f"PP_b:{h}", "ident_b"), writes=(pstk,))
            P.add("scalar", lambda e, h=h, pst=pst: e.copy(ccstg[:, h * 128:h * 128 + 64], pst[0:64, 0:64]),
                  reads=(pstk,), writes=("ccstg",))
            P.add("vector", lambda e, h=h: e.tensor_copy(ccstg[:, h * 128 + 64:h * 128 + 128], S_f[0:64, h * 64:(h + 1) * 64]),
                  reads=(f"S_f:{h}",), writes=("ccstg",))
        if part == "full":
            P.add("gq", lambda e: e.dma_start(out=cc_in_d[:, :], in_=ccstg[:]), reads=("ccstg",), writes=("cc_in",))
            P.add("cc", lambda e: e.collective_compute(
                "AllGather", OP.bypass, replica_groups=[list(range(n_cores))],
                ins=[cc_in_d[:, :]], outs=[cc_out_d[:, :]]),
                reads=("cc_in",), writes=("cc_out",))
        else:
            P.add("aq", lambda e: e.dma_start(out=cc_out1_d[:, :], in_=ccstg[:]), reads=("ccstg",), writes=("cc_out",))
            P.add("gpsimd", lambda e: e.engine_nop(),
                  reads=("cc_out", "msrow_d") + tuple(f"o0d:{c}" for c in range(NC))
                        + tuple(f"qtd:{c}" for c in range(NC)) + tuple(f"ptd:{c}" for c in range(NC)))
        P.barrier()
        stC.close()
        stABC.close()
        P.barrier()

    if run_p2:
        stD = ExitStack()
        sbD = lambda name, shape, dtype: stD.enter_context(nc.sbuf_tensor(name, shape, dtype))
        ccg = [sbD(f"ccgm{i}", [128, H * 192], F32) for i in range(2)]
        Sm = sbD("Sm", [128, H * 64], F32)
        Sacc = sbD("Sacc", [128, H * 64], F32)
        sel_tmp = sbD("sel_tmp", [128, H * 64], F32)
        ohbc = sbD("ohbc", [128, n_cores], F32)
        P.add("gpsimd", lambda e: e.memset(Sm[:], 0.0), writes=("Sm",))
        P.add("gpsimd", lambda e: e.memset(Sacc[:], 0.0), writes=("Sacc",))
        P.add("tensor", lambda e: e.matmul(psb[5][:, 0:n_cores], ones1[:], onehot[:], start=True, stop=True),
              reads=("ones1", "onehot"), writes=("PSB5",))
        P.add("vector", lambda e: e.tensor_copy(ohbc[:], psb[5][:, 0:n_cores]), reads=("PSB5",), writes=("ohbc",))
        for m in range(n_cores):
            if m < n_cores - 1:
                mp = m % 2
                for half in range(2):
                    for dup in range(2):
                        P.add("sp", lambda e, m=m, mp=mp, half=half, dup=dup: e.dma_start(
                            out=ccg[mp][64 * half:64 * half + 64, :].rearrange("p (h x d) -> p h x d", h=H, x=3)[:, :, dup],
                            in_=cc_out_d[m * 64:(m + 1) * 64, :].rearrange("p (h x d) -> p h x d", h=H, x=2)[:, :, 0]),
                            reads=("cc_out",), writes=(f"ccgm{mp}",))
                    P.add("sp", lambda e, m=m, mp=mp, half=half: e.dma_start(
                        out=ccg[mp][64 * half:64 * half + 64, :].rearrange("p (h x d) -> p h x d", h=H, x=3)[:, :, 2],
                        in_=cc_out_d[m * 64:(m + 1) * 64, :].rearrange("p (h x d) -> p h x d", h=H, x=2)[:, :, 1]),
                        reads=("cc_out",), writes=(f"ccgm{mp}",))
            P.add("vector", lambda e, m=m: e.scalar_tensor_tensor(
                sel_tmp[:], Sm[:], ohbc[:, m:m + 1], Sacc[:], op0=OP.mult, op1=OP.add),
                reads=("Sm", "ohbc", "Sacc"), writes=("sel_tmp",))
            P.add("vector", lambda e: e.tensor_copy(Sacc[:], sel_tmp[:]), reads=("sel_tmp",), writes=("Sacc",))
            if m < n_cores - 1:
                mp = m % 2
                for h in range(H):
                    P.add("tensor", lambda e, mp=mp, h=h: e.matmul(
                        psb[5][:, 64:128], ccg[mp][0:64, h * 192:h * 192 + 128],
                        Sm[0:64, h * 64:(h + 1) * 64], start=True, stop=True),
                        reads=(f"ccgm{mp}", "Sm"), writes=("PSB5",))
                    P.add("vector", lambda e, mp=mp, h=h: e.tensor_add(
                        Sm[:, h * 64:(h + 1) * 64], psb[5][:, 64:128],
                        ccg[mp][:, h * 192 + 128:h * 192 + 192]),
                        reads=("PSB5", f"ccgm{mp}"), writes=("Sm",))
        P.add("vector", lambda e: e.tensor_copy(N_f[:], Sacc[:]), reads=("Sacc",),
              writes=tuple(f"N_f:{h}" for h in range(H)))
        P.add("scalar", lambda e: e.copy(N_b[:], N_f[:]), reads=tuple(f"N_f:{h}" for h in range(H)),
              writes=tuple(f"N_b:{h}" for h in range(H)))
        P.barrier()

        Z_sb = sbD("Z_sb", [128, KT * D], BF16)
        e1b = sbD("e1b", [128, 512], BF16)
        e2b = [sbD(f"e2b{i}", [128, 512], BF16) for i in range(4)]
        zacc = sbD("zacc", [128, 512], F32)
        ztmp = sbD("ztmp", [128, 512], F32)
        for ri in "RI":
            for c4 in range(4):
                P.add("gq", lambda e, ri=ri, c4=c4: e.dma_start(out=e2b[c4][:], in_=E2_d[ri][c4]),
                      writes=(f"e2b{c4}",))
            for it in range(4):
                for c4 in range(4):
                    P.add("gq", lambda e, ri=ri, it=it, c4=c4: e.dma_start(out=e1b[:], in_=E1_d[ri][c4, it]),
                          writes=("e1b",))
                    if c4 == 0:
                        P.add("vector", lambda e, c4=c4: e.tensor_tensor(zacc[:], e1b[:], e2b[c4][:], op=OP.mult),
                              reads=("e1b", f"e2b{c4}"), writes=("zacc",))
                    else:
                        P.add("vector", lambda e, c4=c4: e.tensor_tensor(ztmp[:], e1b[:], e2b[c4][:], op=OP.mult),
                              reads=("e1b", f"e2b{c4}"), writes=("ztmp",))
                        P.add("vector", lambda e: e.tensor_add(zacc[:], zacc[:], ztmp[:]),
                              reads=("zacc", "ztmp"), writes=("zacc",))
                if ri == "R":
                    P.add("scalar", lambda e, it=it: e.copy(Z_sb[:, it * D + 0:it * D + 512], zacc[:]),
                          reads=("zacc",), writes=("Z_sb",))
                    P.add("scalar", lambda e, it=it: e.copy(Z_sb[:, (4 + it) * D + 512:(4 + it) * D + 1024], zacc[:]),
                          reads=("zacc",), writes=("Z_sb",))
                else:
                    P.add("scalar", lambda e, it=it: e.copy(Z_sb[:, it * D + 512:it * D + 1024], zacc[:]),
                          reads=("zacc",), writes=("Z_sb",))
                    P.add("scalar", lambda e, it=it: e.mul(Z_sb[:, (4 + it) * D + 0:(4 + it) * D + 512], zacc[:], -1.0),
                          reads=("zacc",), writes=("Z_sb",))
        P.barrier()

        WoT_sb = sbD("WoT_sb", [128, KT * D], BF16)
        P.add("gq", lambda e: e.dma_start(out=WoT_sb[:].rearrange("p (a n) -> p a n", a=KT),
                                          in_=WT_d["o"].rearrange("(a p) n -> p a n", p=128)),
              writes=("WoT_sb",))
        msgD = sbD("msgD", [1, GS], F32)
        ssgD = sbD("ssgD", [1, GS], F32)
        xnfD = sbD("xnfD", [128, KT * GS], BF16)
        xtgD = sbD("xtgD", [128, KT * GS], BF16)
        msbcD = sbD("msbcD", [128, GS], BF16)
        sbcD = sbD("sbcD", [128, GS], BF16)
        o0b = [sbD(f"o0b{i}", [128, D], BF16) for i in range(2)]
        qtb = [sbD(f"qtb{i}", [128, KT * 128], BF16) for i in range(2)]
        ptb = [sbD(f"ptb{i}", [64, H * 128], BF16) for i in range(2)]
        Ofp = [sbD(f"Ofp{i}", [128, D], F32) for i in range(2)]
        ObfR = [sbD(f"ObfR{i}", [128, D], BF16) for i in range(2)]
        Ofeat = sbD("Ofeat", [128, KT * 128], BF16)
        osq = [sbD(f"osq{i}", [128, 128], F32) for i in range(2)]
        srow16 = sbD("srow16", [H, 128], F32)
        obc = sbD("obc", [128, 128], BF16)
        xtok = [sbD(f"xtok{i}", [128, D], F32) for i in range(2)]
        outsb = [sbD(f"outsb{i}", [128, D], F32) for i in range(2)]

        def emit_xnf_D(grp):
            c0 = 3 + grp * GS
            for t in range(KT):
                P.add("gq", lambda e, t=t, c0=c0: e.dma_start(
                    out=xtgD[:, t * GS:(t + 1) * GS], in_=x_t_d[t * 128:(t + 1) * 128, c0:c0 + GS]),
                    writes=(f"xtgD:{t}",))
            P.add("sp", lambda e, c0=c0: e.dma_start(out=msgD[:], in_=msrow_d[0:1, c0:c0 + GS]),
                  reads=("msrow_d",), writes=("msgD",))
            P.add("sp", lambda e, c0=c0: e.dma_start(out=ssgD[:], in_=msrow_d[1:2, c0:c0 + GS]),
                  reads=("msrow_d",), writes=("ssgD",))
            P.add("tensor", lambda e: e.matmul(psb[4][:, 0:GS], ones1[:], msgD[:], start=True, stop=True),
                  reads=("ones1", "msgD"), writes=("PSB4",))
            P.add("tensor", lambda e: e.matmul(psb[5][:, 0:GS], ones1[:], ssgD[:], start=True, stop=True),
                  reads=("ones1", "ssgD"), writes=("PSB5",))
            P.add("scalar", lambda e: e.copy(msbcD[:], psb[4][:, 0:GS]), reads=("PSB4",), writes=("msbcD",))
            P.add("scalar", lambda e: e.copy(sbcD[:], psb[5][:, 0:GS]), reads=("PSB5",), writes=("sbcD",))
            for t in range(KT):
                P.add("vector", lambda e, t=t: e.tensor_tensor(
                    xnfD[:, t * GS:(t + 1) * GS], xtgD[:, t * GS:(t + 1) * GS], msbcD[:], op=OP.add),
                    reads=(f"xtgD:{t}", "msbcD"), writes=(f"xnfD:{t}",))
                P.add("vector", lambda e, t=t: e.tensor_tensor(
                    xnfD[:, t * GS:(t + 1) * GS], xnfD[:, t * GS:(t + 1) * GS], sbcD[:], op=OP.mult),
                    reads=(f"xnfD:{t}", "sbcD"), writes=(f"xnfD:{t}",))

        CPG = GS // C
        for g in range(NG):
            emit_xnf_D(g)
            for ci in range(CPG):
                c = g * CPG + ci
                cp = c % 2
                P.add("sp", lambda e, c=c, cp=cp: e.dma_start(out=o0b[cp][:], in_=o0_d[c * 128:(c + 1) * 128, :]),
                      reads=(f"o0d:{c}",), writes=(f"o0b{cp}",))
                P.add("sp", lambda e, c=c, cp=cp: e.dma_start(
                    out=qtb[cp][:], in_=qt_d[:, c * KT * 128:(c + 1) * KT * 128]),
                    reads=(f"qtd:{c}",), writes=(f"qtb{cp}",))
                P.add("sp", lambda e, c=c, cp=cp: e.dma_start(out=xtok[cp][:], in_=x_tok_d[c * 128:(c + 1) * 128, :]),
                      writes=(f"xtok{cp}",))
                for dup in range(2):
                    P.add("sp", lambda e, c=c, cp=cp, dup=dup: e.dma_start(
                        out=ptb[cp][:].rearrange("p (h u d) -> p h u d", h=H, u=2)[:, :, dup],
                        in_=pt_d[:, c * H * 64:(c + 1) * H * 64].rearrange("p (h d) -> p h d", h=H)),
                        reads=(f"ptd:{c}",), writes=(f"ptb{cp}",))
                for h in range(H):
                    t, hw = h // 2, h % 2
                    P.add("tensor", lambda e, cp=cp, t=t, hw=hw, h=h: e.matmul(
                        psb[2 + hw][:, (h // 2) * 64:(h // 2) * 64 + 64],
                        qtb[cp][64 * hw:64 * hw + 64, t * 128:(t + 1) * 128],
                        N_b[64 * hw:64 * hw + 64, h * 64:(h + 1) * 64], start=True, stop=True,
                        tile_position=(64 * hw, 0)),
                        reads=(f"qtb{cp}", f"N_b:{h}"), writes=(f"PSB{2 + hw}",))
                for h in range(H):
                    hw = h % 2
                    P.add("tensor", lambda e, h=h, cp=cp, hw=hw: e.matmul(
                        psb[4 + hw][:, (h // 2) * 64:(h // 2) * 64 + 64],
                        ptb[cp][:, h * 128:(h + 1) * 128],
                        N_b[0:64, h * 64:(h + 1) * 64], start=True, stop=True),
                        reads=(f"ptb{cp}", f"N_b:{h}"), writes=(f"PSB{4 + hw}",))
                allN = tuple(f"N_b:{hh}" for hh in range(H))
                allNf = tuple(f"N_f:{hh}" for hh in range(H))
                for hw in range(2):
                    P.add("vector", lambda e, cp=cp, hw=hw: e.tensor_add(
                        Ofp[cp][:].rearrange("p (h x d) -> p h x d", h=8, x=2)[:, :, hw],
                        o0b[cp][:].rearrange("p (h x d) -> p h x d", h=8, x=2)[:, :, hw],
                        psb[2 + hw][:, 0:512].rearrange("p (h d) -> p h d", h=8)),
                        reads=(f"o0b{cp}", f"PSB{2 + hw}"), writes=(f"Ofp{cp}",))
                    P.add("vector", lambda e, hw=hw: e.tensor_copy(
                        N_f[:].rearrange("p (h x d) -> p h x d", h=8, x=2)[:, :, hw],
                        psb[4 + hw][:, 0:512].rearrange("p (h d) -> p h d", h=8)),
                        reads=(f"PSB{4 + hw}",), writes=allNf)
                P.add("scalar", lambda e: e.copy(N_b[:], N_f[:]), reads=allNf, writes=allN)
                for half in range(2):
                    P.add("scalar", lambda e, cp=cp, half=half: e.copy(
                        ObfR[cp][:, half * 512:half * 512 + 512], Ofp[cp][:, half * 512:half * 512 + 512]),
                        reads=(f"Ofp{cp}",), writes=(f"ObfR{cp}",))
                for t in range(KT):
                    pt = t % 2
                    pst = psT0 if pt == 0 else psT1
                    pstk = "PSB6" if pt == 0 else "PSB7"
                    P.add("tensor", lambda e, cp=cp, t=t, pst=pst: e.transpose(
                        pst[:, 0:128], ObfR[cp][:, t * 128:(t + 1) * 128], ident_b[:]),
                        reads=(f"ObfR{cp}", "ident_b"), writes=(pstk,))
                    P.add("scalar", lambda e, t=t, pst=pst: e.copy(Ofeat[:, t * 128:(t + 1) * 128], pst[:, 0:128]),
                          reads=(pstk,), writes=(f"Ofeat:{t}",))
                    P.add("scalar", lambda e, t=t: e.activation(osq[t % 2][:], Ofeat[:, t * 128:(t + 1) * 128], AF.Square),
                          reads=(f"Ofeat:{t}",), writes=(f"osq{t % 2}",))
                    P.add("tensor", lambda e, t=t: e.matmul(
                        psb[4][0:H, 0:128], ind[:, t * H:(t + 1) * H], osq[t % 2][:],
                        start=(t == 0), stop=(t == KT - 1)),
                        reads=("ind", f"osq{t % 2}"), writes=("PSB4",))
                P.add("scalar", lambda e: e.activation(srow16[:], psb[4][0:H, 0:128], AF.Sqrt, scale=1.0 / dh, bias=epsrms[0:H, :]),
                      reads=("PSB4", "epsrms"), writes=("srow16",))
                P.add("vector", lambda e: e.reciprocal(srow16[:], srow16[:]), reads=("srow16",), writes=("srow16",))
                for t in range(KT):
                    P.add("tensor", lambda e, t=t: e.matmul(
                        psb[5][:, 0:128], indT[:, t * 128:(t + 1) * 128], srow16[:], start=True, stop=True),
                        reads=("indT", "srow16"), writes=("PSB5",))
                    P.add("scalar", lambda e: e.activation(obc[:], psb[5][:, 0:128], AF.Identity, scale=onw[:]),
                          reads=("PSB5", "onw"), writes=("obc",))
                    P.add("vector", lambda e, t=t: e.tensor_tensor(
                        Ofeat[:, t * 128:(t + 1) * 128], Ofeat[:, t * 128:(t + 1) * 128], obc[:], op=OP.mult),
                        reads=(f"Ofeat:{t}", "obc"), writes=(f"Ofeat:{t}",))
                for half in range(2):
                    bo = half
                    for t in range(KT):
                        P.add("tensor", lambda e, t=t, half=half, bo=bo: e.matmul(
                            psb[bo][:, 0:512], Ofeat[:, t * 128:(t + 1) * 128],
                            WoT_sb[:, t * D + half * 512: t * D + half * 512 + 512],
                            start=(t == 0), stop=False),
                            reads=(f"Ofeat:{t}", "WoT_sb"), writes=(f"PSB{bo}",))
                    for t in range(KT):
                        P.add("tensor", lambda e, t=t, ci=ci, half=half, bo=bo: e.matmul(
                            psb[bo][:, 0:512], xnfD[:, t * GS + ci * C: t * GS + ci * C + 128],
                            Z_sb[:, t * D + half * 512: t * D + half * 512 + 512],
                            start=False, stop=False),
                            reads=(f"xnfD:{t}", "Z_sb"), writes=(f"PSB{bo}",))
                    P.add("tensor", lambda e, half=half, bo=bo: e.matmul(
                        psb[bo][:, 0:512], ones1[:], biasrow[0:1, half * 512:half * 512 + 512],
                        start=False, stop=True),
                        reads=("ones1", "biasrow"), writes=(f"PSB{bo}",))
                    P.add("vector", lambda e, cp=cp, half=half, bo=bo: e.tensor_add(
                        outsb[cp][:, half * 512:half * 512 + 512], psb[bo][:, 0:512],
                        xtok[cp][:, half * 512:half * 512 + 512]),
                        reads=(f"PSB{bo}", f"xtok{cp}"), writes=(f"outsb{cp}",))
                P.add("sp", lambda e, c=c, cp=cp: e.dma_start(out=out_d[c * 128:(c + 1) * 128, :], in_=outsb[cp][:]),
                      reads=(f"outsb{cp}",), writes=("out_final",))

        P.add("gpsimd", lambda e: e.engine_nop(), reads=("out_final",))

    P.mark("end")
    import os
    if os.environ.get("KPRINT_MARKS"):
        print("MARKS:", P.marks)
    P.finalize(truncate=cfg.trunc, skip=getattr(cfg, 'skip_items', ()))
    P.emit(nc, sems)
    return nc


def host_in_maps(cfg: CFG, inputs):
    import ml_dtypes
    n = cfg.n_cores
    R, D, H, KT = cfg.R, cfg.D, cfg.H, cfg.KT
    x = np.asarray(inputs["L0_attn_x"], np.float32).reshape(cfg.L, D)
    WT = {p: np.ascontiguousarray(np.asarray(inputs[f"L0_attn_W{p}"], np.float32).T) for p in "qkvo"}
    WbT = np.ascontiguousarray(np.asarray(inputs["L0_attn_Wb"], np.float32).T)
    onw = np.ascontiguousarray(np.tile(np.asarray(inputs["L0_attn_o_norm_w"], np.float32), 2)[:, None])
    biasrow = np.ascontiguousarray(np.concatenate([
        np.asarray(inputs["L0_ffn_bias_real"], np.float32),
        np.asarray(inputs["L0_ffn_bias_imag"], np.float32)])[None, :])

    def expand(G1, G2):
        G1 = np.asarray(G1, np.float32)[0]
        G2 = np.asarray(G2, np.float32)
        d_of_p = np.arange(128) // 16
        g_of_p = np.arange(128) % 16
        E1 = np.empty((4, 4, 128, 512), np.float32)
        E2 = np.empty((4, 128, 512), np.float32)
        for c4 in range(4):
            colB = G2[c4, :, 0, :]
            E2[c4] = colB.T[g_of_p][:, None, :].repeat(32, axis=1).reshape(128, 512)
            for it in range(4):
                vals = G1[:, c4, it * 8 + d_of_p]
                E1[c4, it] = vals.T[:, :, None].repeat(16, axis=2).reshape(128, 512)
        return E1, E2
    E1R, E2R = expand(inputs["L0_ffn_G1_R"], inputs["L0_ffn_G2_R"])
    E1I, E2I = expand(inputs["L0_ffn_G1_I"], inputs["L0_ffn_G2_I"])

    i_ = np.arange(128)
    mask_sl = -((i_[None, :] < i_[:, None]).astype(np.float32))
    mask_sun = -((i_[None, :] > i_[:, None]).astype(np.float32))
    mask_sui = (i_[None, :] >= i_[:, None]).astype(np.float32)
    ident_f = np.eye(128, dtype=np.float32)
    ident_b = np.eye(128).astype(ml_dtypes.bfloat16)
    ident64_2 = np.ascontiguousarray(np.tile(np.eye(64), (2, 1)).astype(ml_dtypes.bfloat16))
    ones1 = np.ones((1, 128), np.float32)
    ppinit = np.ascontiguousarray(np.tile(np.eye(64), (1, H)).astype(ml_dtypes.bfloat16))
    ind = np.zeros((128, KT * H), np.float32)
    indT = np.zeros((H, KT * 128), np.float32)
    for t in range(KT):
        for p in range(128):
            h = 2 * t + (1 if p >= 64 else 0)
            ind[p, t * H + h] = 1.0
            indT[h, t * 128 + p] = 1.0

    in_maps = []
    for i in range(n):
        r0 = i * R
        xt_sl = np.zeros((D, R + 3), np.float32)
        lo = max(r0 - 3, 0)
        xt_sl[:, 3 - (r0 - lo):] = x[lo:r0 + R].T
        halo = np.zeros((3, D), np.float32)
        if r0 >= 3:
            halo[:] = x[r0 - 3:r0]
        oh = np.zeros((1, n), np.float32)
        oh[0, i] = 1.0
        m = {
            "x_tok": np.ascontiguousarray(x[r0:r0 + R]),
            "x_t": np.ascontiguousarray(xt_sl),
            "x_halo": halo,
            "WbT": WbT,
            "conv_q": np.asarray(inputs["L0_attn_conv_q"], np.float32),
            "conv_k": np.asarray(inputs["L0_attn_conv_k"], np.float32),
            "conv_v": np.asarray(inputs["L0_attn_conv_v"], np.float32),
            "onw128": onw, "biasrow": biasrow,
            "E1R": E1R, "E2R": E2R, "E1I": E1I, "E2I": E2I,
            "mask_sl": mask_sl, "mask_sun": mask_sun, "mask_sui": mask_sui,
            "ident_f": ident_f, "ident_b": ident_b, "ident64_2": ident64_2,
            "ones1": ones1, "ppinit": ppinit, "onehot": oh,
            "ind": ind, "indT": indT,
        }
        for p in "qkvo":
            m[f"W{p}T"] = WT[p]
        in_maps.append(m)
    return in_maps


MID_TENSORS = ("o0_dram", "qt_dram", "pt_dram", "msrow_dram")


def run_pipeline(cfg, inputs, launcher):
    base_maps = host_in_maps(cfg, inputs)
    n = cfg.n_cores
    nc1 = build_program(cfg, "p1")
    res1 = launcher(nc1, base_maps)
    ccg = np.ascontiguousarray(np.concatenate(
        [np.asarray(r["cc_stage_out"], np.float32) for r in res1], axis=0))
    maps2 = []
    for i, m in enumerate(base_maps):
        m2 = dict(m)
        for t in MID_TENSORS:
            m2[t] = np.asarray(res1[i][t])
        m2["cc_gathered"] = ccg
        maps2.append(m2)
    nc2 = build_program(cfg, "p2")
    res2 = launcher(nc2, maps2)
    out = np.concatenate([np.asarray(r["out"], np.float32) for r in res2], axis=0)
    return out


def kernel(**inputs):
    cfg = CFG()

    def launcher(nc, in_maps):
        res = run_bass_kernel_spmd(nc, in_maps, core_ids=list(range(cfg.n_cores)))
        return res.results
    out = run_pipeline(cfg, inputs, launcher)
    return out.reshape(1, cfg.L, cfg.D)


# version 08
# speedup vs baseline: 1.3962x; 1.0051x over previous
import numpy as np
from contextlib import ExitStack

import concourse.bass as bass
import concourse.mybir as mybir
from concourse.alu_op_type import AluOpType as OP
from concourse.bass_utils import run_bass_kernel_spmd

F32 = mybir.dt.float32
BF16 = mybir.dt.bfloat16
AF = mybir.ActivationFunctionType
AX = mybir.AxisListType


class CFG:
    n_cores = 8
    L = 16384
    D = 1024
    H = 16
    dh = 64
    C = 128
    GS = 512
    EPS = 1e-6
    LN_EPS = 1e-5
    debug = False
    trunc = None

    @property
    def R(self): return self.L // self.n_cores
    @property
    def KT(self): return self.D // 128
    @property
    def NG(self): return self.R // self.GS
    @property
    def NC(self): return self.R // self.C


DMA_STREAMS = ("sp", "aq", "gq", "cc")
STREAM_SECTION = {
    "tensor": "tensor", "vector": "vector", "scalar": "scalar",
    "gpsimd": "gpsimd", "sync": "sync",
    "sp": "sync", "aq": "scalar", "gq": "gpsimd", "cc": "gpsimd",
}


class Pl:

    def __init__(self):
        self.items = []
        self.barrier_snap = {}
        self.ordn = {}

    def add(self, stream, fn, reads=(), writes=(), inc=None):
        if inc is None:
            inc = 16 if stream in DMA_STREAMS else 1
        if stream in DMA_STREAMS and stream != "cc":
            o = self.ordn.get(stream, 0)
            self.ordn[stream] = o + 1
            semid = f"{stream}{o % self.NROT}"
        else:
            semid = stream
        self.items.append(dict(s=stream, fn=fn, r=tuple(reads), w=tuple(writes),
                               inc=inc, deps=set(), bar=dict(self.barrier_snap),
                               semid=semid))

    def barrier(self):
        snap = {}
        for i, it in enumerate(self.items):
            snap[it["semid"]] = i
        self.barrier_snap = snap

    NROT = 4

    marks = []

    def mark(self, name):
        self.marks.append((name, len(self.items)))

    def finalize(self, truncate=None, skip=()):
        if truncate is not None:
            self.items = self.items[:truncate]
        if skip:
            self.items = [it for i, it in enumerate(self.items) if i not in skip]
        lastuse = {}
        lastw, readers = {}, {}
        last_psum_rd = {}
        for i, it in enumerate(self.items):
            dep = set(it["bar"].values())
            for r in it["r"]:
                if r in lastw:
                    dep.add(lastw[r])
                if r.startswith("PSB"):
                    pr = last_psum_rd.get(r)
                    if pr is not None and pr[1] != it["s"]:
                        dep.add(pr[0])
                    last_psum_rd[r] = (i, it["s"])
            for w in it["w"]:
                if w in lastw:
                    dep.add(lastw[w])
                for rd in readers.get(w, ()):
                    dep.add(rd)
            for r in it["r"]:
                readers.setdefault(r, []).append(i)
            for w in it["w"]:
                lastw[w] = i
                readers[w] = []
                if w.startswith("PSB"):
                    last_psum_rd.pop(w, None)
            if it["s"] in DMA_STREAMS:
                if it["semid"] in lastuse:
                    dep.add(lastuse[it["semid"]])
                lastuse[it["semid"]] = i
            it["deps"] = {d for d in dep
                          if not (self.items[d]["s"] == it["s"] and it["s"] == "tensor")}
        need = [False] * len(self.items)
        for i, it in enumerate(self.items):
            if it["s"] in DMA_STREAMS:
                need[i] = True
            for d in it["deps"]:
                need[d] = True
        cnt = {}
        self.after = [0] * len(self.items)
        self.neednc = need
        for i, it in enumerate(self.items):
            c = cnt.get(it["semid"], 0)
            if need[i]:
                c += it["inc"]
            cnt[it["semid"]] = c
            self.after[i] = c

    def emit(self, nc, sems):
        by_section = {}
        for i, it in enumerate(self.items):
            by_section.setdefault(STREAM_SECTION[it["s"]], []).append(i)
        with nc.Block() as block:
            for section in ("sync", "scalar", "vector", "tensor", "gpsimd"):
                idxs = by_section.get(section, [])
                dec = getattr(block, section)

                def make(idxs=idxs):
                    def body(eng):
                        waited = {}
                        for i in idxs:
                            it = self.items[i]
                            wmax = {}
                            for d in it["deps"]:
                                ds = self.items[d]["semid"]
                                v = self.after[d]
                                if v > wmax.get(ds, 0):
                                    wmax[ds] = v
                            for ds, v in sorted(wmax.items()):
                                if waited.get(ds, 0) < v:
                                    eng.wait_ge(sems[ds], v)
                                    waited[ds] = v
                            inst = it["fn"](eng)
                            if self.neednc[i]:
                                inst.then_inc(sems[it["semid"]], it["inc"])
                    return body
                dec(make())


class _Rec:

    def __init__(self):
        self.calls = []

    def add(self, *a, **k):
        self.calls.append((a, k))


def _merge_rr(P, recs):
    pos = [0] * len(recs)
    while True:
        any_left = False
        for i, r in enumerate(recs):
            if pos[i] < len(r.calls):
                a, k = r.calls[pos[i]]
                P.add(*a, **k)
                pos[i] += 1
                any_left = True
        if not any_left:
            break


def dup2(ap64):
    return ap64.rearrange("p (o d) -> p o d", o=1).broadcast_to([ap64.shape[0], 2, 64])


def build_program(cfg: CFG, part: str = "p1"):
    R, D, H, dh, C, GS = cfg.R, cfg.D, cfg.H, cfg.dh, cfg.C, cfg.GS
    KT, NG, NC = cfg.KT, cfg.NG, cfg.NC
    NLEV = 6
    assert C == 2 ** (NLEV + 1)
    n_cores = cfg.n_cores
    NT = R // 128

    nc = bass.Bass("TRN2", target_bir_lowering=False)
    P = Pl()

    dt = lambda name, shape, dtype, kind: nc.dram_tensor(name, shape, dtype, kind=kind).ap()
    x_tok_d = dt("x_tok", [R, D], F32, "ExternalInput")
    x_t_d = dt("x_t", [D, R + 3], F32, "ExternalInput")
    x_halo_d = dt("x_halo", [3, D], F32, "ExternalInput")
    WT_d = {p: dt(f"W{p}T", [D, D], F32, "ExternalInput") for p in "qkvo"}
    WbT_d = dt("WbT", [D, H], F32, "ExternalInput")
    conv_d = {p: dt(f"conv_{p}", [D, 4], F32, "ExternalInput") for p in "qkv"}
    onw_d = dt("onw128", [128, 1], F32, "ExternalInput")
    bias_d = dt("biasrow", [1, D], F32, "ExternalInput")
    E1_d = {ri: dt(f"E1{ri}", [4, KT // 2, 128, 512], F32, "ExternalInput") for ri in "RI"}
    E2_d = {ri: dt(f"E2{ri}", [4, 128, 512], F32, "ExternalInput") for ri in "RI"}
    mask_sl_d = dt("mask_sl", [128, 128], F32, "ExternalInput")
    mask_sun_d = dt("mask_sun", [128, 128], F32, "ExternalInput")
    mask_sui_d = dt("mask_sui", [128, 128], F32, "ExternalInput")
    ident_f_d = dt("ident_f", [128, 128], F32, "ExternalInput")
    ident_b_d = dt("ident_b", [128, 128], BF16, "ExternalInput")
    ident64_d = dt("ident64_2", [128, 64], BF16, "ExternalInput")
    ones1_d = dt("ones1", [1, 128], F32, "ExternalInput")
    ppinit_d = dt("ppinit", [64, H * 64], BF16, "ExternalInput")
    onehot_d = dt("onehot", [1, n_cores], F32, "ExternalInput")
    ind_d = dt("ind", [128, KT * H], F32, "ExternalInput")
    indT_d = dt("indT", [H, KT * 128], F32, "ExternalInput")

    out_d = dt("out", [R, D], F32, "ExternalOutput") if part != "p1" else None

    mid_kind = {"full": "Internal", "p1": "ExternalOutput", "p2": "ExternalInput"}[part]
    if part != "p2":
        krm_d = dt("krm_dram", [R, 2 * D], BF16, "Internal")
        vrm_d = dt("vrm_dram", [R, D], BF16, "Internal")
    o0_d = dt("o0_dram", [R, D], BF16, mid_kind)
    qt_d = dt("qt_dram", [128, NC * KT * 128], BF16, mid_kind)
    pt_d = dt("pt_dram", [64, NC * H * 64], BF16, mid_kind)
    msrow_d = dt("msrow_dram", [2, R + 3], F32, mid_kind)
    if part == "full":
        cc_in_d = dt("cc_in", [64, H * 128], F32, "Internal")
        cc_out_d = dt("cc_out", [n_cores * 64, H * 128], F32, "Internal")
    elif part == "p1":
        cc_out1_d = dt("cc_stage_out", [64, H * 128], F32, "ExternalOutput")
    else:
        cc_out_d = dt("cc_gathered", [n_cores * 64, H * 128], F32, "ExternalInput")
    dbg_d = dt("dbg", [128, 32 * 128], F32, "Internal") if cfg.debug else None
    def dbgtap(slot, ap, readkeys):
        if cfg.debug:
            P.add("gq", lambda e, slot=slot, ap=ap: e.dma_start(
                out=dbg_d[:ap.shape[0], slot * 128: slot * 128 + ap.shape[1]], in_=ap),
                reads=tuple(readkeys), writes=("dbg",))

    st = ExitStack()
    sb = lambda name, shape, dtype: st.enter_context(nc.sbuf_tensor(name, shape, dtype))
    ps_ = lambda name, shape, dtype: st.enter_context(nc.psum_tensor(name, shape, dtype))

    sem_names = ["tensor", "vector", "scalar", "gpsimd", "sync", "cc"]
    for s in ("sp", "aq", "gq"):
        sem_names += [f"{s}{r}" for r in range(Pl.NROT)]
    sems = {s: st.enter_context(nc.semaphore(f"sem_{s}")) for s in sem_names}

    psb = [ps_(f"psb{i}", [128, 512], F32) for i in range(6)]
    psT0 = ps_("psT0", [128, 512], BF16)
    psT1 = ps_("psT1", [128, 512], BF16)

    mask_sl = sb("mask_sl_s", [128, 128], F32)
    mask_sun = sb("mask_sun_s", [128, 128], F32)
    mask_sui = sb("mask_sui_s", [128, 128], F32)
    ident_f = sb("ident_f_s", [128, 128], F32)
    ident_b = sb("ident_b_s", [128, 128], BF16)
    ident64 = sb("ident64_s", [128, 64], BF16)
    ones1 = sb("ones1_s", [1, 128], F32)
    onw = sb("onw_s", [128, 1], F32)
    biasrow = sb("biasrow_s", [1, D], F32)
    onehot = sb("onehot_s", [1, n_cores], F32)
    ind = sb("ind_s", [128, KT * H], F32)
    indT = sb("indT_s", [H, KT * 128], F32)
    S_f = sb("S_f", [128, H * 64], F32)
    S_b = sb("S_b", [128, H * 64], BF16)
    PP_b = sb("PP_b", [64, H * 64], BF16)
    N_f = sb("N_f", [128, H * 64], F32)
    N_b = sb("N_b", [128, H * 64], BF16)
    epsln = sb("epsln", [128, 1], F32)
    epsrms = sb("epsrms", [128, 1], F32)

    def ld(dst, src, key, stream="sp"):
        P.add(stream, lambda e, d=dst, s=src: e.dma_start(out=d, in_=s), writes=(key,))

    ld(mask_sl[:], mask_sl_d[:, :], "mask_sl")
    ld(mask_sun[:], mask_sun_d[:, :], "mask_sun")
    ld(mask_sui[:], mask_sui_d[:, :], "mask_sui")
    ld(ident_f[:], ident_f_d[:, :], "ident_f")
    ld(ident_b[:], ident_b_d[:, :], "ident_b")
    ld(ident64[:], ident64_d[:, :], "ident64")
    ld(ones1[:], ones1_d[:, :], "ones1")
    ld(onw[:], onw_d[:, :], "onw")
    ld(biasrow[:], bias_d[:, :], "biasrow")
    ld(onehot[:], onehot_d[:, :], "onehot")
    ld(ind[:], ind_d[:, :], "ind")
    ld(indT[:], indT_d[:, :], "indT")
    P.add("sp", lambda e: e.dma_start(out=PP_b[:], in_=ppinit_d[:, :]), writes=tuple(f"PP_b:{h}" for h in range(H)))
    P.add("gpsimd", lambda e: e.memset(epsln[:], cfg.LN_EPS), writes=("epsln",))
    P.add("gpsimd", lambda e: e.memset(epsrms[:], cfg.EPS), writes=("epsrms",))
    P.add("gpsimd", lambda e: e.memset(S_f[:], 0.0), writes=tuple(f"S_f:{h}" for h in range(H)))
    P.add("gpsimd", lambda e: e.memset(S_b[:], 0.0), writes=tuple(f"S_b:{h}" for h in range(H)))

    run_p1 = part in ("full", "p1")
    run_p2 = part in ("full", "p2")
    stA0 = ExitStack()
    if run_p1:
        sb0 = lambda name, shape, dtype: stA0.enter_context(nc.sbuf_tensor(name, shape, dtype))
        stat_pack = sb0("stat_pack", [128, 2 * NT], F32)
        stat_sum = sb0("stat_sum", [128, NT], F32)
        stat_ssq = sb0("stat_ssq", [128, NT], F32)
        stat_tmp = sb0("stat_tmp", [128, NT], F32)
        evac32 = sb0("evac32", [2 * NT, 128], F32)
        xa = [sb0(f"xa{i}", [128, D], F32) for i in range(4)]
        xsq = sb0("xsq", [128, D], F32)
        for t in range(NT):
            b = t % 2
            P.add("sp", lambda e, b=b, t=t: e.dma_start(out=xa[b][:], in_=x_tok_d[t * 128:(t + 1) * 128, :]),
                  writes=(f"xa{b}",))
            P.add("vector", lambda e, b=b, t=t: e.reduce_sum(stat_sum[:, t:t + 1], xa[b][:], axis=AX.X),
                  reads=(f"xa{b}",), writes=("stat_sum",))
            P.add("scalar", lambda e, b=b, t=t: e.activation(xsq[:], xa[b][:], AF.Square, accum_out=stat_ssq[:, t:t + 1]),
                  reads=(f"xa{b}",), writes=("stat_ssq", "xsq"))
        xh = sb0("xh", [3, D], F32)
        xhsq = sb0("xhsq", [3, D], F32)
        halo_sum = sb0("halo_sum", [3, 1], F32)
        halo_var = sb0("halo_var", [3, 1], F32)
        halo_m2 = sb0("halo_m2", [3, 1], F32)
        halo_s = sb0("halo_s", [3, 2], F32)
        P.add("sp", lambda e: e.dma_start(out=xh[:], in_=x_halo_d[:, :]), writes=("xh",))
        P.add("vector", lambda e: e.reduce_sum(halo_sum[:], xh[:], axis=AX.X),
              reads=("xh",), writes=("halo_sum",))
        P.add("scalar", lambda e: e.activation(xhsq[:], xh[:], AF.Square, accum_out=halo_var[:]),
              reads=("xh",), writes=("halo_var", "xhsq"))
        P.add("vector", lambda e: e.tensor_scalar(stat_sum[:], stat_sum[:], 1.0 / D, 0.0, op0=OP.mult, op1=OP.add),
              reads=("stat_sum",), writes=("stat_sum",))
        P.add("vector", lambda e: e.scalar_tensor_tensor(stat_tmp[:], stat_sum[:], 1.0, stat_sum[:], op0=OP.mult, op1=OP.mult),
              reads=("stat_sum",), writes=("stat_tmp",))
        P.add("vector", lambda e: e.scalar_tensor_tensor(stat_ssq[:], stat_ssq[:], 1.0 / D, stat_tmp[:], op0=OP.mult, op1=OP.subtract),
              reads=("stat_ssq", "stat_tmp"), writes=("stat_ssq",))
        P.add("scalar", lambda e: e.activation(stat_ssq[:], stat_ssq[:], AF.Sqrt, bias=epsln[:]),
              reads=("stat_ssq", "epsln"), writes=("stat_ssq",))
        P.add("vector", lambda e: e.reciprocal(stat_pack[:, NT:2 * NT], stat_ssq[:]),
              reads=("stat_ssq",), writes=("stat_pack",))
        P.add("vector", lambda e: e.scalar_tensor_tensor(stat_pack[:, 0:NT], stat_sum[:], -1.0, stat_pack[:, NT:2 * NT], op0=OP.mult, op1=OP.mult),
              reads=("stat_sum", "stat_pack"), writes=("stat_pack",))
        P.add("vector", lambda e: e.tensor_scalar(halo_sum[:], halo_sum[:], 1.0 / D, 0.0, op0=OP.mult, op1=OP.add),
              reads=("halo_sum",), writes=("halo_sum",))
        P.add("vector", lambda e: e.scalar_tensor_tensor(halo_m2[:], halo_sum[:], 1.0, halo_sum[:], op0=OP.mult, op1=OP.mult),
              reads=("halo_sum",), writes=("halo_m2",))
        P.add("vector", lambda e: e.scalar_tensor_tensor(halo_var[:], halo_var[:], 1.0 / D, halo_m2[:], op0=OP.mult, op1=OP.subtract),
              reads=("halo_var", "halo_m2"), writes=("halo_var",))
        P.add("scalar", lambda e: e.activation(halo_var[:], halo_var[:], AF.Sqrt, bias=epsln[0:3, :]),
              reads=("halo_var", "epsln"), writes=("halo_var",))
        P.add("vector", lambda e: e.reciprocal(halo_s[:, 1:2], halo_var[:]),
              reads=("halo_var",), writes=("halo_s",))
        P.add("vector", lambda e: e.scalar_tensor_tensor(halo_s[:, 0:1], halo_sum[:], -1.0, halo_s[:, 1:2], op0=OP.mult, op1=OP.mult),
              reads=("halo_sum", "halo_s"), writes=("halo_s",))
        P.add("tensor", lambda e: e.transpose(psb[0][0:2 * NT, 0:128], stat_pack[:], ident_f[:]),
              reads=("stat_pack", "ident_f"), writes=("PSB0",))
        P.add("vector", lambda e: e.tensor_copy(evac32[:], psb[0][0:2 * NT, 0:128]),
              reads=("PSB0",), writes=("evac32",))
        P.add("aq", lambda e: e.dma_start(out=msrow_d[0:1, 3:3 + NT * 128].rearrange("o (t p) -> (o t) p", t=NT),
                                          in_=evac32[0:NT, :]),
              reads=("evac32",), writes=("msrow_d",))
        P.add("aq", lambda e: e.dma_start(out=msrow_d[1:2, 3:3 + NT * 128].rearrange("o (t p) -> (o t) p", t=NT),
                                          in_=evac32[NT:2 * NT, :]),
              reads=("evac32",), writes=("msrow_d",))
        P.add("aq", lambda e: e.dma_start(out=msrow_d[0:1, 0:3].rearrange("o t -> t o"), in_=halo_s[:, 0:1]),
              reads=("halo_s",), writes=("msrow_d",))
        P.add("aq", lambda e: e.dma_start(out=msrow_d[1:2, 0:3].rearrange("o t -> t o"), in_=halo_s[:, 1:2]),
              reads=("halo_s",), writes=("msrow_d",))
        P.barrier()
        stA0.close()

        P.mark("A0_done")
        stABC = ExitStack()
        sbB = lambda name, shape, dtype: stABC.enter_context(nc.sbuf_tensor(name, shape, dtype))
        qtf = sbB("qtf", [128, KT * R], BF16)
        ktf = sbB("ktf", [128, KT * R], BF16)
        vtf = sbB("vtf", [128, KT * R], BF16)
        RES = {"q": qtf, "k": ktf, "v": vtf}

        stA = ExitStack()
        sbA = lambda name, shape, dtype: stA.enter_context(nc.sbuf_tensor(name, shape, dtype))
        wbuf = [sbA(f"wbuf{i}", [128, D], BF16) for i in range(4)]
        pbuf = [sbA(f"pbuf{i}", [128, GS + 3], BF16) for i in range(KT)]
        tails = sbA("tails", [128, 3 * 3 * KT], BF16)
        xnf = sbA("xnf", [128, KT * GS], BF16)
        xnf_h = sbA("xnf_h", [128, KT * 3], BF16)
        xtg = sbA("xtg", [128, KT * GS], BF16)
        xtg_h = sbA("xtg_h", [128, KT * 3], BF16)
        msbc = sbA("msbc", [128, GS], BF16)
        sbc = sbA("sbc", [128, GS], BF16)
        msbc_h = sbA("msbc_h", [128, 3], BF16)
        sbc_h = sbA("sbc_h", [128, 3], BF16)
        conv_sb = sbA("conv_sb", [128, KT * 4 * 3], F32)
        ybuf = sbA("ybuf", [128, GS], BF16)
        WbT_sb = sbA("WbT_sb", [128, KT * H], BF16)
        msg = sbA("msg", [1, GS], F32)
        ssg = sbA("ssg", [1, GS], F32)
        msg_h = sbA("msg_h", [1, 3], F32)
        ssg_h = sbA("ssg_h", [1, 3], F32)
        beta_hm = sbA("beta_hm", [H, R], F32)
        scale_q = sbA("scale_q", [H, R], F32)
        scale_k = sbA("scale_k", [H, R], F32)
    
        for pi, p in enumerate("qkv"):
            P.add("gq", lambda e, pi=pi, p=p: e.dma_start(
                out=conv_sb[:, pi * KT * 4:(pi + 1) * KT * 4].rearrange("p (a k) -> p a k", k=4),
                in_=conv_d[p].rearrange("(a p) k -> p a k", p=128)),
                writes=("conv_sb",))
        P.add("gq", lambda e: e.dma_start(out=WbT_sb[:].rearrange("p (a h) -> p a h", a=KT),
                                          in_=WbT_d.rearrange("(a p) h -> p a h", p=128)),
              writes=("WbT_sb",))

        def emit_xnf(grp, dst, dkey, xsrc, xkey, mb, sbuf_b, with_halo, dst_h=None, hx=None, mbh=None, sbh=None,
                     mrow=None, srw=None, mrow_h=None, srw_h=None):
            c0 = 3 + grp * GS
            for t in range(KT):
                P.add("gq", lambda e, t=t, c0=c0, xsrc=xsrc: e.dma_start(
                    out=xsrc[:, t * GS:(t + 1) * GS], in_=x_t_d[t * 128:(t + 1) * 128, c0:c0 + GS]),
                    writes=(f"{xkey}:{t}",))
            P.add("sp", lambda e, c0=c0, mrow=mrow: e.dma_start(out=mrow[:], in_=msrow_d[0:1, c0:c0 + GS]),
                  reads=("msrow_d",), writes=(f"{dkey}_mrow",))
            P.add("sp", lambda e, c0=c0, srw=srw: e.dma_start(out=srw[:], in_=msrow_d[1:2, c0:c0 + GS]),
                  reads=("msrow_d",), writes=(f"{dkey}_srow",))
            P.add("tensor", lambda e, mrow=mrow: e.matmul(psb[3][:, 0:GS], ones1[:], mrow[:], start=True, stop=True),
                  reads=("ones1", f"{dkey}_mrow"), writes=("PSB3",))
            P.add("tensor", lambda e, srw=srw: e.matmul(psb[4][:, 0:GS], ones1[:], srw[:], start=True, stop=True),
                  reads=("ones1", f"{dkey}_srow"), writes=("PSB4",))
            P.add("scalar", lambda e, mb=mb: e.copy(mb[:], psb[3][:, 0:GS]), reads=("PSB3",), writes=(f"{dkey}_mb",))
            P.add("scalar", lambda e, sbuf_b=sbuf_b: e.copy(sbuf_b[:], psb[4][:, 0:GS]), reads=("PSB4",), writes=(f"{dkey}_sb",))
            if with_halo:
                for t in range(KT):
                    P.add("gq", lambda e, t=t, hx=hx: e.dma_start(
                        out=hx[:, t * 3:(t + 1) * 3], in_=x_t_d[t * 128:(t + 1) * 128, 0:3]),
                        writes=(f"xtg_h:{t}",))
                P.add("sp", lambda e, mrow_h=mrow_h: e.dma_start(out=mrow_h[:], in_=msrow_d[0:1, 0:3]),
                      reads=("msrow_d",), writes=("mrow_h",))
                P.add("sp", lambda e, srw_h=srw_h: e.dma_start(out=srw_h[:], in_=msrow_d[1:2, 0:3]),
                      reads=("msrow_d",), writes=("srow_h",))
                P.add("tensor", lambda e, mrow_h=mrow_h: e.matmul(psb[3][:, 0:3], ones1[:], mrow_h[:], start=True, stop=True),
                      reads=("ones1", "mrow_h"), writes=("PSB3",))
                P.add("tensor", lambda e, srw_h=srw_h: e.matmul(psb[4][:, 0:3], ones1[:], srw_h[:], start=True, stop=True),
                      reads=("ones1", "srow_h"), writes=("PSB4",))
                P.add("scalar", lambda e, mbh=mbh: e.copy(mbh[:], psb[3][:, 0:3]), reads=("PSB3",), writes=("msbc_h",))
                P.add("scalar", lambda e, sbh=sbh: e.copy(sbh[:], psb[4][:, 0:3]), reads=("PSB4",), writes=("sbc_h",))
            for t in range(KT):
                P.add("vector", lambda e, t=t, dst=dst, xsrc=xsrc, mb=mb: e.tensor_tensor(
                    dst[:, t * GS:(t + 1) * GS], xsrc[:, t * GS:(t + 1) * GS], mb[:], op=OP.add),
                    reads=(f"{xkey}:{t}", f"{dkey}_mb"), writes=(f"{dkey}:{t}",))
                P.add("vector", lambda e, t=t, dst=dst, sbuf_b=sbuf_b: e.tensor_tensor(
                    dst[:, t * GS:(t + 1) * GS], dst[:, t * GS:(t + 1) * GS], sbuf_b[:], op=OP.mult),
                    reads=(f"{dkey}:{t}", f"{dkey}_sb"), writes=(f"{dkey}:{t}",))
                if with_halo:
                    P.add("vector", lambda e, t=t, dst_h=dst_h, hx=hx, mbh=mbh: e.tensor_tensor(
                        dst_h[:, t * 3:(t + 1) * 3], hx[:, t * 3:(t + 1) * 3], mbh[:], op=OP.add),
                        reads=(f"xtg_h:{t}", "msbc_h"), writes=(f"xnf_h:{t}",))
                    P.add("vector", lambda e, t=t, dst_h=dst_h, sbh=sbh: e.tensor_tensor(
                        dst_h[:, t * 3:(t + 1) * 3], dst_h[:, t * 3:(t + 1) * 3], sbh[:], op=OP.mult),
                        reads=(f"xnf_h:{t}", "sbc_h"), writes=(f"xnf_h:{t}",))

        for g in range(NG):
            emit_xnf(g, xnf, "xnf", xtg, "xtg", msbc, sbc, g == 0, xnf_h, xtg_h, msbc_h, sbc_h,
                     mrow=msg, srw=ssg, mrow_h=msg_h, srw_h=ssg_h)
            for t in range(KT):
                P.add("tensor", lambda e, t=t: e.matmul(
                    psb[2][0:H, 0:GS], WbT_sb[:, t * H:(t + 1) * H], xnf[:, t * GS:(t + 1) * GS],
                    start=(t == 0), stop=(t == KT - 1)),
                    reads=("WbT_sb", f"xnf:{t}"), writes=("PSB2",))
            P.add("scalar", lambda e, g=g: e.activation(beta_hm[:, g * GS:(g + 1) * GS], psb[2][0:H, 0:GS], AF.Tanh, scale=0.5),
                  reads=("PSB2",), writes=("beta_hm",))
            P.add("vector", lambda e, g=g: e.tensor_scalar(
                beta_hm[:, g * GS:(g + 1) * GS], beta_hm[:, g * GS:(g + 1) * GS], 0.5, 0.5, op0=OP.mult, op1=OP.add),
                reads=("beta_hm",), writes=("beta_hm",))
            for pi, p in enumerate("qkv"):
                for o in range(KT):
                    bk = o % 2
                    wb = o % 2
                    P.add("gq", lambda e, p=p, o=o, wb=wb: e.dma_start(
                        out=wbuf[wb][:].rearrange("p (a n) -> p a n", a=KT),
                        in_=WT_d[p][:, o * 128:(o + 1) * 128].rearrange("(a p) n -> p a n", p=128)),
                        writes=(f"wbuf{wb}",))
                    for k in range(KT):
                        P.add("tensor", lambda e, k=k, wb=wb, bk=bk: e.matmul(
                            psb[bk][:, 0:GS], wbuf[wb][:, k * 128:(k + 1) * 128], xnf[:, k * GS:(k + 1) * GS],
                            start=(k == 0), stop=(k == KT - 1)),
                            reads=(f"wbuf{wb}", f"xnf:{k}"), writes=(f"PSB{bk}",))
                    if g == 0:
                        for k in range(KT):
                            P.add("tensor", lambda e, k=k, wb=wb, bk=bk: e.matmul(
                                psb[5][:, bk * 4:bk * 4 + 3], wbuf[wb][:, k * 128:(k + 1) * 128], xnf_h[:, k * 3:(k + 1) * 3],
                                start=(k == 0), stop=(k == KT - 1)),
                                reads=(f"wbuf{wb}", f"xnf_h:{k}"), writes=("PSB5",))
                        P.add("scalar", lambda e, o=o, bk=bk: e.copy(pbuf[o][:, 0:3], psb[5][:, bk * 4:bk * 4 + 3]),
                              reads=("PSB5",), writes=(f"pbuf:{o}",))
                    else:
                        P.add("gpsimd", lambda e, pi=pi, o=o: e.tensor_copy(
                            pbuf[o][:, 0:3], tails[:, (pi * KT + o) * 3:(pi * KT + o) * 3 + 3]),
                            reads=(f"tails:{pi}:{o}",), writes=(f"pbuf:{o}",))
                    P.add("scalar", lambda e, o=o, bk=bk: e.copy(pbuf[o][:, 3:3 + GS], psb[bk][:, 0:GS]),
                          reads=(f"PSB{bk}",), writes=(f"pbuf:{o}",))
                    P.add("gpsimd", lambda e, pi=pi, o=o: e.tensor_copy(
                        tails[:, (pi * KT + o) * 3:(pi * KT + o) * 3 + 3], pbuf[o][:, GS:GS + 3]),
                        reads=(f"pbuf:{o}",), writes=(f"tails:{pi}:{o}",))
                    cwbase = pi * KT * 4 + o * 4
                    P.add("vector", lambda e, o=o, cwbase=cwbase: e.tensor_scalar(
                        ybuf[:], pbuf[o][:, 0:GS], conv_sb[:, cwbase:cwbase + 1], 0.0, op0=OP.mult, op1=OP.add),
                        reads=(f"pbuf:{o}", "conv_sb"), writes=("ybuf",))
                    for j in (1, 2, 3):
                        P.add("vector", lambda e, o=o, j=j, cwbase=cwbase: e.scalar_tensor_tensor(
                            ybuf[:], pbuf[o][:, j:j + GS], conv_sb[:, cwbase + j:cwbase + j + 1], ybuf[:],
                            op0=OP.mult, op1=OP.add),
                            reads=(f"pbuf:{o}", "conv_sb", "ybuf"), writes=("ybuf",))
                    P.add("scalar", lambda e, p=p, o=o, g=g: e.activation(
                        RES[p][:, o * R + g * GS: o * R + g * GS + GS], ybuf[:], AF.Silu),
                        reads=("ybuf",), writes=(f"res:{p}:{o}",))
        P.barrier()

        P.mark("A_done")
        sqs = [sbA(f"sqs{i}", [128, GS], F32) for i in range(2)]
        sqb = sbA("sqb", [H, R], F32)
        scbc = sbA("scbc", [128, GS], BF16)
        for g in range(NG):
            for p in ("q", "k"):
                scale = scale_q if p == "q" else scale_k
                res = RES[p]
                for t in range(KT):
                    sq = sqs[t % 2]
                    P.add("scalar", lambda e, res=res, t=t, g=g, sq=sq, p=p: e.activation(
                        sq[:], res[:, t * R + g * GS: t * R + g * GS + GS], AF.Square),
                        reads=(f"res:{p}:{t}",), writes=(f"sqs{t % 2}",))
                    P.add("tensor", lambda e, t=t, sq=sq: e.matmul(
                        psb[2][0:H, 0:GS], ind[:, t * H:(t + 1) * H], sq[:],
                        start=(t == 0), stop=(t == KT - 1)),
                        reads=("ind", f"sqs{t % 2}"), writes=("PSB2",))
                P.add("vector", lambda e, scale=scale, g=g, p=p: e.tensor_copy(
                    scale[:, g * GS:(g + 1) * GS], psb[2][0:H, 0:GS]),
                    reads=("PSB2",), writes=(f"scale_{p}",))
        P.add("scalar", lambda e: e.activation(sqb[:], beta_hm[:], AF.Sqrt), reads=("beta_hm",), writes=("sqb",))
        for p in ("q", "k"):
            scale = scale_q if p == "q" else scale_k
            P.add("scalar", lambda e, scale=scale: e.activation(scale[:], scale[:], AF.Sqrt),
                  reads=(f"scale_{p}",), writes=(f"scale_{p}",))
            P.add("vector", lambda e, scale=scale: e.tensor_scalar(scale[:], scale[:], cfg.EPS, 0.0, op0=OP.max, op1=OP.add),
                  reads=(f"scale_{p}",), writes=(f"scale_{p}",))
            P.add("vector", lambda e, scale=scale: e.reciprocal(scale[:], scale[:]),
                  reads=(f"scale_{p}",), writes=(f"scale_{p}",))
        P.add("vector", lambda e: e.tensor_scalar(scale_q[:], scale_q[:], 0.125, 0.0, op0=OP.mult, op1=OP.add),
              reads=("scale_q",), writes=("scale_q",))
        P.add("vector", lambda e: e.tensor_tensor(scale_k[:], scale_k[:], sqb[:], op=OP.mult),
              reads=("scale_k", "sqb"), writes=("scale_k",))
        for g in range(NG):
            for p in ("q", "k", "v"):
                scale = {"q": scale_q, "k": scale_k, "v": sqb}[p]
                res = RES[p]
                for t in range(KT):
                    bb = 3 + (t % 2)
                    P.add("tensor", lambda e, scale=scale, t=t, g=g, bb=bb, p=p: e.matmul(
                        psb[bb][:, 0:GS], indT[:, t * 128:(t + 1) * 128], scale[:, g * GS:(g + 1) * GS],
                        start=True, stop=True),
                        reads=("indT", "sqb" if p == "v" else f"scale_{p}"), writes=(f"PSB{bb}",))
                    P.add("scalar", lambda e, bb=bb: e.copy(scbc[:], psb[bb][:, 0:GS]),
                          reads=(f"PSB{bb}",), writes=("scbc",))
                    P.add("vector", lambda e, res=res, t=t, g=g, p=p: e.tensor_tensor(
                        res[:, t * R + g * GS: t * R + g * GS + GS],
                        res[:, t * R + g * GS: t * R + g * GS + GS], scbc[:], op=OP.mult),
                        reads=(f"res:{p}:{t}", "scbc"), writes=(f"res:{p}:{t}",))
        P.barrier()

        P.mark("B_done")
        stg = [sbA(f"stg{i}", [128, D], BF16) for i in range(2)]
        for nm in ("k", "v"):
            src = RES[nm]
            dram = krm_d if nm == "k" else vrm_d
            for c in range(NC):
                par = c % 2
                for t in range(KT):
                    pt = t % 2
                    pst = psT0 if pt == 0 else psT1
                    pstk = "PSB6" if pt == 0 else "PSB7"
                    P.add("tensor", lambda e, src=src, c=c, t=t, pst=pst: e.transpose(
                        pst[:, 0:128], src[:, t * R + c * 128: t * R + c * 128 + 128], ident_b[:]),
                        reads=(f"res:{nm}:{t}", "ident_b"), writes=(pstk,))
                    P.add("scalar", lambda e, t=t, par=par, pst=pst: e.copy(
                        stg[par][:, t * 128:(t + 1) * 128], pst[:, 0:128]),
                        reads=(pstk,), writes=(f"stg{par}",))
                if nm == "v":
                    P.add("aq", lambda e, dram=dram, c=c, par=par: e.dma_start(
                        out=dram[c * 128:(c + 1) * 128, :], in_=stg[par][:]),
                        reads=(f"stg{par}",), writes=(f"{nm}rmd:{c}",))
                else:
                    for dup in range(2):
                        P.add("aq", lambda e, dram=dram, c=c, par=par, dup=dup: e.dma_start(
                            out=dram[c * 128:(c + 1) * 128, :].rearrange("p (h u d) -> p h u d", h=H, u=2)[:, :, dup],
                            in_=stg[par][:].rearrange("p (h d) -> p h d", h=H)),
                            reads=(f"stg{par}",), writes=(f"{nm}rmd:{c}:{dup}",))
        P.barrier()
        stA.close()

        P.mark("B2_done")
        stC = ExitStack()
        sbC = lambda name, shape, dtype: stC.enter_context(nc.sbuf_tensor(name, shape, dtype))
        krm = [sbC(f"krm{i}", [128, 2 * D], BF16) for i in range(2)]
        vrm = [sbC(f"vrm{i}", [128, D], BF16) for i in range(2)]
        o0stg = [sbC(f"o0stg{i}", [128, H * 64], BF16) for i in range(2)]
        qtstg = [sbC(f"qtstg{i}", [128, KT * 128], BF16) for i in range(2)]
        ptstg = [sbC(f"ptstg{i}", [64, H * 64], BF16) for i in range(2)]

        GsL = [sbC(f"GsL{i}", [128, 256], F32) for i in range(3)]
        A_sb = [sbC(f"A_sb{i}", [128, 128], BF16) for i in range(3)]
        AT_sb = [sbC(f"AT_sb{i}", [128, 128], BF16) for i in range(3)]
        LT_sb = [sbC(f"LT_sb{i}", [128, 128], BF16) for i in range(3)]
        X_sb = [sbC(f"X_sb{i}", [128, 128], BF16) for i in range(6)]
        RRT = [[sbC(f"RRT{s}_{l}", [128, 256], BF16) for l in range(NLEV)] for s in range(3)]
        for c in range(NC):
            cp = c % 2
            P.add("sp", lambda e, c=c, cp=cp: e.dma_start(out=krm[cp][:], in_=krm_d[c * 128:(c + 1) * 128, :]),
                  reads=(f"krmd:{c}:0", f"krmd:{c}:1"), writes=(f"krm{cp}",))
            P.add("sp", lambda e, c=c, cp=cp: e.dma_start(out=vrm[cp][:], in_=vrm_d[c * 128:(c + 1) * 128, :]),
                  reads=(f"vrmd:{c}",), writes=(f"vrm{cp}",))
            for g0 in range(0, H, 3):
              grecs = []
              for h in range(g0, min(g0 + 3, H)):
                PX = _Rec()
                grecs.append(PX)
                t, hw = h // 2, h % 2
                wp = h % 3
                bA, bB = f"PSB{wp}", f"PSB{3 + wp}"
                pA, pB = psb[wp], psb[3 + wp]
                kf = ktf[64 * hw:64 * hw + 64, t * R + c * 128: t * R + c * 128 + 128]
                qf = qtf[64 * hw:64 * hw + 64, t * R + c * 128: t * R + c * 128 + 128]
                rk, rq = f"res:k:{t}", f"res:q:{t}"
                Xc, Xn = X_sb[2 * wp], X_sb[2 * wp + 1]
                XcK, XnK = f"X{2 * wp}", f"X{2 * wp + 1}"
                Sb_sl = S_b[64 * hw:64 * hw + 64, h * 64:(h + 1) * 64]
                PX.add("tensor", lambda e, kf=kf, pA=pA, hw=hw: e.matmul(pA[:, 0:128], kf, kf, start=True, stop=True,
                      tile_position=(64 * hw, 0)),
                      reads=(rk,), writes=(bA,))
                PX.add("tensor", lambda e, kf=kf, qf=qf, pA=pA, hw=hw: e.matmul(pA[:, 128:256], kf, qf, start=True, stop=True,
                      tile_position=(64 * hw, 0)),
                      reads=(rk, rq), writes=(bA,))
                PX.add("tensor", lambda e, kf=kf, pA=pA, Sb_sl=Sb_sl, hw=hw: e.matmul(
                    pA[:, 256:320], kf, Sb_sl, start=True, stop=True, tile_position=(64 * hw, 0)),
                    reads=(rk, f"S_b:{h}"), writes=(bA,))
                PX.add("scalar", lambda e, pA=pA, wp=wp: e.copy(GsL[wp][:], pA[:, 0:256]),
                      reads=(bA,), writes=(f"GsL{wp}",))
                PX.add("gpsimd", lambda e, wp=wp: e.tensor_tensor(A_sb[wp][:], GsL[wp][:, 0:128], mask_sl[:], op=OP.mult),
                      reads=(f"GsL{wp}", "mask_sl"), writes=(f"A{wp}",))
                PX.add("gpsimd", lambda e, wp=wp: e.tensor_tensor(AT_sb[wp][:], GsL[wp][:, 0:128], mask_sun[:], op=OP.mult),
                      reads=(f"GsL{wp}", "mask_sun"), writes=(f"AT{wp}",))
                PX.add("vector", lambda e, wp=wp: e.tensor_tensor(LT_sb[wp][:], GsL[wp][:, 128:256], mask_sui[:], op=OP.mult),
                      reads=(f"GsL{wp}", "mask_sui"), writes=(f"LT{wp}",))
                PX.add("gpsimd", lambda e, Xc=Xc, cp=cp, h=h: e.tensor_scalar_mul(
                    Xc[:, 0:64], krm[cp][:, h * 128:h * 128 + 64], -1.0),
                    reads=(f"krm{cp}",), writes=(XcK,))
                PX.add("vector", lambda e, Xc=Xc, cp=cp, h=h, pA=pA: e.tensor_sub(
                    Xc[:, 64:128], vrm[cp][:, h * 64:(h + 1) * 64], pA[:, 256:320]),
                    reads=(f"vrm{cp}", bA), writes=(XcK,))
                prevR, prevRK = A_sb[wp][:], f"A{wp}"
                prevRT, prevRTK = AT_sb[wp][:], f"AT{wp}"
                for lev in range(1, NLEV + 1):
                    buf = RRT[wp][lev - 1]
                    bK = f"RRT{wp}:{lev}"
                    if lev < NLEV:
                        PX.add("tensor", lambda e, pA=pA, prevR=prevR, prevRT=prevRT: e.matmul(
                            pA[:, 0:128], prevRT, prevR, start=True, stop=True),
                            reads=(prevRK, prevRTK), writes=(bA,))
                    PX.add("tensor", lambda e, pA=pA, prevR=prevR, prevRT=prevRT: e.matmul(
                        pA[:, 128:256], prevR, prevRT, start=True, stop=True),
                        reads=(prevRK, prevRTK), writes=(bA,))
                    if lev < NLEV:
                        PX.add("scalar", lambda e, pA=pA, buf=buf: e.copy(buf[:], pA[:, 0:256]),
                              reads=(bA,), writes=(bK,))
                        prevR, prevRK = buf[:, 0:128], bK
                        prevRT, prevRTK = buf[:, 128:256], bK
                    else:
                        PX.add("scalar", lambda e, pA=pA, buf=buf: e.copy(buf[:, 128:256], pA[:, 128:256]),
                              reads=(bA,), writes=(bK,))
                for j in list(range(NLEV, 0, -1)) + [0]:
                    if j > 0:
                        rt, rtK = RRT[wp][j - 1][:, 128:256], f"RRT{wp}:{j}"
                    else:
                        rt, rtK = AT_sb[wp][:], f"AT{wp}"
                    PX.add("tensor", lambda e, pB=pB, rt=rt, Xc=Xc: e.matmul(
                        pB[:, 0:128], rt, Xc[:], start=True, stop=True),
                        reads=(rtK, XcK), writes=(bB,))
                    PX.add("vector", lambda e, pB=pB, Xn=Xn, Xc=Xc: e.tensor_add(Xn[:], Xc[:], pB[:, 0:128]),
                          reads=(bB, XcK), writes=(XnK,))
                    Xc, Xn = Xn, Xc
                    XcK, XnK = XnK, XcK
                PX.add("tensor", lambda e, qf=qf, pB=pB, Sb_sl=Sb_sl, hw=hw: e.matmul(
                    pB[:, 128:192], qf, Sb_sl, start=True, stop=False, tile_position=(64 * hw, 0)),
                    reads=(rq, f"S_b:{h}"), writes=(bB,))
                PX.add("tensor", lambda e, pB=pB, wp=wp, Xc=Xc: e.matmul(
                    pB[:, 128:192], LT_sb[wp][:], Xc[:, 64:128], start=False, stop=True),
                    reads=(f"LT{wp}", XcK), writes=(bB,))
                PX.add("tensor", lambda e, pB=pB, cp=cp, h=h, Xc=Xc: e.matmul(
                    pB[:, 192:256], krm[cp][:, h * 128:(h + 1) * 128], Xc[:, 64:128], start=True, stop=True),
                    reads=(f"krm{cp}", XcK), writes=(bB,))
                PX.add("tensor", lambda e, pB=pB: e.matmul(
                    pB[0:64, 256:320], ident64[0:64, :], ident64[0:64, :], start=True, stop=False),
                    reads=("ident64",), writes=(bB,))
                PX.add("tensor", lambda e, pB=pB, cp=cp, h=h, Xc=Xc: e.matmul(
                    pB[0:64, 256:320], Xc[:, 0:64], krm[cp][:, h * 128:h * 128 + 64], start=False, stop=True),
                    reads=(f"krm{cp}", XcK), writes=(bB,))
                PX.add("tensor", lambda e, pB=pB, qf=qf, hw=hw: e.matmul(
                    pB[64 * hw:64 * hw + 64, 320:448], ident64[64 * hw:64 * hw + 64, :], qf, start=True, stop=False,
                    tile_position=(64 * hw, 64 * hw)),
                    reads=("ident64", rq), writes=(bB,))
                PX.add("tensor", lambda e, pB=pB, wp=wp, Xc=Xc, hw=hw: e.matmul(
                    pB[64 * hw:64 * hw + 64, 320:448], Xc[:, 0:64], LT_sb[wp][:], start=False, stop=True,
                    tile_position=(0, 64 * hw)),
                    reads=(f"LT{wp}", XcK), writes=(bB,))
                PX.add("vector", lambda e, pB=pB, h=h: e.tensor_add(
                    S_f[:, h * 64:(h + 1) * 64], S_f[:, h * 64:(h + 1) * 64], pB[:, 192:256]),
                    reads=(bB, f"S_f:{h}"), writes=(f"S_f:{h}",))
                PX.add("scalar", lambda e, h=h: e.copy(S_b[:, h * 64:(h + 1) * 64], S_f[:, h * 64:(h + 1) * 64]),
                      reads=(f"S_f:{h}",), writes=(f"S_b:{h}",))
                PX.add("scalar", lambda e, pB=pB, cp=cp, h=h: e.copy(
                    ptstg[cp][:, h * 64:(h + 1) * 64], pB[0:64, 256:320]),
                    reads=(bB,), writes=(f"PT:{c}:{h}",))
                PX.add("scalar", lambda e, pB=pB, cp=cp, h=h: e.copy(
                    o0stg[cp][:, h * 64:(h + 1) * 64], pB[:, 128:192]),
                    reads=(bB,), writes=(f"o0stg{cp}",))
                PX.add("scalar", lambda e, pB=pB, cp=cp, t=t, hw=hw: e.copy(
                    qtstg[cp][64 * hw:64 * hw + 64, t * 128:(t + 1) * 128], pB[64 * hw:64 * hw + 64, 320:448]),
                    reads=(bB,), writes=(f"qtstg{cp}",))
                PX.add("tensor", lambda e, pB=pB, cp=cp, h=h: e.matmul(
                    pB[0:64, 448:512], ptstg[cp][:, h * 64:(h + 1) * 64], PP_b[0:64, h * 64:(h + 1) * 64],
                    start=True, stop=True),
                    reads=(f"PT:{c}:{h}", f"PP_b:{h}"), writes=(bB,))
                PX.add("vector", lambda e, pB=pB, h=h: e.tensor_copy(PP_b[0:64, h * 64:(h + 1) * 64], pB[0:64, 448:512]),
                      reads=(bB,), writes=(f"PP_b:{h}",))
              _merge_rr(P, grecs)
            P.add("aq", lambda e, c=c, cp=cp: e.dma_start(out=o0_d[c * 128:(c + 1) * 128, :], in_=o0stg[cp][:]),
                  reads=(f"o0stg{cp}",), writes=(f"o0d:{c}",))
            P.add("aq", lambda e, c=c, cp=cp: e.dma_start(
                out=qt_d[:, c * KT * 128:(c + 1) * KT * 128], in_=qtstg[cp][:]),
                reads=(f"qtstg{cp}",), writes=(f"qtd:{c}",))
            P.add("aq", lambda e, c=c, cp=cp: e.dma_start(
                out=pt_d[:, c * H * 64:(c + 1) * H * 64], in_=ptstg[cp][:]),
                reads=tuple(f"PT:{c}:{hh}" for hh in range(H)), writes=(f"ptd:{c}",))
        P.mark("C_done")
        ccstg = sbC("ccstg", [64, H * 128], F32)
        for h in range(H):
            pt = h % 2
            pst = psT0 if pt == 0 else psT1
            pstk = "PSB6" if pt == 0 else "PSB7"
            P.add("tensor", lambda e, h=h, pst=pst: e.transpose(
                pst[0:64, 0:64], PP_b[0:64, h * 64:(h + 1) * 64], ident_b[0:64, 0:64]),
                reads=(f"PP_b:{h}", "ident_b"), writes=(pstk,))
            P.add("scalar", lambda e, h=h, pst=pst: e.copy(ccstg[:, h * 128:h * 128 + 64], pst[0:64, 0:64]),
                  reads=(pstk,), writes=("ccstg",))
            P.add("vector", lambda e, h=h: e.tensor_copy(ccstg[:, h * 128 + 64:h * 128 + 128], S_f[0:64, h * 64:(h + 1) * 64]),
                  reads=(f"S_f:{h}",), writes=("ccstg",))
        if part == "full":
            P.add("gq", lambda e: e.dma_start(out=cc_in_d[:, :], in_=ccstg[:]), reads=("ccstg",), writes=("cc_in",))
            P.add("cc", lambda e: e.collective_compute(
                "AllGather", OP.bypass, replica_groups=[list(range(n_cores))],
                ins=[cc_in_d[:, :]], outs=[cc_out_d[:, :]]),
                reads=("cc_in",), writes=("cc_out",))
        else:
            P.add("aq", lambda e: e.dma_start(out=cc_out1_d[:, :], in_=ccstg[:]), reads=("ccstg",), writes=("cc_out",))
            P.add("gpsimd", lambda e: e.engine_nop(),
                  reads=("cc_out", "msrow_d") + tuple(f"o0d:{c}" for c in range(NC))
                        + tuple(f"qtd:{c}" for c in range(NC)) + tuple(f"ptd:{c}" for c in range(NC)))
        P.barrier()
        stC.close()
        stABC.close()
        P.barrier()

    if run_p2:
        stD = ExitStack()
        sbD = lambda name, shape, dtype: stD.enter_context(nc.sbuf_tensor(name, shape, dtype))
        ccg = [sbD(f"ccgm{i}", [128, H * 192], F32) for i in range(2)]
        Sm = sbD("Sm", [128, H * 64], F32)
        Sacc = sbD("Sacc", [128, H * 64], F32)
        sel_tmp = sbD("sel_tmp", [128, H * 64], F32)
        ohbc = sbD("ohbc", [128, n_cores], F32)
        P.add("gpsimd", lambda e: e.memset(Sm[:], 0.0), writes=("Sm",))
        P.add("gpsimd", lambda e: e.memset(Sacc[:], 0.0), writes=("Sacc",))
        P.add("tensor", lambda e: e.matmul(psb[5][:, 0:n_cores], ones1[:], onehot[:], start=True, stop=True),
              reads=("ones1", "onehot"), writes=("PSB5",))
        P.add("vector", lambda e: e.tensor_copy(ohbc[:], psb[5][:, 0:n_cores]), reads=("PSB5",), writes=("ohbc",))
        for m in range(n_cores):
            if m < n_cores - 1:
                mp = m % 2
                for half in range(2):
                    for dup in range(2):
                        P.add("sp", lambda e, m=m, mp=mp, half=half, dup=dup: e.dma_start(
                            out=ccg[mp][64 * half:64 * half + 64, :].rearrange("p (h x d) -> p h x d", h=H, x=3)[:, :, dup],
                            in_=cc_out_d[m * 64:(m + 1) * 64, :].rearrange("p (h x d) -> p h x d", h=H, x=2)[:, :, 0]),
                            reads=("cc_out",), writes=(f"ccgm{mp}",))
                    P.add("sp", lambda e, m=m, mp=mp, half=half: e.dma_start(
                        out=ccg[mp][64 * half:64 * half + 64, :].rearrange("p (h x d) -> p h x d", h=H, x=3)[:, :, 2],
                        in_=cc_out_d[m * 64:(m + 1) * 64, :].rearrange("p (h x d) -> p h x d", h=H, x=2)[:, :, 1]),
                        reads=("cc_out",), writes=(f"ccgm{mp}",))
            P.add("vector", lambda e, m=m: e.scalar_tensor_tensor(
                sel_tmp[:], Sm[:], ohbc[:, m:m + 1], Sacc[:], op0=OP.mult, op1=OP.add),
                reads=("Sm", "ohbc", "Sacc"), writes=("sel_tmp",))
            P.add("vector", lambda e: e.tensor_copy(Sacc[:], sel_tmp[:]), reads=("sel_tmp",), writes=("Sacc",))
            if m < n_cores - 1:
                mp = m % 2
                for h in range(H):
                    P.add("tensor", lambda e, mp=mp, h=h: e.matmul(
                        psb[5][:, 64:128], ccg[mp][0:64, h * 192:h * 192 + 128],
                        Sm[0:64, h * 64:(h + 1) * 64], start=True, stop=True),
                        reads=(f"ccgm{mp}", "Sm"), writes=("PSB5",))
                    P.add("vector", lambda e, mp=mp, h=h: e.tensor_add(
                        Sm[:, h * 64:(h + 1) * 64], psb[5][:, 64:128],
                        ccg[mp][:, h * 192 + 128:h * 192 + 192]),
                        reads=("PSB5", f"ccgm{mp}"), writes=("Sm",))
        P.add("vector", lambda e: e.tensor_copy(N_f[:], Sacc[:]), reads=("Sacc",),
              writes=tuple(f"N_f:{h}" for h in range(H)))
        P.add("scalar", lambda e: e.copy(N_b[:], N_f[:]), reads=tuple(f"N_f:{h}" for h in range(H)),
              writes=tuple(f"N_b:{h}" for h in range(H)))
        P.barrier()

        Z_sb = sbD("Z_sb", [128, KT * D], BF16)
        e1b = sbD("e1b", [128, 512], BF16)
        e2b = [sbD(f"e2b{i}", [128, 512], BF16) for i in range(4)]
        zacc = sbD("zacc", [128, 512], F32)
        ztmp = sbD("ztmp", [128, 512], F32)
        for ri in "RI":
            for c4 in range(4):
                P.add("gq", lambda e, ri=ri, c4=c4: e.dma_start(out=e2b[c4][:], in_=E2_d[ri][c4]),
                      writes=(f"e2b{c4}",))
            for it in range(4):
                for c4 in range(4):
                    P.add("gq", lambda e, ri=ri, it=it, c4=c4: e.dma_start(out=e1b[:], in_=E1_d[ri][c4, it]),
                          writes=("e1b",))
                    if c4 == 0:
                        P.add("vector", lambda e, c4=c4: e.tensor_tensor(zacc[:], e1b[:], e2b[c4][:], op=OP.mult),
                              reads=("e1b", f"e2b{c4}"), writes=("zacc",))
                    else:
                        P.add("vector", lambda e, c4=c4: e.tensor_tensor(ztmp[:], e1b[:], e2b[c4][:], op=OP.mult),
                              reads=("e1b", f"e2b{c4}"), writes=("ztmp",))
                        P.add("vector", lambda e: e.tensor_add(zacc[:], zacc[:], ztmp[:]),
                              reads=("zacc", "ztmp"), writes=("zacc",))
                if ri == "R":
                    P.add("scalar", lambda e, it=it: e.copy(Z_sb[:, it * D + 0:it * D + 512], zacc[:]),
                          reads=("zacc",), writes=("Z_sb",))
                    P.add("scalar", lambda e, it=it: e.copy(Z_sb[:, (4 + it) * D + 512:(4 + it) * D + 1024], zacc[:]),
                          reads=("zacc",), writes=("Z_sb",))
                else:
                    P.add("scalar", lambda e, it=it: e.copy(Z_sb[:, it * D + 512:it * D + 1024], zacc[:]),
                          reads=("zacc",), writes=("Z_sb",))
                    P.add("scalar", lambda e, it=it: e.mul(Z_sb[:, (4 + it) * D + 0:(4 + it) * D + 512], zacc[:], -1.0),
                          reads=("zacc",), writes=("Z_sb",))
        P.barrier()

        WoT_sb = sbD("WoT_sb", [128, KT * D], BF16)
        P.add("gq", lambda e: e.dma_start(out=WoT_sb[:].rearrange("p (a n) -> p a n", a=KT),
                                          in_=WT_d["o"].rearrange("(a p) n -> p a n", p=128)),
              writes=("WoT_sb",))
        msgD = sbD("msgD", [1, GS], F32)
        ssgD = sbD("ssgD", [1, GS], F32)
        xnfD = sbD("xnfD", [128, KT * GS], BF16)
        xtgD = sbD("xtgD", [128, KT * GS], BF16)
        msbcD = sbD("msbcD", [128, GS], BF16)
        sbcD = sbD("sbcD", [128, GS], BF16)
        o0b = [sbD(f"o0b{i}", [128, D], BF16) for i in range(2)]
        qtb = [sbD(f"qtb{i}", [128, KT * 128], BF16) for i in range(2)]
        ptb = [sbD(f"ptb{i}", [64, H * 128], BF16) for i in range(2)]
        Ofp = [sbD(f"Ofp{i}", [128, D], F32) for i in range(2)]
        ObfR = [sbD(f"ObfR{i}", [128, D], BF16) for i in range(2)]
        Ofeat = sbD("Ofeat", [128, KT * 128], BF16)
        osq = [sbD(f"osq{i}", [128, 128], F32) for i in range(2)]
        srow16 = sbD("srow16", [H, 128], F32)
        obc = [sbD(f"obc{i}", [128, 128], BF16) for i in range(2)]
        xtok = [sbD(f"xtok{i}", [128, D], F32) for i in range(2)]
        outsb = [sbD(f"outsb{i}", [128, D], F32) for i in range(2)]

        def emit_xnf_D(grp):
            c0 = 3 + grp * GS
            for t in range(KT):
                P.add("gq", lambda e, t=t, c0=c0: e.dma_start(
                    out=xtgD[:, t * GS:(t + 1) * GS], in_=x_t_d[t * 128:(t + 1) * 128, c0:c0 + GS]),
                    writes=(f"xtgD:{t}",))
            P.add("sp", lambda e, c0=c0: e.dma_start(out=msgD[:], in_=msrow_d[0:1, c0:c0 + GS]),
                  reads=("msrow_d",), writes=("msgD",))
            P.add("sp", lambda e, c0=c0: e.dma_start(out=ssgD[:], in_=msrow_d[1:2, c0:c0 + GS]),
                  reads=("msrow_d",), writes=("ssgD",))
            P.add("tensor", lambda e: e.matmul(psb[4][:, 0:GS], ones1[:], msgD[:], start=True, stop=True),
                  reads=("ones1", "msgD"), writes=("PSB4",))
            P.add("tensor", lambda e: e.matmul(psb[5][:, 0:GS], ones1[:], ssgD[:], start=True, stop=True),
                  reads=("ones1", "ssgD"), writes=("PSB5",))
            P.add("scalar", lambda e: e.copy(msbcD[:], psb[4][:, 0:GS]), reads=("PSB4",), writes=("msbcD",))
            P.add("scalar", lambda e: e.copy(sbcD[:], psb[5][:, 0:GS]), reads=("PSB5",), writes=("sbcD",))
            for t in range(KT):
                P.add("vector", lambda e, t=t: e.tensor_tensor(
                    xnfD[:, t * GS:(t + 1) * GS], xtgD[:, t * GS:(t + 1) * GS], msbcD[:], op=OP.add),
                    reads=(f"xtgD:{t}", "msbcD"), writes=(f"xnfD:{t}",))
                P.add("vector", lambda e, t=t: e.tensor_tensor(
                    xnfD[:, t * GS:(t + 1) * GS], xnfD[:, t * GS:(t + 1) * GS], sbcD[:], op=OP.mult),
                    reads=(f"xnfD:{t}", "sbcD"), writes=(f"xnfD:{t}",))

        CPG = GS // C
        for g in range(NG):
            emit_xnf_D(g)
            for ci in range(CPG):
                c = g * CPG + ci
                cp = c % 2
                P.add("sp", lambda e, c=c, cp=cp: e.dma_start(out=o0b[cp][:], in_=o0_d[c * 128:(c + 1) * 128, :]),
                      reads=(f"o0d:{c}",), writes=(f"o0b{cp}",))
                P.add("sp", lambda e, c=c, cp=cp: e.dma_start(
                    out=qtb[cp][:], in_=qt_d[:, c * KT * 128:(c + 1) * KT * 128]),
                    reads=(f"qtd:{c}",), writes=(f"qtb{cp}",))
                P.add("sp", lambda e, c=c, cp=cp: e.dma_start(out=xtok[cp][:], in_=x_tok_d[c * 128:(c + 1) * 128, :]),
                      writes=(f"xtok{cp}",))
                for dup in range(2):
                    P.add("sp", lambda e, c=c, cp=cp, dup=dup: e.dma_start(
                        out=ptb[cp][:].rearrange("p (h u d) -> p h u d", h=H, u=2)[:, :, dup],
                        in_=pt_d[:, c * H * 64:(c + 1) * H * 64].rearrange("p (h d) -> p h d", h=H)),
                        reads=(f"ptd:{c}",), writes=(f"ptb{cp}",))
                for h in range(H):
                    t, hw = h // 2, h % 2
                    P.add("tensor", lambda e, cp=cp, t=t, hw=hw, h=h: e.matmul(
                        psb[2 + hw][:, (h // 2) * 64:(h // 2) * 64 + 64],
                        qtb[cp][64 * hw:64 * hw + 64, t * 128:(t + 1) * 128],
                        N_b[64 * hw:64 * hw + 64, h * 64:(h + 1) * 64], start=True, stop=True,
                        tile_position=(64 * hw, 0)),
                        reads=(f"qtb{cp}", f"N_b:{h}"), writes=(f"PSB{2 + hw}",))
                for h in range(H):
                    hw = h % 2
                    P.add("tensor", lambda e, h=h, cp=cp, hw=hw: e.matmul(
                        psb[4 + hw][:, (h // 2) * 64:(h // 2) * 64 + 64],
                        ptb[cp][:, h * 128:(h + 1) * 128],
                        N_b[0:64, h * 64:(h + 1) * 64], start=True, stop=True),
                        reads=(f"ptb{cp}", f"N_b:{h}"), writes=(f"PSB{4 + hw}",))
                allN = tuple(f"N_b:{hh}" for hh in range(H))
                allNf = tuple(f"N_f:{hh}" for hh in range(H))
                for hw in range(2):
                    P.add("vector", lambda e, cp=cp, hw=hw: e.tensor_add(
                        Ofp[cp][:].rearrange("p (h x d) -> p h x d", h=8, x=2)[:, :, hw],
                        o0b[cp][:].rearrange("p (h x d) -> p h x d", h=8, x=2)[:, :, hw],
                        psb[2 + hw][:, 0:512].rearrange("p (h d) -> p h d", h=8)),
                        reads=(f"o0b{cp}", f"PSB{2 + hw}"), writes=(f"Ofp{cp}",))
                    P.add("vector", lambda e, hw=hw: e.tensor_copy(
                        N_f[:].rearrange("p (h x d) -> p h x d", h=8, x=2)[:, :, hw],
                        psb[4 + hw][:, 0:512].rearrange("p (h d) -> p h d", h=8)),
                        reads=(f"PSB{4 + hw}",), writes=allNf)
                P.add("scalar", lambda e: e.copy(N_b[:], N_f[:]), reads=allNf, writes=allN)
                for half in range(2):
                    P.add("scalar", lambda e, cp=cp, half=half: e.copy(
                        ObfR[cp][:, half * 512:half * 512 + 512], Ofp[cp][:, half * 512:half * 512 + 512]),
                        reads=(f"Ofp{cp}",), writes=(f"ObfR{cp}",))
                for t in range(KT):
                    pt = t % 2
                    pst = psT0 if pt == 0 else psT1
                    pstk = "PSB6" if pt == 0 else "PSB7"
                    P.add("tensor", lambda e, cp=cp, t=t, pst=pst: e.transpose(
                        pst[:, 0:128], ObfR[cp][:, t * 128:(t + 1) * 128], ident_b[:]),
                        reads=(f"ObfR{cp}", "ident_b"), writes=(pstk,))
                    P.add("scalar", lambda e, t=t, pst=pst: e.copy(Ofeat[:, t * 128:(t + 1) * 128], pst[:, 0:128]),
                          reads=(pstk,), writes=(f"Ofeat:{t}",))
                    P.add("scalar", lambda e, t=t: e.activation(osq[t % 2][:], Ofeat[:, t * 128:(t + 1) * 128], AF.Square),
                          reads=(f"Ofeat:{t}",), writes=(f"osq{t % 2}",))
                    P.add("tensor", lambda e, t=t: e.matmul(
                        psb[4][0:H, 0:128], ind[:, t * H:(t + 1) * H], osq[t % 2][:],
                        start=(t == 0), stop=(t == KT - 1)),
                        reads=("ind", f"osq{t % 2}"), writes=("PSB4",))
                P.add("scalar", lambda e: e.activation(srow16[:], psb[4][0:H, 0:128], AF.Sqrt, scale=1.0 / dh, bias=epsrms[0:H, :]),
                      reads=("PSB4", "epsrms"), writes=("srow16",))
                P.add("vector", lambda e: e.reciprocal(srow16[:], srow16[:]), reads=("srow16",), writes=("srow16",))
                for t in range(KT):
                    P.add("tensor", lambda e, t=t: e.matmul(
                        psb[5][:, 0:128], indT[:, t * 128:(t + 1) * 128], srow16[:], start=True, stop=True),
                        reads=("indT", "srow16"), writes=("PSB5",))
                    P.add("scalar", lambda e, t=t: e.activation(obc[t % 2][:], psb[5][:, 0:128], AF.Identity, scale=onw[:]),
                          reads=("PSB5", "onw"), writes=(f"obc{t % 2}",))
                    P.add("vector", lambda e, t=t: e.tensor_tensor(
                        Ofeat[:, t * 128:(t + 1) * 128], Ofeat[:, t * 128:(t + 1) * 128], obc[t % 2][:], op=OP.mult),
                        reads=(f"Ofeat:{t}", f"obc{t % 2}"), writes=(f"Ofeat:{t}",))
                for half in range(2):
                    bo = half
                    for t in range(KT):
                        P.add("tensor", lambda e, t=t, half=half, bo=bo: e.matmul(
                            psb[bo][:, 0:512], Ofeat[:, t * 128:(t + 1) * 128],
                            WoT_sb[:, t * D + half * 512: t * D + half * 512 + 512],
                            start=(t == 0), stop=False),
                            reads=(f"Ofeat:{t}", "WoT_sb"), writes=(f"PSB{bo}",))
                    for t in range(KT):
                        P.add("tensor", lambda e, t=t, ci=ci, half=half, bo=bo: e.matmul(
                            psb[bo][:, 0:512], xnfD[:, t * GS + ci * C: t * GS + ci * C + 128],
                            Z_sb[:, t * D + half * 512: t * D + half * 512 + 512],
                            start=False, stop=False),
                            reads=(f"xnfD:{t}", "Z_sb"), writes=(f"PSB{bo}",))
                    P.add("tensor", lambda e, half=half, bo=bo: e.matmul(
                        psb[bo][:, 0:512], ones1[:], biasrow[0:1, half * 512:half * 512 + 512],
                        start=False, stop=True),
                        reads=("ones1", "biasrow"), writes=(f"PSB{bo}",))
                    P.add("vector", lambda e, cp=cp, half=half, bo=bo: e.tensor_add(
                        outsb[cp][:, half * 512:half * 512 + 512], psb[bo][:, 0:512],
                        xtok[cp][:, half * 512:half * 512 + 512]),
                        reads=(f"PSB{bo}", f"xtok{cp}"), writes=(f"outsb{cp}",))
                P.add("sp", lambda e, c=c, cp=cp: e.dma_start(out=out_d[c * 128:(c + 1) * 128, :], in_=outsb[cp][:]),
                      reads=(f"outsb{cp}",), writes=("out_final",))

        P.add("gpsimd", lambda e: e.engine_nop(), reads=("out_final",))

    P.mark("end")
    import os
    if os.environ.get("KPRINT_MARKS"):
        print("MARKS:", P.marks)
    P.finalize(truncate=cfg.trunc, skip=getattr(cfg, 'skip_items', ()))
    P.emit(nc, sems)
    return nc


def host_in_maps(cfg: CFG, inputs):
    import ml_dtypes
    n = cfg.n_cores
    R, D, H, KT = cfg.R, cfg.D, cfg.H, cfg.KT
    x = np.asarray(inputs["L0_attn_x"], np.float32).reshape(cfg.L, D)
    WT = {p: np.ascontiguousarray(np.asarray(inputs[f"L0_attn_W{p}"], np.float32).T) for p in "qkvo"}
    WbT = np.ascontiguousarray(np.asarray(inputs["L0_attn_Wb"], np.float32).T)
    onw = np.ascontiguousarray(np.tile(np.asarray(inputs["L0_attn_o_norm_w"], np.float32), 2)[:, None])
    biasrow = np.ascontiguousarray(np.concatenate([
        np.asarray(inputs["L0_ffn_bias_real"], np.float32),
        np.asarray(inputs["L0_ffn_bias_imag"], np.float32)])[None, :])

    def expand(G1, G2):
        G1 = np.asarray(G1, np.float32)[0]
        G2 = np.asarray(G2, np.float32)
        d_of_p = np.arange(128) // 16
        g_of_p = np.arange(128) % 16
        E1 = np.empty((4, 4, 128, 512), np.float32)
        E2 = np.empty((4, 128, 512), np.float32)
        for c4 in range(4):
            colB = G2[c4, :, 0, :]
            E2[c4] = colB.T[g_of_p][:, None, :].repeat(32, axis=1).reshape(128, 512)
            for it in range(4):
                vals = G1[:, c4, it * 8 + d_of_p]
                E1[c4, it] = vals.T[:, :, None].repeat(16, axis=2).reshape(128, 512)
        return E1, E2
    E1R, E2R = expand(inputs["L0_ffn_G1_R"], inputs["L0_ffn_G2_R"])
    E1I, E2I = expand(inputs["L0_ffn_G1_I"], inputs["L0_ffn_G2_I"])

    i_ = np.arange(128)
    mask_sl = -((i_[None, :] < i_[:, None]).astype(np.float32))
    mask_sun = -((i_[None, :] > i_[:, None]).astype(np.float32))
    mask_sui = (i_[None, :] >= i_[:, None]).astype(np.float32)
    ident_f = np.eye(128, dtype=np.float32)
    ident_b = np.eye(128).astype(ml_dtypes.bfloat16)
    ident64_2 = np.ascontiguousarray(np.tile(np.eye(64), (2, 1)).astype(ml_dtypes.bfloat16))
    ones1 = np.ones((1, 128), np.float32)
    ppinit = np.ascontiguousarray(np.tile(np.eye(64), (1, H)).astype(ml_dtypes.bfloat16))
    ind = np.zeros((128, KT * H), np.float32)
    indT = np.zeros((H, KT * 128), np.float32)
    for t in range(KT):
        for p in range(128):
            h = 2 * t + (1 if p >= 64 else 0)
            ind[p, t * H + h] = 1.0
            indT[h, t * 128 + p] = 1.0

    in_maps = []
    for i in range(n):
        r0 = i * R
        xt_sl = np.zeros((D, R + 3), np.float32)
        lo = max(r0 - 3, 0)
        xt_sl[:, 3 - (r0 - lo):] = x[lo:r0 + R].T
        halo = np.zeros((3, D), np.float32)
        if r0 >= 3:
            halo[:] = x[r0 - 3:r0]
        oh = np.zeros((1, n), np.float32)
        oh[0, i] = 1.0
        m = {
            "x_tok": np.ascontiguousarray(x[r0:r0 + R]),
            "x_t": np.ascontiguousarray(xt_sl),
            "x_halo": halo,
            "WbT": WbT,
            "conv_q": np.asarray(inputs["L0_attn_conv_q"], np.float32),
            "conv_k": np.asarray(inputs["L0_attn_conv_k"], np.float32),
            "conv_v": np.asarray(inputs["L0_attn_conv_v"], np.float32),
            "onw128": onw, "biasrow": biasrow,
            "E1R": E1R, "E2R": E2R, "E1I": E1I, "E2I": E2I,
            "mask_sl": mask_sl, "mask_sun": mask_sun, "mask_sui": mask_sui,
            "ident_f": ident_f, "ident_b": ident_b, "ident64_2": ident64_2,
            "ones1": ones1, "ppinit": ppinit, "onehot": oh,
            "ind": ind, "indT": indT,
        }
        for p in "qkvo":
            m[f"W{p}T"] = WT[p]
        in_maps.append(m)
    return in_maps


MID_TENSORS = ("o0_dram", "qt_dram", "pt_dram", "msrow_dram")


def run_pipeline(cfg, inputs, launcher):
    base_maps = host_in_maps(cfg, inputs)
    n = cfg.n_cores
    nc1 = build_program(cfg, "p1")
    res1 = launcher(nc1, base_maps)
    ccg = np.ascontiguousarray(np.concatenate(
        [np.asarray(r["cc_stage_out"], np.float32) for r in res1], axis=0))
    maps2 = []
    for i, m in enumerate(base_maps):
        m2 = dict(m)
        for t in MID_TENSORS:
            m2[t] = np.asarray(res1[i][t])
        m2["cc_gathered"] = ccg
        maps2.append(m2)
    nc2 = build_program(cfg, "p2")
    res2 = launcher(nc2, maps2)
    out = np.concatenate([np.asarray(r["out"], np.float32) for r in res2], axis=0)
    return out


def kernel(**inputs):
    cfg = CFG()

    def launcher(nc, in_maps):
        res = run_bass_kernel_spmd(nc, in_maps, core_ids=list(range(cfg.n_cores)))
        return res.results
    out = run_pipeline(cfg, inputs, launcher)
    return out.reshape(1, cfg.L, cfg.D)


# version 10
# speedup vs baseline: 1.4787x; 1.0591x over previous
import numpy as np
from contextlib import ExitStack

import concourse.bass as bass
import concourse.mybir as mybir
from concourse.alu_op_type import AluOpType as OP
from concourse.bass_utils import run_bass_kernel_spmd

F32 = mybir.dt.float32
BF16 = mybir.dt.bfloat16
AF = mybir.ActivationFunctionType
AX = mybir.AxisListType


class CFG:
    n_cores = 8
    L = 16384
    D = 1024
    H = 16
    dh = 64
    C = 128
    GS = 512
    EPS = 1e-6
    LN_EPS = 1e-5
    debug = False
    trunc = None

    @property
    def R(self): return self.L // self.n_cores
    @property
    def KT(self): return self.D // 128
    @property
    def NG(self): return self.R // self.GS
    @property
    def NC(self): return self.R // self.C


DMA_STREAMS = ("sp", "aq", "gq", "cc")
STREAM_SECTION = {
    "tensor": "tensor", "vector": "vector", "scalar": "scalar",
    "gpsimd": "gpsimd", "sync": "sync",
    "sp": "sync", "aq": "scalar", "gq": "gpsimd", "cc": "gpsimd",
}


class Pl:

    def __init__(self):
        self.items = []
        self.barrier_snap = {}
        self.ordn = {}

    def add(self, stream, fn, reads=(), writes=(), inc=None):
        if inc is None:
            inc = 16 if stream in DMA_STREAMS else 1
        if stream in DMA_STREAMS and stream != "cc":
            o = self.ordn.get(stream, 0)
            self.ordn[stream] = o + 1
            semid = f"{stream}{o % self.NROT}"
        else:
            semid = stream
        self.items.append(dict(s=stream, fn=fn, r=tuple(reads), w=tuple(writes),
                               inc=inc, deps=set(), bar=dict(self.barrier_snap),
                               semid=semid))

    def barrier(self):
        snap = {}
        for i, it in enumerate(self.items):
            snap[it["semid"]] = i
        self.barrier_snap = snap

    NROT = 4

    marks = []

    def mark(self, name):
        self.marks.append((name, len(self.items)))

    def finalize(self, truncate=None, skip=()):
        if truncate is not None:
            self.items = self.items[:truncate]
        if skip:
            self.items = [it for i, it in enumerate(self.items) if i not in skip]
        lastuse = {}
        lastw, readers = {}, {}
        last_psum_rd = {}
        for i, it in enumerate(self.items):
            dep = set(it["bar"].values())
            for r in it["r"]:
                if r in lastw:
                    dep.add(lastw[r])
                if r.startswith("PSB"):
                    pr = last_psum_rd.get(r)
                    if pr is not None and pr[1] != it["s"]:
                        dep.add(pr[0])
                    last_psum_rd[r] = (i, it["s"])
            for w in it["w"]:
                if w in lastw:
                    dep.add(lastw[w])
                for rd in readers.get(w, ()):
                    dep.add(rd)
            for r in it["r"]:
                readers.setdefault(r, []).append(i)
            for w in it["w"]:
                lastw[w] = i
                readers[w] = []
                if w.startswith("PSB"):
                    last_psum_rd.pop(w, None)
            if it["s"] in DMA_STREAMS:
                if it["semid"] in lastuse:
                    dep.add(lastuse[it["semid"]])
                lastuse[it["semid"]] = i
            it["deps"] = {d for d in dep
                          if not (self.items[d]["s"] == it["s"] and it["s"] == "tensor")}
        need = [False] * len(self.items)
        for i, it in enumerate(self.items):
            if it["s"] in DMA_STREAMS:
                need[i] = True
            for d in it["deps"]:
                need[d] = True
        cnt = {}
        self.after = [0] * len(self.items)
        self.neednc = need
        for i, it in enumerate(self.items):
            c = cnt.get(it["semid"], 0)
            if need[i]:
                c += it["inc"]
            cnt[it["semid"]] = c
            self.after[i] = c

    def emit(self, nc, sems):
        by_section = {}
        for i, it in enumerate(self.items):
            by_section.setdefault(STREAM_SECTION[it["s"]], []).append(i)
        with nc.Block() as block:
            for section in ("sync", "scalar", "vector", "tensor", "gpsimd"):
                idxs = by_section.get(section, [])
                dec = getattr(block, section)

                def make(idxs=idxs):
                    def body(eng):
                        waited = {}
                        for i in idxs:
                            it = self.items[i]
                            wmax = {}
                            for d in it["deps"]:
                                ds = self.items[d]["semid"]
                                v = self.after[d]
                                if v > wmax.get(ds, 0):
                                    wmax[ds] = v
                            for ds, v in sorted(wmax.items()):
                                if waited.get(ds, 0) < v:
                                    eng.wait_ge(sems[ds], v)
                                    waited[ds] = v
                            inst = it["fn"](eng)
                            if self.neednc[i]:
                                inst.then_inc(sems[it["semid"]], it["inc"])
                    return body
                dec(make())


class _Rec:

    def __init__(self):
        self.calls = []

    def add(self, *a, **k):
        self.calls.append((a, k))


def _merge_rr(P, recs):
    pos = [0] * len(recs)
    while True:
        any_left = False
        for i, r in enumerate(recs):
            if pos[i] < len(r.calls):
                a, k = r.calls[pos[i]]
                P.add(*a, **k)
                pos[i] += 1
                any_left = True
        if not any_left:
            break


def dup2(ap64):
    return ap64.rearrange("p (o d) -> p o d", o=1).broadcast_to([ap64.shape[0], 2, 64])


def build_program(cfg: CFG, part: str = "p1"):
    R, D, H, dh, C, GS = cfg.R, cfg.D, cfg.H, cfg.dh, cfg.C, cfg.GS
    KT, NG, NC = cfg.KT, cfg.NG, cfg.NC
    NLEV = 6
    assert C == 2 ** (NLEV + 1)
    n_cores = cfg.n_cores
    NT = R // 128

    nc = bass.Bass("TRN2", target_bir_lowering=False)
    P = Pl()

    dt = lambda name, shape, dtype, kind: nc.dram_tensor(name, shape, dtype, kind=kind).ap()
    x_tok_d = dt("x_tok", [R, D], F32, "ExternalInput")
    x_t_d = dt("x_t", [D, R + 3], F32, "ExternalInput")
    x_halo_d = dt("x_halo", [3, D], F32, "ExternalInput")
    WT_d = {p: dt(f"W{p}T", [D, D], F32, "ExternalInput") for p in "qkvo"}
    WbT_d = dt("WbT", [D, H], F32, "ExternalInput")
    conv_d = {p: dt(f"conv_{p}", [D, 4], F32, "ExternalInput") for p in "qkv"}
    onw_d = dt("onw128", [128, 1], F32, "ExternalInput")
    bias_d = dt("biasrow", [1, D], F32, "ExternalInput")
    E1_d = {ri: dt(f"E1{ri}", [4, KT // 2, 128, 512], F32, "ExternalInput") for ri in "RI"}
    E2_d = {ri: dt(f"E2{ri}", [4, 128, 512], F32, "ExternalInput") for ri in "RI"}
    mask_sl_d = dt("mask_sl", [128, 128], F32, "ExternalInput")
    mask_sun_d = dt("mask_sun", [128, 128], F32, "ExternalInput")
    mask_sui_d = dt("mask_sui", [128, 128], F32, "ExternalInput")
    ident_f_d = dt("ident_f", [128, 128], F32, "ExternalInput")
    ident_b_d = dt("ident_b", [128, 128], BF16, "ExternalInput")
    ident64_d = dt("ident64_2", [128, 64], BF16, "ExternalInput")
    ones1_d = dt("ones1", [1, 128], F32, "ExternalInput")
    ppinit_d = dt("ppinit", [64, H * 64], BF16, "ExternalInput")
    onehot_d = dt("onehot", [1, n_cores], F32, "ExternalInput")
    ind_d = dt("ind", [128, KT * H], F32, "ExternalInput")
    indT_d = dt("indT", [H, KT * 128], F32, "ExternalInput")

    out_d = dt("out", [R, D], F32, "ExternalOutput") if part != "p1" else None

    mid_kind = {"full": "Internal", "p1": "ExternalOutput", "p2": "ExternalInput"}[part]
    if part != "p2":
        krm_d = dt("krm_dram", [R, 2 * D], BF16, "Internal")
        vrm_d = dt("vrm_dram", [R, D], BF16, "Internal")
    o0_d = dt("o0_dram", [R, D], BF16, mid_kind)
    qt_d = dt("qt_dram", [128, NC * KT * 128], BF16, mid_kind)
    pt_d = dt("pt_dram", [64, NC * H * 64], BF16, mid_kind)
    msrow_d = dt("msrow_dram", [2, R + 3], F32, mid_kind)
    if part == "full":
        cc_in_d = dt("cc_in", [64, H * 128], F32, "Internal")
        cc_out_d = dt("cc_out", [n_cores * 64, H * 128], F32, "Internal")
    elif part == "p1":
        cc_out1_d = dt("cc_stage_out", [64, H * 128], F32, "ExternalOutput")
    else:
        cc_out_d = dt("cc_gathered", [n_cores * 64, H * 128], F32, "ExternalInput")
    dbg_d = dt("dbg", [128, 32 * 128], F32, "Internal") if cfg.debug else None
    def dbgtap(slot, ap, readkeys):
        if cfg.debug:
            P.add("gq", lambda e, slot=slot, ap=ap: e.dma_start(
                out=dbg_d[:ap.shape[0], slot * 128: slot * 128 + ap.shape[1]], in_=ap),
                reads=tuple(readkeys), writes=("dbg",))

    st = ExitStack()
    sb = lambda name, shape, dtype: st.enter_context(nc.sbuf_tensor(name, shape, dtype))
    ps_ = lambda name, shape, dtype: st.enter_context(nc.psum_tensor(name, shape, dtype))

    sem_names = ["tensor", "vector", "scalar", "gpsimd", "sync", "cc"]
    for s in ("sp", "aq", "gq"):
        sem_names += [f"{s}{r}" for r in range(Pl.NROT)]
    sems = {s: st.enter_context(nc.semaphore(f"sem_{s}")) for s in sem_names}

    psb = [ps_(f"psb{i}", [128, 512], F32) for i in range(6)]
    psT0 = ps_("psT0", [128, 512], BF16)
    psT1 = ps_("psT1", [128, 512], BF16)

    mask_sl = sb("mask_sl_s", [128, 128], F32)
    mask_sun = sb("mask_sun_s", [128, 128], F32)
    mask_sui = sb("mask_sui_s", [128, 128], F32)
    ident_f = sb("ident_f_s", [128, 128], F32)
    ident_b = sb("ident_b_s", [128, 128], BF16)
    ident64 = sb("ident64_s", [128, 64], BF16)
    ones1 = sb("ones1_s", [1, 128], F32)
    onw = sb("onw_s", [128, 1], F32)
    biasrow = sb("biasrow_s", [1, D], F32)
    onehot = sb("onehot_s", [1, n_cores], F32)
    ind = sb("ind_s", [128, KT * H], F32)
    indT = sb("indT_s", [H, KT * 128], F32)
    S_f = sb("S_f", [128, H * 64], F32)
    S_b = sb("S_b", [128, H * 64], BF16)
    PP_b = sb("PP_b", [64, H * 64], BF16)
    N_f = sb("N_f", [128, H * 64], F32)
    N_b = sb("N_b", [128, H * 64], BF16)
    epsln = sb("epsln", [128, 1], F32)
    epsrms = sb("epsrms", [128, 1], F32)

    def ld(dst, src, key, stream="sp"):
        P.add(stream, lambda e, d=dst, s=src: e.dma_start(out=d, in_=s), writes=(key,))

    ld(mask_sl[:], mask_sl_d[:, :], "mask_sl")
    ld(mask_sun[:], mask_sun_d[:, :], "mask_sun")
    ld(mask_sui[:], mask_sui_d[:, :], "mask_sui")
    ld(ident_f[:], ident_f_d[:, :], "ident_f")
    ld(ident_b[:], ident_b_d[:, :], "ident_b")
    ld(ident64[:], ident64_d[:, :], "ident64")
    ld(ones1[:], ones1_d[:, :], "ones1")
    ld(onw[:], onw_d[:, :], "onw")
    ld(biasrow[:], bias_d[:, :], "biasrow")
    ld(onehot[:], onehot_d[:, :], "onehot")
    ld(ind[:], ind_d[:, :], "ind")
    ld(indT[:], indT_d[:, :], "indT")
    P.add("sp", lambda e: e.dma_start(out=PP_b[:], in_=ppinit_d[:, :]), writes=tuple(f"PP_b:{h}" for h in range(H)))
    P.add("gpsimd", lambda e: e.memset(epsln[:], cfg.LN_EPS), writes=("epsln",))
    P.add("gpsimd", lambda e: e.memset(epsrms[:], cfg.EPS), writes=("epsrms",))
    P.add("gpsimd", lambda e: e.memset(S_f[:], 0.0), writes=tuple(f"S_f:{h}" for h in range(H)))
    P.add("gpsimd", lambda e: e.memset(S_b[:], 0.0), writes=tuple(f"S_b:{h}" for h in range(H)))

    run_p1 = part in ("full", "p1")
    run_p2 = part in ("full", "p2")
    stA0 = ExitStack()
    if run_p1:
        sb0 = lambda name, shape, dtype: stA0.enter_context(nc.sbuf_tensor(name, shape, dtype))
        stat_pack = sb0("stat_pack", [128, 2 * NT], F32)
        stat_sum = sb0("stat_sum", [128, NT], F32)
        stat_ssq = sb0("stat_ssq", [128, NT], F32)
        stat_tmp = sb0("stat_tmp", [128, NT], F32)
        evac32 = sb0("evac32", [2 * NT, 128], F32)
        xa = [sb0(f"xa{i}", [128, D], F32) for i in range(4)]
        xsq = sb0("xsq", [128, D], F32)
        for t in range(NT):
            b = t % 4
            P.add("sp", lambda e, b=b, t=t: e.dma_start(out=xa[b][:], in_=x_tok_d[t * 128:(t + 1) * 128, :]),
                  writes=(f"xa{b}",))
            P.add("vector", lambda e, b=b, t=t: e.reduce_sum(stat_sum[:, t:t + 1], xa[b][:], axis=AX.X),
                  reads=(f"xa{b}",), writes=("stat_sum",))
            P.add("scalar", lambda e, b=b, t=t: e.activation(xsq[:], xa[b][:], AF.Square, accum_out=stat_ssq[:, t:t + 1]),
                  reads=(f"xa{b}",), writes=("stat_ssq", "xsq"))
        xh = sb0("xh", [3, D], F32)
        xhsq = sb0("xhsq", [3, D], F32)
        halo_sum = sb0("halo_sum", [3, 1], F32)
        halo_var = sb0("halo_var", [3, 1], F32)
        halo_m2 = sb0("halo_m2", [3, 1], F32)
        halo_s = sb0("halo_s", [3, 2], F32)
        P.add("sp", lambda e: e.dma_start(out=xh[:], in_=x_halo_d[:, :]), writes=("xh",))
        P.add("vector", lambda e: e.reduce_sum(halo_sum[:], xh[:], axis=AX.X),
              reads=("xh",), writes=("halo_sum",))
        P.add("scalar", lambda e: e.activation(xhsq[:], xh[:], AF.Square, accum_out=halo_var[:]),
              reads=("xh",), writes=("halo_var", "xhsq"))
        P.add("vector", lambda e: e.tensor_scalar(stat_sum[:], stat_sum[:], 1.0 / D, 0.0, op0=OP.mult, op1=OP.add),
              reads=("stat_sum",), writes=("stat_sum",))
        P.add("vector", lambda e: e.scalar_tensor_tensor(stat_tmp[:], stat_sum[:], 1.0, stat_sum[:], op0=OP.mult, op1=OP.mult),
              reads=("stat_sum",), writes=("stat_tmp",))
        P.add("vector", lambda e: e.scalar_tensor_tensor(stat_ssq[:], stat_ssq[:], 1.0 / D, stat_tmp[:], op0=OP.mult, op1=OP.subtract),
              reads=("stat_ssq", "stat_tmp"), writes=("stat_ssq",))
        P.add("scalar", lambda e: e.activation(stat_ssq[:], stat_ssq[:], AF.Sqrt, bias=epsln[:]),
              reads=("stat_ssq", "epsln"), writes=("stat_ssq",))
        P.add("vector", lambda e: e.reciprocal(stat_pack[:, NT:2 * NT], stat_ssq[:]),
              reads=("stat_ssq",), writes=("stat_pack",))
        P.add("vector", lambda e: e.scalar_tensor_tensor(stat_pack[:, 0:NT], stat_sum[:], -1.0, stat_pack[:, NT:2 * NT], op0=OP.mult, op1=OP.mult),
              reads=("stat_sum", "stat_pack"), writes=("stat_pack",))
        P.add("vector", lambda e: e.tensor_scalar(halo_sum[:], halo_sum[:], 1.0 / D, 0.0, op0=OP.mult, op1=OP.add),
              reads=("halo_sum",), writes=("halo_sum",))
        P.add("vector", lambda e: e.scalar_tensor_tensor(halo_m2[:], halo_sum[:], 1.0, halo_sum[:], op0=OP.mult, op1=OP.mult),
              reads=("halo_sum",), writes=("halo_m2",))
        P.add("vector", lambda e: e.scalar_tensor_tensor(halo_var[:], halo_var[:], 1.0 / D, halo_m2[:], op0=OP.mult, op1=OP.subtract),
              reads=("halo_var", "halo_m2"), writes=("halo_var",))
        P.add("scalar", lambda e: e.activation(halo_var[:], halo_var[:], AF.Sqrt, bias=epsln[0:3, :]),
              reads=("halo_var", "epsln"), writes=("halo_var",))
        P.add("vector", lambda e: e.reciprocal(halo_s[:, 1:2], halo_var[:]),
              reads=("halo_var",), writes=("halo_s",))
        P.add("vector", lambda e: e.scalar_tensor_tensor(halo_s[:, 0:1], halo_sum[:], -1.0, halo_s[:, 1:2], op0=OP.mult, op1=OP.mult),
              reads=("halo_sum", "halo_s"), writes=("halo_s",))
        P.add("tensor", lambda e: e.transpose(psb[0][0:2 * NT, 0:128], stat_pack[:], ident_f[:]),
              reads=("stat_pack", "ident_f"), writes=("PSB0",))
        P.add("vector", lambda e: e.tensor_copy(evac32[:], psb[0][0:2 * NT, 0:128]),
              reads=("PSB0",), writes=("evac32",))
        P.add("aq", lambda e: e.dma_start(out=msrow_d[0:1, 3:3 + NT * 128].rearrange("o (t p) -> (o t) p", t=NT),
                                          in_=evac32[0:NT, :]),
              reads=("evac32",), writes=("msrow_d",))
        P.add("aq", lambda e: e.dma_start(out=msrow_d[1:2, 3:3 + NT * 128].rearrange("o (t p) -> (o t) p", t=NT),
                                          in_=evac32[NT:2 * NT, :]),
              reads=("evac32",), writes=("msrow_d",))
        P.add("aq", lambda e: e.dma_start(out=msrow_d[0:1, 0:3].rearrange("o t -> t o"), in_=halo_s[:, 0:1]),
              reads=("halo_s",), writes=("msrow_d",))
        P.add("aq", lambda e: e.dma_start(out=msrow_d[1:2, 0:3].rearrange("o t -> t o"), in_=halo_s[:, 1:2]),
              reads=("halo_s",), writes=("msrow_d",))
        P.barrier()
        stA0.close()

        P.mark("A0_done")
        stABC = ExitStack()
        sbB = lambda name, shape, dtype: stABC.enter_context(nc.sbuf_tensor(name, shape, dtype))
        qtf = sbB("qtf", [128, KT * R], BF16)
        ktf = sbB("ktf", [128, KT * R], BF16)
        vtf = sbB("vtf", [128, KT * R], BF16)
        RES = {"q": qtf, "k": ktf, "v": vtf}

        stA = ExitStack()
        sbA = lambda name, shape, dtype: stA.enter_context(nc.sbuf_tensor(name, shape, dtype))
        wbuf = [sbA(f"wbuf{i}", [128, D], BF16) for i in range(4)]
        pbuf = [sbA(f"pbuf{i}", [128, GS + 3], BF16) for i in range(KT)]
        tails = sbA("tails", [128, 3 * 3 * KT], BF16)
        xnf = sbA("xnf", [128, KT * GS], BF16)
        xnf_h = sbA("xnf_h", [128, KT * 3], BF16)
        xtg = sbA("xtg", [128, KT * GS], BF16)
        xtg_h = sbA("xtg_h", [128, KT * 3], BF16)
        msbc = sbA("msbc", [128, GS], BF16)
        sbc = sbA("sbc", [128, GS], BF16)
        msbc_h = sbA("msbc_h", [128, 3], BF16)
        sbc_h = sbA("sbc_h", [128, 3], BF16)
        conv_sb = sbA("conv_sb", [128, KT * 4 * 3], F32)
        ybuf = [sbA(f"ybuf{i}", [128, GS], BF16) for i in range(2)]
        WbT_sb = sbA("WbT_sb", [128, KT * H], BF16)
        msg = sbA("msg", [1, GS], F32)
        ssg = sbA("ssg", [1, GS], F32)
        msg_h = sbA("msg_h", [1, 3], F32)
        ssg_h = sbA("ssg_h", [1, 3], F32)
        beta_hm = sbA("beta_hm", [H, R], F32)
        scale_q = sbA("scale_q", [H, R], F32)
        scale_k = sbA("scale_k", [H, R], F32)
    
        for pi, p in enumerate("qkv"):
            P.add("gq", lambda e, pi=pi, p=p: e.dma_start(
                out=conv_sb[:, pi * KT * 4:(pi + 1) * KT * 4].rearrange("p (a k) -> p a k", k=4),
                in_=conv_d[p].rearrange("(a p) k -> p a k", p=128)),
                writes=("conv_sb",))
        P.add("gq", lambda e: e.dma_start(out=WbT_sb[:].rearrange("p (a h) -> p a h", a=KT),
                                          in_=WbT_d.rearrange("(a p) h -> p a h", p=128)),
              writes=("WbT_sb",))

        def emit_xnf(grp, dst, dkey, xsrc, xkey, mb, sbuf_b, with_halo, dst_h=None, hx=None, mbh=None, sbh=None,
                     mrow=None, srw=None, mrow_h=None, srw_h=None):
            c0 = 3 + grp * GS
            for t in range(KT):
                P.add("gq", lambda e, t=t, c0=c0, xsrc=xsrc: e.dma_start(
                    out=xsrc[:, t * GS:(t + 1) * GS], in_=x_t_d[t * 128:(t + 1) * 128, c0:c0 + GS]),
                    writes=(f"{xkey}:{t}",))
            P.add("sp", lambda e, c0=c0, mrow=mrow: e.dma_start(out=mrow[:], in_=msrow_d[0:1, c0:c0 + GS]),
                  reads=("msrow_d",), writes=(f"{dkey}_mrow",))
            P.add("sp", lambda e, c0=c0, srw=srw: e.dma_start(out=srw[:], in_=msrow_d[1:2, c0:c0 + GS]),
                  reads=("msrow_d",), writes=(f"{dkey}_srow",))
            P.add("tensor", lambda e, mrow=mrow: e.matmul(psb[3][:, 0:GS], ones1[:], mrow[:], start=True, stop=True),
                  reads=("ones1", f"{dkey}_mrow"), writes=("PSB3",))
            P.add("tensor", lambda e, srw=srw: e.matmul(psb[4][:, 0:GS], ones1[:], srw[:], start=True, stop=True),
                  reads=("ones1", f"{dkey}_srow"), writes=("PSB4",))
            P.add("scalar", lambda e, mb=mb: e.copy(mb[:], psb[3][:, 0:GS]), reads=("PSB3",), writes=(f"{dkey}_mb",))
            P.add("scalar", lambda e, sbuf_b=sbuf_b: e.copy(sbuf_b[:], psb[4][:, 0:GS]), reads=("PSB4",), writes=(f"{dkey}_sb",))
            if with_halo:
                for t in range(KT):
                    P.add("gq", lambda e, t=t, hx=hx: e.dma_start(
                        out=hx[:, t * 3:(t + 1) * 3], in_=x_t_d[t * 128:(t + 1) * 128, 0:3]),
                        writes=(f"xtg_h:{t}",))
                P.add("sp", lambda e, mrow_h=mrow_h: e.dma_start(out=mrow_h[:], in_=msrow_d[0:1, 0:3]),
                      reads=("msrow_d",), writes=("mrow_h",))
                P.add("sp", lambda e, srw_h=srw_h: e.dma_start(out=srw_h[:], in_=msrow_d[1:2, 0:3]),
                      reads=("msrow_d",), writes=("srow_h",))
                P.add("tensor", lambda e, mrow_h=mrow_h: e.matmul(psb[3][:, 0:3], ones1[:], mrow_h[:], start=True, stop=True),
                      reads=("ones1", "mrow_h"), writes=("PSB3",))
                P.add("tensor", lambda e, srw_h=srw_h: e.matmul(psb[4][:, 0:3], ones1[:], srw_h[:], start=True, stop=True),
                      reads=("ones1", "srow_h"), writes=("PSB4",))
                P.add("scalar", lambda e, mbh=mbh: e.copy(mbh[:], psb[3][:, 0:3]), reads=("PSB3",), writes=("msbc_h",))
                P.add("scalar", lambda e, sbh=sbh: e.copy(sbh[:], psb[4][:, 0:3]), reads=("PSB4",), writes=("sbc_h",))
            for t in range(KT):
                P.add("vector", lambda e, t=t, dst=dst, xsrc=xsrc, mb=mb: e.tensor_tensor(
                    dst[:, t * GS:(t + 1) * GS], xsrc[:, t * GS:(t + 1) * GS], mb[:], op=OP.add),
                    reads=(f"{xkey}:{t}", f"{dkey}_mb"), writes=(f"{dkey}:{t}",))
                P.add("vector", lambda e, t=t, dst=dst, sbuf_b=sbuf_b: e.tensor_tensor(
                    dst[:, t * GS:(t + 1) * GS], dst[:, t * GS:(t + 1) * GS], sbuf_b[:], op=OP.mult),
                    reads=(f"{dkey}:{t}", f"{dkey}_sb"), writes=(f"{dkey}:{t}",))
                if with_halo:
                    P.add("vector", lambda e, t=t, dst_h=dst_h, hx=hx, mbh=mbh: e.tensor_tensor(
                        dst_h[:, t * 3:(t + 1) * 3], hx[:, t * 3:(t + 1) * 3], mbh[:], op=OP.add),
                        reads=(f"xtg_h:{t}", "msbc_h"), writes=(f"xnf_h:{t}",))
                    P.add("vector", lambda e, t=t, dst_h=dst_h, sbh=sbh: e.tensor_tensor(
                        dst_h[:, t * 3:(t + 1) * 3], dst_h[:, t * 3:(t + 1) * 3], sbh[:], op=OP.mult),
                        reads=(f"xnf_h:{t}", "sbc_h"), writes=(f"xnf_h:{t}",))

        for g in range(NG):
            emit_xnf(g, xnf, "xnf", xtg, "xtg", msbc, sbc, g == 0, xnf_h, xtg_h, msbc_h, sbc_h,
                     mrow=msg, srw=ssg, mrow_h=msg_h, srw_h=ssg_h)
            for t in range(KT):
                P.add("tensor", lambda e, t=t: e.matmul(
                    psb[2][0:H, 0:GS], WbT_sb[:, t * H:(t + 1) * H], xnf[:, t * GS:(t + 1) * GS],
                    start=(t == 0), stop=(t == KT - 1)),
                    reads=("WbT_sb", f"xnf:{t}"), writes=("PSB2",))
            P.add("scalar", lambda e, g=g: e.activation(beta_hm[:, g * GS:(g + 1) * GS], psb[2][0:H, 0:GS], AF.Tanh, scale=0.5),
                  reads=("PSB2",), writes=("beta_hm",))
            P.add("vector", lambda e, g=g: e.tensor_scalar(
                beta_hm[:, g * GS:(g + 1) * GS], beta_hm[:, g * GS:(g + 1) * GS], 0.5, 0.5, op0=OP.mult, op1=OP.add),
                reads=("beta_hm",), writes=("beta_hm",))
            for pi, p in enumerate("qkv"):
                for o0_ in range(0, KT, 2):
                  orecs = []
                  for o in range(o0_, o0_ + 2):
                    PX = _Rec()
                    orecs.append(PX)
                    bk = o % 2
                    wb = (pi * KT + o) % 4
                    PX.add("gq", lambda e, p=p, o=o, wb=wb: e.dma_start(
                        out=wbuf[wb][:].rearrange("p (a n) -> p a n", a=KT),
                        in_=WT_d[p][:, o * 128:(o + 1) * 128].rearrange("(a p) n -> p a n", p=128)),
                        writes=(f"wbuf{wb}",))
                    for k in range(KT):
                        PX.add("tensor", lambda e, k=k, wb=wb, bk=bk: e.matmul(
                            psb[bk][:, 0:GS], wbuf[wb][:, k * 128:(k + 1) * 128], xnf[:, k * GS:(k + 1) * GS],
                            start=(k == 0), stop=(k == KT - 1)),
                            reads=(f"wbuf{wb}", f"xnf:{k}"), writes=(f"PSB{bk}",))
                    if g == 0:
                        hb = 5 if bk == 0 else 3
                        for k in range(KT):
                            PX.add("tensor", lambda e, k=k, wb=wb, hb=hb: e.matmul(
                                psb[hb][:, 0:3], wbuf[wb][:, k * 128:(k + 1) * 128], xnf_h[:, k * 3:(k + 1) * 3],
                                start=(k == 0), stop=(k == KT - 1)),
                                reads=(f"wbuf{wb}", f"xnf_h:{k}"), writes=(f"PSB{hb}",))
                        PX.add("scalar", lambda e, o=o, hb=hb: e.copy(pbuf[o][:, 0:3], psb[hb][:, 0:3]),
                              reads=(f"PSB{hb}",), writes=(f"pbuf:{o}",))
                    else:
                        PX.add("gpsimd", lambda e, pi=pi, o=o: e.tensor_copy(
                            pbuf[o][:, 0:3], tails[:, (pi * KT + o) * 3:(pi * KT + o) * 3 + 3]),
                            reads=(f"tails:{pi}:{o}",), writes=(f"pbuf:{o}",))
                    PX.add("scalar", lambda e, o=o, bk=bk: e.copy(pbuf[o][:, 3:3 + GS], psb[bk][:, 0:GS]),
                          reads=(f"PSB{bk}",), writes=(f"pbuf:{o}",))
                    PX.add("gpsimd", lambda e, pi=pi, o=o: e.tensor_copy(
                        tails[:, (pi * KT + o) * 3:(pi * KT + o) * 3 + 3], pbuf[o][:, GS:GS + 3]),
                        reads=(f"pbuf:{o}",), writes=(f"tails:{pi}:{o}",))
                    cwbase = pi * KT * 4 + o * 4
                    PX.add("vector", lambda e, o=o, cwbase=cwbase: e.tensor_scalar(
                        ybuf[o % 2][:], pbuf[o][:, 0:GS], conv_sb[:, cwbase:cwbase + 1], 0.0, op0=OP.mult, op1=OP.add),
                        reads=(f"pbuf:{o}", "conv_sb"), writes=(f"ybuf{o % 2}",))
                    for j in (1, 2, 3):
                        PX.add("vector", lambda e, o=o, j=j, cwbase=cwbase: e.scalar_tensor_tensor(
                            ybuf[o % 2][:], pbuf[o][:, j:j + GS], conv_sb[:, cwbase + j:cwbase + j + 1], ybuf[o % 2][:],
                            op0=OP.mult, op1=OP.add),
                            reads=(f"pbuf:{o}", "conv_sb", f"ybuf{o % 2}"), writes=(f"ybuf{o % 2}",))
                    PX.add("scalar", lambda e, p=p, o=o, g=g: e.activation(
                        RES[p][:, o * R + g * GS: o * R + g * GS + GS], ybuf[o % 2][:], AF.Silu),
                        reads=(f"ybuf{o % 2}",), writes=(f"res:{p}:{o}",))
                  _merge_rr(P, orecs)
        P.barrier()

        P.mark("A_done")
        sqs = [sbA(f"sqs{i}", [128, GS], F32) for i in range(2)]
        sqb = sbA("sqb", [H, R], F32)
        scbc = sbA("scbc", [128, GS], BF16)
        for g in range(NG):
            for p in ("q", "k"):
                scale = scale_q if p == "q" else scale_k
                res = RES[p]
                for t in range(KT):
                    sq = sqs[t % 2]
                    P.add("scalar", lambda e, res=res, t=t, g=g, sq=sq, p=p: e.activation(
                        sq[:], res[:, t * R + g * GS: t * R + g * GS + GS], AF.Square),
                        reads=(f"res:{p}:{t}",), writes=(f"sqs{t % 2}",))
                    P.add("tensor", lambda e, t=t, sq=sq: e.matmul(
                        psb[2][0:H, 0:GS], ind[:, t * H:(t + 1) * H], sq[:],
                        start=(t == 0), stop=(t == KT - 1)),
                        reads=("ind", f"sqs{t % 2}"), writes=("PSB2",))
                P.add("vector", lambda e, scale=scale, g=g, p=p: e.tensor_copy(
                    scale[:, g * GS:(g + 1) * GS], psb[2][0:H, 0:GS]),
                    reads=("PSB2",), writes=(f"scale_{p}",))
        P.add("scalar", lambda e: e.activation(sqb[:], beta_hm[:], AF.Sqrt), reads=("beta_hm",), writes=("sqb",))
        for p in ("q", "k"):
            scale = scale_q if p == "q" else scale_k
            P.add("scalar", lambda e, scale=scale: e.activation(scale[:], scale[:], AF.Sqrt),
                  reads=(f"scale_{p}",), writes=(f"scale_{p}",))
            P.add("vector", lambda e, scale=scale: e.tensor_scalar(scale[:], scale[:], cfg.EPS, 0.0, op0=OP.max, op1=OP.add),
                  reads=(f"scale_{p}",), writes=(f"scale_{p}",))
            P.add("vector", lambda e, scale=scale: e.reciprocal(scale[:], scale[:]),
                  reads=(f"scale_{p}",), writes=(f"scale_{p}",))
        P.add("vector", lambda e: e.tensor_scalar(scale_q[:], scale_q[:], 0.125, 0.0, op0=OP.mult, op1=OP.add),
              reads=("scale_q",), writes=("scale_q",))
        P.add("vector", lambda e: e.tensor_tensor(scale_k[:], scale_k[:], sqb[:], op=OP.mult),
              reads=("scale_k", "sqb"), writes=("scale_k",))
        for g in range(NG):
            for p in ("q", "k", "v"):
                scale = {"q": scale_q, "k": scale_k, "v": sqb}[p]
                res = RES[p]
                for t in range(KT):
                    bb = 3 + (t % 2)
                    P.add("tensor", lambda e, scale=scale, t=t, g=g, bb=bb, p=p: e.matmul(
                        psb[bb][:, 0:GS], indT[:, t * 128:(t + 1) * 128], scale[:, g * GS:(g + 1) * GS],
                        start=True, stop=True),
                        reads=("indT", "sqb" if p == "v" else f"scale_{p}"), writes=(f"PSB{bb}",))
                    P.add("scalar", lambda e, bb=bb: e.copy(scbc[:], psb[bb][:, 0:GS]),
                          reads=(f"PSB{bb}",), writes=("scbc",))
                    P.add("vector", lambda e, res=res, t=t, g=g, p=p: e.tensor_tensor(
                        res[:, t * R + g * GS: t * R + g * GS + GS],
                        res[:, t * R + g * GS: t * R + g * GS + GS], scbc[:], op=OP.mult),
                        reads=(f"res:{p}:{t}", "scbc"), writes=(f"res:{p}:{t}",))
        P.barrier()
        stA.close()

        P.mark("B_done")
        stC = ExitStack()
        sbC = lambda name, shape, dtype: stC.enter_context(nc.sbuf_tensor(name, shape, dtype))
        stg = [sbC(f"stg{i}", [128, D], BF16) for i in range(2)]
        for nm in ("k", "v"):
            src = RES[nm]
            dram = krm_d if nm == "k" else vrm_d
            for c in range(NC):
                par = c % 2
                for t in range(KT):
                    pt = t % 2
                    pst = psT0 if pt == 0 else psT1
                    pstk = "PSB6" if pt == 0 else "PSB7"
                    P.add("tensor", lambda e, src=src, c=c, t=t, pst=pst: e.transpose(
                        pst[:, 0:128], src[:, t * R + c * 128: t * R + c * 128 + 128], ident_b[:]),
                        reads=(f"res:{nm}:{t}", "ident_b"), writes=(pstk,))
                    P.add("scalar", lambda e, t=t, par=par, pst=pst: e.copy(
                        stg[par][:, t * 128:(t + 1) * 128], pst[:, 0:128]),
                        reads=(pstk,), writes=(f"stg{par}",))
                if nm == "v":
                    P.add("aq", lambda e, dram=dram, c=c, par=par: e.dma_start(
                        out=dram[c * 128:(c + 1) * 128, :], in_=stg[par][:]),
                        reads=(f"stg{par}",), writes=(f"{nm}rmd:{c}",))
                else:
                    for dup in range(2):
                        P.add("aq", lambda e, dram=dram, c=c, par=par, dup=dup: e.dma_start(
                            out=dram[c * 128:(c + 1) * 128, :].rearrange("p (h u d) -> p h u d", h=H, u=2)[:, :, dup],
                            in_=stg[par][:].rearrange("p (h d) -> p h d", h=H)),
                            reads=(f"stg{par}",), writes=(f"{nm}rmd:{c}:{dup}",))
        P.mark("B2_done")
        krm = [sbC(f"krm{i}", [128, 2 * D], BF16) for i in range(2)]
        vrm = [sbC(f"vrm{i}", [128, D], BF16) for i in range(2)]
        o0stg = [sbC(f"o0stg{i}", [128, H * 64], BF16) for i in range(2)]
        qtstg = [sbC(f"qtstg{i}", [128, KT * 128], BF16) for i in range(2)]
        ptstg = [sbC(f"ptstg{i}", [64, H * 64], BF16) for i in range(2)]

        GsL = [sbC(f"GsL{i}", [128, 256], F32) for i in range(3)]
        A_sb = [sbC(f"A_sb{i}", [128, 128], BF16) for i in range(3)]
        AT_sb = [sbC(f"AT_sb{i}", [128, 128], BF16) for i in range(3)]
        LT_sb = [sbC(f"LT_sb{i}", [128, 128], BF16) for i in range(3)]
        X_sb = [sbC(f"X_sb{i}", [128, 128], BF16) for i in range(6)]
        RRT = [[sbC(f"RRT{s}_{l}", [128, 256], BF16) for l in range(NLEV)] for s in range(3)]
        for c in range(NC):
            cp = c % 2
            P.add("sp", lambda e, c=c, cp=cp: e.dma_start(out=krm[cp][:], in_=krm_d[c * 128:(c + 1) * 128, :]),
                  reads=(f"krmd:{c}:0", f"krmd:{c}:1"), writes=(f"krm{cp}",))
            P.add("sp", lambda e, c=c, cp=cp: e.dma_start(out=vrm[cp][:], in_=vrm_d[c * 128:(c + 1) * 128, :]),
                  reads=(f"vrmd:{c}",), writes=(f"vrm{cp}",))
            for g0 in range(0, H, 3):
              grecs = []
              for h in range(g0, min(g0 + 3, H)):
                PX = _Rec()
                grecs.append(PX)
                t, hw = h // 2, h % 2
                wp = h % 3
                bA, bB = f"PSB{wp}", f"PSB{3 + wp}"
                pA, pB = psb[wp], psb[3 + wp]
                kf = ktf[64 * hw:64 * hw + 64, t * R + c * 128: t * R + c * 128 + 128]
                qf = qtf[64 * hw:64 * hw + 64, t * R + c * 128: t * R + c * 128 + 128]
                rk, rq = f"res:k:{t}", f"res:q:{t}"
                Xc, Xn = X_sb[2 * wp], X_sb[2 * wp + 1]
                XcK, XnK = f"X{2 * wp}", f"X{2 * wp + 1}"
                Sb_sl = S_b[64 * hw:64 * hw + 64, h * 64:(h + 1) * 64]
                PX.add("tensor", lambda e, kf=kf, pA=pA, hw=hw: e.matmul(pA[:, 0:128], kf, kf, start=True, stop=True,
                      tile_position=(64 * hw, 0)),
                      reads=(rk,), writes=(bA,))
                PX.add("tensor", lambda e, kf=kf, qf=qf, pA=pA, hw=hw: e.matmul(pA[:, 128:256], kf, qf, start=True, stop=True,
                      tile_position=(64 * hw, 0)),
                      reads=(rk, rq), writes=(bA,))
                PX.add("tensor", lambda e, kf=kf, pA=pA, Sb_sl=Sb_sl, hw=hw: e.matmul(
                    pA[:, 256:320], kf, Sb_sl, start=True, stop=True, tile_position=(64 * hw, 0)),
                    reads=(rk, f"S_b:{h}"), writes=(bA,))
                PX.add("scalar", lambda e, pA=pA, wp=wp: e.copy(GsL[wp][:], pA[:, 0:256]),
                      reads=(bA,), writes=(f"GsL{wp}",))
                PX.add("gpsimd", lambda e, wp=wp: e.tensor_tensor(A_sb[wp][:], GsL[wp][:, 0:128], mask_sl[:], op=OP.mult),
                      reads=(f"GsL{wp}", "mask_sl"), writes=(f"A{wp}",))
                PX.add("gpsimd", lambda e, wp=wp: e.tensor_tensor(AT_sb[wp][:], GsL[wp][:, 0:128], mask_sun[:], op=OP.mult),
                      reads=(f"GsL{wp}", "mask_sun"), writes=(f"AT{wp}",))
                PX.add("vector", lambda e, wp=wp: e.tensor_tensor(LT_sb[wp][:], GsL[wp][:, 128:256], mask_sui[:], op=OP.mult),
                      reads=(f"GsL{wp}", "mask_sui"), writes=(f"LT{wp}",))
                PX.add("gpsimd", lambda e, Xc=Xc, cp=cp, h=h: e.tensor_scalar_mul(
                    Xc[:, 0:64], krm[cp][:, h * 128:h * 128 + 64], -1.0),
                    reads=(f"krm{cp}",), writes=(XcK,))
                PX.add("vector", lambda e, Xc=Xc, cp=cp, h=h, pA=pA: e.tensor_sub(
                    Xc[:, 64:128], vrm[cp][:, h * 64:(h + 1) * 64], pA[:, 256:320]),
                    reads=(f"vrm{cp}", bA), writes=(XcK,))
                prevR, prevRK = A_sb[wp][:], f"A{wp}"
                prevRT, prevRTK = AT_sb[wp][:], f"AT{wp}"
                for lev in range(1, NLEV + 1):
                    buf = RRT[wp][lev - 1]
                    bK = f"RRT{wp}:{lev}"
                    if lev < NLEV:
                        PX.add("tensor", lambda e, pA=pA, prevR=prevR, prevRT=prevRT: e.matmul(
                            pA[:, 0:128], prevRT, prevR, start=True, stop=True),
                            reads=(prevRK, prevRTK), writes=(bA,))
                    PX.add("tensor", lambda e, pA=pA, prevR=prevR, prevRT=prevRT: e.matmul(
                        pA[:, 128:256], prevR, prevRT, start=True, stop=True),
                        reads=(prevRK, prevRTK), writes=(bA,))
                    if lev < NLEV:
                        PX.add("scalar", lambda e, pA=pA, buf=buf: e.copy(buf[:], pA[:, 0:256]),
                              reads=(bA,), writes=(bK,))
                        prevR, prevRK = buf[:, 0:128], bK
                        prevRT, prevRTK = buf[:, 128:256], bK
                    else:
                        PX.add("scalar", lambda e, pA=pA, buf=buf: e.copy(buf[:, 128:256], pA[:, 128:256]),
                              reads=(bA,), writes=(bK,))
                for j in list(range(NLEV, 0, -1)) + [0]:
                    if j > 0:
                        rt, rtK = RRT[wp][j - 1][:, 128:256], f"RRT{wp}:{j}"
                    else:
                        rt, rtK = AT_sb[wp][:], f"AT{wp}"
                    PX.add("tensor", lambda e, pB=pB, rt=rt, Xc=Xc: e.matmul(
                        pB[:, 0:128], rt, Xc[:], start=True, stop=True),
                        reads=(rtK, XcK), writes=(bB,))
                    PX.add("vector", lambda e, pB=pB, Xn=Xn, Xc=Xc: e.tensor_add(Xn[:], Xc[:], pB[:, 0:128]),
                          reads=(bB, XcK), writes=(XnK,))
                    Xc, Xn = Xn, Xc
                    XcK, XnK = XnK, XcK
                PX.add("tensor", lambda e, qf=qf, pB=pB, Sb_sl=Sb_sl, hw=hw: e.matmul(
                    pB[:, 128:192], qf, Sb_sl, start=True, stop=False, tile_position=(64 * hw, 0)),
                    reads=(rq, f"S_b:{h}"), writes=(bB,))
                PX.add("tensor", lambda e, pB=pB, wp=wp, Xc=Xc: e.matmul(
                    pB[:, 128:192], LT_sb[wp][:], Xc[:, 64:128], start=False, stop=True),
                    reads=(f"LT{wp}", XcK), writes=(bB,))
                PX.add("tensor", lambda e, pB=pB, cp=cp, h=h, Xc=Xc: e.matmul(
                    pB[:, 192:256], krm[cp][:, h * 128:(h + 1) * 128], Xc[:, 64:128], start=True, stop=True),
                    reads=(f"krm{cp}", XcK), writes=(bB,))
                PX.add("tensor", lambda e, pB=pB: e.matmul(
                    pB[0:64, 256:320], ident64[0:64, :], ident64[0:64, :], start=True, stop=False),
                    reads=("ident64",), writes=(bB,))
                PX.add("tensor", lambda e, pB=pB, cp=cp, h=h, Xc=Xc: e.matmul(
                    pB[0:64, 256:320], Xc[:, 0:64], krm[cp][:, h * 128:h * 128 + 64], start=False, stop=True),
                    reads=(f"krm{cp}", XcK), writes=(bB,))
                PX.add("tensor", lambda e, pB=pB, qf=qf, hw=hw: e.matmul(
                    pB[64 * hw:64 * hw + 64, 320:448], ident64[64 * hw:64 * hw + 64, :], qf, start=True, stop=False,
                    tile_position=(64 * hw, 64 * hw)),
                    reads=("ident64", rq), writes=(bB,))
                PX.add("tensor", lambda e, pB=pB, wp=wp, Xc=Xc, hw=hw: e.matmul(
                    pB[64 * hw:64 * hw + 64, 320:448], Xc[:, 0:64], LT_sb[wp][:], start=False, stop=True,
                    tile_position=(0, 64 * hw)),
                    reads=(f"LT{wp}", XcK), writes=(bB,))
                PX.add("vector", lambda e, pB=pB, h=h: e.tensor_add(
                    S_f[:, h * 64:(h + 1) * 64], S_f[:, h * 64:(h + 1) * 64], pB[:, 192:256]),
                    reads=(bB, f"S_f:{h}"), writes=(f"S_f:{h}",))
                PX.add("scalar", lambda e, h=h: e.copy(S_b[:, h * 64:(h + 1) * 64], S_f[:, h * 64:(h + 1) * 64]),
                      reads=(f"S_f:{h}",), writes=(f"S_b:{h}",))
                PX.add("scalar", lambda e, pB=pB, cp=cp, h=h: e.copy(
                    ptstg[cp][:, h * 64:(h + 1) * 64], pB[0:64, 256:320]),
                    reads=(bB,), writes=(f"PT:{c}:{h}",))
                PX.add("scalar", lambda e, pB=pB, cp=cp, h=h: e.copy(
                    o0stg[cp][:, h * 64:(h + 1) * 64], pB[:, 128:192]),
                    reads=(bB,), writes=(f"o0stg{cp}",))
                PX.add("scalar", lambda e, pB=pB, cp=cp, t=t, hw=hw: e.copy(
                    qtstg[cp][64 * hw:64 * hw + 64, t * 128:(t + 1) * 128], pB[64 * hw:64 * hw + 64, 320:448]),
                    reads=(bB,), writes=(f"qtstg{cp}",))
                PX.add("tensor", lambda e, pB=pB, cp=cp, h=h: e.matmul(
                    pB[0:64, 448:512], ptstg[cp][:, h * 64:(h + 1) * 64], PP_b[0:64, h * 64:(h + 1) * 64],
                    start=True, stop=True),
                    reads=(f"PT:{c}:{h}", f"PP_b:{h}"), writes=(bB,))
                PX.add("vector", lambda e, pB=pB, h=h: e.tensor_copy(PP_b[0:64, h * 64:(h + 1) * 64], pB[0:64, 448:512]),
                      reads=(bB,), writes=(f"PP_b:{h}",))
              _merge_rr(P, grecs)
            P.add("aq", lambda e, c=c, cp=cp: e.dma_start(out=o0_d[c * 128:(c + 1) * 128, :], in_=o0stg[cp][:]),
                  reads=(f"o0stg{cp}",), writes=(f"o0d:{c}",))
            P.add("aq", lambda e, c=c, cp=cp: e.dma_start(
                out=qt_d[:, c * KT * 128:(c + 1) * KT * 128], in_=qtstg[cp][:]),
                reads=(f"qtstg{cp}",), writes=(f"qtd:{c}",))
            P.add("aq", lambda e, c=c, cp=cp: e.dma_start(
                out=pt_d[:, c * H * 64:(c + 1) * H * 64], in_=ptstg[cp][:]),
                reads=tuple(f"PT:{c}:{hh}" for hh in range(H)), writes=(f"ptd:{c}",))
        P.mark("C_done")
        ccstg = sbC("ccstg", [64, H * 128], F32)
        for h in range(H):
            pt = h % 2
            pst = psT0 if pt == 0 else psT1
            pstk = "PSB6" if pt == 0 else "PSB7"
            P.add("tensor", lambda e, h=h, pst=pst: e.transpose(
                pst[0:64, 0:64], PP_b[0:64, h * 64:(h + 1) * 64], ident_b[0:64, 0:64]),
                reads=(f"PP_b:{h}", "ident_b"), writes=(pstk,))
            P.add("scalar", lambda e, h=h, pst=pst: e.copy(ccstg[:, h * 128:h * 128 + 64], pst[0:64, 0:64]),
                  reads=(pstk,), writes=("ccstg",))
            P.add("vector", lambda e, h=h: e.tensor_copy(ccstg[:, h * 128 + 64:h * 128 + 128], S_f[0:64, h * 64:(h + 1) * 64]),
                  reads=(f"S_f:{h}",), writes=("ccstg",))
        if part == "full":
            P.add("gq", lambda e: e.dma_start(out=cc_in_d[:, :], in_=ccstg[:]), reads=("ccstg",), writes=("cc_in",))
            P.add("cc", lambda e: e.collective_compute(
                "AllGather", OP.bypass, replica_groups=[list(range(n_cores))],
                ins=[cc_in_d[:, :]], outs=[cc_out_d[:, :]]),
                reads=("cc_in",), writes=("cc_out",))
        else:
            P.add("aq", lambda e: e.dma_start(out=cc_out1_d[:, :], in_=ccstg[:]), reads=("ccstg",), writes=("cc_out",))
            P.add("gpsimd", lambda e: e.engine_nop(),
                  reads=("cc_out", "msrow_d") + tuple(f"o0d:{c}" for c in range(NC))
                        + tuple(f"qtd:{c}" for c in range(NC)) + tuple(f"ptd:{c}" for c in range(NC)))
        P.barrier()
        stC.close()
        stABC.close()
        P.barrier()

    if run_p2:
        stD = ExitStack()
        sbD = lambda name, shape, dtype: stD.enter_context(nc.sbuf_tensor(name, shape, dtype))
        ccg = [sbD(f"ccgm{i}", [128, H * 192], F32) for i in range(2)]
        Sm = sbD("Sm", [128, H * 64], F32)
        Sacc = sbD("Sacc", [128, H * 64], F32)
        sel_tmp = sbD("sel_tmp", [128, H * 64], F32)
        ohbc = sbD("ohbc", [128, n_cores], F32)
        P.add("gpsimd", lambda e: e.memset(Sm[:], 0.0), writes=("Sm",))
        P.add("gpsimd", lambda e: e.memset(Sacc[:], 0.0), writes=("Sacc",))
        P.add("tensor", lambda e: e.matmul(psb[5][:, 0:n_cores], ones1[:], onehot[:], start=True, stop=True),
              reads=("ones1", "onehot"), writes=("PSB5",))
        P.add("vector", lambda e: e.tensor_copy(ohbc[:], psb[5][:, 0:n_cores]), reads=("PSB5",), writes=("ohbc",))
        for m in range(n_cores):
            if m < n_cores - 1:
                mp = m % 2
                for half in range(2):
                    for dup in range(2):
                        P.add("sp", lambda e, m=m, mp=mp, half=half, dup=dup: e.dma_start(
                            out=ccg[mp][64 * half:64 * half + 64, :].rearrange("p (h x d) -> p h x d", h=H, x=3)[:, :, dup],
                            in_=cc_out_d[m * 64:(m + 1) * 64, :].rearrange("p (h x d) -> p h x d", h=H, x=2)[:, :, 0]),
                            reads=("cc_out",), writes=(f"ccgm{mp}",))
                    P.add("sp", lambda e, m=m, mp=mp, half=half: e.dma_start(
                        out=ccg[mp][64 * half:64 * half + 64, :].rearrange("p (h x d) -> p h x d", h=H, x=3)[:, :, 2],
                        in_=cc_out_d[m * 64:(m + 1) * 64, :].rearrange("p (h x d) -> p h x d", h=H, x=2)[:, :, 1]),
                        reads=("cc_out",), writes=(f"ccgm{mp}",))
            P.add("vector", lambda e, m=m: e.scalar_tensor_tensor(
                sel_tmp[:], Sm[:], ohbc[:, m:m + 1], Sacc[:], op0=OP.mult, op1=OP.add),
                reads=("Sm", "ohbc", "Sacc"), writes=("sel_tmp",))
            P.add("vector", lambda e: e.tensor_copy(Sacc[:], sel_tmp[:]), reads=("sel_tmp",), writes=("Sacc",))
            if m < n_cores - 1:
                mp = m % 2
                for h in range(H):
                    P.add("tensor", lambda e, mp=mp, h=h: e.matmul(
                        psb[5][:, 64:128], ccg[mp][0:64, h * 192:h * 192 + 128],
                        Sm[0:64, h * 64:(h + 1) * 64], start=True, stop=True),
                        reads=(f"ccgm{mp}", "Sm"), writes=("PSB5",))
                    P.add("vector", lambda e, mp=mp, h=h: e.tensor_add(
                        Sm[:, h * 64:(h + 1) * 64], psb[5][:, 64:128],
                        ccg[mp][:, h * 192 + 128:h * 192 + 192]),
                        reads=("PSB5", f"ccgm{mp}"), writes=("Sm",))
        P.add("vector", lambda e: e.tensor_copy(N_f[:], Sacc[:]), reads=("Sacc",),
              writes=tuple(f"N_f:{h}" for h in range(H)))
        P.add("scalar", lambda e: e.copy(N_b[:], N_f[:]), reads=tuple(f"N_f:{h}" for h in range(H)),
              writes=tuple(f"N_b:{h}" for h in range(H)))

        Z_sb = sbD("Z_sb", [128, KT * D], BF16)
        e1b = sbD("e1b", [128, 512], BF16)
        e2b = [sbD(f"e2b{i}", [128, 512], BF16) for i in range(4)]
        zacc = sbD("zacc", [128, 512], F32)
        ztmp = sbD("ztmp", [128, 512], F32)
        for ri in "RI":
            for c4 in range(4):
                P.add("gq", lambda e, ri=ri, c4=c4: e.dma_start(out=e2b[c4][:], in_=E2_d[ri][c4]),
                      writes=(f"e2b{c4}",))
            for it in range(4):
                for c4 in range(4):
                    P.add("gq", lambda e, ri=ri, it=it, c4=c4: e.dma_start(out=e1b[:], in_=E1_d[ri][c4, it]),
                          writes=("e1b",))
                    if c4 == 0:
                        P.add("vector", lambda e, c4=c4: e.tensor_tensor(zacc[:], e1b[:], e2b[c4][:], op=OP.mult),
                              reads=("e1b", f"e2b{c4}"), writes=("zacc",))
                    else:
                        P.add("vector", lambda e, c4=c4: e.tensor_tensor(ztmp[:], e1b[:], e2b[c4][:], op=OP.mult),
                              reads=("e1b", f"e2b{c4}"), writes=("ztmp",))
                        P.add("vector", lambda e: e.tensor_add(zacc[:], zacc[:], ztmp[:]),
                              reads=("zacc", "ztmp"), writes=("zacc",))
                if ri == "R":
                    P.add("scalar", lambda e, it=it: e.copy(Z_sb[:, it * D + 0:it * D + 512], zacc[:]),
                          reads=("zacc",), writes=("Z_sb",))
                    P.add("scalar", lambda e, it=it: e.copy(Z_sb[:, (4 + it) * D + 512:(4 + it) * D + 1024], zacc[:]),
                          reads=("zacc",), writes=("Z_sb",))
                else:
                    P.add("scalar", lambda e, it=it: e.copy(Z_sb[:, it * D + 512:it * D + 1024], zacc[:]),
                          reads=("zacc",), writes=("Z_sb",))
                    P.add("scalar", lambda e, it=it: e.mul(Z_sb[:, (4 + it) * D + 0:(4 + it) * D + 512], zacc[:], -1.0),
                          reads=("zacc",), writes=("Z_sb",))

        WoT_sb = sbD("WoT_sb", [128, KT * D], BF16)
        P.add("gq", lambda e: e.dma_start(out=WoT_sb[:].rearrange("p (a n) -> p a n", a=KT),
                                          in_=WT_d["o"].rearrange("(a p) n -> p a n", p=128)),
              writes=("WoT_sb",))
        msgD = sbD("msgD", [1, GS], F32)
        ssgD = sbD("ssgD", [1, GS], F32)
        xnfD = sbD("xnfD", [128, KT * GS], BF16)
        xtgD = sbD("xtgD", [128, KT * GS], BF16)
        msbcD = sbD("msbcD", [128, GS], BF16)
        sbcD = sbD("sbcD", [128, GS], BF16)
        o0b = [sbD(f"o0b{i}", [128, D], BF16) for i in range(2)]
        qtb = [sbD(f"qtb{i}", [128, KT * 128], BF16) for i in range(2)]
        ptb = [sbD(f"ptb{i}", [64, H * 128], BF16) for i in range(2)]
        Ofp = [sbD(f"Ofp{i}", [128, D], F32) for i in range(2)]
        ObfR = [sbD(f"ObfR{i}", [128, D], BF16) for i in range(2)]
        Ofeat = sbD("Ofeat", [128, KT * 128], BF16)
        osq = [sbD(f"osq{i}", [128, 128], F32) for i in range(2)]
        srow16 = sbD("srow16", [H, 128], F32)
        obc = [sbD(f"obc{i}", [128, 128], BF16) for i in range(2)]
        xtok = [sbD(f"xtok{i}", [128, D], F32) for i in range(2)]
        outsb = [sbD(f"outsb{i}", [128, D], F32) for i in range(2)]

        def emit_xnf_D(grp):
            c0 = 3 + grp * GS
            for t in range(KT):
                P.add("gq", lambda e, t=t, c0=c0: e.dma_start(
                    out=xtgD[:, t * GS:(t + 1) * GS], in_=x_t_d[t * 128:(t + 1) * 128, c0:c0 + GS]),
                    writes=(f"xtgD:{t}",))
            P.add("sp", lambda e, c0=c0: e.dma_start(out=msgD[:], in_=msrow_d[0:1, c0:c0 + GS]),
                  reads=("msrow_d",), writes=("msgD",))
            P.add("sp", lambda e, c0=c0: e.dma_start(out=ssgD[:], in_=msrow_d[1:2, c0:c0 + GS]),
                  reads=("msrow_d",), writes=("ssgD",))
            P.add("tensor", lambda e: e.matmul(psb[4][:, 0:GS], ones1[:], msgD[:], start=True, stop=True),
                  reads=("ones1", "msgD"), writes=("PSB4",))
            P.add("tensor", lambda e: e.matmul(psb[5][:, 0:GS], ones1[:], ssgD[:], start=True, stop=True),
                  reads=("ones1", "ssgD"), writes=("PSB5",))
            P.add("scalar", lambda e: e.copy(msbcD[:], psb[4][:, 0:GS]), reads=("PSB4",), writes=("msbcD",))
            P.add("scalar", lambda e: e.copy(sbcD[:], psb[5][:, 0:GS]), reads=("PSB5",), writes=("sbcD",))
            for t in range(KT):
                P.add("vector", lambda e, t=t: e.tensor_tensor(
                    xnfD[:, t * GS:(t + 1) * GS], xtgD[:, t * GS:(t + 1) * GS], msbcD[:], op=OP.add),
                    reads=(f"xtgD:{t}", "msbcD"), writes=(f"xnfD:{t}",))
                P.add("vector", lambda e, t=t: e.tensor_tensor(
                    xnfD[:, t * GS:(t + 1) * GS], xnfD[:, t * GS:(t + 1) * GS], sbcD[:], op=OP.mult),
                    reads=(f"xnfD:{t}", "sbcD"), writes=(f"xnfD:{t}",))

        CPG = GS // C
        for g in range(NG):
            emit_xnf_D(g)
            for ci in range(CPG):
                c = g * CPG + ci
                cp = c % 2
                P.add("sp", lambda e, c=c, cp=cp: e.dma_start(out=o0b[cp][:], in_=o0_d[c * 128:(c + 1) * 128, :]),
                      reads=(f"o0d:{c}",), writes=(f"o0b{cp}",))
                P.add("sp", lambda e, c=c, cp=cp: e.dma_start(
                    out=qtb[cp][:], in_=qt_d[:, c * KT * 128:(c + 1) * KT * 128]),
                    reads=(f"qtd:{c}",), writes=(f"qtb{cp}",))
                P.add("sp", lambda e, c=c, cp=cp: e.dma_start(out=xtok[cp][:], in_=x_tok_d[c * 128:(c + 1) * 128, :]),
                      writes=(f"xtok{cp}",))
                for dup in range(2):
                    P.add("sp", lambda e, c=c, cp=cp, dup=dup: e.dma_start(
                        out=ptb[cp][:].rearrange("p (h u d) -> p h u d", h=H, u=2)[:, :, dup],
                        in_=pt_d[:, c * H * 64:(c + 1) * H * 64].rearrange("p (h d) -> p h d", h=H)),
                        reads=(f"ptd:{c}",), writes=(f"ptb{cp}",))
                for h in range(H):
                    t, hw = h // 2, h % 2
                    P.add("tensor", lambda e, cp=cp, t=t, hw=hw, h=h: e.matmul(
                        psb[2 + hw][:, (h // 2) * 64:(h // 2) * 64 + 64],
                        qtb[cp][64 * hw:64 * hw + 64, t * 128:(t + 1) * 128],
                        N_b[64 * hw:64 * hw + 64, h * 64:(h + 1) * 64], start=True, stop=True,
                        tile_position=(64 * hw, 0)),
                        reads=(f"qtb{cp}", f"N_b:{h}"), writes=(f"PSB{2 + hw}",))
                for h in range(H):
                    hw = h % 2
                    P.add("tensor", lambda e, h=h, cp=cp, hw=hw: e.matmul(
                        psb[4 + hw][:, (h // 2) * 64:(h // 2) * 64 + 64],
                        ptb[cp][:, h * 128:(h + 1) * 128],
                        N_b[0:64, h * 64:(h + 1) * 64], start=True, stop=True),
                        reads=(f"ptb{cp}", f"N_b:{h}"), writes=(f"PSB{4 + hw}",))
                allN = tuple(f"N_b:{hh}" for hh in range(H))
                allNf = tuple(f"N_f:{hh}" for hh in range(H))
                for hw in range(2):
                    P.add("vector", lambda e, cp=cp, hw=hw: e.tensor_add(
                        Ofp[cp][:].rearrange("p (h x d) -> p h x d", h=8, x=2)[:, :, hw],
                        o0b[cp][:].rearrange("p (h x d) -> p h x d", h=8, x=2)[:, :, hw],
                        psb[2 + hw][:, 0:512].rearrange("p (h d) -> p h d", h=8)),
                        reads=(f"o0b{cp}", f"PSB{2 + hw}"), writes=(f"Ofp{cp}",))
                    P.add("vector", lambda e, hw=hw: e.tensor_copy(
                        N_f[:].rearrange("p (h x d) -> p h x d", h=8, x=2)[:, :, hw],
                        psb[4 + hw][:, 0:512].rearrange("p (h d) -> p h d", h=8)),
                        reads=(f"PSB{4 + hw}",), writes=allNf)
                P.add("scalar", lambda e: e.copy(N_b[:], N_f[:]), reads=allNf, writes=allN)
                for half in range(2):
                    P.add("scalar", lambda e, cp=cp, half=half: e.copy(
                        ObfR[cp][:, half * 512:half * 512 + 512], Ofp[cp][:, half * 512:half * 512 + 512]),
                        reads=(f"Ofp{cp}",), writes=(f"ObfR{cp}",))
                for t in range(KT):
                    pt = t % 2
                    pst = psT0 if pt == 0 else psT1
                    pstk = "PSB6" if pt == 0 else "PSB7"
                    P.add("tensor", lambda e, cp=cp, t=t, pst=pst: e.transpose(
                        pst[:, 0:128], ObfR[cp][:, t * 128:(t + 1) * 128], ident_b[:]),
                        reads=(f"ObfR{cp}", "ident_b"), writes=(pstk,))
                    P.add("scalar", lambda e, t=t, pst=pst: e.copy(Ofeat[:, t * 128:(t + 1) * 128], pst[:, 0:128]),
                          reads=(pstk,), writes=(f"Ofeat:{t}",))
                    P.add("scalar", lambda e, t=t: e.activation(osq[t % 2][:], Ofeat[:, t * 128:(t + 1) * 128], AF.Square),
                          reads=(f"Ofeat:{t}",), writes=(f"osq{t % 2}",))
                    P.add("tensor", lambda e, t=t: e.matmul(
                        psb[4][0:H, 0:128], ind[:, t * H:(t + 1) * H], osq[t % 2][:],
                        start=(t == 0), stop=(t == KT - 1)),
                        reads=("ind", f"osq{t % 2}"), writes=("PSB4",))
                P.add("scalar", lambda e: e.activation(srow16[:], psb[4][0:H, 0:128], AF.Sqrt, scale=1.0 / dh, bias=epsrms[0:H, :]),
                      reads=("PSB4", "epsrms"), writes=("srow16",))
                P.add("vector", lambda e: e.reciprocal(srow16[:], srow16[:]), reads=("srow16",), writes=("srow16",))
                for t in range(KT):
                    P.add("tensor", lambda e, t=t: e.matmul(
                        psb[5][:, 0:128], indT[:, t * 128:(t + 1) * 128], srow16[:], start=True, stop=True),
                        reads=("indT", "srow16"), writes=("PSB5",))
                    P.add("scalar", lambda e, t=t: e.activation(obc[t % 2][:], psb[5][:, 0:128], AF.Identity, scale=onw[:]),
                          reads=("PSB5", "onw"), writes=(f"obc{t % 2}",))
                    P.add("vector", lambda e, t=t: e.tensor_tensor(
                        Ofeat[:, t * 128:(t + 1) * 128], Ofeat[:, t * 128:(t + 1) * 128], obc[t % 2][:], op=OP.mult),
                        reads=(f"Ofeat:{t}", f"obc{t % 2}"), writes=(f"Ofeat:{t}",))
                for half in range(2):
                    bo = half
                    for t in range(KT):
                        P.add("tensor", lambda e, t=t, half=half, bo=bo: e.matmul(
                            psb[bo][:, 0:512], Ofeat[:, t * 128:(t + 1) * 128],
                            WoT_sb[:, t * D + half * 512: t * D + half * 512 + 512],
                            start=(t == 0), stop=False),
                            reads=(f"Ofeat:{t}", "WoT_sb"), writes=(f"PSB{bo}",))
                    for t in range(KT):
                        P.add("tensor", lambda e, t=t, ci=ci, half=half, bo=bo: e.matmul(
                            psb[bo][:, 0:512], xnfD[:, t * GS + ci * C: t * GS + ci * C + 128],
                            Z_sb[:, t * D + half * 512: t * D + half * 512 + 512],
                            start=False, stop=False),
                            reads=(f"xnfD:{t}", "Z_sb"), writes=(f"PSB{bo}",))
                    P.add("tensor", lambda e, half=half, bo=bo: e.matmul(
                        psb[bo][:, 0:512], ones1[:], biasrow[0:1, half * 512:half * 512 + 512],
                        start=False, stop=True),
                        reads=("ones1", "biasrow"), writes=(f"PSB{bo}",))
                    P.add("vector", lambda e, cp=cp, half=half, bo=bo: e.tensor_add(
                        outsb[cp][:, half * 512:half * 512 + 512], psb[bo][:, 0:512],
                        xtok[cp][:, half * 512:half * 512 + 512]),
                        reads=(f"PSB{bo}", f"xtok{cp}"), writes=(f"outsb{cp}",))
                P.add("sp", lambda e, c=c, cp=cp: e.dma_start(out=out_d[c * 128:(c + 1) * 128, :], in_=outsb[cp][:]),
                      reads=(f"outsb{cp}",), writes=("out_final",))

        P.add("gpsimd", lambda e: e.engine_nop(), reads=("out_final",))

    P.mark("end")
    import os
    if os.environ.get("KPRINT_MARKS"):
        print("MARKS:", P.marks)
    P.finalize(truncate=cfg.trunc, skip=getattr(cfg, 'skip_items', ()))
    P.emit(nc, sems)
    return nc


def host_in_maps(cfg: CFG, inputs):
    import ml_dtypes
    n = cfg.n_cores
    R, D, H, KT = cfg.R, cfg.D, cfg.H, cfg.KT
    x = np.asarray(inputs["L0_attn_x"], np.float32).reshape(cfg.L, D)
    WT = {p: np.ascontiguousarray(np.asarray(inputs[f"L0_attn_W{p}"], np.float32).T) for p in "qkvo"}
    WbT = np.ascontiguousarray(np.asarray(inputs["L0_attn_Wb"], np.float32).T)
    onw = np.ascontiguousarray(np.tile(np.asarray(inputs["L0_attn_o_norm_w"], np.float32), 2)[:, None])
    biasrow = np.ascontiguousarray(np.concatenate([
        np.asarray(inputs["L0_ffn_bias_real"], np.float32),
        np.asarray(inputs["L0_ffn_bias_imag"], np.float32)])[None, :])

    def expand(G1, G2):
        G1 = np.asarray(G1, np.float32)[0]
        G2 = np.asarray(G2, np.float32)
        d_of_p = np.arange(128) // 16
        g_of_p = np.arange(128) % 16
        E1 = np.empty((4, 4, 128, 512), np.float32)
        E2 = np.empty((4, 128, 512), np.float32)
        for c4 in range(4):
            colB = G2[c4, :, 0, :]
            E2[c4] = colB.T[g_of_p][:, None, :].repeat(32, axis=1).reshape(128, 512)
            for it in range(4):
                vals = G1[:, c4, it * 8 + d_of_p]
                E1[c4, it] = vals.T[:, :, None].repeat(16, axis=2).reshape(128, 512)
        return E1, E2
    E1R, E2R = expand(inputs["L0_ffn_G1_R"], inputs["L0_ffn_G2_R"])
    E1I, E2I = expand(inputs["L0_ffn_G1_I"], inputs["L0_ffn_G2_I"])

    i_ = np.arange(128)
    mask_sl = -((i_[None, :] < i_[:, None]).astype(np.float32))
    mask_sun = -((i_[None, :] > i_[:, None]).astype(np.float32))
    mask_sui = (i_[None, :] >= i_[:, None]).astype(np.float32)
    ident_f = np.eye(128, dtype=np.float32)
    ident_b = np.eye(128).astype(ml_dtypes.bfloat16)
    ident64_2 = np.ascontiguousarray(np.tile(np.eye(64), (2, 1)).astype(ml_dtypes.bfloat16))
    ones1 = np.ones((1, 128), np.float32)
    ppinit = np.ascontiguousarray(np.tile(np.eye(64), (1, H)).astype(ml_dtypes.bfloat16))
    ind = np.zeros((128, KT * H), np.float32)
    indT = np.zeros((H, KT * 128), np.float32)
    for t in range(KT):
        for p in range(128):
            h = 2 * t + (1 if p >= 64 else 0)
            ind[p, t * H + h] = 1.0
            indT[h, t * 128 + p] = 1.0

    in_maps = []
    for i in range(n):
        r0 = i * R
        xt_sl = np.zeros((D, R + 3), np.float32)
        lo = max(r0 - 3, 0)
        xt_sl[:, 3 - (r0 - lo):] = x[lo:r0 + R].T
        halo = np.zeros((3, D), np.float32)
        if r0 >= 3:
            halo[:] = x[r0 - 3:r0]
        oh = np.zeros((1, n), np.float32)
        oh[0, i] = 1.0
        m = {
            "x_tok": np.ascontiguousarray(x[r0:r0 + R]),
            "x_t": np.ascontiguousarray(xt_sl),
            "x_halo": halo,
            "WbT": WbT,
            "conv_q": np.asarray(inputs["L0_attn_conv_q"], np.float32),
            "conv_k": np.asarray(inputs["L0_attn_conv_k"], np.float32),
            "conv_v": np.asarray(inputs["L0_attn_conv_v"], np.float32),
            "onw128": onw, "biasrow": biasrow,
            "E1R": E1R, "E2R": E2R, "E1I": E1I, "E2I": E2I,
            "mask_sl": mask_sl, "mask_sun": mask_sun, "mask_sui": mask_sui,
            "ident_f": ident_f, "ident_b": ident_b, "ident64_2": ident64_2,
            "ones1": ones1, "ppinit": ppinit, "onehot": oh,
            "ind": ind, "indT": indT,
        }
        for p in "qkvo":
            m[f"W{p}T"] = WT[p]
        in_maps.append(m)
    return in_maps


MID_TENSORS = ("o0_dram", "qt_dram", "pt_dram", "msrow_dram")


def run_pipeline(cfg, inputs, launcher):
    base_maps = host_in_maps(cfg, inputs)
    n = cfg.n_cores
    nc1 = build_program(cfg, "p1")
    res1 = launcher(nc1, base_maps)
    ccg = np.ascontiguousarray(np.concatenate(
        [np.asarray(r["cc_stage_out"], np.float32) for r in res1], axis=0))
    maps2 = []
    for i, m in enumerate(base_maps):
        m2 = dict(m)
        for t in MID_TENSORS:
            m2[t] = np.asarray(res1[i][t])
        m2["cc_gathered"] = ccg
        maps2.append(m2)
    nc2 = build_program(cfg, "p2")
    res2 = launcher(nc2, maps2)
    out = np.concatenate([np.asarray(r["out"], np.float32) for r in res2], axis=0)
    return out


def kernel(**inputs):
    cfg = CFG()

    def launcher(nc, in_maps):
        res = run_bass_kernel_spmd(nc, in_maps, core_ids=list(range(cfg.n_cores)))
        return res.results
    out = run_pipeline(cfg, inputs, launcher)
    return out.reshape(1, cfg.L, cfg.D)


# version 11
# speedup vs baseline: 1.6345x; 1.1054x over previous
import numpy as np
from contextlib import ExitStack

import concourse.bass as bass
import concourse.mybir as mybir
from concourse.alu_op_type import AluOpType as OP
from concourse.bass_utils import run_bass_kernel_spmd

F32 = mybir.dt.float32
BF16 = mybir.dt.bfloat16
AF = mybir.ActivationFunctionType
AX = mybir.AxisListType


class CFG:
    n_cores = 8
    L = 16384
    D = 1024
    H = 16
    dh = 64
    C = 128
    GS = 512
    EPS = 1e-6
    LN_EPS = 1e-5
    debug = False
    trunc = None

    @property
    def R(self): return self.L // self.n_cores
    @property
    def KT(self): return self.D // 128
    @property
    def NG(self): return self.R // self.GS
    @property
    def NC(self): return self.R // self.C


DMA_STREAMS = ("sp", "aq", "gq", "cc")
STREAM_SECTION = {
    "tensor": "tensor", "vector": "vector", "scalar": "scalar",
    "gpsimd": "gpsimd", "sync": "sync",
    "sp": "sync", "aq": "scalar", "gq": "gpsimd", "cc": "gpsimd",
}


class Pl:

    def __init__(self):
        self.items = []
        self.barrier_snap = {}
        self.ordn = {}

    def add(self, stream, fn, reads=(), writes=(), inc=None):
        if inc is None:
            inc = 16 if stream in DMA_STREAMS else 1
        if stream in DMA_STREAMS and stream != "cc":
            o = self.ordn.get(stream, 0)
            self.ordn[stream] = o + 1
            semid = f"{stream}{o % self.NROT}"
        else:
            semid = stream
        self.items.append(dict(s=stream, fn=fn, r=tuple(reads), w=tuple(writes),
                               inc=inc, deps=set(), bar=dict(self.barrier_snap),
                               semid=semid))

    def barrier(self):
        snap = {}
        for i, it in enumerate(self.items):
            snap[it["semid"]] = i
        self.barrier_snap = snap

    NROT = 4

    marks = []

    def mark(self, name):
        self.marks.append((name, len(self.items)))

    def finalize(self, truncate=None, skip=()):
        if truncate is not None:
            self.items = self.items[:truncate]
        if skip:
            self.items = [it for i, it in enumerate(self.items) if i not in skip]
        lastuse = {}
        lastw, readers = {}, {}
        last_psum_rd = {}
        for i, it in enumerate(self.items):
            dep = set(it["bar"].values())
            for r in it["r"]:
                if r in lastw:
                    dep.add(lastw[r])
                if r.startswith("PSB"):
                    pr = last_psum_rd.get(r)
                    if pr is not None and pr[1] != it["s"]:
                        dep.add(pr[0])
                    last_psum_rd[r] = (i, it["s"])
            for w in it["w"]:
                if w in lastw:
                    dep.add(lastw[w])
                for rd in readers.get(w, ()):
                    dep.add(rd)
            for r in it["r"]:
                readers.setdefault(r, []).append(i)
            for w in it["w"]:
                lastw[w] = i
                readers[w] = []
                if w.startswith("PSB"):
                    last_psum_rd.pop(w, None)
            if it["s"] in DMA_STREAMS:
                if it["semid"] in lastuse:
                    dep.add(lastuse[it["semid"]])
                lastuse[it["semid"]] = i
            it["deps"] = {d for d in dep
                          if not (self.items[d]["s"] == it["s"] and it["s"] == "tensor")}
        need = [False] * len(self.items)
        for i, it in enumerate(self.items):
            if it["s"] in DMA_STREAMS:
                need[i] = True
            for d in it["deps"]:
                need[d] = True
        cnt = {}
        self.after = [0] * len(self.items)
        self.neednc = need
        for i, it in enumerate(self.items):
            c = cnt.get(it["semid"], 0)
            if need[i]:
                c += it["inc"]
            cnt[it["semid"]] = c
            self.after[i] = c

    def emit(self, nc, sems):
        by_section = {}
        for i, it in enumerate(self.items):
            by_section.setdefault(STREAM_SECTION[it["s"]], []).append(i)
        with nc.Block() as block:
            for section in ("sync", "scalar", "vector", "tensor", "gpsimd"):
                idxs = by_section.get(section, [])
                dec = getattr(block, section)

                def make(idxs=idxs):
                    def body(eng):
                        waited = {}
                        for i in idxs:
                            it = self.items[i]
                            wmax = {}
                            for d in it["deps"]:
                                ds = self.items[d]["semid"]
                                v = self.after[d]
                                if v > wmax.get(ds, 0):
                                    wmax[ds] = v
                            for ds, v in sorted(wmax.items()):
                                if waited.get(ds, 0) < v:
                                    eng.wait_ge(sems[ds], v)
                                    waited[ds] = v
                            inst = it["fn"](eng)
                            if self.neednc[i]:
                                inst.then_inc(sems[it["semid"]], it["inc"])
                    return body
                dec(make())


class _Rec:

    def __init__(self):
        self.calls = []

    def add(self, *a, **k):
        self.calls.append((a, k))


def _merge_rr(P, recs):
    pos = [0] * len(recs)
    while True:
        any_left = False
        for i, r in enumerate(recs):
            if pos[i] < len(r.calls):
                a, k = r.calls[pos[i]]
                P.add(*a, **k)
                pos[i] += 1
                any_left = True
        if not any_left:
            break


def dup2(ap64):
    return ap64.rearrange("p (o d) -> p o d", o=1).broadcast_to([ap64.shape[0], 2, 64])


def build_program(cfg: CFG, part: str = "p1"):
    R, D, H, dh, C, GS = cfg.R, cfg.D, cfg.H, cfg.dh, cfg.C, cfg.GS
    KT, NG, NC = cfg.KT, cfg.NG, cfg.NC
    NLEV = 6
    assert C == 2 ** (NLEV + 1)
    n_cores = cfg.n_cores
    NT = R // 128

    nc = bass.Bass("TRN2", target_bir_lowering=False)
    P = Pl()

    dt = lambda name, shape, dtype, kind: nc.dram_tensor(name, shape, dtype, kind=kind).ap()
    x_tok_d = dt("x_tok", [R, D], F32, "ExternalInput")
    x_t_d = dt("x_t", [D, R + 3], F32, "ExternalInput")
    x_halo_d = dt("x_halo", [3, D], F32, "ExternalInput")
    WT_d = {p: dt(f"W{p}T", [D, D], F32, "ExternalInput") for p in "qkvo"}
    WbT_d = dt("WbT", [D, H], F32, "ExternalInput")
    conv_d = {p: dt(f"conv_{p}", [D, 4], F32, "ExternalInput") for p in "qkv"}
    onw_d = dt("onw128", [128, 1], F32, "ExternalInput")
    bias_d = dt("biasrow", [1, D], F32, "ExternalInput")
    E1_d = {ri: dt(f"E1{ri}", [4, KT // 2, 128, 512], F32, "ExternalInput") for ri in "RI"}
    E2_d = {ri: dt(f"E2{ri}", [4, 128, 512], F32, "ExternalInput") for ri in "RI"}
    mask_sl_d = dt("mask_sl", [128, 128], F32, "ExternalInput")
    mask_sun_d = dt("mask_sun", [128, 128], F32, "ExternalInput")
    mask_sui_d = dt("mask_sui", [128, 128], F32, "ExternalInput")
    ident_f_d = dt("ident_f", [128, 128], F32, "ExternalInput")
    ident_b_d = dt("ident_b", [128, 128], BF16, "ExternalInput")
    ident64_d = dt("ident64_2", [128, 64], BF16, "ExternalInput")
    ones1_d = dt("ones1", [1, 128], F32, "ExternalInput")
    ppinit_d = dt("ppinit", [64, H * 64], BF16, "ExternalInput")
    onehot_d = dt("onehot", [1, n_cores], F32, "ExternalInput")
    ind_d = dt("ind", [128, KT * H], F32, "ExternalInput")
    indT_d = dt("indT", [H, KT * 128], F32, "ExternalInput")

    out_d = dt("out", [R, D], F32, "ExternalOutput") if part != "p1" else None

    mid_kind = {"full": "Internal", "p1": "ExternalOutput", "p2": "ExternalInput"}[part]
    if part != "p2":
        krm_d = dt("krm_dram", [R, 2 * D], BF16, "Internal")
        vrm_d = dt("vrm_dram", [R, D], BF16, "Internal")
    o0_d = dt("o0_dram", [R, D], BF16, mid_kind)
    qt_d = dt("qt_dram", [128, NC * KT * 128], BF16, mid_kind)
    pt_d = dt("pt_dram", [64, NC * H * 64], BF16, mid_kind)
    msrow_d = dt("msrow_dram", [2, R + 3], F32, mid_kind)
    if part == "full":
        cc_in_d = dt("cc_in", [64, H * 128], F32, "Internal")
        cc_out_d = dt("cc_out", [n_cores * 64, H * 128], F32, "Internal")
    elif part == "p1":
        cc_out1_d = dt("cc_stage_out", [64, H * 128], F32, "ExternalOutput")
    else:
        cc_out_d = dt("cc_gathered", [n_cores * 64, H * 128], F32, "ExternalInput")
    dbg_d = dt("dbg", [128, 32 * 128], F32, "Internal") if cfg.debug else None
    def dbgtap(slot, ap, readkeys):
        if cfg.debug:
            P.add("gq", lambda e, slot=slot, ap=ap: e.dma_start(
                out=dbg_d[:ap.shape[0], slot * 128: slot * 128 + ap.shape[1]], in_=ap),
                reads=tuple(readkeys), writes=("dbg",))

    st = ExitStack()
    sb = lambda name, shape, dtype: st.enter_context(nc.sbuf_tensor(name, shape, dtype))
    ps_ = lambda name, shape, dtype: st.enter_context(nc.psum_tensor(name, shape, dtype))

    sem_names = ["tensor", "vector", "scalar", "gpsimd", "sync", "cc"]
    for s in ("sp", "aq", "gq"):
        sem_names += [f"{s}{r}" for r in range(Pl.NROT)]
    sems = {s: st.enter_context(nc.semaphore(f"sem_{s}")) for s in sem_names}

    psb = [ps_(f"psb{i}", [128, 512], F32) for i in range(6)]
    psT0 = ps_("psT0", [128, 512], BF16)
    psT1 = ps_("psT1", [128, 512], BF16)

    mask_sl = sb("mask_sl_s", [128, 128], F32)
    mask_sun = sb("mask_sun_s", [128, 128], F32)
    mask_sui = sb("mask_sui_s", [128, 128], F32)
    ident_f = sb("ident_f_s", [128, 128], F32)
    ident_b = sb("ident_b_s", [128, 128], BF16)
    ident64 = sb("ident64_s", [128, 64], BF16)
    ones1 = sb("ones1_s", [1, 128], F32)
    onw = sb("onw_s", [128, 1], F32)
    biasrow = sb("biasrow_s", [1, D], F32)
    onehot = sb("onehot_s", [1, n_cores], F32)
    ind = sb("ind_s", [128, KT * H], F32)
    indT = sb("indT_s", [H, KT * 128], F32)
    S_f = sb("S_f", [128, H * 64], F32)
    S_b = sb("S_b", [128, H * 64], BF16)
    PP_b = sb("PP_b", [64, H * 64], BF16)
    N_f = sb("N_f", [128, H * 64], F32)
    N_b = sb("N_b", [128, H * 64], BF16)
    epsln = sb("epsln", [128, 1], F32)
    epsrms = sb("epsrms", [128, 1], F32)

    def ld(dst, src, key, stream="sp"):
        P.add(stream, lambda e, d=dst, s=src: e.dma_start(out=d, in_=s), writes=(key,))

    ld(mask_sl[:], mask_sl_d[:, :], "mask_sl")
    ld(mask_sun[:], mask_sun_d[:, :], "mask_sun")
    ld(mask_sui[:], mask_sui_d[:, :], "mask_sui")
    ld(ident_f[:], ident_f_d[:, :], "ident_f")
    ld(ident_b[:], ident_b_d[:, :], "ident_b")
    ld(ident64[:], ident64_d[:, :], "ident64")
    ld(ones1[:], ones1_d[:, :], "ones1")
    ld(onw[:], onw_d[:, :], "onw")
    ld(biasrow[:], bias_d[:, :], "biasrow")
    ld(onehot[:], onehot_d[:, :], "onehot")
    ld(ind[:], ind_d[:, :], "ind")
    ld(indT[:], indT_d[:, :], "indT")
    P.add("sp", lambda e: e.dma_start(out=PP_b[:], in_=ppinit_d[:, :]), writes=tuple(f"PP_b:{h}" for h in range(H)))
    P.add("gpsimd", lambda e: e.memset(epsln[:], cfg.LN_EPS), writes=("epsln",))
    P.add("gpsimd", lambda e: e.memset(epsrms[:], cfg.EPS), writes=("epsrms",))
    P.add("gpsimd", lambda e: e.memset(S_f[:], 0.0), writes=tuple(f"S_f:{h}" for h in range(H)))
    P.add("gpsimd", lambda e: e.memset(S_b[:], 0.0), writes=tuple(f"S_b:{h}" for h in range(H)))

    run_p1 = part in ("full", "p1")
    run_p2 = part in ("full", "p2")
    stA0 = ExitStack()
    if run_p1:
        sb0 = lambda name, shape, dtype: stA0.enter_context(nc.sbuf_tensor(name, shape, dtype))
        stat_pack = sb0("stat_pack", [128, 2 * NT], F32)
        stat_sum = sb0("stat_sum", [128, NT], F32)
        stat_ssq = sb0("stat_ssq", [128, NT], F32)
        stat_tmp = sb0("stat_tmp", [128, NT], F32)
        evac32 = sb0("evac32", [2 * NT, 128], F32)
        xa = [sb0(f"xa{i}", [128, D], F32) for i in range(4)]
        xsq = sb0("xsq", [128, D], F32)
        for t in range(NT):
            b = t % 4
            P.add("sp", lambda e, b=b, t=t: e.dma_start(out=xa[b][:], in_=x_tok_d[t * 128:(t + 1) * 128, :]),
                  writes=(f"xa{b}",))
            P.add("vector", lambda e, b=b, t=t: e.reduce_sum(stat_sum[:, t:t + 1], xa[b][:], axis=AX.X),
                  reads=(f"xa{b}",), writes=("stat_sum",))
            P.add("scalar", lambda e, b=b, t=t: e.activation(xsq[:], xa[b][:], AF.Square, accum_out=stat_ssq[:, t:t + 1]),
                  reads=(f"xa{b}",), writes=("stat_ssq", "xsq"))
        xh = sb0("xh", [3, D], F32)
        xhsq = sb0("xhsq", [3, D], F32)
        halo_sum = sb0("halo_sum", [3, 1], F32)
        halo_var = sb0("halo_var", [3, 1], F32)
        halo_m2 = sb0("halo_m2", [3, 1], F32)
        halo_s = sb0("halo_s", [3, 2], F32)
        P.add("sp", lambda e: e.dma_start(out=xh[:], in_=x_halo_d[:, :]), writes=("xh",))
        P.add("vector", lambda e: e.reduce_sum(halo_sum[:], xh[:], axis=AX.X),
              reads=("xh",), writes=("halo_sum",))
        P.add("scalar", lambda e: e.activation(xhsq[:], xh[:], AF.Square, accum_out=halo_var[:]),
              reads=("xh",), writes=("halo_var", "xhsq"))
        P.add("vector", lambda e: e.tensor_scalar(stat_sum[:], stat_sum[:], 1.0 / D, 0.0, op0=OP.mult, op1=OP.add),
              reads=("stat_sum",), writes=("stat_sum",))
        P.add("vector", lambda e: e.scalar_tensor_tensor(stat_tmp[:], stat_sum[:], 1.0, stat_sum[:], op0=OP.mult, op1=OP.mult),
              reads=("stat_sum",), writes=("stat_tmp",))
        P.add("vector", lambda e: e.scalar_tensor_tensor(stat_ssq[:], stat_ssq[:], 1.0 / D, stat_tmp[:], op0=OP.mult, op1=OP.subtract),
              reads=("stat_ssq", "stat_tmp"), writes=("stat_ssq",))
        P.add("scalar", lambda e: e.activation(stat_ssq[:], stat_ssq[:], AF.Sqrt, bias=epsln[:]),
              reads=("stat_ssq", "epsln"), writes=("stat_ssq",))
        P.add("vector", lambda e: e.reciprocal(stat_pack[:, NT:2 * NT], stat_ssq[:]),
              reads=("stat_ssq",), writes=("stat_pack",))
        P.add("vector", lambda e: e.scalar_tensor_tensor(stat_pack[:, 0:NT], stat_sum[:], -1.0, stat_pack[:, NT:2 * NT], op0=OP.mult, op1=OP.mult),
              reads=("stat_sum", "stat_pack"), writes=("stat_pack",))
        P.add("vector", lambda e: e.tensor_scalar(halo_sum[:], halo_sum[:], 1.0 / D, 0.0, op0=OP.mult, op1=OP.add),
              reads=("halo_sum",), writes=("halo_sum",))
        P.add("vector", lambda e: e.scalar_tensor_tensor(halo_m2[:], halo_sum[:], 1.0, halo_sum[:], op0=OP.mult, op1=OP.mult),
              reads=("halo_sum",), writes=("halo_m2",))
        P.add("vector", lambda e: e.scalar_tensor_tensor(halo_var[:], halo_var[:], 1.0 / D, halo_m2[:], op0=OP.mult, op1=OP.subtract),
              reads=("halo_var", "halo_m2"), writes=("halo_var",))
        P.add("scalar", lambda e: e.activation(halo_var[:], halo_var[:], AF.Sqrt, bias=epsln[0:3, :]),
              reads=("halo_var", "epsln"), writes=("halo_var",))
        P.add("vector", lambda e: e.reciprocal(halo_s[:, 1:2], halo_var[:]),
              reads=("halo_var",), writes=("halo_s",))
        P.add("vector", lambda e: e.scalar_tensor_tensor(halo_s[:, 0:1], halo_sum[:], -1.0, halo_s[:, 1:2], op0=OP.mult, op1=OP.mult),
              reads=("halo_sum", "halo_s"), writes=("halo_s",))
        P.add("tensor", lambda e: e.transpose(psb[0][0:2 * NT, 0:128], stat_pack[:], ident_f[:]),
              reads=("stat_pack", "ident_f"), writes=("PSB0",))
        P.add("vector", lambda e: e.tensor_copy(evac32[:], psb[0][0:2 * NT, 0:128]),
              reads=("PSB0",), writes=("evac32",))
        P.add("aq", lambda e: e.dma_start(out=msrow_d[0:1, 3:3 + NT * 128].rearrange("o (t p) -> (o t) p", t=NT),
                                          in_=evac32[0:NT, :]),
              reads=("evac32",), writes=("msrow_d",))
        P.add("aq", lambda e: e.dma_start(out=msrow_d[1:2, 3:3 + NT * 128].rearrange("o (t p) -> (o t) p", t=NT),
                                          in_=evac32[NT:2 * NT, :]),
              reads=("evac32",), writes=("msrow_d",))
        P.add("aq", lambda e: e.dma_start(out=msrow_d[0:1, 0:3].rearrange("o t -> t o"), in_=halo_s[:, 0:1]),
              reads=("halo_s",), writes=("msrow_d",))
        P.add("aq", lambda e: e.dma_start(out=msrow_d[1:2, 0:3].rearrange("o t -> t o"), in_=halo_s[:, 1:2]),
              reads=("halo_s",), writes=("msrow_d",))
        P.barrier()
        stA0.close()

        P.mark("A0_done")
        stABC = ExitStack()
        sbB = lambda name, shape, dtype: stABC.enter_context(nc.sbuf_tensor(name, shape, dtype))
        qtf = sbB("qtf", [128, KT * R], BF16)
        ktf = sbB("ktf", [128, KT * R], BF16)
        vtf = sbB("vtf", [128, KT * R], BF16)
        RES = {"q": qtf, "k": ktf, "v": vtf}

        stA = ExitStack()
        sbA = lambda name, shape, dtype: stA.enter_context(nc.sbuf_tensor(name, shape, dtype))
        wbuf = [sbA(f"wbuf{i}", [128, D], BF16) for i in range(4)]
        pbuf = [sbA(f"pbuf{i}", [128, GS + 3], BF16) for i in range(KT)]
        tails = sbA("tails", [128, 3 * 3 * KT], BF16)
        xnf = sbA("xnf", [128, KT * GS], BF16)
        xnf_h = sbA("xnf_h", [128, KT * 3], BF16)
        xtg = sbA("xtg", [128, KT * GS], BF16)
        xtg_h = sbA("xtg_h", [128, KT * 3], BF16)
        msbc = sbA("msbc", [128, GS], BF16)
        sbc = sbA("sbc", [128, GS], BF16)
        msbc_h = sbA("msbc_h", [128, 3], BF16)
        sbc_h = sbA("sbc_h", [128, 3], BF16)
        conv_sb = sbA("conv_sb", [128, KT * 4 * 3], F32)
        ybuf = [sbA(f"ybuf{i}", [128, GS], BF16) for i in range(2)]
        WbT_sb = sbA("WbT_sb", [128, KT * H], BF16)
        msg = sbA("msg", [1, GS], F32)
        ssg = sbA("ssg", [1, GS], F32)
        msg_h = sbA("msg_h", [1, 3], F32)
        ssg_h = sbA("ssg_h", [1, 3], F32)
        beta_hm = sbA("beta_hm", [H, R], F32)
        scale_q = sbA("scale_q", [H, R], F32)
        scale_k = sbA("scale_k", [H, R], F32)
    
        for pi, p in enumerate("qkv"):
            P.add("gq", lambda e, pi=pi, p=p: e.dma_start(
                out=conv_sb[:, pi * KT * 4:(pi + 1) * KT * 4].rearrange("p (a k) -> p a k", k=4),
                in_=conv_d[p].rearrange("(a p) k -> p a k", p=128)),
                writes=("conv_sb",))
        P.add("gq", lambda e: e.dma_start(out=WbT_sb[:].rearrange("p (a h) -> p a h", a=KT),
                                          in_=WbT_d.rearrange("(a p) h -> p a h", p=128)),
              writes=("WbT_sb",))

        def emit_xnf(grp, dst, dkey, xsrc, xkey, mb, sbuf_b, with_halo, dst_h=None, hx=None, mbh=None, sbh=None,
                     mrow=None, srw=None, mrow_h=None, srw_h=None):
            c0 = 3 + grp * GS
            for t in range(KT):
                P.add("gq", lambda e, t=t, c0=c0, xsrc=xsrc: e.dma_start(
                    out=xsrc[:, t * GS:(t + 1) * GS], in_=x_t_d[t * 128:(t + 1) * 128, c0:c0 + GS]),
                    writes=(f"{xkey}:{t}",))
            P.add("sp", lambda e, c0=c0, mrow=mrow: e.dma_start(out=mrow[:], in_=msrow_d[0:1, c0:c0 + GS]),
                  reads=("msrow_d",), writes=(f"{dkey}_mrow",))
            P.add("sp", lambda e, c0=c0, srw=srw: e.dma_start(out=srw[:], in_=msrow_d[1:2, c0:c0 + GS]),
                  reads=("msrow_d",), writes=(f"{dkey}_srow",))
            P.add("tensor", lambda e, mrow=mrow: e.matmul(psb[3][:, 0:GS], ones1[:], mrow[:], start=True, stop=True),
                  reads=("ones1", f"{dkey}_mrow"), writes=("PSB3",))
            P.add("tensor", lambda e, srw=srw: e.matmul(psb[4][:, 0:GS], ones1[:], srw[:], start=True, stop=True),
                  reads=("ones1", f"{dkey}_srow"), writes=("PSB4",))
            P.add("scalar", lambda e, mb=mb: e.copy(mb[:], psb[3][:, 0:GS]), reads=("PSB3",), writes=(f"{dkey}_mb",))
            P.add("scalar", lambda e, sbuf_b=sbuf_b: e.copy(sbuf_b[:], psb[4][:, 0:GS]), reads=("PSB4",), writes=(f"{dkey}_sb",))
            if with_halo:
                for t in range(KT):
                    P.add("gq", lambda e, t=t, hx=hx: e.dma_start(
                        out=hx[:, t * 3:(t + 1) * 3], in_=x_t_d[t * 128:(t + 1) * 128, 0:3]),
                        writes=(f"xtg_h:{t}",))
                P.add("sp", lambda e, mrow_h=mrow_h: e.dma_start(out=mrow_h[:], in_=msrow_d[0:1, 0:3]),
                      reads=("msrow_d",), writes=("mrow_h",))
                P.add("sp", lambda e, srw_h=srw_h: e.dma_start(out=srw_h[:], in_=msrow_d[1:2, 0:3]),
                      reads=("msrow_d",), writes=("srow_h",))
                P.add("tensor", lambda e, mrow_h=mrow_h: e.matmul(psb[3][:, 0:3], ones1[:], mrow_h[:], start=True, stop=True),
                      reads=("ones1", "mrow_h"), writes=("PSB3",))
                P.add("tensor", lambda e, srw_h=srw_h: e.matmul(psb[4][:, 0:3], ones1[:], srw_h[:], start=True, stop=True),
                      reads=("ones1", "srow_h"), writes=("PSB4",))
                P.add("scalar", lambda e, mbh=mbh: e.copy(mbh[:], psb[3][:, 0:3]), reads=("PSB3",), writes=("msbc_h",))
                P.add("scalar", lambda e, sbh=sbh: e.copy(sbh[:], psb[4][:, 0:3]), reads=("PSB4",), writes=("sbc_h",))
            for t in range(KT):
                P.add("vector", lambda e, t=t, dst=dst, xsrc=xsrc, mb=mb: e.tensor_tensor(
                    dst[:, t * GS:(t + 1) * GS], xsrc[:, t * GS:(t + 1) * GS], mb[:], op=OP.add),
                    reads=(f"{xkey}:{t}", f"{dkey}_mb"), writes=(f"{dkey}:{t}",))
                P.add("vector", lambda e, t=t, dst=dst, sbuf_b=sbuf_b: e.tensor_tensor(
                    dst[:, t * GS:(t + 1) * GS], dst[:, t * GS:(t + 1) * GS], sbuf_b[:], op=OP.mult),
                    reads=(f"{dkey}:{t}", f"{dkey}_sb"), writes=(f"{dkey}:{t}",))
                if with_halo:
                    P.add("vector", lambda e, t=t, dst_h=dst_h, hx=hx, mbh=mbh: e.tensor_tensor(
                        dst_h[:, t * 3:(t + 1) * 3], hx[:, t * 3:(t + 1) * 3], mbh[:], op=OP.add),
                        reads=(f"xtg_h:{t}", "msbc_h"), writes=(f"xnf_h:{t}",))
                    P.add("vector", lambda e, t=t, dst_h=dst_h, sbh=sbh: e.tensor_tensor(
                        dst_h[:, t * 3:(t + 1) * 3], dst_h[:, t * 3:(t + 1) * 3], sbh[:], op=OP.mult),
                        reads=(f"xnf_h:{t}", "sbc_h"), writes=(f"xnf_h:{t}",))

        for g in range(NG):
            emit_xnf(g, xnf, "xnf", xtg, "xtg", msbc, sbc, g == 0, xnf_h, xtg_h, msbc_h, sbc_h,
                     mrow=msg, srw=ssg, mrow_h=msg_h, srw_h=ssg_h)
            for t in range(KT):
                P.add("tensor", lambda e, t=t: e.matmul(
                    psb[2][0:H, 0:GS], WbT_sb[:, t * H:(t + 1) * H], xnf[:, t * GS:(t + 1) * GS],
                    start=(t == 0), stop=(t == KT - 1)),
                    reads=("WbT_sb", f"xnf:{t}"), writes=("PSB2",))
            P.add("scalar", lambda e, g=g: e.activation(beta_hm[:, g * GS:(g + 1) * GS], psb[2][0:H, 0:GS], AF.Tanh, scale=0.5),
                  reads=("PSB2",), writes=("beta_hm",))
            P.add("vector", lambda e, g=g: e.tensor_scalar(
                beta_hm[:, g * GS:(g + 1) * GS], beta_hm[:, g * GS:(g + 1) * GS], 0.5, 0.5, op0=OP.mult, op1=OP.add),
                reads=("beta_hm",), writes=("beta_hm",))
            for pi, p in enumerate("qkv"):
                for o0_ in range(0, KT, 2):
                  orecs = []
                  for o in range(o0_, o0_ + 2):
                    PX = _Rec()
                    orecs.append(PX)
                    bk = o % 2
                    wb = (pi * KT + o) % 4
                    PX.add("gq", lambda e, p=p, o=o, wb=wb: e.dma_start(
                        out=wbuf[wb][:].rearrange("p (a n) -> p a n", a=KT),
                        in_=WT_d[p][:, o * 128:(o + 1) * 128].rearrange("(a p) n -> p a n", p=128)),
                        writes=(f"wbuf{wb}",))
                    for k in range(KT):
                        PX.add("tensor", lambda e, k=k, wb=wb, bk=bk: e.matmul(
                            psb[bk][:, 0:GS], wbuf[wb][:, k * 128:(k + 1) * 128], xnf[:, k * GS:(k + 1) * GS],
                            start=(k == 0), stop=(k == KT - 1)),
                            reads=(f"wbuf{wb}", f"xnf:{k}"), writes=(f"PSB{bk}",))
                    if g == 0:
                        hb = 5 if bk == 0 else 3
                        for k in range(KT):
                            PX.add("tensor", lambda e, k=k, wb=wb, hb=hb: e.matmul(
                                psb[hb][:, 0:3], wbuf[wb][:, k * 128:(k + 1) * 128], xnf_h[:, k * 3:(k + 1) * 3],
                                start=(k == 0), stop=(k == KT - 1)),
                                reads=(f"wbuf{wb}", f"xnf_h:{k}"), writes=(f"PSB{hb}",))
                        PX.add("scalar", lambda e, o=o, hb=hb: e.copy(pbuf[o][:, 0:3], psb[hb][:, 0:3]),
                              reads=(f"PSB{hb}",), writes=(f"pbuf:{o}",))
                    else:
                        PX.add("gpsimd", lambda e, pi=pi, o=o: e.tensor_copy(
                            pbuf[o][:, 0:3], tails[:, (pi * KT + o) * 3:(pi * KT + o) * 3 + 3]),
                            reads=(f"tails:{pi}:{o}",), writes=(f"pbuf:{o}",))
                    PX.add("scalar", lambda e, o=o, bk=bk: e.copy(pbuf[o][:, 3:3 + GS], psb[bk][:, 0:GS]),
                          reads=(f"PSB{bk}",), writes=(f"pbuf:{o}",))
                    PX.add("gpsimd", lambda e, pi=pi, o=o: e.tensor_copy(
                        tails[:, (pi * KT + o) * 3:(pi * KT + o) * 3 + 3], pbuf[o][:, GS:GS + 3]),
                        reads=(f"pbuf:{o}",), writes=(f"tails:{pi}:{o}",))
                    cwbase = pi * KT * 4 + o * 4
                    PX.add("vector", lambda e, o=o, cwbase=cwbase: e.tensor_scalar(
                        ybuf[o % 2][:], pbuf[o][:, 0:GS], conv_sb[:, cwbase:cwbase + 1], 0.0, op0=OP.mult, op1=OP.add),
                        reads=(f"pbuf:{o}", "conv_sb"), writes=(f"ybuf{o % 2}",))
                    for j in (1, 2, 3):
                        PX.add("vector", lambda e, o=o, j=j, cwbase=cwbase: e.scalar_tensor_tensor(
                            ybuf[o % 2][:], pbuf[o][:, j:j + GS], conv_sb[:, cwbase + j:cwbase + j + 1], ybuf[o % 2][:],
                            op0=OP.mult, op1=OP.add),
                            reads=(f"pbuf:{o}", "conv_sb", f"ybuf{o % 2}"), writes=(f"ybuf{o % 2}",))
                    PX.add("scalar", lambda e, p=p, o=o, g=g: e.activation(
                        RES[p][:, o * R + g * GS: o * R + g * GS + GS], ybuf[o % 2][:], AF.Silu),
                        reads=(f"ybuf{o % 2}",), writes=(f"res:{p}:{o}",))
                  _merge_rr(P, orecs)
        P.barrier()

        P.mark("A_done")
        sqs = [sbA(f"sqs{i}", [128, GS], F32) for i in range(2)]
        sqb = sbA("sqb", [H, R], F32)
        scbc = sbA("scbc", [128, GS], BF16)
        for g in range(NG):
            for p in ("q", "k"):
                scale = scale_q if p == "q" else scale_k
                res = RES[p]
                for t in range(KT):
                    sq = sqs[t % 2]
                    P.add("scalar", lambda e, res=res, t=t, g=g, sq=sq, p=p: e.activation(
                        sq[:], res[:, t * R + g * GS: t * R + g * GS + GS], AF.Square),
                        reads=(f"res:{p}:{t}",), writes=(f"sqs{t % 2}",))
                    P.add("tensor", lambda e, t=t, sq=sq: e.matmul(
                        psb[2][0:H, 0:GS], ind[:, t * H:(t + 1) * H], sq[:],
                        start=(t == 0), stop=(t == KT - 1)),
                        reads=("ind", f"sqs{t % 2}"), writes=("PSB2",))
                P.add("vector", lambda e, scale=scale, g=g, p=p: e.tensor_copy(
                    scale[:, g * GS:(g + 1) * GS], psb[2][0:H, 0:GS]),
                    reads=("PSB2",), writes=(f"scale_{p}",))
        P.add("scalar", lambda e: e.activation(sqb[:], beta_hm[:], AF.Sqrt), reads=("beta_hm",), writes=("sqb",))
        for p in ("q", "k"):
            scale = scale_q if p == "q" else scale_k
            P.add("scalar", lambda e, scale=scale: e.activation(scale[:], scale[:], AF.Sqrt),
                  reads=(f"scale_{p}",), writes=(f"scale_{p}",))
            P.add("vector", lambda e, scale=scale: e.tensor_scalar(scale[:], scale[:], cfg.EPS, 0.0, op0=OP.max, op1=OP.add),
                  reads=(f"scale_{p}",), writes=(f"scale_{p}",))
            P.add("vector", lambda e, scale=scale: e.reciprocal(scale[:], scale[:]),
                  reads=(f"scale_{p}",), writes=(f"scale_{p}",))
        P.add("vector", lambda e: e.tensor_scalar(scale_q[:], scale_q[:], 0.125, 0.0, op0=OP.mult, op1=OP.add),
              reads=("scale_q",), writes=("scale_q",))
        P.add("vector", lambda e: e.tensor_tensor(scale_k[:], scale_k[:], sqb[:], op=OP.mult),
              reads=("scale_k", "sqb"), writes=("scale_k",))
        for g in range(NG):
            for p in ("q", "k", "v"):
                scale = {"q": scale_q, "k": scale_k, "v": sqb}[p]
                res = RES[p]
                for t in range(KT):
                    bb = 3 + (t % 2)
                    P.add("tensor", lambda e, scale=scale, t=t, g=g, bb=bb, p=p: e.matmul(
                        psb[bb][:, 0:GS], indT[:, t * 128:(t + 1) * 128], scale[:, g * GS:(g + 1) * GS],
                        start=True, stop=True),
                        reads=("indT", "sqb" if p == "v" else f"scale_{p}"), writes=(f"PSB{bb}",))
                    P.add("scalar", lambda e, bb=bb: e.copy(scbc[:], psb[bb][:, 0:GS]),
                          reads=(f"PSB{bb}",), writes=("scbc",))
                    P.add("vector", lambda e, res=res, t=t, g=g, p=p: e.tensor_tensor(
                        res[:, t * R + g * GS: t * R + g * GS + GS],
                        res[:, t * R + g * GS: t * R + g * GS + GS], scbc[:], op=OP.mult),
                        reads=(f"res:{p}:{t}", "scbc"), writes=(f"res:{p}:{t}",))
        P.barrier()
        stA.close()

        P.mark("B_done")
        stC = ExitStack()
        sbC = lambda name, shape, dtype: stC.enter_context(nc.sbuf_tensor(name, shape, dtype))
        stg = [sbC(f"stg{i}", [128, D], BF16) for i in range(2)]
        for nm in ("k", "v"):
            src = RES[nm]
            dram = krm_d if nm == "k" else vrm_d
            for c in range(NC):
                par = c % 2
                for t in range(KT):
                    pt = t % 2
                    pst = psT0 if pt == 0 else psT1
                    pstk = "PSB6" if pt == 0 else "PSB7"
                    P.add("tensor", lambda e, src=src, c=c, t=t, pst=pst: e.transpose(
                        pst[:, 0:128], src[:, t * R + c * 128: t * R + c * 128 + 128], ident_b[:]),
                        reads=(f"res:{nm}:{t}", "ident_b"), writes=(pstk,))
                    P.add("scalar", lambda e, t=t, par=par, pst=pst: e.copy(
                        stg[par][:, t * 128:(t + 1) * 128], pst[:, 0:128]),
                        reads=(pstk,), writes=(f"stg{par}",))
                if nm == "v":
                    P.add("aq", lambda e, dram=dram, c=c, par=par: e.dma_start(
                        out=dram[c * 128:(c + 1) * 128, :], in_=stg[par][:]),
                        reads=(f"stg{par}",), writes=(f"{nm}rmd:{c}",))
                else:
                    for dup in range(2):
                        P.add("aq", lambda e, dram=dram, c=c, par=par, dup=dup: e.dma_start(
                            out=dram[c * 128:(c + 1) * 128, :].rearrange("p (h u d) -> p h u d", h=H, u=2)[:, :, dup],
                            in_=stg[par][:].rearrange("p (h d) -> p h d", h=H)),
                            reads=(f"stg{par}",), writes=(f"{nm}rmd:{c}:{dup}",))
        P.mark("B2_done")
        krm = [sbC(f"krm{i}", [128, 2 * D], BF16) for i in range(2)]
        vrm = [sbC(f"vrm{i}", [128, D], BF16) for i in range(2)]
        o0stg = [sbC(f"o0stg{i}", [128, H * 64], BF16) for i in range(2)]
        qtstg = [sbC(f"qtstg{i}", [128, KT * 128], BF16) for i in range(2)]
        ptstg = [sbC(f"ptstg{i}", [64, H * 64], BF16) for i in range(2)]

        GsL = [sbC(f"GsL{i}", [128, 256], F32) for i in range(3)]
        A_sb = [sbC(f"A_sb{i}", [128, 128], BF16) for i in range(3)]
        AT_sb = [sbC(f"AT_sb{i}", [128, 128], BF16) for i in range(3)]
        LT_sb = [sbC(f"LT_sb{i}", [128, 128], BF16) for i in range(3)]
        X_sb = [sbC(f"X_sb{i}", [128, 128], BF16) for i in range(6)]
        RRT = [[sbC(f"RRT{s}_{l}", [128, 256], BF16) for l in range(NLEV)] for s in range(3)]
        def rec_head(c, cp, h):
            Pp, Ps, Pm = _Rec(), _Rec(), _Rec()
            t, hw = h // 2, h % 2
            wp = h % 3
            bA, bB = f"PSB{wp}", f"PSB{3 + wp}"
            pA, pB = psb[wp], psb[3 + wp]
            kf = ktf[64 * hw:64 * hw + 64, t * R + c * 128: t * R + c * 128 + 128]
            qf = qtf[64 * hw:64 * hw + 64, t * R + c * 128: t * R + c * 128 + 128]
            rk, rq = f"res:k:{t}", f"res:q:{t}"
            Xc, Xn = X_sb[2 * wp], X_sb[2 * wp + 1]
            XcK, XnK = f"X{2 * wp}", f"X{2 * wp + 1}"
            Sb_sl = S_b[64 * hw:64 * hw + 64, h * 64:(h + 1) * 64]
            Pp.add("tensor", lambda e, kf=kf, pA=pA, hw=hw: e.matmul(pA[:, 0:128], kf, kf, start=True, stop=True,
                  tile_position=(64 * hw, 0)),
                  reads=(rk,), writes=(bA,))
            Pp.add("tensor", lambda e, kf=kf, qf=qf, pA=pA, hw=hw: e.matmul(pA[:, 128:256], kf, qf, start=True, stop=True,
                  tile_position=(64 * hw, 0)),
                  reads=(rk, rq), writes=(bA,))
            Pp.add("tensor", lambda e, kf=kf, pA=pA, Sb_sl=Sb_sl, hw=hw: e.matmul(
                pA[:, 256:320], kf, Sb_sl, start=True, stop=True, tile_position=(64 * hw, 0)),
                reads=(rk, f"S_b:{h}"), writes=(bA,))
            Pp.add("scalar", lambda e, pA=pA, wp=wp: e.copy(GsL[wp][:], pA[:, 0:256]),
                  reads=(bA,), writes=(f"GsL{wp}",))
            Ps.add("gpsimd", lambda e, wp=wp: e.tensor_tensor(A_sb[wp][:], GsL[wp][:, 0:128], mask_sl[:], op=OP.mult),
                  reads=(f"GsL{wp}", "mask_sl"), writes=(f"A{wp}",))
            Ps.add("gpsimd", lambda e, wp=wp: e.tensor_tensor(AT_sb[wp][:], GsL[wp][:, 0:128], mask_sun[:], op=OP.mult),
                  reads=(f"GsL{wp}", "mask_sun"), writes=(f"AT{wp}",))
            Ps.add("vector", lambda e, wp=wp: e.tensor_tensor(LT_sb[wp][:], GsL[wp][:, 128:256], mask_sui[:], op=OP.mult),
                  reads=(f"GsL{wp}", "mask_sui"), writes=(f"LT{wp}",))
            Ps.add("gpsimd", lambda e, Xc=Xc, cp=cp, h=h: e.tensor_scalar_mul(
                Xc[:, 0:64], krm[cp][:, h * 128:h * 128 + 64], -1.0),
                reads=(f"krm{cp}",), writes=(XcK,))
            Ps.add("vector", lambda e, Xc=Xc, cp=cp, h=h, pA=pA: e.tensor_sub(
                Xc[:, 64:128], vrm[cp][:, h * 64:(h + 1) * 64], pA[:, 256:320]),
                reads=(f"vrm{cp}", bA), writes=(XcK,))
            prevR, prevRK = A_sb[wp][:], f"A{wp}"
            prevRT, prevRTK = AT_sb[wp][:], f"AT{wp}"
            for lev in range(1, NLEV + 1):
                buf = RRT[wp][lev - 1]
                bK = f"RRT{wp}:{lev}"
                if lev < NLEV:
                    Ps.add("tensor", lambda e, pA=pA, prevR=prevR, prevRT=prevRT: e.matmul(
                        pA[:, 0:128], prevRT, prevR, start=True, stop=True),
                        reads=(prevRK, prevRTK), writes=(bA,))
                Ps.add("tensor", lambda e, pA=pA, prevR=prevR, prevRT=prevRT: e.matmul(
                    pA[:, 128:256], prevR, prevRT, start=True, stop=True),
                    reads=(prevRK, prevRTK), writes=(bA,))
                if lev < NLEV:
                    Ps.add("scalar", lambda e, pA=pA, buf=buf: e.copy(buf[:], pA[:, 0:256]),
                          reads=(bA,), writes=(bK,))
                    prevR, prevRK = buf[:, 0:128], bK
                    prevRT, prevRTK = buf[:, 128:256], bK
                else:
                    Ps.add("scalar", lambda e, pA=pA, buf=buf: e.copy(buf[:, 128:256], pA[:, 128:256]),
                          reads=(bA,), writes=(bK,))
            for j in list(range(NLEV, 0, -1)) + [0]:
                if j > 0:
                    rt, rtK = RRT[wp][j - 1][:, 128:256], f"RRT{wp}:{j}"
                else:
                    rt, rtK = AT_sb[wp][:], f"AT{wp}"
                Pm.add("tensor", lambda e, pB=pB, rt=rt, Xc=Xc: e.matmul(
                    pB[:, 0:128], rt, Xc[:], start=True, stop=True),
                    reads=(rtK, XcK), writes=(bB,))
                Pm.add("vector", lambda e, pB=pB, Xn=Xn, Xc=Xc: e.tensor_add(Xn[:], Xc[:], pB[:, 0:128]),
                      reads=(bB, XcK), writes=(XnK,))
                Xc, Xn = Xn, Xc
                XcK, XnK = XnK, XcK
            Pm.add("tensor", lambda e, qf=qf, pB=pB, Sb_sl=Sb_sl, hw=hw: e.matmul(
                pB[:, 128:192], qf, Sb_sl, start=True, stop=False, tile_position=(64 * hw, 0)),
                reads=(rq, f"S_b:{h}"), writes=(bB,))
            Pm.add("tensor", lambda e, pB=pB, wp=wp, Xc=Xc: e.matmul(
                pB[:, 128:192], LT_sb[wp][:], Xc[:, 64:128], start=False, stop=True),
                reads=(f"LT{wp}", XcK), writes=(bB,))
            Pm.add("tensor", lambda e, pB=pB, cp=cp, h=h, Xc=Xc: e.matmul(
                pB[:, 192:256], krm[cp][:, h * 128:(h + 1) * 128], Xc[:, 64:128], start=True, stop=True),
                reads=(f"krm{cp}", XcK), writes=(bB,))
            Pm.add("tensor", lambda e, pB=pB: e.matmul(
                pB[0:64, 256:320], ident64[0:64, :], ident64[0:64, :], start=True, stop=False),
                reads=("ident64",), writes=(bB,))
            Pm.add("tensor", lambda e, pB=pB, cp=cp, h=h, Xc=Xc: e.matmul(
                pB[0:64, 256:320], Xc[:, 0:64], krm[cp][:, h * 128:h * 128 + 64], start=False, stop=True),
                reads=(f"krm{cp}", XcK), writes=(bB,))
            Pm.add("tensor", lambda e, pB=pB, qf=qf, hw=hw: e.matmul(
                pB[64 * hw:64 * hw + 64, 320:448], ident64[64 * hw:64 * hw + 64, :], qf, start=True, stop=False,
                tile_position=(64 * hw, 64 * hw)),
                reads=("ident64", rq), writes=(bB,))
            Pm.add("tensor", lambda e, pB=pB, wp=wp, Xc=Xc, hw=hw: e.matmul(
                pB[64 * hw:64 * hw + 64, 320:448], Xc[:, 0:64], LT_sb[wp][:], start=False, stop=True,
                tile_position=(0, 64 * hw)),
                reads=(f"LT{wp}", XcK), writes=(bB,))
            Pm.add("vector", lambda e, pB=pB, h=h: e.tensor_add(
                S_f[:, h * 64:(h + 1) * 64], S_f[:, h * 64:(h + 1) * 64], pB[:, 192:256]),
                reads=(bB, f"S_f:{h}"), writes=(f"S_f:{h}",))
            Pm.add("scalar", lambda e, h=h: e.copy(S_b[:, h * 64:(h + 1) * 64], S_f[:, h * 64:(h + 1) * 64]),
                  reads=(f"S_f:{h}",), writes=(f"S_b:{h}",))
            Pm.add("scalar", lambda e, pB=pB, cp=cp, h=h: e.copy(
                ptstg[cp][:, h * 64:(h + 1) * 64], pB[0:64, 256:320]),
                reads=(bB,), writes=(f"PT:{c}:{h}",))
            Pm.add("scalar", lambda e, pB=pB, cp=cp, h=h: e.copy(
                o0stg[cp][:, h * 64:(h + 1) * 64], pB[:, 128:192]),
                reads=(bB,), writes=(f"o0stg{cp}",))
            Pm.add("scalar", lambda e, pB=pB, cp=cp, t=t, hw=hw: e.copy(
                qtstg[cp][64 * hw:64 * hw + 64, t * 128:(t + 1) * 128], pB[64 * hw:64 * hw + 64, 320:448]),
                reads=(bB,), writes=(f"qtstg{cp}",))
            Pm.add("tensor", lambda e, pB=pB, cp=cp, h=h: e.matmul(
                pB[0:64, 448:512], ptstg[cp][:, h * 64:(h + 1) * 64], PP_b[0:64, h * 64:(h + 1) * 64],
                start=True, stop=True),
                reads=(f"PT:{c}:{h}", f"PP_b:{h}"), writes=(bB,))
            Pm.add("vector", lambda e, pB=pB, h=h: e.tensor_copy(PP_b[0:64, h * 64:(h + 1) * 64], pB[0:64, 448:512]),
                  reads=(bB,), writes=(f"PP_b:{h}",))
            return Pp, Ps, Pm

        prev_rests = None
        prev_post = None
        for c in range(NC):
            cp = c % 2
            pre = [
                (lambda e, c=c, cp=cp: e.dma_start(out=krm[cp][:], in_=krm_d[c * 128:(c + 1) * 128, :]),
                 (f"krmd:{c}:0", f"krmd:{c}:1"), (f"krm{cp}",)),
                (lambda e, c=c, cp=cp: e.dma_start(out=vrm[cp][:], in_=vrm_d[c * 128:(c + 1) * 128, :]),
                 (f"vrmd:{c}",), (f"vrm{cp}",)),
            ]
            for g0 in range(0, H, 3):
                recs = [rec_head(c, cp, h) for h in range(g0, min(g0 + 3, H))]
                for fn, rd, wr in pre:
                    P.add("sp", fn, reads=rd, writes=wr)
                pre = []
                if prev_rests is None:
                    _merge_rr(P, [r[0] for r in recs])
                else:
                    _merge_rr(P, prev_rests + [r[0] for r in recs])
                    for fn, rd, wr in (prev_post or ()):
                        P.add("aq", fn, reads=rd, writes=wr)
                    prev_post = None
                _merge_rr(P, [r[1] for r in recs])
                prev_rests = [r[2] for r in recs]
            prev_post = [
                (lambda e, c=c, cp=cp: e.dma_start(out=o0_d[c * 128:(c + 1) * 128, :], in_=o0stg[cp][:]),
                 (f"o0stg{cp}",), (f"o0d:{c}",)),
                (lambda e, c=c, cp=cp: e.dma_start(
                    out=qt_d[:, c * KT * 128:(c + 1) * KT * 128], in_=qtstg[cp][:]),
                 (f"qtstg{cp}",), (f"qtd:{c}",)),
                (lambda e, c=c, cp=cp: e.dma_start(
                    out=pt_d[:, c * H * 64:(c + 1) * H * 64], in_=ptstg[cp][:]),
                 tuple(f"PT:{c}:{hh}" for hh in range(H)), (f"ptd:{c}",)),
            ]
        _merge_rr(P, prev_rests)
        for fn, rd, wr in prev_post:
            P.add("aq", fn, reads=rd, writes=wr)
        P.mark("C_done")
        ccstg = sbC("ccstg", [64, H * 128], F32)
        for h in range(H):
            pt = h % 2
            pst = psT0 if pt == 0 else psT1
            pstk = "PSB6" if pt == 0 else "PSB7"
            P.add("tensor", lambda e, h=h, pst=pst: e.transpose(
                pst[0:64, 0:64], PP_b[0:64, h * 64:(h + 1) * 64], ident_b[0:64, 0:64]),
                reads=(f"PP_b:{h}", "ident_b"), writes=(pstk,))
            P.add("scalar", lambda e, h=h, pst=pst: e.copy(ccstg[:, h * 128:h * 128 + 64], pst[0:64, 0:64]),
                  reads=(pstk,), writes=("ccstg",))
            P.add("vector", lambda e, h=h: e.tensor_copy(ccstg[:, h * 128 + 64:h * 128 + 128], S_f[0:64, h * 64:(h + 1) * 64]),
                  reads=(f"S_f:{h}",), writes=("ccstg",))
        if part == "full":
            P.add("gq", lambda e: e.dma_start(out=cc_in_d[:, :], in_=ccstg[:]), reads=("ccstg",), writes=("cc_in",))
            P.add("cc", lambda e: e.collective_compute(
                "AllGather", OP.bypass, replica_groups=[list(range(n_cores))],
                ins=[cc_in_d[:, :]], outs=[cc_out_d[:, :]]),
                reads=("cc_in",), writes=("cc_out",))
        else:
            P.add("aq", lambda e: e.dma_start(out=cc_out1_d[:, :], in_=ccstg[:]), reads=("ccstg",), writes=("cc_out",))
            P.add("gpsimd", lambda e: e.engine_nop(),
                  reads=("cc_out", "msrow_d") + tuple(f"o0d:{c}" for c in range(NC))
                        + tuple(f"qtd:{c}" for c in range(NC)) + tuple(f"ptd:{c}" for c in range(NC)))
        P.barrier()
        stC.close()
        stABC.close()
        P.barrier()

    if run_p2:
        stD = ExitStack()
        sbD = lambda name, shape, dtype: stD.enter_context(nc.sbuf_tensor(name, shape, dtype))
        ccg = [sbD(f"ccgm{i}", [128, H * 192], F32) for i in range(2)]
        Sm = sbD("Sm", [128, H * 64], F32)
        Sacc = sbD("Sacc", [128, H * 64], F32)
        sel_tmp = sbD("sel_tmp", [128, H * 64], F32)
        ohbc = sbD("ohbc", [128, n_cores], F32)
        P.add("gpsimd", lambda e: e.memset(Sm[:], 0.0), writes=("Sm",))
        P.add("gpsimd", lambda e: e.memset(Sacc[:], 0.0), writes=("Sacc",))
        P.add("tensor", lambda e: e.matmul(psb[5][:, 0:n_cores], ones1[:], onehot[:], start=True, stop=True),
              reads=("ones1", "onehot"), writes=("PSB5",))
        P.add("vector", lambda e: e.tensor_copy(ohbc[:], psb[5][:, 0:n_cores]), reads=("PSB5",), writes=("ohbc",))
        for m in range(n_cores):
            if m < n_cores - 1:
                mp = m % 2
                for half in range(2):
                    for dup in range(2):
                        P.add("sp", lambda e, m=m, mp=mp, half=half, dup=dup: e.dma_start(
                            out=ccg[mp][64 * half:64 * half + 64, :].rearrange("p (h x d) -> p h x d", h=H, x=3)[:, :, dup],
                            in_=cc_out_d[m * 64:(m + 1) * 64, :].rearrange("p (h x d) -> p h x d", h=H, x=2)[:, :, 0]),
                            reads=("cc_out",), writes=(f"ccgm{mp}",))
                    P.add("sp", lambda e, m=m, mp=mp, half=half: e.dma_start(
                        out=ccg[mp][64 * half:64 * half + 64, :].rearrange("p (h x d) -> p h x d", h=H, x=3)[:, :, 2],
                        in_=cc_out_d[m * 64:(m + 1) * 64, :].rearrange("p (h x d) -> p h x d", h=H, x=2)[:, :, 1]),
                        reads=("cc_out",), writes=(f"ccgm{mp}",))
            P.add("vector", lambda e, m=m: e.scalar_tensor_tensor(
                sel_tmp[:], Sm[:], ohbc[:, m:m + 1], Sacc[:], op0=OP.mult, op1=OP.add),
                reads=("Sm", "ohbc", "Sacc"), writes=("sel_tmp",))
            P.add("vector", lambda e: e.tensor_copy(Sacc[:], sel_tmp[:]), reads=("sel_tmp",), writes=("Sacc",))
            if m < n_cores - 1:
                mp = m % 2
                for h in range(H):
                    P.add("tensor", lambda e, mp=mp, h=h: e.matmul(
                        psb[5][:, 64:128], ccg[mp][0:64, h * 192:h * 192 + 128],
                        Sm[0:64, h * 64:(h + 1) * 64], start=True, stop=True),
                        reads=(f"ccgm{mp}", "Sm"), writes=("PSB5",))
                    P.add("vector", lambda e, mp=mp, h=h: e.tensor_add(
                        Sm[:, h * 64:(h + 1) * 64], psb[5][:, 64:128],
                        ccg[mp][:, h * 192 + 128:h * 192 + 192]),
                        reads=("PSB5", f"ccgm{mp}"), writes=("Sm",))
        P.add("vector", lambda e: e.tensor_copy(N_f[:], Sacc[:]), reads=("Sacc",),
              writes=tuple(f"N_f:{h}" for h in range(H)))
        P.add("scalar", lambda e: e.copy(N_b[:], N_f[:]), reads=tuple(f"N_f:{h}" for h in range(H)),
              writes=tuple(f"N_b:{h}" for h in range(H)))

        Z_sb = sbD("Z_sb", [128, KT * D], BF16)
        e1b = sbD("e1b", [128, 512], BF16)
        e2b = [sbD(f"e2b{i}", [128, 512], BF16) for i in range(4)]
        zacc = sbD("zacc", [128, 512], F32)
        ztmp = sbD("ztmp", [128, 512], F32)
        for ri in "RI":
            for c4 in range(4):
                P.add("gq", lambda e, ri=ri, c4=c4: e.dma_start(out=e2b[c4][:], in_=E2_d[ri][c4]),
                      writes=(f"e2b{c4}",))
            for it in range(4):
                for c4 in range(4):
                    P.add("gq", lambda e, ri=ri, it=it, c4=c4: e.dma_start(out=e1b[:], in_=E1_d[ri][c4, it]),
                          writes=("e1b",))
                    if c4 == 0:
                        P.add("vector", lambda e, c4=c4: e.tensor_tensor(zacc[:], e1b[:], e2b[c4][:], op=OP.mult),
                              reads=("e1b", f"e2b{c4}"), writes=("zacc",))
                    else:
                        P.add("vector", lambda e, c4=c4: e.tensor_tensor(ztmp[:], e1b[:], e2b[c4][:], op=OP.mult),
                              reads=("e1b", f"e2b{c4}"), writes=("ztmp",))
                        P.add("vector", lambda e: e.tensor_add(zacc[:], zacc[:], ztmp[:]),
                              reads=("zacc", "ztmp"), writes=("zacc",))
                if ri == "R":
                    P.add("scalar", lambda e, it=it: e.copy(Z_sb[:, it * D + 0:it * D + 512], zacc[:]),
                          reads=("zacc",), writes=("Z_sb",))
                    P.add("scalar", lambda e, it=it: e.copy(Z_sb[:, (4 + it) * D + 512:(4 + it) * D + 1024], zacc[:]),
                          reads=("zacc",), writes=("Z_sb",))
                else:
                    P.add("scalar", lambda e, it=it: e.copy(Z_sb[:, it * D + 512:it * D + 1024], zacc[:]),
                          reads=("zacc",), writes=("Z_sb",))
                    P.add("scalar", lambda e, it=it: e.mul(Z_sb[:, (4 + it) * D + 0:(4 + it) * D + 512], zacc[:], -1.0),
                          reads=("zacc",), writes=("Z_sb",))

        WoT_sb = sbD("WoT_sb", [128, KT * D], BF16)
        P.add("gq", lambda e: e.dma_start(out=WoT_sb[:].rearrange("p (a n) -> p a n", a=KT),
                                          in_=WT_d["o"].rearrange("(a p) n -> p a n", p=128)),
              writes=("WoT_sb",))
        msgD = sbD("msgD", [1, GS], F32)
        ssgD = sbD("ssgD", [1, GS], F32)
        xnfD = sbD("xnfD", [128, KT * GS], BF16)
        xtgD = sbD("xtgD", [128, KT * GS], BF16)
        msbcD = sbD("msbcD", [128, GS], BF16)
        sbcD = sbD("sbcD", [128, GS], BF16)
        o0b = [sbD(f"o0b{i}", [128, D], BF16) for i in range(2)]
        qtb = [sbD(f"qtb{i}", [128, KT * 128], BF16) for i in range(2)]
        ptb = [sbD(f"ptb{i}", [64, H * 128], BF16) for i in range(2)]
        Ofp = [sbD(f"Ofp{i}", [128, D], F32) for i in range(2)]
        ObfR = [sbD(f"ObfR{i}", [128, D], BF16) for i in range(2)]
        Ofeat = sbD("Ofeat", [128, KT * 128], BF16)
        osq = [sbD(f"osq{i}", [128, 128], F32) for i in range(2)]
        srow16 = sbD("srow16", [H, 128], F32)
        obc = [sbD(f"obc{i}", [128, 128], BF16) for i in range(2)]
        xtok = [sbD(f"xtok{i}", [128, D], F32) for i in range(2)]
        outsb = [sbD(f"outsb{i}", [128, D], F32) for i in range(2)]

        def emit_xnf_D(grp):
            c0 = 3 + grp * GS
            for t in range(KT):
                P.add("gq", lambda e, t=t, c0=c0: e.dma_start(
                    out=xtgD[:, t * GS:(t + 1) * GS], in_=x_t_d[t * 128:(t + 1) * 128, c0:c0 + GS]),
                    writes=(f"xtgD:{t}",))
            P.add("sp", lambda e, c0=c0: e.dma_start(out=msgD[:], in_=msrow_d[0:1, c0:c0 + GS]),
                  reads=("msrow_d",), writes=("msgD",))
            P.add("sp", lambda e, c0=c0: e.dma_start(out=ssgD[:], in_=msrow_d[1:2, c0:c0 + GS]),
                  reads=("msrow_d",), writes=("ssgD",))
            P.add("tensor", lambda e: e.matmul(psb[4][:, 0:GS], ones1[:], msgD[:], start=True, stop=True),
                  reads=("ones1", "msgD"), writes=("PSB4",))
            P.add("tensor", lambda e: e.matmul(psb[5][:, 0:GS], ones1[:], ssgD[:], start=True, stop=True),
                  reads=("ones1", "ssgD"), writes=("PSB5",))
            P.add("scalar", lambda e: e.copy(msbcD[:], psb[4][:, 0:GS]), reads=("PSB4",), writes=("msbcD",))
            P.add("scalar", lambda e: e.copy(sbcD[:], psb[5][:, 0:GS]), reads=("PSB5",), writes=("sbcD",))
            for t in range(KT):
                P.add("vector", lambda e, t=t: e.tensor_tensor(
                    xnfD[:, t * GS:(t + 1) * GS], xtgD[:, t * GS:(t + 1) * GS], msbcD[:], op=OP.add),
                    reads=(f"xtgD:{t}", "msbcD"), writes=(f"xnfD:{t}",))
                P.add("vector", lambda e, t=t: e.tensor_tensor(
                    xnfD[:, t * GS:(t + 1) * GS], xnfD[:, t * GS:(t + 1) * GS], sbcD[:], op=OP.mult),
                    reads=(f"xnfD:{t}", "sbcD"), writes=(f"xnfD:{t}",))

        CPG = GS // C
        for g in range(NG):
            emit_xnf_D(g)
            for ci in range(CPG):
                c = g * CPG + ci
                cp = c % 2
                P.add("sp", lambda e, c=c, cp=cp: e.dma_start(out=o0b[cp][:], in_=o0_d[c * 128:(c + 1) * 128, :]),
                      reads=(f"o0d:{c}",), writes=(f"o0b{cp}",))
                P.add("sp", lambda e, c=c, cp=cp: e.dma_start(
                    out=qtb[cp][:], in_=qt_d[:, c * KT * 128:(c + 1) * KT * 128]),
                    reads=(f"qtd:{c}",), writes=(f"qtb{cp}",))
                P.add("sp", lambda e, c=c, cp=cp: e.dma_start(out=xtok[cp][:], in_=x_tok_d[c * 128:(c + 1) * 128, :]),
                      writes=(f"xtok{cp}",))
                for dup in range(2):
                    P.add("sp", lambda e, c=c, cp=cp, dup=dup: e.dma_start(
                        out=ptb[cp][:].rearrange("p (h u d) -> p h u d", h=H, u=2)[:, :, dup],
                        in_=pt_d[:, c * H * 64:(c + 1) * H * 64].rearrange("p (h d) -> p h d", h=H)),
                        reads=(f"ptd:{c}",), writes=(f"ptb{cp}",))
                for h in range(H):
                    t, hw = h // 2, h % 2
                    P.add("tensor", lambda e, cp=cp, t=t, hw=hw, h=h: e.matmul(
                        psb[2 + hw][:, (h // 2) * 64:(h // 2) * 64 + 64],
                        qtb[cp][64 * hw:64 * hw + 64, t * 128:(t + 1) * 128],
                        N_b[64 * hw:64 * hw + 64, h * 64:(h + 1) * 64], start=True, stop=True,
                        tile_position=(64 * hw, 0)),
                        reads=(f"qtb{cp}", f"N_b:{h}"), writes=(f"PSB{2 + hw}",))
                for h in range(H):
                    hw = h % 2
                    P.add("tensor", lambda e, h=h, cp=cp, hw=hw: e.matmul(
                        psb[4 + hw][:, (h // 2) * 64:(h // 2) * 64 + 64],
                        ptb[cp][:, h * 128:(h + 1) * 128],
                        N_b[0:64, h * 64:(h + 1) * 64], start=True, stop=True),
                        reads=(f"ptb{cp}", f"N_b:{h}"), writes=(f"PSB{4 + hw}",))
                allN = tuple(f"N_b:{hh}" for hh in range(H))
                allNf = tuple(f"N_f:{hh}" for hh in range(H))
                for hw in range(2):
                    P.add("vector", lambda e, cp=cp, hw=hw: e.tensor_add(
                        Ofp[cp][:].rearrange("p (h x d) -> p h x d", h=8, x=2)[:, :, hw],
                        o0b[cp][:].rearrange("p (h x d) -> p h x d", h=8, x=2)[:, :, hw],
                        psb[2 + hw][:, 0:512].rearrange("p (h d) -> p h d", h=8)),
                        reads=(f"o0b{cp}", f"PSB{2 + hw}"), writes=(f"Ofp{cp}",))
                    P.add("vector", lambda e, hw=hw: e.tensor_copy(
                        N_f[:].rearrange("p (h x d) -> p h x d", h=8, x=2)[:, :, hw],
                        psb[4 + hw][:, 0:512].rearrange("p (h d) -> p h d", h=8)),
                        reads=(f"PSB{4 + hw}",), writes=allNf)
                P.add("scalar", lambda e: e.copy(N_b[:], N_f[:]), reads=allNf, writes=allN)
                for half in range(2):
                    P.add("scalar", lambda e, cp=cp, half=half: e.copy(
                        ObfR[cp][:, half * 512:half * 512 + 512], Ofp[cp][:, half * 512:half * 512 + 512]),
                        reads=(f"Ofp{cp}",), writes=(f"ObfR{cp}",))
                for t in range(KT):
                    pt = t % 2
                    pst = psT0 if pt == 0 else psT1
                    pstk = "PSB6" if pt == 0 else "PSB7"
                    P.add("tensor", lambda e, cp=cp, t=t, pst=pst: e.transpose(
                        pst[:, 0:128], ObfR[cp][:, t * 128:(t + 1) * 128], ident_b[:]),
                        reads=(f"ObfR{cp}", "ident_b"), writes=(pstk,))
                    P.add("scalar", lambda e, t=t, pst=pst: e.copy(Ofeat[:, t * 128:(t + 1) * 128], pst[:, 0:128]),
                          reads=(pstk,), writes=(f"Ofeat:{t}",))
                    P.add("scalar", lambda e, t=t: e.activation(osq[t % 2][:], Ofeat[:, t * 128:(t + 1) * 128], AF.Square),
                          reads=(f"Ofeat:{t}",), writes=(f"osq{t % 2}",))
                    P.add("tensor", lambda e, t=t: e.matmul(
                        psb[4][0:H, 0:128], ind[:, t * H:(t + 1) * H], osq[t % 2][:],
                        start=(t == 0), stop=(t == KT - 1)),
                        reads=("ind", f"osq{t % 2}"), writes=("PSB4",))
                P.add("scalar", lambda e: e.activation(srow16[:], psb[4][0:H, 0:128], AF.Sqrt, scale=1.0 / dh, bias=epsrms[0:H, :]),
                      reads=("PSB4", "epsrms"), writes=("srow16",))
                P.add("vector", lambda e: e.reciprocal(srow16[:], srow16[:]), reads=("srow16",), writes=("srow16",))
                for t in range(KT):
                    P.add("tensor", lambda e, t=t: e.matmul(
                        psb[5][:, 0:128], indT[:, t * 128:(t + 1) * 128], srow16[:], start=True, stop=True),
                        reads=("indT", "srow16"), writes=("PSB5",))
                    P.add("scalar", lambda e, t=t: e.activation(obc[t % 2][:], psb[5][:, 0:128], AF.Identity, scale=onw[:]),
                          reads=("PSB5", "onw"), writes=(f"obc{t % 2}",))
                    P.add("vector", lambda e, t=t: e.tensor_tensor(
                        Ofeat[:, t * 128:(t + 1) * 128], Ofeat[:, t * 128:(t + 1) * 128], obc[t % 2][:], op=OP.mult),
                        reads=(f"Ofeat:{t}", f"obc{t % 2}"), writes=(f"Ofeat:{t}",))
                for half in range(2):
                    bo = half
                    for t in range(KT):
                        P.add("tensor", lambda e, t=t, half=half, bo=bo: e.matmul(
                            psb[bo][:, 0:512], Ofeat[:, t * 128:(t + 1) * 128],
                            WoT_sb[:, t * D + half * 512: t * D + half * 512 + 512],
                            start=(t == 0), stop=False),
                            reads=(f"Ofeat:{t}", "WoT_sb"), writes=(f"PSB{bo}",))
                    for t in range(KT):
                        P.add("tensor", lambda e, t=t, ci=ci, half=half, bo=bo: e.matmul(
                            psb[bo][:, 0:512], xnfD[:, t * GS + ci * C: t * GS + ci * C + 128],
                            Z_sb[:, t * D + half * 512: t * D + half * 512 + 512],
                            start=False, stop=False),
                            reads=(f"xnfD:{t}", "Z_sb"), writes=(f"PSB{bo}",))
                    P.add("tensor", lambda e, half=half, bo=bo: e.matmul(
                        psb[bo][:, 0:512], ones1[:], biasrow[0:1, half * 512:half * 512 + 512],
                        start=False, stop=True),
                        reads=("ones1", "biasrow"), writes=(f"PSB{bo}",))
                    P.add("vector", lambda e, cp=cp, half=half, bo=bo: e.tensor_add(
                        outsb[cp][:, half * 512:half * 512 + 512], psb[bo][:, 0:512],
                        xtok[cp][:, half * 512:half * 512 + 512]),
                        reads=(f"PSB{bo}", f"xtok{cp}"), writes=(f"outsb{cp}",))
                P.add("sp", lambda e, c=c, cp=cp: e.dma_start(out=out_d[c * 128:(c + 1) * 128, :], in_=outsb[cp][:]),
                      reads=(f"outsb{cp}",), writes=("out_final",))

        P.add("gpsimd", lambda e: e.engine_nop(), reads=("out_final",))

    P.mark("end")
    import os
    if os.environ.get("KPRINT_MARKS"):
        print("MARKS:", P.marks)
    P.finalize(truncate=cfg.trunc, skip=getattr(cfg, 'skip_items', ()))
    P.emit(nc, sems)
    return nc


def host_in_maps(cfg: CFG, inputs):
    import ml_dtypes
    n = cfg.n_cores
    R, D, H, KT = cfg.R, cfg.D, cfg.H, cfg.KT
    x = np.asarray(inputs["L0_attn_x"], np.float32).reshape(cfg.L, D)
    WT = {p: np.ascontiguousarray(np.asarray(inputs[f"L0_attn_W{p}"], np.float32).T) for p in "qkvo"}
    WbT = np.ascontiguousarray(np.asarray(inputs["L0_attn_Wb"], np.float32).T)
    onw = np.ascontiguousarray(np.tile(np.asarray(inputs["L0_attn_o_norm_w"], np.float32), 2)[:, None])
    biasrow = np.ascontiguousarray(np.concatenate([
        np.asarray(inputs["L0_ffn_bias_real"], np.float32),
        np.asarray(inputs["L0_ffn_bias_imag"], np.float32)])[None, :])

    def expand(G1, G2):
        G1 = np.asarray(G1, np.float32)[0]
        G2 = np.asarray(G2, np.float32)
        d_of_p = np.arange(128) // 16
        g_of_p = np.arange(128) % 16
        E1 = np.empty((4, 4, 128, 512), np.float32)
        E2 = np.empty((4, 128, 512), np.float32)
        for c4 in range(4):
            colB = G2[c4, :, 0, :]
            E2[c4] = colB.T[g_of_p][:, None, :].repeat(32, axis=1).reshape(128, 512)
            for it in range(4):
                vals = G1[:, c4, it * 8 + d_of_p]
                E1[c4, it] = vals.T[:, :, None].repeat(16, axis=2).reshape(128, 512)
        return E1, E2
    E1R, E2R = expand(inputs["L0_ffn_G1_R"], inputs["L0_ffn_G2_R"])
    E1I, E2I = expand(inputs["L0_ffn_G1_I"], inputs["L0_ffn_G2_I"])

    i_ = np.arange(128)
    mask_sl = -((i_[None, :] < i_[:, None]).astype(np.float32))
    mask_sun = -((i_[None, :] > i_[:, None]).astype(np.float32))
    mask_sui = (i_[None, :] >= i_[:, None]).astype(np.float32)
    ident_f = np.eye(128, dtype=np.float32)
    ident_b = np.eye(128).astype(ml_dtypes.bfloat16)
    ident64_2 = np.ascontiguousarray(np.tile(np.eye(64), (2, 1)).astype(ml_dtypes.bfloat16))
    ones1 = np.ones((1, 128), np.float32)
    ppinit = np.ascontiguousarray(np.tile(np.eye(64), (1, H)).astype(ml_dtypes.bfloat16))
    ind = np.zeros((128, KT * H), np.float32)
    indT = np.zeros((H, KT * 128), np.float32)
    for t in range(KT):
        for p in range(128):
            h = 2 * t + (1 if p >= 64 else 0)
            ind[p, t * H + h] = 1.0
            indT[h, t * 128 + p] = 1.0

    in_maps = []
    for i in range(n):
        r0 = i * R
        xt_sl = np.zeros((D, R + 3), np.float32)
        lo = max(r0 - 3, 0)
        xt_sl[:, 3 - (r0 - lo):] = x[lo:r0 + R].T
        halo = np.zeros((3, D), np.float32)
        if r0 >= 3:
            halo[:] = x[r0 - 3:r0]
        oh = np.zeros((1, n), np.float32)
        oh[0, i] = 1.0
        m = {
            "x_tok": np.ascontiguousarray(x[r0:r0 + R]),
            "x_t": np.ascontiguousarray(xt_sl),
            "x_halo": halo,
            "WbT": WbT,
            "conv_q": np.asarray(inputs["L0_attn_conv_q"], np.float32),
            "conv_k": np.asarray(inputs["L0_attn_conv_k"], np.float32),
            "conv_v": np.asarray(inputs["L0_attn_conv_v"], np.float32),
            "onw128": onw, "biasrow": biasrow,
            "E1R": E1R, "E2R": E2R, "E1I": E1I, "E2I": E2I,
            "mask_sl": mask_sl, "mask_sun": mask_sun, "mask_sui": mask_sui,
            "ident_f": ident_f, "ident_b": ident_b, "ident64_2": ident64_2,
            "ones1": ones1, "ppinit": ppinit, "onehot": oh,
            "ind": ind, "indT": indT,
        }
        for p in "qkvo":
            m[f"W{p}T"] = WT[p]
        in_maps.append(m)
    return in_maps


MID_TENSORS = ("o0_dram", "qt_dram", "pt_dram", "msrow_dram")


def run_pipeline(cfg, inputs, launcher):
    base_maps = host_in_maps(cfg, inputs)
    n = cfg.n_cores
    nc1 = build_program(cfg, "p1")
    res1 = launcher(nc1, base_maps)
    ccg = np.ascontiguousarray(np.concatenate(
        [np.asarray(r["cc_stage_out"], np.float32) for r in res1], axis=0))
    maps2 = []
    for i, m in enumerate(base_maps):
        m2 = dict(m)
        for t in MID_TENSORS:
            m2[t] = np.asarray(res1[i][t])
        m2["cc_gathered"] = ccg
        maps2.append(m2)
    nc2 = build_program(cfg, "p2")
    res2 = launcher(nc2, maps2)
    out = np.concatenate([np.asarray(r["out"], np.float32) for r in res2], axis=0)
    return out


def kernel(**inputs):
    cfg = CFG()

    def launcher(nc, in_maps):
        res = run_bass_kernel_spmd(nc, in_maps, core_ids=list(range(cfg.n_cores)))
        return res.results
    out = run_pipeline(cfg, inputs, launcher)
    return out.reshape(1, cfg.L, cfg.D)


# version 13
# speedup vs baseline: 1.7149x; 1.0492x over previous
import numpy as np
from contextlib import ExitStack

import concourse.bass as bass
import concourse.mybir as mybir
from concourse.alu_op_type import AluOpType as OP
from concourse.bass_utils import run_bass_kernel_spmd

F32 = mybir.dt.float32
BF16 = mybir.dt.bfloat16
AF = mybir.ActivationFunctionType
AX = mybir.AxisListType


class CFG:
    n_cores = 8
    L = 16384
    D = 1024
    H = 16
    dh = 64
    C = 128
    GS = 512
    EPS = 1e-6
    LN_EPS = 1e-5
    debug = False
    trunc = None

    @property
    def R(self): return self.L // self.n_cores
    @property
    def KT(self): return self.D // 128
    @property
    def NG(self): return self.R // self.GS
    @property
    def NC(self): return self.R // self.C


DMA_STREAMS = ("sp", "aq", "gq", "cc")
STREAM_SECTION = {
    "tensor": "tensor", "vector": "vector", "scalar": "scalar",
    "gpsimd": "gpsimd", "sync": "sync",
    "sp": "sync", "aq": "scalar", "gq": "gpsimd", "cc": "gpsimd",
}


class Pl:

    def __init__(self):
        self.items = []
        self.barrier_snap = {}
        self.ordn = {}

    def add(self, stream, fn, reads=(), writes=(), inc=None):
        if inc is None:
            inc = 16 if stream in DMA_STREAMS else 1
        if stream in DMA_STREAMS and stream != "cc":
            o = self.ordn.get(stream, 0)
            self.ordn[stream] = o + 1
            semid = f"{stream}{o % self.NROT}"
        else:
            semid = stream
        self.items.append(dict(s=stream, fn=fn, r=tuple(reads), w=tuple(writes),
                               inc=inc, deps=set(), bar=dict(self.barrier_snap),
                               semid=semid))

    def barrier(self):
        snap = {}
        for i, it in enumerate(self.items):
            snap[it["semid"]] = i
        self.barrier_snap = snap

    NROT = 4

    marks = []

    def mark(self, name):
        self.marks.append((name, len(self.items)))

    def finalize(self, truncate=None, skip=()):
        if truncate is not None:
            self.items = self.items[:truncate]
        if skip:
            self.items = [it for i, it in enumerate(self.items) if i not in skip]
        lastuse = {}
        lastw, readers = {}, {}
        last_psum_rd = {}
        for i, it in enumerate(self.items):
            dep = set(it["bar"].values())
            for r in it["r"]:
                if r in lastw:
                    dep.add(lastw[r])
                if r.startswith("PSB"):
                    pr = last_psum_rd.get(r)
                    if pr is not None and pr[1] != it["s"]:
                        dep.add(pr[0])
                    last_psum_rd[r] = (i, it["s"])
            for w in it["w"]:
                if w in lastw:
                    dep.add(lastw[w])
                for rd in readers.get(w, ()):
                    dep.add(rd)
            for r in it["r"]:
                readers.setdefault(r, []).append(i)
            for w in it["w"]:
                lastw[w] = i
                readers[w] = []
                if w.startswith("PSB"):
                    last_psum_rd.pop(w, None)
            if it["s"] in DMA_STREAMS:
                if it["semid"] in lastuse:
                    dep.add(lastuse[it["semid"]])
                lastuse[it["semid"]] = i
            it["deps"] = {d for d in dep
                          if not (self.items[d]["s"] == it["s"] and it["s"] == "tensor")}
        need = [False] * len(self.items)
        for i, it in enumerate(self.items):
            if it["s"] in DMA_STREAMS:
                need[i] = True
            for d in it["deps"]:
                need[d] = True
        cnt = {}
        self.after = [0] * len(self.items)
        self.neednc = need
        for i, it in enumerate(self.items):
            c = cnt.get(it["semid"], 0)
            if need[i]:
                c += it["inc"]
            cnt[it["semid"]] = c
            self.after[i] = c

    def emit(self, nc, sems):
        by_section = {}
        for i, it in enumerate(self.items):
            by_section.setdefault(STREAM_SECTION[it["s"]], []).append(i)
        with nc.Block() as block:
            for section in ("sync", "scalar", "vector", "tensor", "gpsimd"):
                idxs = by_section.get(section, [])
                dec = getattr(block, section)

                def make(idxs=idxs):
                    def body(eng):
                        waited = {}
                        for i in idxs:
                            it = self.items[i]
                            wmax = {}
                            for d in it["deps"]:
                                ds = self.items[d]["semid"]
                                v = self.after[d]
                                if v > wmax.get(ds, 0):
                                    wmax[ds] = v
                            for ds, v in sorted(wmax.items()):
                                if waited.get(ds, 0) < v:
                                    eng.wait_ge(sems[ds], v)
                                    waited[ds] = v
                            inst = it["fn"](eng)
                            if self.neednc[i]:
                                inst.then_inc(sems[it["semid"]], it["inc"])
                    return body
                dec(make())


class _Rec:

    def __init__(self):
        self.calls = []

    def add(self, *a, **k):
        self.calls.append((a, k))


def _merge_rr(P, recs):
    pos = [0] * len(recs)
    while True:
        any_left = False
        for i, r in enumerate(recs):
            if pos[i] < len(r.calls):
                a, k = r.calls[pos[i]]
                P.add(*a, **k)
                pos[i] += 1
                any_left = True
        if not any_left:
            break


def dup2(ap64):
    return ap64.rearrange("p (o d) -> p o d", o=1).broadcast_to([ap64.shape[0], 2, 64])


def build_program(cfg: CFG, part: str = "p1"):
    R, D, H, dh, C, GS = cfg.R, cfg.D, cfg.H, cfg.dh, cfg.C, cfg.GS
    KT, NG, NC = cfg.KT, cfg.NG, cfg.NC
    NLEV = 6
    assert C == 2 ** (NLEV + 1)
    n_cores = cfg.n_cores
    NT = R // 128

    nc = bass.Bass("TRN2", target_bir_lowering=False)
    P = Pl()

    dt = lambda name, shape, dtype, kind: nc.dram_tensor(name, shape, dtype, kind=kind).ap()
    x_tok_d = dt("x_tok", [R, D], F32, "ExternalInput")
    x_t_d = dt("x_t", [D, R + 3], F32, "ExternalInput")
    x_halo_d = dt("x_halo", [3, D], F32, "ExternalInput")
    WT_d = {p: dt(f"W{p}T", [D, D], F32, "ExternalInput") for p in "qkvo"}
    WbT_d = dt("WbT", [D, H], F32, "ExternalInput")
    conv_d = {p: dt(f"conv_{p}", [D, 4], F32, "ExternalInput") for p in "qkv"}
    onw_d = dt("onw128", [128, 1], F32, "ExternalInput")
    bias_d = dt("biasrow", [1, D], F32, "ExternalInput")
    E1_d = {ri: dt(f"E1{ri}", [4, KT // 2, 128, 512], F32, "ExternalInput") for ri in "RI"}
    E2_d = {ri: dt(f"E2{ri}", [4, 128, 512], F32, "ExternalInput") for ri in "RI"}
    mask_sl_d = dt("mask_sl", [128, 128], F32, "ExternalInput")
    mask_sun_d = dt("mask_sun", [128, 128], F32, "ExternalInput")
    mask_sui_d = dt("mask_sui", [128, 128], F32, "ExternalInput")
    ident_f_d = dt("ident_f", [128, 128], F32, "ExternalInput")
    ident_b_d = dt("ident_b", [128, 128], BF16, "ExternalInput")
    ident64_d = dt("ident64_2", [128, 64], BF16, "ExternalInput")
    ones1_d = dt("ones1", [1, 128], F32, "ExternalInput")
    ppinit_d = dt("ppinit", [64, H * 64], BF16, "ExternalInput")
    onehot_d = dt("onehot", [1, n_cores], F32, "ExternalInput")
    ind_d = dt("ind", [128, KT * H], F32, "ExternalInput")
    indT_d = dt("indT", [H, KT * 128], F32, "ExternalInput")

    out_d = dt("out", [R, D], F32, "ExternalOutput") if part != "p1" else None

    mid_kind = {"full": "Internal", "p1": "ExternalOutput", "p2": "ExternalInput"}[part]
    if part != "p2":
        krm_d = dt("krm_dram", [R, 2 * D], BF16, "Internal")
        vrm_d = dt("vrm_dram", [R, D], BF16, "Internal")
    o0_d = dt("o0_dram", [R, D], BF16, mid_kind)
    qt_d = dt("qt_dram", [128, NC * KT * 128], BF16, mid_kind)
    pt_d = dt("pt_dram", [64, NC * H * 64], BF16, mid_kind)
    msrow_d = dt("msrow_dram", [2, R + 3], F32, mid_kind)
    if part == "full":
        cc_in_d = dt("cc_in", [64, H * 128], F32, "Internal")
        cc_out_d = dt("cc_out", [n_cores * 64, H * 128], F32, "Internal")
    elif part == "p1":
        cc_out1_d = dt("cc_stage_out", [64, H * 128], F32, "ExternalOutput")
    else:
        cc_out_d = dt("cc_gathered", [n_cores * 64, H * 128], F32, "ExternalInput")
    dbg_d = dt("dbg", [128, 32 * 128], F32, "Internal") if cfg.debug else None
    def dbgtap(slot, ap, readkeys):
        if cfg.debug:
            P.add("gq", lambda e, slot=slot, ap=ap: e.dma_start(
                out=dbg_d[:ap.shape[0], slot * 128: slot * 128 + ap.shape[1]], in_=ap),
                reads=tuple(readkeys), writes=("dbg",))

    st = ExitStack()
    sb = lambda name, shape, dtype: st.enter_context(nc.sbuf_tensor(name, shape, dtype))
    ps_ = lambda name, shape, dtype: st.enter_context(nc.psum_tensor(name, shape, dtype))

    sem_names = ["tensor", "vector", "scalar", "gpsimd", "sync", "cc"]
    for s in ("sp", "aq", "gq"):
        sem_names += [f"{s}{r}" for r in range(Pl.NROT)]
    sems = {s: st.enter_context(nc.semaphore(f"sem_{s}")) for s in sem_names}

    psb = [ps_(f"psb{i}", [128, 512], F32) for i in range(6)]
    psT0 = ps_("psT0", [128, 512], BF16)
    psT1 = ps_("psT1", [128, 512], BF16)

    mask_sl = sb("mask_sl_s", [128, 128], F32)
    mask_sun = sb("mask_sun_s", [128, 128], F32)
    mask_sui = sb("mask_sui_s", [128, 128], F32)
    ident_f = sb("ident_f_s", [128, 128], F32)
    ident_b = sb("ident_b_s", [128, 128], BF16)
    ident64 = sb("ident64_s", [128, 64], BF16)
    ones1 = sb("ones1_s", [1, 128], F32)
    onw = sb("onw_s", [128, 1], F32)
    biasrow = sb("biasrow_s", [1, D], F32)
    onehot = sb("onehot_s", [1, n_cores], F32)
    ind = sb("ind_s", [128, KT * H], F32)
    indT = sb("indT_s", [H, KT * 128], F32)
    S_f = sb("S_f", [128, H * 64], F32)
    S_b = sb("S_b", [128, H * 64], BF16)
    PP_b = sb("PP_b", [64, H * 64], BF16)
    N_f = sb("N_f", [128, H * 64], F32)
    N_b = sb("N_b", [128, H * 64], BF16)
    epsln = sb("epsln", [128, 1], F32)
    epsrms = sb("epsrms", [128, 1], F32)

    def ld(dst, src, key, stream="sp"):
        P.add(stream, lambda e, d=dst, s=src: e.dma_start(out=d, in_=s), writes=(key,))

    ld(mask_sl[:], mask_sl_d[:, :], "mask_sl")
    ld(mask_sun[:], mask_sun_d[:, :], "mask_sun")
    ld(mask_sui[:], mask_sui_d[:, :], "mask_sui")
    ld(ident_f[:], ident_f_d[:, :], "ident_f")
    ld(ident_b[:], ident_b_d[:, :], "ident_b")
    ld(ident64[:], ident64_d[:, :], "ident64")
    ld(ones1[:], ones1_d[:, :], "ones1")
    ld(onw[:], onw_d[:, :], "onw")
    ld(biasrow[:], bias_d[:, :], "biasrow")
    ld(onehot[:], onehot_d[:, :], "onehot")
    ld(ind[:], ind_d[:, :], "ind")
    ld(indT[:], indT_d[:, :], "indT")
    P.add("sp", lambda e: e.dma_start(out=PP_b[:], in_=ppinit_d[:, :]), writes=tuple(f"PP_b:{h}" for h in range(H)))
    P.add("gpsimd", lambda e: e.memset(epsln[:], cfg.LN_EPS), writes=("epsln",))
    P.add("gpsimd", lambda e: e.memset(epsrms[:], cfg.EPS), writes=("epsrms",))
    P.add("gpsimd", lambda e: e.memset(S_f[:], 0.0), writes=tuple(f"S_f:{h}" for h in range(H)))
    P.add("gpsimd", lambda e: e.memset(S_b[:], 0.0), writes=tuple(f"S_b:{h}" for h in range(H)))

    run_p1 = part in ("full", "p1")
    run_p2 = part in ("full", "p2")
    stA0 = ExitStack()
    if run_p1:
        sb0 = lambda name, shape, dtype: stA0.enter_context(nc.sbuf_tensor(name, shape, dtype))
        stat_pack = sb0("stat_pack", [128, 2 * NT], F32)
        stat_sum = sb0("stat_sum", [128, NT], F32)
        stat_ssq = sb0("stat_ssq", [128, NT], F32)
        stat_tmp = sb0("stat_tmp", [128, NT], F32)
        evac32 = sb0("evac32", [2 * NT, 128], F32)
        xa = [sb0(f"xa{i}", [128, D], F32) for i in range(4)]
        xsq = sb0("xsq", [128, D], F32)
        for t in range(NT):
            b = t % 4
            P.add("sp", lambda e, b=b, t=t: e.dma_start(out=xa[b][:], in_=x_tok_d[t * 128:(t + 1) * 128, :]),
                  writes=(f"xa{b}",))
            P.add("vector", lambda e, b=b, t=t: e.reduce_sum(stat_sum[:, t:t + 1], xa[b][:], axis=AX.X),
                  reads=(f"xa{b}",), writes=("stat_sum",))
            P.add("scalar", lambda e, b=b, t=t: e.activation(xsq[:], xa[b][:], AF.Square, accum_out=stat_ssq[:, t:t + 1]),
                  reads=(f"xa{b}",), writes=("stat_ssq", "xsq"))
        xh = sb0("xh", [3, D], F32)
        xhsq = sb0("xhsq", [3, D], F32)
        halo_sum = sb0("halo_sum", [3, 1], F32)
        halo_var = sb0("halo_var", [3, 1], F32)
        halo_m2 = sb0("halo_m2", [3, 1], F32)
        halo_s = sb0("halo_s", [3, 2], F32)
        P.add("sp", lambda e: e.dma_start(out=xh[:], in_=x_halo_d[:, :]), writes=("xh",))
        P.add("vector", lambda e: e.reduce_sum(halo_sum[:], xh[:], axis=AX.X),
              reads=("xh",), writes=("halo_sum",))
        P.add("scalar", lambda e: e.activation(xhsq[:], xh[:], AF.Square, accum_out=halo_var[:]),
              reads=("xh",), writes=("halo_var", "xhsq"))
        P.add("vector", lambda e: e.tensor_scalar(stat_sum[:], stat_sum[:], 1.0 / D, 0.0, op0=OP.mult, op1=OP.add),
              reads=("stat_sum",), writes=("stat_sum",))
        P.add("vector", lambda e: e.scalar_tensor_tensor(stat_tmp[:], stat_sum[:], 1.0, stat_sum[:], op0=OP.mult, op1=OP.mult),
              reads=("stat_sum",), writes=("stat_tmp",))
        P.add("vector", lambda e: e.scalar_tensor_tensor(stat_ssq[:], stat_ssq[:], 1.0 / D, stat_tmp[:], op0=OP.mult, op1=OP.subtract),
              reads=("stat_ssq", "stat_tmp"), writes=("stat_ssq",))
        P.add("scalar", lambda e: e.activation(stat_ssq[:], stat_ssq[:], AF.Sqrt, bias=epsln[:]),
              reads=("stat_ssq", "epsln"), writes=("stat_ssq",))
        P.add("vector", lambda e: e.reciprocal(stat_pack[:, NT:2 * NT], stat_ssq[:]),
              reads=("stat_ssq",), writes=("stat_pack",))
        P.add("vector", lambda e: e.scalar_tensor_tensor(stat_pack[:, 0:NT], stat_sum[:], -1.0, stat_pack[:, NT:2 * NT], op0=OP.mult, op1=OP.mult),
              reads=("stat_sum", "stat_pack"), writes=("stat_pack",))
        P.add("vector", lambda e: e.tensor_scalar(halo_sum[:], halo_sum[:], 1.0 / D, 0.0, op0=OP.mult, op1=OP.add),
              reads=("halo_sum",), writes=("halo_sum",))
        P.add("vector", lambda e: e.scalar_tensor_tensor(halo_m2[:], halo_sum[:], 1.0, halo_sum[:], op0=OP.mult, op1=OP.mult),
              reads=("halo_sum",), writes=("halo_m2",))
        P.add("vector", lambda e: e.scalar_tensor_tensor(halo_var[:], halo_var[:], 1.0 / D, halo_m2[:], op0=OP.mult, op1=OP.subtract),
              reads=("halo_var", "halo_m2"), writes=("halo_var",))
        P.add("scalar", lambda e: e.activation(halo_var[:], halo_var[:], AF.Sqrt, bias=epsln[0:3, :]),
              reads=("halo_var", "epsln"), writes=("halo_var",))
        P.add("vector", lambda e: e.reciprocal(halo_s[:, 1:2], halo_var[:]),
              reads=("halo_var",), writes=("halo_s",))
        P.add("vector", lambda e: e.scalar_tensor_tensor(halo_s[:, 0:1], halo_sum[:], -1.0, halo_s[:, 1:2], op0=OP.mult, op1=OP.mult),
              reads=("halo_sum", "halo_s"), writes=("halo_s",))
        P.add("tensor", lambda e: e.transpose(psb[0][0:2 * NT, 0:128], stat_pack[:], ident_f[:]),
              reads=("stat_pack", "ident_f"), writes=("PSB0",))
        P.add("vector", lambda e: e.tensor_copy(evac32[:], psb[0][0:2 * NT, 0:128]),
              reads=("PSB0",), writes=("evac32",))
        P.add("aq", lambda e: e.dma_start(out=msrow_d[0:1, 3:3 + NT * 128].rearrange("o (t p) -> (o t) p", t=NT),
                                          in_=evac32[0:NT, :]),
              reads=("evac32",), writes=("msrow_d",))
        P.add("aq", lambda e: e.dma_start(out=msrow_d[1:2, 3:3 + NT * 128].rearrange("o (t p) -> (o t) p", t=NT),
                                          in_=evac32[NT:2 * NT, :]),
              reads=("evac32",), writes=("msrow_d",))
        P.add("aq", lambda e: e.dma_start(out=msrow_d[0:1, 0:3].rearrange("o t -> t o"), in_=halo_s[:, 0:1]),
              reads=("halo_s",), writes=("msrow_d",))
        P.add("aq", lambda e: e.dma_start(out=msrow_d[1:2, 0:3].rearrange("o t -> t o"), in_=halo_s[:, 1:2]),
              reads=("halo_s",), writes=("msrow_d",))
        P.barrier()
        stA0.close()

        P.mark("A0_done")
        stABC = ExitStack()
        sbB = lambda name, shape, dtype: stABC.enter_context(nc.sbuf_tensor(name, shape, dtype))
        qtf = sbB("qtf", [128, KT * R], BF16)
        ktf = sbB("ktf", [128, KT * R], BF16)
        vtf = sbB("vtf", [128, KT * R], BF16)
        RES = {"q": qtf, "k": ktf, "v": vtf}

        stA = ExitStack()
        sbA = lambda name, shape, dtype: stA.enter_context(nc.sbuf_tensor(name, shape, dtype))
        wbuf = [sbA(f"wbuf{i}", [128, D], BF16) for i in range(4)]
        pbuf = [sbA(f"pbuf{i}", [128, GS + 3], BF16) for i in range(KT)]
        tails = sbA("tails", [128, 3 * 3 * KT], BF16)
        xnf = sbA("xnf", [128, KT * GS], BF16)
        xnf_h = sbA("xnf_h", [128, KT * 3], BF16)
        xtg = sbA("xtg", [128, KT * GS], BF16)
        xtg_h = sbA("xtg_h", [128, KT * 3], BF16)
        msbc = sbA("msbc", [128, GS], BF16)
        sbc = sbA("sbc", [128, GS], BF16)
        msbc_h = sbA("msbc_h", [128, 3], BF16)
        sbc_h = sbA("sbc_h", [128, 3], BF16)
        conv_sb = sbA("conv_sb", [128, KT * 4 * 3], F32)
        ybuf = [sbA(f"ybuf{i}", [128, GS], BF16) for i in range(2)]
        WbT_sb = sbA("WbT_sb", [128, KT * H], BF16)
        msg = sbA("msg", [1, GS], F32)
        ssg = sbA("ssg", [1, GS], F32)
        msg_h = sbA("msg_h", [1, 3], F32)
        ssg_h = sbA("ssg_h", [1, 3], F32)
        beta_hm = sbA("beta_hm", [H, R], F32)
        scale_q = sbA("scale_q", [H, R], F32)
        scale_k = sbA("scale_k", [H, R], F32)
    
        for pi, p in enumerate("qkv"):
            P.add("gq", lambda e, pi=pi, p=p: e.dma_start(
                out=conv_sb[:, pi * KT * 4:(pi + 1) * KT * 4].rearrange("p (a k) -> p a k", k=4),
                in_=conv_d[p].rearrange("(a p) k -> p a k", p=128)),
                writes=("conv_sb",))
        P.add("gq", lambda e: e.dma_start(out=WbT_sb[:].rearrange("p (a h) -> p a h", a=KT),
                                          in_=WbT_d.rearrange("(a p) h -> p a h", p=128)),
              writes=("WbT_sb",))

        def emit_xnf(grp, dst, dkey, xsrc, xkey, mb, sbuf_b, with_halo, dst_h=None, hx=None, mbh=None, sbh=None,
                     mrow=None, srw=None, mrow_h=None, srw_h=None):
            c0 = 3 + grp * GS
            for t in range(KT):
                P.add("gq", lambda e, t=t, c0=c0, xsrc=xsrc: e.dma_start(
                    out=xsrc[:, t * GS:(t + 1) * GS], in_=x_t_d[t * 128:(t + 1) * 128, c0:c0 + GS]),
                    writes=(f"{xkey}:{t}",))
            P.add("sp", lambda e, c0=c0, mrow=mrow: e.dma_start(out=mrow[:], in_=msrow_d[0:1, c0:c0 + GS]),
                  reads=("msrow_d",), writes=(f"{dkey}_mrow",))
            P.add("sp", lambda e, c0=c0, srw=srw: e.dma_start(out=srw[:], in_=msrow_d[1:2, c0:c0 + GS]),
                  reads=("msrow_d",), writes=(f"{dkey}_srow",))
            P.add("tensor", lambda e, mrow=mrow: e.matmul(psb[3][:, 0:GS], ones1[:], mrow[:], start=True, stop=True),
                  reads=("ones1", f"{dkey}_mrow"), writes=("PSB3",))
            P.add("tensor", lambda e, srw=srw: e.matmul(psb[4][:, 0:GS], ones1[:], srw[:], start=True, stop=True),
                  reads=("ones1", f"{dkey}_srow"), writes=("PSB4",))
            P.add("scalar", lambda e, mb=mb: e.copy(mb[:], psb[3][:, 0:GS]), reads=("PSB3",), writes=(f"{dkey}_mb",))
            P.add("scalar", lambda e, sbuf_b=sbuf_b: e.copy(sbuf_b[:], psb[4][:, 0:GS]), reads=("PSB4",), writes=(f"{dkey}_sb",))
            if with_halo:
                for t in range(KT):
                    P.add("gq", lambda e, t=t, hx=hx: e.dma_start(
                        out=hx[:, t * 3:(t + 1) * 3], in_=x_t_d[t * 128:(t + 1) * 128, 0:3]),
                        writes=(f"xtg_h:{t}",))
                P.add("sp", lambda e, mrow_h=mrow_h: e.dma_start(out=mrow_h[:], in_=msrow_d[0:1, 0:3]),
                      reads=("msrow_d",), writes=("mrow_h",))
                P.add("sp", lambda e, srw_h=srw_h: e.dma_start(out=srw_h[:], in_=msrow_d[1:2, 0:3]),
                      reads=("msrow_d",), writes=("srow_h",))
                P.add("tensor", lambda e, mrow_h=mrow_h: e.matmul(psb[3][:, 0:3], ones1[:], mrow_h[:], start=True, stop=True),
                      reads=("ones1", "mrow_h"), writes=("PSB3",))
                P.add("tensor", lambda e, srw_h=srw_h: e.matmul(psb[4][:, 0:3], ones1[:], srw_h[:], start=True, stop=True),
                      reads=("ones1", "srow_h"), writes=("PSB4",))
                P.add("scalar", lambda e, mbh=mbh: e.copy(mbh[:], psb[3][:, 0:3]), reads=("PSB3",), writes=("msbc_h",))
                P.add("scalar", lambda e, sbh=sbh: e.copy(sbh[:], psb[4][:, 0:3]), reads=("PSB4",), writes=("sbc_h",))
            for t in range(KT):
                P.add("vector", lambda e, t=t, dst=dst, xsrc=xsrc, mb=mb: e.tensor_tensor(
                    dst[:, t * GS:(t + 1) * GS], xsrc[:, t * GS:(t + 1) * GS], mb[:], op=OP.add),
                    reads=(f"{xkey}:{t}", f"{dkey}_mb"), writes=(f"{dkey}:{t}",))
                P.add("vector", lambda e, t=t, dst=dst, sbuf_b=sbuf_b: e.tensor_tensor(
                    dst[:, t * GS:(t + 1) * GS], dst[:, t * GS:(t + 1) * GS], sbuf_b[:], op=OP.mult),
                    reads=(f"{dkey}:{t}", f"{dkey}_sb"), writes=(f"{dkey}:{t}",))
                if with_halo:
                    P.add("vector", lambda e, t=t, dst_h=dst_h, hx=hx, mbh=mbh: e.tensor_tensor(
                        dst_h[:, t * 3:(t + 1) * 3], hx[:, t * 3:(t + 1) * 3], mbh[:], op=OP.add),
                        reads=(f"xtg_h:{t}", "msbc_h"), writes=(f"xnf_h:{t}",))
                    P.add("vector", lambda e, t=t, dst_h=dst_h, sbh=sbh: e.tensor_tensor(
                        dst_h[:, t * 3:(t + 1) * 3], dst_h[:, t * 3:(t + 1) * 3], sbh[:], op=OP.mult),
                        reads=(f"xnf_h:{t}", "sbc_h"), writes=(f"xnf_h:{t}",))

        for g in range(NG):
            emit_xnf(g, xnf, "xnf", xtg, "xtg", msbc, sbc, g == 0, xnf_h, xtg_h, msbc_h, sbc_h,
                     mrow=msg, srw=ssg, mrow_h=msg_h, srw_h=ssg_h)
            for t in range(KT):
                P.add("tensor", lambda e, t=t: e.matmul(
                    psb[2][0:H, 0:GS], WbT_sb[:, t * H:(t + 1) * H], xnf[:, t * GS:(t + 1) * GS],
                    start=(t == 0), stop=(t == KT - 1)),
                    reads=("WbT_sb", f"xnf:{t}"), writes=("PSB2",))
            P.add("scalar", lambda e, g=g: e.activation(beta_hm[:, g * GS:(g + 1) * GS], psb[2][0:H, 0:GS], AF.Tanh, scale=0.5),
                  reads=("PSB2",), writes=("beta_hm",))
            P.add("vector", lambda e, g=g: e.tensor_scalar(
                beta_hm[:, g * GS:(g + 1) * GS], beta_hm[:, g * GS:(g + 1) * GS], 0.5, 0.5, op0=OP.mult, op1=OP.add),
                reads=("beta_hm",), writes=("beta_hm",))
            for pi, p in enumerate("qkv"):
                for o0_ in range(0, KT, 2):
                  orecs = []
                  for o in range(o0_, o0_ + 2):
                    PX = _Rec()
                    orecs.append(PX)
                    bk = o % 2
                    wb = (pi * KT + o) % 4
                    PX.add("gq", lambda e, p=p, o=o, wb=wb: e.dma_start(
                        out=wbuf[wb][:].rearrange("p (a n) -> p a n", a=KT),
                        in_=WT_d[p][:, o * 128:(o + 1) * 128].rearrange("(a p) n -> p a n", p=128)),
                        writes=(f"wbuf{wb}",))
                    for k in range(KT):
                        PX.add("tensor", lambda e, k=k, wb=wb, bk=bk: e.matmul(
                            psb[bk][:, 0:GS], wbuf[wb][:, k * 128:(k + 1) * 128], xnf[:, k * GS:(k + 1) * GS],
                            start=(k == 0), stop=(k == KT - 1)),
                            reads=(f"wbuf{wb}", f"xnf:{k}"), writes=(f"PSB{bk}",))
                    if g == 0:
                        hb = 5 if bk == 0 else 3
                        for k in range(KT):
                            PX.add("tensor", lambda e, k=k, wb=wb, hb=hb: e.matmul(
                                psb[hb][:, 0:3], wbuf[wb][:, k * 128:(k + 1) * 128], xnf_h[:, k * 3:(k + 1) * 3],
                                start=(k == 0), stop=(k == KT - 1)),
                                reads=(f"wbuf{wb}", f"xnf_h:{k}"), writes=(f"PSB{hb}",))
                        PX.add("scalar", lambda e, o=o, hb=hb: e.copy(pbuf[o][:, 0:3], psb[hb][:, 0:3]),
                              reads=(f"PSB{hb}",), writes=(f"pbuf:{o}",))
                    else:
                        PX.add("gpsimd", lambda e, pi=pi, o=o: e.tensor_copy(
                            pbuf[o][:, 0:3], tails[:, (pi * KT + o) * 3:(pi * KT + o) * 3 + 3]),
                            reads=(f"tails:{pi}:{o}",), writes=(f"pbuf:{o}",))
                    PX.add("scalar", lambda e, o=o, bk=bk: e.copy(pbuf[o][:, 3:3 + GS], psb[bk][:, 0:GS]),
                          reads=(f"PSB{bk}",), writes=(f"pbuf:{o}",))
                    PX.add("gpsimd", lambda e, pi=pi, o=o: e.tensor_copy(
                        tails[:, (pi * KT + o) * 3:(pi * KT + o) * 3 + 3], pbuf[o][:, GS:GS + 3]),
                        reads=(f"pbuf:{o}",), writes=(f"tails:{pi}:{o}",))
                    cwbase = pi * KT * 4 + o * 4
                    PX.add("vector", lambda e, o=o, cwbase=cwbase: e.tensor_scalar(
                        ybuf[o % 2][:], pbuf[o][:, 0:GS], conv_sb[:, cwbase:cwbase + 1], 0.0, op0=OP.mult, op1=OP.add),
                        reads=(f"pbuf:{o}", "conv_sb"), writes=(f"ybuf{o % 2}",))
                    for j in (1, 2, 3):
                        PX.add("vector", lambda e, o=o, j=j, cwbase=cwbase: e.scalar_tensor_tensor(
                            ybuf[o % 2][:], pbuf[o][:, j:j + GS], conv_sb[:, cwbase + j:cwbase + j + 1], ybuf[o % 2][:],
                            op0=OP.mult, op1=OP.add),
                            reads=(f"pbuf:{o}", "conv_sb", f"ybuf{o % 2}"), writes=(f"ybuf{o % 2}",))
                    PX.add("scalar", lambda e, p=p, o=o, g=g: e.activation(
                        RES[p][:, o * R + g * GS: o * R + g * GS + GS], ybuf[o % 2][:], AF.Silu),
                        reads=(f"ybuf{o % 2}",), writes=(f"res:{p}:{o}",))
                  _merge_rr(P, orecs)
        P.barrier()

        P.mark("A_done")
        sqs = [sbA(f"sqs{i}", [128, GS], F32) for i in range(2)]
        sqb = sbA("sqb", [H, R], F32)
        scbc = sbA("scbc", [128, GS], BF16)
        for g in range(NG):
            for p in ("q", "k"):
                scale = scale_q if p == "q" else scale_k
                res = RES[p]
                for t in range(KT):
                    sq = sqs[t % 2]
                    P.add("scalar", lambda e, res=res, t=t, g=g, sq=sq, p=p: e.activation(
                        sq[:], res[:, t * R + g * GS: t * R + g * GS + GS], AF.Square),
                        reads=(f"res:{p}:{t}",), writes=(f"sqs{t % 2}",))
                    P.add("tensor", lambda e, t=t, sq=sq: e.matmul(
                        psb[2][0:H, 0:GS], ind[:, t * H:(t + 1) * H], sq[:],
                        start=(t == 0), stop=(t == KT - 1)),
                        reads=("ind", f"sqs{t % 2}"), writes=("PSB2",))
                P.add("vector", lambda e, scale=scale, g=g, p=p: e.tensor_copy(
                    scale[:, g * GS:(g + 1) * GS], psb[2][0:H, 0:GS]),
                    reads=("PSB2",), writes=(f"scale_{p}",))
        P.add("scalar", lambda e: e.activation(sqb[:], beta_hm[:], AF.Sqrt), reads=("beta_hm",), writes=("sqb",))
        for p in ("q", "k"):
            scale = scale_q if p == "q" else scale_k
            P.add("scalar", lambda e, scale=scale: e.activation(scale[:], scale[:], AF.Sqrt),
                  reads=(f"scale_{p}",), writes=(f"scale_{p}",))
            P.add("vector", lambda e, scale=scale: e.tensor_scalar(scale[:], scale[:], cfg.EPS, 0.0, op0=OP.max, op1=OP.add),
                  reads=(f"scale_{p}",), writes=(f"scale_{p}",))
            P.add("vector", lambda e, scale=scale: e.reciprocal(scale[:], scale[:]),
                  reads=(f"scale_{p}",), writes=(f"scale_{p}",))
        P.add("vector", lambda e: e.tensor_scalar(scale_q[:], scale_q[:], 0.125, 0.0, op0=OP.mult, op1=OP.add),
              reads=("scale_q",), writes=("scale_q",))
        P.add("vector", lambda e: e.tensor_tensor(scale_k[:], scale_k[:], sqb[:], op=OP.mult),
              reads=("scale_k", "sqb"), writes=("scale_k",))
        for g in range(NG):
            for p in ("q", "k", "v"):
                scale = {"q": scale_q, "k": scale_k, "v": sqb}[p]
                res = RES[p]
                for t in range(KT):
                    bb = 3 + (t % 2)
                    P.add("tensor", lambda e, scale=scale, t=t, g=g, bb=bb, p=p: e.matmul(
                        psb[bb][:, 0:GS], indT[:, t * 128:(t + 1) * 128], scale[:, g * GS:(g + 1) * GS],
                        start=True, stop=True),
                        reads=("indT", "sqb" if p == "v" else f"scale_{p}"), writes=(f"PSB{bb}",))
                    P.add("scalar", lambda e, bb=bb: e.copy(scbc[:], psb[bb][:, 0:GS]),
                          reads=(f"PSB{bb}",), writes=("scbc",))
                    P.add("vector", lambda e, res=res, t=t, g=g, p=p: e.tensor_tensor(
                        res[:, t * R + g * GS: t * R + g * GS + GS],
                        res[:, t * R + g * GS: t * R + g * GS + GS], scbc[:], op=OP.mult),
                        reads=(f"res:{p}:{t}", "scbc"), writes=(f"res:{p}:{t}",))
        P.barrier()
        stA.close()

        P.mark("B_done")
        stC = ExitStack()
        sbC = lambda name, shape, dtype: stC.enter_context(nc.sbuf_tensor(name, shape, dtype))
        stg = [sbC(f"stg{i}", [128, D], BF16) for i in range(2)]
        for nm in ("k", "v"):
            src = RES[nm]
            dram = krm_d if nm == "k" else vrm_d
            for c in range(NC):
                par = c % 2
                for t in range(KT):
                    pt = t % 2
                    pst = psT0 if pt == 0 else psT1
                    pstk = "PSB6" if pt == 0 else "PSB7"
                    P.add("tensor", lambda e, src=src, c=c, t=t, pst=pst: e.transpose(
                        pst[:, 0:128], src[:, t * R + c * 128: t * R + c * 128 + 128], ident_b[:]),
                        reads=(f"res:{nm}:{t}", "ident_b"), writes=(pstk,))
                    P.add("scalar", lambda e, t=t, par=par, pst=pst: e.copy(
                        stg[par][:, t * 128:(t + 1) * 128], pst[:, 0:128]),
                        reads=(pstk,), writes=(f"stg{par}",))
                if nm == "v":
                    P.add("aq", lambda e, dram=dram, c=c, par=par: e.dma_start(
                        out=dram[c * 128:(c + 1) * 128, :], in_=stg[par][:]),
                        reads=(f"stg{par}",), writes=(f"{nm}rmd:{c}",))
                else:
                    for dup in range(2):
                        P.add("aq", lambda e, dram=dram, c=c, par=par, dup=dup: e.dma_start(
                            out=dram[c * 128:(c + 1) * 128, :].rearrange("p (h u d) -> p h u d", h=H, u=2)[:, :, dup],
                            in_=stg[par][:].rearrange("p (h d) -> p h d", h=H)),
                            reads=(f"stg{par}",), writes=(f"{nm}rmd:{c}:{dup}",))
        P.mark("B2_done")
        krm = [sbC(f"krm{i}", [128, 2 * D], BF16) for i in range(2)]
        vrm = [sbC(f"vrm{i}", [128, D], BF16) for i in range(2)]
        o0stg = [sbC(f"o0stg{i}", [128, H * 64], BF16) for i in range(2)]
        qtstg = [sbC(f"qtstg{i}", [128, KT * 128], BF16) for i in range(2)]
        ptstg = [sbC(f"ptstg{i}", [64, H * 64], BF16) for i in range(2)]

        GsL = [sbC(f"GsL{i}", [128, 320], F32) for i in range(3)]
        A_sb = [sbC(f"A_sb{i}", [128, 128], BF16) for i in range(3)]
        AT_sb = [sbC(f"AT_sb{i}", [128, 128], BF16) for i in range(3)]
        LT_sb = [sbC(f"LT_sb{i}", [128, 128], BF16) for i in range(3)]
        X_sb = [sbC(f"X_sb{i}", [128, 128], BF16) for i in range(6)]
        RRT = [[sbC(f"RRT{s}_{l}", [128, 256], BF16) for l in range(NLEV)] for s in range(3)]
        def rec_head(c, cp, h):
            Pp, Ps, Pm = _Rec(), _Rec(), _Rec()
            t, hw = h // 2, h % 2
            wp = h % 3
            bA, bB = f"PSB{wp}", f"PSB{3 + wp}"
            pA, pB = psb[wp], psb[3 + wp]
            kf = ktf[64 * hw:64 * hw + 64, t * R + c * 128: t * R + c * 128 + 128]
            qf = qtf[64 * hw:64 * hw + 64, t * R + c * 128: t * R + c * 128 + 128]
            rk, rq = f"res:k:{t}", f"res:q:{t}"
            Xc, Xn = X_sb[2 * wp], X_sb[2 * wp + 1]
            XcK, XnK = f"X{2 * wp}", f"X{2 * wp + 1}"
            Sb_sl = S_b[64 * hw:64 * hw + 64, h * 64:(h + 1) * 64]
            Pp.add("tensor", lambda e, kf=kf, pA=pA, hw=hw: e.matmul(pA[:, 0:128], kf, kf, start=True, stop=True,
                  tile_position=(64 * hw, 0)),
                  reads=(rk,), writes=(bA,))
            Pp.add("tensor", lambda e, kf=kf, qf=qf, pA=pA, hw=hw: e.matmul(pA[:, 128:256], kf, qf, start=True, stop=True,
                  tile_position=(64 * hw, 0)),
                  reads=(rk, rq), writes=(bA,))
            Pp.add("tensor", lambda e, kf=kf, pA=pA, Sb_sl=Sb_sl, hw=hw: e.matmul(
                pA[:, 256:320], kf, Sb_sl, start=True, stop=True, tile_position=(64 * hw, 0)),
                reads=(rk, f"S_b:{h}"), writes=(bA,))
            Pp.add("scalar", lambda e, pA=pA, wp=wp: e.copy(GsL[wp][:], pA[:, 0:320]),
                  reads=(bA,), writes=(f"GsL{wp}",))
            Ps.add("gpsimd", lambda e, wp=wp: e.tensor_tensor(A_sb[wp][:], GsL[wp][:, 0:128], mask_sl[:], op=OP.mult),
                  reads=(f"GsL{wp}", "mask_sl"), writes=(f"A{wp}",))
            Ps.add("gpsimd", lambda e, wp=wp: e.tensor_tensor(AT_sb[wp][:], GsL[wp][:, 0:128], mask_sun[:], op=OP.mult),
                  reads=(f"GsL{wp}", "mask_sun"), writes=(f"AT{wp}",))
            Ps.add("gpsimd", lambda e, wp=wp: e.tensor_tensor(LT_sb[wp][:], GsL[wp][:, 128:256], mask_sui[:], op=OP.mult),
                  reads=(f"GsL{wp}", "mask_sui"), writes=(f"LT{wp}",))
            Ps.add("scalar", lambda e, Xc=Xc, cp=cp, h=h: e.mul(
                Xc[:, 0:64], krm[cp][:, h * 128:h * 128 + 64], -1.0),
                reads=(f"krm{cp}",), writes=(XcK,))
            Ps.add("vector", lambda e, Xc=Xc, cp=cp, h=h, wp=wp: e.tensor_sub(
                Xc[:, 64:128], vrm[cp][:, h * 64:(h + 1) * 64], GsL[wp][:, 256:320]),
                reads=(f"vrm{cp}", f"GsL{wp}"), writes=(XcK,))
            def sqevac(dst, src, rd, wr, lev=0, wp=wp):
                if wp == 0 or (wp == 2 and lev % 2 == 1):
                    Ps.add("scalar", lambda e, dst=dst, src=src: e.copy(dst, src), reads=rd, writes=wr)
                else:
                    Ps.add("vector", lambda e, dst=dst, src=src: e.tensor_copy(dst, src), reads=rd, writes=wr)
            prevR, prevRK = A_sb[wp][:], f"A{wp}"
            prevRT, prevRTK = AT_sb[wp][:], f"AT{wp}"
            for lev in range(1, NLEV + 1):
                buf = RRT[wp][lev - 1]
                bK = f"RRT{wp}:{lev}"
                if lev < NLEV:
                    Ps.add("tensor", lambda e, pA=pA, prevR=prevR, prevRT=prevRT: e.matmul(
                        pA[:, 0:128], prevRT, prevR, start=True, stop=True),
                        reads=(prevRK, prevRTK), writes=(bA,))
                Ps.add("tensor", lambda e, pA=pA, prevR=prevR, prevRT=prevRT: e.matmul(
                    pA[:, 128:256], prevR, prevRT, start=True, stop=True),
                    reads=(prevRK, prevRTK), writes=(bA,))
                if lev < NLEV:
                    sqevac(buf[:], pA[:, 0:256], (bA,), (bK,), lev=lev)
                    prevR, prevRK = buf[:, 0:128], bK
                    prevRT, prevRTK = buf[:, 128:256], bK
                else:
                    sqevac(buf[:, 128:256], pA[:, 128:256], (bA,), (bK,), lev=lev)
            for j in list(range(NLEV, 0, -1)) + [0]:
                if j > 0:
                    rt, rtK = RRT[wp][j - 1][:, 128:256], f"RRT{wp}:{j}"
                else:
                    rt, rtK = AT_sb[wp][:], f"AT{wp}"
                Pm.add("tensor", lambda e, pB=pB, rt=rt, Xc=Xc: e.matmul(
                    pB[:, 0:128], rt, Xc[:], start=True, stop=True),
                    reads=(rtK, XcK), writes=(bB,))
                Pm.add("vector", lambda e, pB=pB, Xn=Xn, Xc=Xc: e.tensor_add(Xn[:], Xc[:], pB[:, 0:128]),
                      reads=(bB, XcK), writes=(XnK,))
                Xc, Xn = Xn, Xc
                XcK, XnK = XnK, XcK
            Pm.add("tensor", lambda e, qf=qf, pB=pB, Sb_sl=Sb_sl, hw=hw: e.matmul(
                pB[:, 128:192], qf, Sb_sl, start=True, stop=False, tile_position=(64 * hw, 0)),
                reads=(rq, f"S_b:{h}"), writes=(bB,))
            Pm.add("tensor", lambda e, pB=pB, wp=wp, Xc=Xc: e.matmul(
                pB[:, 128:192], LT_sb[wp][:], Xc[:, 64:128], start=False, stop=True),
                reads=(f"LT{wp}", XcK), writes=(bB,))
            Pm.add("tensor", lambda e, pB=pB, cp=cp, h=h, Xc=Xc: e.matmul(
                pB[:, 192:256], krm[cp][:, h * 128:(h + 1) * 128], Xc[:, 64:128], start=True, stop=True),
                reads=(f"krm{cp}", XcK), writes=(bB,))
            Pm.add("tensor", lambda e, pB=pB: e.matmul(
                pB[0:64, 256:320], ident64[0:64, :], ident64[0:64, :], start=True, stop=False),
                reads=("ident64",), writes=(bB,))
            Pm.add("tensor", lambda e, pB=pB, cp=cp, h=h, Xc=Xc: e.matmul(
                pB[0:64, 256:320], Xc[:, 0:64], krm[cp][:, h * 128:h * 128 + 64], start=False, stop=True),
                reads=(f"krm{cp}", XcK), writes=(bB,))
            Pm.add("tensor", lambda e, pB=pB, qf=qf, hw=hw: e.matmul(
                pB[64 * hw:64 * hw + 64, 320:448], ident64[64 * hw:64 * hw + 64, :], qf, start=True, stop=False,
                tile_position=(64 * hw, 64 * hw)),
                reads=("ident64", rq), writes=(bB,))
            Pm.add("tensor", lambda e, pB=pB, wp=wp, Xc=Xc, hw=hw: e.matmul(
                pB[64 * hw:64 * hw + 64, 320:448], Xc[:, 0:64], LT_sb[wp][:], start=False, stop=True,
                tile_position=(0, 64 * hw)),
                reads=(f"LT{wp}", XcK), writes=(bB,))
            Pm.add("vector", lambda e, pB=pB, h=h: e.tensor_add(
                S_f[:, h * 64:(h + 1) * 64], S_f[:, h * 64:(h + 1) * 64], pB[:, 192:256]),
                reads=(bB, f"S_f:{h}"), writes=(f"S_f:{h}",))
            Pm.add("scalar", lambda e, pB=pB, cp=cp, h=h: e.copy(
                ptstg[cp][:, h * 64:(h + 1) * 64], pB[0:64, 256:320]),
                reads=(bB,), writes=(f"PT:{c}:{h}",))
            Pm.add("scalar", lambda e, h=h: e.copy(S_b[:, h * 64:(h + 1) * 64], S_f[:, h * 64:(h + 1) * 64]),
                  reads=(f"S_f:{h}",), writes=(f"S_b:{h}",))
            Pm.add("scalar", lambda e, pB=pB, cp=cp, h=h: e.copy(
                o0stg[cp][:, h * 64:(h + 1) * 64], pB[:, 128:192]),
                reads=(bB,), writes=(f"o0stg{cp}",))
            Pm.add("scalar", lambda e, pB=pB, cp=cp, t=t, hw=hw: e.copy(
                qtstg[cp][64 * hw:64 * hw + 64, t * 128:(t + 1) * 128], pB[64 * hw:64 * hw + 64, 320:448]),
                reads=(bB,), writes=(f"qtstg{cp}",))
            Pm.add("tensor", lambda e, pB=pB, cp=cp, h=h: e.matmul(
                pB[0:64, 448:512], ptstg[cp][:, h * 64:(h + 1) * 64], PP_b[0:64, h * 64:(h + 1) * 64],
                start=True, stop=True),
                reads=(f"PT:{c}:{h}", f"PP_b:{h}"), writes=(bB,))
            Pm.add("vector", lambda e, pB=pB, h=h: e.tensor_copy(PP_b[0:64, h * 64:(h + 1) * 64], pB[0:64, 448:512]),
                  reads=(bB,), writes=(f"PP_b:{h}",))
            return Pp, Ps, Pm

        prev_rests = None
        prev_post = None
        for c in range(NC):
            cp = c % 2
            pre = [
                (lambda e, c=c, cp=cp: e.dma_start(out=krm[cp][:], in_=krm_d[c * 128:(c + 1) * 128, :]),
                 (f"krmd:{c}:0", f"krmd:{c}:1"), (f"krm{cp}",)),
                (lambda e, c=c, cp=cp: e.dma_start(out=vrm[cp][:], in_=vrm_d[c * 128:(c + 1) * 128, :]),
                 (f"vrmd:{c}",), (f"vrm{cp}",)),
            ]
            for g0 in range(0, H, 3):
                recs = [rec_head(c, cp, h) for h in range(g0, min(g0 + 3, H))]
                for fn, rd, wr in pre:
                    P.add("sp", fn, reads=rd, writes=wr)
                pre = []
                if prev_rests is None:
                    _merge_rr(P, [r[0] for r in recs])
                else:
                    _merge_rr(P, prev_rests + [r[0] for r in recs])
                    for fn, rd, wr in (prev_post or ()):
                        P.add("aq", fn, reads=rd, writes=wr)
                    prev_post = None
                _merge_rr(P, [r[1] for r in recs])
                prev_rests = [r[2] for r in recs]
            prev_post = [
                (lambda e, c=c, cp=cp: e.dma_start(out=o0_d[c * 128:(c + 1) * 128, :], in_=o0stg[cp][:]),
                 (f"o0stg{cp}",), (f"o0d:{c}",)),
                (lambda e, c=c, cp=cp: e.dma_start(
                    out=qt_d[:, c * KT * 128:(c + 1) * KT * 128], in_=qtstg[cp][:]),
                 (f"qtstg{cp}",), (f"qtd:{c}",)),
                (lambda e, c=c, cp=cp: e.dma_start(
                    out=pt_d[:, c * H * 64:(c + 1) * H * 64], in_=ptstg[cp][:]),
                 tuple(f"PT:{c}:{hh}" for hh in range(H)), (f"ptd:{c}",)),
            ]
        _merge_rr(P, prev_rests)
        for fn, rd, wr in prev_post:
            P.add("aq", fn, reads=rd, writes=wr)
        P.mark("C_done")
        ccstg = sbC("ccstg", [64, H * 128], F32)
        for h in range(H):
            pt = h % 2
            pst = psT0 if pt == 0 else psT1
            pstk = "PSB6" if pt == 0 else "PSB7"
            P.add("tensor", lambda e, h=h, pst=pst: e.transpose(
                pst[0:64, 0:64], PP_b[0:64, h * 64:(h + 1) * 64], ident_b[0:64, 0:64]),
                reads=(f"PP_b:{h}", "ident_b"), writes=(pstk,))
            P.add("scalar", lambda e, h=h, pst=pst: e.copy(ccstg[:, h * 128:h * 128 + 64], pst[0:64, 0:64]),
                  reads=(pstk,), writes=("ccstg",))
            P.add("vector", lambda e, h=h: e.tensor_copy(ccstg[:, h * 128 + 64:h * 128 + 128], S_f[0:64, h * 64:(h + 1) * 64]),
                  reads=(f"S_f:{h}",), writes=("ccstg",))
        if part == "full":
            P.add("gq", lambda e: e.dma_start(out=cc_in_d[:, :], in_=ccstg[:]), reads=("ccstg",), writes=("cc_in",))
            P.add("cc", lambda e: e.collective_compute(
                "AllGather", OP.bypass, replica_groups=[list(range(n_cores))],
                ins=[cc_in_d[:, :]], outs=[cc_out_d[:, :]]),
                reads=("cc_in",), writes=("cc_out",))
        else:
            P.add("aq", lambda e: e.dma_start(out=cc_out1_d[:, :], in_=ccstg[:]), reads=("ccstg",), writes=("cc_out",))
            P.add("gpsimd", lambda e: e.engine_nop(),
                  reads=("cc_out", "msrow_d") + tuple(f"o0d:{c}" for c in range(NC))
                        + tuple(f"qtd:{c}" for c in range(NC)) + tuple(f"ptd:{c}" for c in range(NC)))
        P.barrier()
        stC.close()
        stABC.close()
        P.barrier()

    if run_p2:
        stD = ExitStack()
        sbD = lambda name, shape, dtype: stD.enter_context(nc.sbuf_tensor(name, shape, dtype))
        ccg = [sbD(f"ccgm{i}", [128, H * 192], F32) for i in range(2)]
        Sm = sbD("Sm", [128, H * 64], F32)
        Sacc = sbD("Sacc", [128, H * 64], F32)
        sel_tmp = sbD("sel_tmp", [128, H * 64], F32)
        ohbc = sbD("ohbc", [128, n_cores], F32)
        P.add("gpsimd", lambda e: e.memset(Sm[:], 0.0), writes=("Sm",))
        P.add("gpsimd", lambda e: e.memset(Sacc[:], 0.0), writes=("Sacc",))
        P.add("tensor", lambda e: e.matmul(psb[5][:, 0:n_cores], ones1[:], onehot[:], start=True, stop=True),
              reads=("ones1", "onehot"), writes=("PSB5",))
        P.add("vector", lambda e: e.tensor_copy(ohbc[:], psb[5][:, 0:n_cores]), reads=("PSB5",), writes=("ohbc",))
        for m in range(n_cores):
            if m < n_cores - 1:
                mp = m % 2
                for half in range(2):
                    for dup in range(2):
                        P.add("sp", lambda e, m=m, mp=mp, half=half, dup=dup: e.dma_start(
                            out=ccg[mp][64 * half:64 * half + 64, :].rearrange("p (h x d) -> p h x d", h=H, x=3)[:, :, dup],
                            in_=cc_out_d[m * 64:(m + 1) * 64, :].rearrange("p (h x d) -> p h x d", h=H, x=2)[:, :, 0]),
                            reads=("cc_out",), writes=(f"ccgm{mp}",))
                    P.add("sp", lambda e, m=m, mp=mp, half=half: e.dma_start(
                        out=ccg[mp][64 * half:64 * half + 64, :].rearrange("p (h x d) -> p h x d", h=H, x=3)[:, :, 2],
                        in_=cc_out_d[m * 64:(m + 1) * 64, :].rearrange("p (h x d) -> p h x d", h=H, x=2)[:, :, 1]),
                        reads=("cc_out",), writes=(f"ccgm{mp}",))
            P.add("vector", lambda e, m=m: e.scalar_tensor_tensor(
                sel_tmp[:], Sm[:], ohbc[:, m:m + 1], Sacc[:], op0=OP.mult, op1=OP.add),
                reads=("Sm", "ohbc", "Sacc"), writes=("sel_tmp",))
            P.add("vector", lambda e: e.tensor_copy(Sacc[:], sel_tmp[:]), reads=("sel_tmp",), writes=("Sacc",))
            if m < n_cores - 1:
                mp = m % 2
                for h in range(H):
                    P.add("tensor", lambda e, mp=mp, h=h: e.matmul(
                        psb[5][:, 64:128], ccg[mp][0:64, h * 192:h * 192 + 128],
                        Sm[0:64, h * 64:(h + 1) * 64], start=True, stop=True),
                        reads=(f"ccgm{mp}", "Sm"), writes=("PSB5",))
                    P.add("vector", lambda e, mp=mp, h=h: e.tensor_add(
                        Sm[:, h * 64:(h + 1) * 64], psb[5][:, 64:128],
                        ccg[mp][:, h * 192 + 128:h * 192 + 192]),
                        reads=("PSB5", f"ccgm{mp}"), writes=("Sm",))
        P.add("vector", lambda e: e.tensor_copy(N_f[:], Sacc[:]), reads=("Sacc",),
              writes=tuple(f"N_f:{h}" for h in range(H)))
        P.add("scalar", lambda e: e.copy(N_b[:], N_f[:]), reads=tuple(f"N_f:{h}" for h in range(H)),
              writes=tuple(f"N_b:{h}" for h in range(H)))

        Z_sb = sbD("Z_sb", [128, KT * D], BF16)
        e1b = sbD("e1b", [128, 512], BF16)
        e2b = [sbD(f"e2b{i}", [128, 512], BF16) for i in range(4)]
        zacc = sbD("zacc", [128, 512], F32)
        ztmp = sbD("ztmp", [128, 512], F32)
        for ri in "RI":
            for c4 in range(4):
                P.add("gq", lambda e, ri=ri, c4=c4: e.dma_start(out=e2b[c4][:], in_=E2_d[ri][c4]),
                      writes=(f"e2b{c4}",))
            for it in range(4):
                for c4 in range(4):
                    P.add("gq", lambda e, ri=ri, it=it, c4=c4: e.dma_start(out=e1b[:], in_=E1_d[ri][c4, it]),
                          writes=("e1b",))
                    if c4 == 0:
                        P.add("vector", lambda e, c4=c4: e.tensor_tensor(zacc[:], e1b[:], e2b[c4][:], op=OP.mult),
                              reads=("e1b", f"e2b{c4}"), writes=("zacc",))
                    else:
                        P.add("vector", lambda e, c4=c4: e.tensor_tensor(ztmp[:], e1b[:], e2b[c4][:], op=OP.mult),
                              reads=("e1b", f"e2b{c4}"), writes=("ztmp",))
                        P.add("vector", lambda e: e.tensor_add(zacc[:], zacc[:], ztmp[:]),
                              reads=("zacc", "ztmp"), writes=("zacc",))
                if ri == "R":
                    P.add("scalar", lambda e, it=it: e.copy(Z_sb[:, it * D + 0:it * D + 512], zacc[:]),
                          reads=("zacc",), writes=("Z_sb",))
                    P.add("scalar", lambda e, it=it: e.copy(Z_sb[:, (4 + it) * D + 512:(4 + it) * D + 1024], zacc[:]),
                          reads=("zacc",), writes=("Z_sb",))
                else:
                    P.add("scalar", lambda e, it=it: e.copy(Z_sb[:, it * D + 512:it * D + 1024], zacc[:]),
                          reads=("zacc",), writes=("Z_sb",))
                    P.add("scalar", lambda e, it=it: e.mul(Z_sb[:, (4 + it) * D + 0:(4 + it) * D + 512], zacc[:], -1.0),
                          reads=("zacc",), writes=("Z_sb",))

        WoT_sb = sbD("WoT_sb", [128, KT * D], BF16)
        P.add("gq", lambda e: e.dma_start(out=WoT_sb[:].rearrange("p (a n) -> p a n", a=KT),
                                          in_=WT_d["o"].rearrange("(a p) n -> p a n", p=128)),
              writes=("WoT_sb",))
        msgD = sbD("msgD", [1, GS], F32)
        ssgD = sbD("ssgD", [1, GS], F32)
        xnfD = sbD("xnfD", [128, KT * GS], BF16)
        xtgD = sbD("xtgD", [128, KT * GS], BF16)
        msbcD = sbD("msbcD", [128, GS], BF16)
        sbcD = sbD("sbcD", [128, GS], BF16)
        o0b = [sbD(f"o0b{i}", [128, D], BF16) for i in range(2)]
        qtb = [sbD(f"qtb{i}", [128, KT * 128], BF16) for i in range(2)]
        ptb = [sbD(f"ptb{i}", [64, H * 128], BF16) for i in range(2)]
        Ofp = [sbD(f"Ofp{i}", [128, D], F32) for i in range(2)]
        ObfR = [sbD(f"ObfR{i}", [128, D], BF16) for i in range(2)]
        Ofeat = sbD("Ofeat", [128, KT * 128], BF16)
        osq = [sbD(f"osq{i}", [128, 128], F32) for i in range(2)]
        srow16 = sbD("srow16", [H, 128], F32)
        obc = [sbD(f"obc{i}", [128, 128], BF16) for i in range(2)]
        xtok = [sbD(f"xtok{i}", [128, D], F32) for i in range(2)]
        outsb = [sbD(f"outsb{i}", [128, D], F32) for i in range(2)]

        def emit_xnf_D(grp):
            c0 = 3 + grp * GS
            for t in range(KT):
                P.add("gq", lambda e, t=t, c0=c0: e.dma_start(
                    out=xtgD[:, t * GS:(t + 1) * GS], in_=x_t_d[t * 128:(t + 1) * 128, c0:c0 + GS]),
                    writes=(f"xtgD:{t}",))
            P.add("sp", lambda e, c0=c0: e.dma_start(out=msgD[:], in_=msrow_d[0:1, c0:c0 + GS]),
                  reads=("msrow_d",), writes=("msgD",))
            P.add("sp", lambda e, c0=c0: e.dma_start(out=ssgD[:], in_=msrow_d[1:2, c0:c0 + GS]),
                  reads=("msrow_d",), writes=("ssgD",))
            P.add("tensor", lambda e: e.matmul(psb[4][:, 0:GS], ones1[:], msgD[:], start=True, stop=True),
                  reads=("ones1", "msgD"), writes=("PSB4",))
            P.add("tensor", lambda e: e.matmul(psb[5][:, 0:GS], ones1[:], ssgD[:], start=True, stop=True),
                  reads=("ones1", "ssgD"), writes=("PSB5",))
            P.add("scalar", lambda e: e.copy(msbcD[:], psb[4][:, 0:GS]), reads=("PSB4",), writes=("msbcD",))
            P.add("scalar", lambda e: e.copy(sbcD[:], psb[5][:, 0:GS]), reads=("PSB5",), writes=("sbcD",))
            for t in range(KT):
                P.add("vector", lambda e, t=t: e.tensor_tensor(
                    xnfD[:, t * GS:(t + 1) * GS], xtgD[:, t * GS:(t + 1) * GS], msbcD[:], op=OP.add),
                    reads=(f"xtgD:{t}", "msbcD"), writes=(f"xnfD:{t}",))
                P.add("vector", lambda e, t=t: e.tensor_tensor(
                    xnfD[:, t * GS:(t + 1) * GS], xnfD[:, t * GS:(t + 1) * GS], sbcD[:], op=OP.mult),
                    reads=(f"xnfD:{t}", "sbcD"), writes=(f"xnfD:{t}",))

        CPG = GS // C
        for g in range(NG):
            emit_xnf_D(g)
            for ci in range(CPG):
                c = g * CPG + ci
                cp = c % 2
                P.add("sp", lambda e, c=c, cp=cp: e.dma_start(out=o0b[cp][:], in_=o0_d[c * 128:(c + 1) * 128, :]),
                      reads=(f"o0d:{c}",), writes=(f"o0b{cp}",))
                P.add("sp", lambda e, c=c, cp=cp: e.dma_start(
                    out=qtb[cp][:], in_=qt_d[:, c * KT * 128:(c + 1) * KT * 128]),
                    reads=(f"qtd:{c}",), writes=(f"qtb{cp}",))
                P.add("sp", lambda e, c=c, cp=cp: e.dma_start(out=xtok[cp][:], in_=x_tok_d[c * 128:(c + 1) * 128, :]),
                      writes=(f"xtok{cp}",))
                for dup in range(2):
                    P.add("sp", lambda e, c=c, cp=cp, dup=dup: e.dma_start(
                        out=ptb[cp][:].rearrange("p (h u d) -> p h u d", h=H, u=2)[:, :, dup],
                        in_=pt_d[:, c * H * 64:(c + 1) * H * 64].rearrange("p (h d) -> p h d", h=H)),
                        reads=(f"ptd:{c}",), writes=(f"ptb{cp}",))
                for h in range(H):
                    t, hw = h // 2, h % 2
                    P.add("tensor", lambda e, cp=cp, t=t, hw=hw, h=h: e.matmul(
                        psb[2 + hw][:, (h // 2) * 64:(h // 2) * 64 + 64],
                        qtb[cp][64 * hw:64 * hw + 64, t * 128:(t + 1) * 128],
                        N_b[64 * hw:64 * hw + 64, h * 64:(h + 1) * 64], start=True, stop=True,
                        tile_position=(64 * hw, 0)),
                        reads=(f"qtb{cp}", f"N_b:{h}"), writes=(f"PSB{2 + hw}",))
                for h in range(H):
                    hw = h % 2
                    P.add("tensor", lambda e, h=h, cp=cp, hw=hw: e.matmul(
                        psb[4 + hw][:, (h // 2) * 64:(h // 2) * 64 + 64],
                        ptb[cp][:, h * 128:(h + 1) * 128],
                        N_b[0:64, h * 64:(h + 1) * 64], start=True, stop=True),
                        reads=(f"ptb{cp}", f"N_b:{h}"), writes=(f"PSB{4 + hw}",))
                allN = tuple(f"N_b:{hh}" for hh in range(H))
                allNf = tuple(f"N_f:{hh}" for hh in range(H))
                for hw in range(2):
                    P.add("vector", lambda e, cp=cp, hw=hw: e.tensor_add(
                        Ofp[cp][:].rearrange("p (h x d) -> p h x d", h=8, x=2)[:, :, hw],
                        o0b[cp][:].rearrange("p (h x d) -> p h x d", h=8, x=2)[:, :, hw],
                        psb[2 + hw][:, 0:512].rearrange("p (h d) -> p h d", h=8)),
                        reads=(f"o0b{cp}", f"PSB{2 + hw}"), writes=(f"Ofp{cp}",))
                    P.add("vector", lambda e, hw=hw: e.tensor_copy(
                        N_f[:].rearrange("p (h x d) -> p h x d", h=8, x=2)[:, :, hw],
                        psb[4 + hw][:, 0:512].rearrange("p (h d) -> p h d", h=8)),
                        reads=(f"PSB{4 + hw}",), writes=allNf)
                P.add("scalar", lambda e: e.copy(N_b[:], N_f[:]), reads=allNf, writes=allN)
                for half in range(2):
                    P.add("scalar", lambda e, cp=cp, half=half: e.copy(
                        ObfR[cp][:, half * 512:half * 512 + 512], Ofp[cp][:, half * 512:half * 512 + 512]),
                        reads=(f"Ofp{cp}",), writes=(f"ObfR{cp}",))
                for t in range(KT):
                    pt = t % 2
                    pst = psT0 if pt == 0 else psT1
                    pstk = "PSB6" if pt == 0 else "PSB7"
                    P.add("tensor", lambda e, cp=cp, t=t, pst=pst: e.transpose(
                        pst[:, 0:128], ObfR[cp][:, t * 128:(t + 1) * 128], ident_b[:]),
                        reads=(f"ObfR{cp}", "ident_b"), writes=(pstk,))
                    P.add("scalar", lambda e, t=t, pst=pst: e.copy(Ofeat[:, t * 128:(t + 1) * 128], pst[:, 0:128]),
                          reads=(pstk,), writes=(f"Ofeat:{t}",))
                    P.add("scalar", lambda e, t=t: e.activation(osq[t % 2][:], Ofeat[:, t * 128:(t + 1) * 128], AF.Square),
                          reads=(f"Ofeat:{t}",), writes=(f"osq{t % 2}",))
                    P.add("tensor", lambda e, t=t: e.matmul(
                        psb[4][0:H, 0:128], ind[:, t * H:(t + 1) * H], osq[t % 2][:],
                        start=(t == 0), stop=(t == KT - 1)),
                        reads=("ind", f"osq{t % 2}"), writes=("PSB4",))
                P.add("scalar", lambda e: e.activation(srow16[:], psb[4][0:H, 0:128], AF.Sqrt, scale=1.0 / dh, bias=epsrms[0:H, :]),
                      reads=("PSB4", "epsrms"), writes=("srow16",))
                P.add("vector", lambda e: e.reciprocal(srow16[:], srow16[:]), reads=("srow16",), writes=("srow16",))
                for t in range(KT):
                    P.add("tensor", lambda e, t=t: e.matmul(
                        psb[5][:, 0:128], indT[:, t * 128:(t + 1) * 128], srow16[:], start=True, stop=True),
                        reads=("indT", "srow16"), writes=("PSB5",))
                    P.add("scalar", lambda e, t=t: e.activation(obc[t % 2][:], psb[5][:, 0:128], AF.Identity, scale=onw[:]),
                          reads=("PSB5", "onw"), writes=(f"obc{t % 2}",))
                    P.add("vector", lambda e, t=t: e.tensor_tensor(
                        Ofeat[:, t * 128:(t + 1) * 128], Ofeat[:, t * 128:(t + 1) * 128], obc[t % 2][:], op=OP.mult),
                        reads=(f"Ofeat:{t}", f"obc{t % 2}"), writes=(f"Ofeat:{t}",))
                for half in range(2):
                    bo = half
                    for t in range(KT):
                        P.add("tensor", lambda e, t=t, half=half, bo=bo: e.matmul(
                            psb[bo][:, 0:512], Ofeat[:, t * 128:(t + 1) * 128],
                            WoT_sb[:, t * D + half * 512: t * D + half * 512 + 512],
                            start=(t == 0), stop=False),
                            reads=(f"Ofeat:{t}", "WoT_sb"), writes=(f"PSB{bo}",))
                    for t in range(KT):
                        P.add("tensor", lambda e, t=t, ci=ci, half=half, bo=bo: e.matmul(
                            psb[bo][:, 0:512], xnfD[:, t * GS + ci * C: t * GS + ci * C + 128],
                            Z_sb[:, t * D + half * 512: t * D + half * 512 + 512],
                            start=False, stop=False),
                            reads=(f"xnfD:{t}", "Z_sb"), writes=(f"PSB{bo}",))
                    P.add("tensor", lambda e, half=half, bo=bo: e.matmul(
                        psb[bo][:, 0:512], ones1[:], biasrow[0:1, half * 512:half * 512 + 512],
                        start=False, stop=True),
                        reads=("ones1", "biasrow"), writes=(f"PSB{bo}",))
                    P.add("vector", lambda e, cp=cp, half=half, bo=bo: e.tensor_add(
                        outsb[cp][:, half * 512:half * 512 + 512], psb[bo][:, 0:512],
                        xtok[cp][:, half * 512:half * 512 + 512]),
                        reads=(f"PSB{bo}", f"xtok{cp}"), writes=(f"outsb{cp}",))
                P.add("sp", lambda e, c=c, cp=cp: e.dma_start(out=out_d[c * 128:(c + 1) * 128, :], in_=outsb[cp][:]),
                      reads=(f"outsb{cp}",), writes=("out_final",))

        P.add("gpsimd", lambda e: e.engine_nop(), reads=("out_final",))

    P.mark("end")
    import os
    if os.environ.get("KPRINT_MARKS"):
        print("MARKS:", P.marks)
    P.finalize(truncate=cfg.trunc, skip=getattr(cfg, 'skip_items', ()))
    P.emit(nc, sems)
    return nc


def host_in_maps(cfg: CFG, inputs):
    import ml_dtypes
    n = cfg.n_cores
    R, D, H, KT = cfg.R, cfg.D, cfg.H, cfg.KT
    x = np.asarray(inputs["L0_attn_x"], np.float32).reshape(cfg.L, D)
    WT = {p: np.ascontiguousarray(np.asarray(inputs[f"L0_attn_W{p}"], np.float32).T) for p in "qkvo"}
    WbT = np.ascontiguousarray(np.asarray(inputs["L0_attn_Wb"], np.float32).T)
    onw = np.ascontiguousarray(np.tile(np.asarray(inputs["L0_attn_o_norm_w"], np.float32), 2)[:, None])
    biasrow = np.ascontiguousarray(np.concatenate([
        np.asarray(inputs["L0_ffn_bias_real"], np.float32),
        np.asarray(inputs["L0_ffn_bias_imag"], np.float32)])[None, :])

    def expand(G1, G2):
        G1 = np.asarray(G1, np.float32)[0]
        G2 = np.asarray(G2, np.float32)
        d_of_p = np.arange(128) // 16
        g_of_p = np.arange(128) % 16
        E1 = np.empty((4, 4, 128, 512), np.float32)
        E2 = np.empty((4, 128, 512), np.float32)
        for c4 in range(4):
            colB = G2[c4, :, 0, :]
            E2[c4] = colB.T[g_of_p][:, None, :].repeat(32, axis=1).reshape(128, 512)
            for it in range(4):
                vals = G1[:, c4, it * 8 + d_of_p]
                E1[c4, it] = vals.T[:, :, None].repeat(16, axis=2).reshape(128, 512)
        return E1, E2
    E1R, E2R = expand(inputs["L0_ffn_G1_R"], inputs["L0_ffn_G2_R"])
    E1I, E2I = expand(inputs["L0_ffn_G1_I"], inputs["L0_ffn_G2_I"])

    i_ = np.arange(128)
    mask_sl = -((i_[None, :] < i_[:, None]).astype(np.float32))
    mask_sun = -((i_[None, :] > i_[:, None]).astype(np.float32))
    mask_sui = (i_[None, :] >= i_[:, None]).astype(np.float32)
    ident_f = np.eye(128, dtype=np.float32)
    ident_b = np.eye(128).astype(ml_dtypes.bfloat16)
    ident64_2 = np.ascontiguousarray(np.tile(np.eye(64), (2, 1)).astype(ml_dtypes.bfloat16))
    ones1 = np.ones((1, 128), np.float32)
    ppinit = np.ascontiguousarray(np.tile(np.eye(64), (1, H)).astype(ml_dtypes.bfloat16))
    ind = np.zeros((128, KT * H), np.float32)
    indT = np.zeros((H, KT * 128), np.float32)
    for t in range(KT):
        for p in range(128):
            h = 2 * t + (1 if p >= 64 else 0)
            ind[p, t * H + h] = 1.0
            indT[h, t * 128 + p] = 1.0

    in_maps = []
    for i in range(n):
        r0 = i * R
        xt_sl = np.zeros((D, R + 3), np.float32)
        lo = max(r0 - 3, 0)
        xt_sl[:, 3 - (r0 - lo):] = x[lo:r0 + R].T
        halo = np.zeros((3, D), np.float32)
        if r0 >= 3:
            halo[:] = x[r0 - 3:r0]
        oh = np.zeros((1, n), np.float32)
        oh[0, i] = 1.0
        m = {
            "x_tok": np.ascontiguousarray(x[r0:r0 + R]),
            "x_t": np.ascontiguousarray(xt_sl),
            "x_halo": halo,
            "WbT": WbT,
            "conv_q": np.asarray(inputs["L0_attn_conv_q"], np.float32),
            "conv_k": np.asarray(inputs["L0_attn_conv_k"], np.float32),
            "conv_v": np.asarray(inputs["L0_attn_conv_v"], np.float32),
            "onw128": onw, "biasrow": biasrow,
            "E1R": E1R, "E2R": E2R, "E1I": E1I, "E2I": E2I,
            "mask_sl": mask_sl, "mask_sun": mask_sun, "mask_sui": mask_sui,
            "ident_f": ident_f, "ident_b": ident_b, "ident64_2": ident64_2,
            "ones1": ones1, "ppinit": ppinit, "onehot": oh,
            "ind": ind, "indT": indT,
        }
        for p in "qkvo":
            m[f"W{p}T"] = WT[p]
        in_maps.append(m)
    return in_maps


MID_TENSORS = ("o0_dram", "qt_dram", "pt_dram", "msrow_dram")


def run_pipeline(cfg, inputs, launcher):
    base_maps = host_in_maps(cfg, inputs)
    n = cfg.n_cores
    nc1 = build_program(cfg, "p1")
    res1 = launcher(nc1, base_maps)
    ccg = np.ascontiguousarray(np.concatenate(
        [np.asarray(r["cc_stage_out"], np.float32) for r in res1], axis=0))
    maps2 = []
    for i, m in enumerate(base_maps):
        m2 = dict(m)
        for t in MID_TENSORS:
            m2[t] = np.asarray(res1[i][t])
        m2["cc_gathered"] = ccg
        maps2.append(m2)
    nc2 = build_program(cfg, "p2")
    res2 = launcher(nc2, maps2)
    out = np.concatenate([np.asarray(r["out"], np.float32) for r in res2], axis=0)
    return out


def kernel(**inputs):
    cfg = CFG()

    def launcher(nc, in_maps):
        res = run_bass_kernel_spmd(nc, in_maps, core_ids=list(range(cfg.n_cores)))
        return res.results
    out = run_pipeline(cfg, inputs, launcher)
    return out.reshape(1, cfg.L, cfg.D)
